# version 12
# baseline (speedup 1.0000x reference)
import numpy as np
import ml_dtypes
from contextlib import ExitStack

import concourse.bass as bass
import concourse.mybir as mybir
from concourse.bass_utils import run_bass_kernel_spmd

F32 = mybir.dt.float32
BF16 = mybir.dt.bfloat16
Alu = mybir.AluOpType
Act = mybir.ActivationFunctionType
Ax = mybir.AxisListType
BF = ml_dtypes.bfloat16

D, DC = 1024, 8
INNER, IC = 2048, 16
T, CH, NCH = 2048, 512, 4
NH, DH = 8, 64
HID, HIDP = 2730, 2816
F1P = 5632
NBK = 22
HALO = 516
EPS = 1e-5
NEG = -1.0e30
R1024 = 1.0 / 1024.0
RG = [[0, 1], [2, 3], [4, 5], [6, 7]]



def _feat_tiles(v):
    v = np.asarray(v, np.float32)
    return np.ascontiguousarray(v.reshape(v.shape[0] // 128, 128).T)


def _rotary_tables(t0):
    n = 4096
    pos = np.clip(np.arange(t0 - 512, t0 + 2048, dtype=np.float64), 0, n - 1)
    half = np.arange(0, DH, 2, dtype=np.float64)
    inv_freq = 1.0 / 10000.0 ** (half / DH)
    fr = pos[:, None] * inv_freq[None, :]
    fr = np.concatenate([fr, fr], axis=1)
    base = (half + 0.4 * DH) / (1.4 * DH)
    sc = base[None, :] ** ((pos - n // 2) / 256.0)[:, None]
    sc = np.concatenate([sc, sc], axis=1)
    sign = np.where(np.arange(DH) < 32, -1.0, 1.0)[None, :]
    tab = np.concatenate([np.cos(fr) * sc, np.sin(fr) * sc * sign,
                          np.cos(fr) / sc, np.sin(fr) / sc * sign], axis=1)
    tab = tab.reshape(20, 128, 256).transpose(1, 0, 2)
    hi = tab.astype(np.float32)
    lo = (tab - hi.astype(np.float64)).astype(np.float32)
    return np.ascontiguousarray(hi), np.ascontiguousarray(lo)


def _masks(first_half):
    i = np.arange(CH)[:, None]
    j = np.arange(1024)[None, :]
    mn = np.where((j >= i) & (j <= i + 512), 0.0, NEG)
    mc0 = np.where((j >= i) & (j <= i + 512) & (j >= 512), 0.0, NEG) \
        if first_half else mn
    lay = lambda m: np.ascontiguousarray(
        m.reshape(4, 128, 1024).transpose(1, 0, 2).astype(BF))
    return lay(mn), lay(mc0)


def _pad_w1(w1, b1):
    a = np.zeros((D, HIDP), np.float32); a[:, :HID] = w1[:, :HID]
    g = np.zeros((D, HIDP), np.float32); g[:, :HID] = w1[:, HID:]
    ba = np.zeros(HIDP, np.float32); ba[:HID] = b1[:HID]
    bg = np.zeros(HIDP, np.float32); bg[:HID] = b1[HID:]
    w1p = np.empty((D, F1P), np.float32)
    b1p = np.empty(F1P, np.float32)
    for i in range(NBK):
        w1p[:, 256 * i:256 * i + 128] = a[:, 128 * i:128 * i + 128]
        w1p[:, 256 * i + 128:256 * i + 256] = g[:, 128 * i:128 * i + 128]
        b1p[256 * i:256 * i + 128] = ba[128 * i:128 * i + 128]
        b1p[256 * i + 128:256 * i + 256] = bg[128 * i:128 * i + 128]
    return (np.ascontiguousarray(w1p),
            np.ascontiguousarray(b1p.reshape(F1P // 128, 128).T))


def prepare_inputs(inputs):
    x = np.asarray(inputs["L0_attn_x"])
    sh = {}
    sh["w_in"] = np.ascontiguousarray(np.asarray(inputs["L0_attn_W_in"], np.float32))
    sh["w_delta"] = np.ascontiguousarray(np.asarray(inputs["L0_attn_W_delta"], np.float32))
    sh["w_out"] = np.ascontiguousarray(np.asarray(inputs["L0_attn_W_out"], np.float32))
    sh["b_in_t"] = _feat_tiles(inputs["L0_attn_b_in"])
    cw = np.asarray(inputs["L0_attn_conv_w"], np.float32)
    sh["conv_w_t"] = np.ascontiguousarray(cw.reshape(IC, 128, 3).transpose(1, 0, 2))
    sh["conv_b_t"] = _feat_tiles(inputs["L0_attn_conv_b"])
    sh["b_delta_t"] = _feat_tiles(inputs["L0_attn_b_delta"])
    sh["b_out_t"] = _feat_tiles(inputs["L0_attn_b_out"])
    sh["mgam_t"] = _feat_tiles(inputs["L0_attn_ln_gamma"])
    sh["mbeta_t"] = _feat_tiles(inputs["L0_attn_ln_beta"])
    for l in range(2):
        sh[f"wqkv{l}"] = np.ascontiguousarray(np.asarray(inputs["L0_ffn_wqkv"][l], np.float32))
        sh[f"wo{l}"] = np.ascontiguousarray(np.asarray(inputs["L0_ffn_wo"][l], np.float32))
        w1p, b1t = _pad_w1(np.asarray(inputs["L0_ffn_w1"][l], np.float32),
                           np.asarray(inputs["L0_ffn_b1"][l], np.float32))
        sh[f"w1p{l}"], sh[f"b1p_t{l}"] = w1p, b1t
        w2p = np.zeros((HIDP, D), np.float32)
        w2p[:HID] = np.asarray(inputs["L0_ffn_w2"][l], np.float32)
        sh[f"w2p{l}"] = w2p
        sh[f"b2_t{l}"] = _feat_tiles(inputs["L0_ffn_b2"][l])
        sh[f"lng_t{l}"] = _feat_tiles(inputs["L0_ffn_ln_g"][l])
        sh[f"lnb_t{l}"] = _feat_tiles(inputs["L0_ffn_ln_b"][l])
        sh[f"qs{l}"] = np.ascontiguousarray(
            np.asarray(inputs["L0_ffn_q_scale"][l], np.float32).reshape(1, DH))
        sh[f"ks{l}"] = np.ascontiguousarray(
            np.asarray(inputs["L0_ffn_k_scale"][l], np.float32).reshape(1, DH))
    sh["id_bf"] = np.eye(128, dtype=BF)
    sh["id_f32"] = np.eye(128, dtype=np.float32)
    sh["ones_row"] = np.ones((1, 128), np.float32)
    sh["ones_col"] = np.ones((128, 1), np.float32)
    sh["eps_col"] = np.full((128, 1), EPS, np.float32)

    maps = []
    for c in range(8):
        b, h = c // 2, c % 2
        t0 = h * T
        d = dict(sh)
        d["x_own"] = np.ascontiguousarray(x[b, t0:t0 + T])
        hs = t0 - HALO if h == 1 else 0
        d["x_halo"] = np.ascontiguousarray(x[b, hs:hs + HALO])
        d["tabs_hi"], d["tabs_lo"] = _rotary_tables(t0)
        d["mask_n"], d["mask_c0"] = _masks(h == 0)
        one = np.ones((128, 1), np.float32)
        zero = np.zeros((128, 1), np.float32)
        d["sel_first"] = one if h == 0 else zero
        d["sel_second"] = zero if h == 0 else one
        maps.append(d)
    return maps



class Ctr:
    def __init__(self, sem, step=1):
        self.sem, self.n, self.step = sem, 0, step

    def inc(self, instr, k=None):
        k = self.step if k is None else k
        self.n += k
        instr.then_inc(self.sem, k)
        return self.n

    def dma(self, eng, **kw):
        eng.wait_ge(self.sem, self.n)
        self.n += 16
        eng.dma_start(**kw).then_inc(self.sem, 16)
        return self.n


class Q:
    def __init__(self, sems):
        self.sems = list(sems)
        self.cnt = [0] * len(self.sems)
        self.i = 0
        self.n = 0
        self.snap = {}

    def dma(self, eng, **kw):
        i = self.i
        self.i = (i + 1) % len(self.sems)
        if self.cnt[i]:
            eng.wait_ge(self.sems[i], self.cnt[i])
        self.cnt[i] += 16
        eng.dma_start(**kw).then_inc(self.sems[i], 16)
        self.n += 1
        self.snap[self.n] = tuple(self.cnt)
        return self.n

    def wait(self, eng, mark):
        if mark <= 0:
            return
        for s, c in zip(self.sems, self.snap[mark]):
            if c:
                eng.wait_ge(s, c)


def build_nc(debug=(), upto="full", detect=False):
    nc = bass.Bass("TRN2", target_bir_lowering=False,
                   detect_race_conditions=detect)
    dbg = set(debug)

    h_ = {}
    def din(name, shape, dt=F32):
        h_[name] = nc.dram_tensor(name, list(shape), dt, kind="ExternalInput").ap()

    din("x_own", (T, D)); din("x_halo", (HALO, D))
    din("w_in", (D, 2 * INNER)); din("w_delta", (INNER, INNER)); din("w_out", (INNER, D))
    din("b_in_t", (128, 32)); din("conv_w_t", (128, IC, 3)); din("conv_b_t", (128, IC))
    din("b_delta_t", (128, IC)); din("b_out_t", (128, DC))
    din("mgam_t", (128, DC)); din("mbeta_t", (128, DC))
    for l in range(2):
        din(f"wqkv{l}", (D, 1536)); din(f"wo{l}", (512, D))
        din(f"w1p{l}", (D, F1P)); din(f"b1p_t{l}", (128, F1P // 128))
        din(f"w2p{l}", (HIDP, D)); din(f"b2_t{l}", (128, DC))
        din(f"lng_t{l}", (128, DC)); din(f"lnb_t{l}", (128, DC))
        din(f"qs{l}", (1, DH)); din(f"ks{l}", (1, DH))
    din("tabs_hi", (128, 20, 256)); din("tabs_lo", (128, 20, 256))
    din("mask_n", (128, 4, 1024), BF16); din("mask_c0", (128, 4, 1024), BF16)
    din("id_bf", (128, 128), BF16); din("id_f32", (128, 128))
    din("ones_row", (1, 128)); din("ones_col", (128, 1)); din("eps_col", (128, 1))
    din("sel_first", (128, 1)); din("sel_second", (128, 1))

    out_h = nc.dram_tensor("out", [T, D], F32, kind="ExternalOutput").ap()

    def dint(name, shape, dt=F32):
        return nc.dram_tensor(name, list(shape), dt).ap()

    sp_xn = dint("sp_xn", (128, DC, T))
    sp_vs = dint("sp_vs", (128, IC, T))
    sp_sg = dint("sp_sg", (128, IC, T))
    sp_fw = dint("sp_fw", (128, IC, T))
    sp_pf = dint("sp_pf", (128, IC, T))
    sp_bw = dint("sp_bw", (128, IC, T))
    sp_pb = dint("sp_pb", (128, IC, T))
    sp_xat = dint("sp_xat", (128, DC, T))
    sp_xl1 = dint("sp_xl1", (128, DC, T))
    sp_s = dint("sp_s", (128, DC, T))
    sp_xh = dint("sp_xh", (128, DC, HALO), BF16)
    sp_tab = dint("sp_tab", (128, 20, 256))
    sp_brow = dint("sp_brow", (32, 128))
    cc1_in = dint("cc1_in", (128, 2 * IC))
    cc1_out = dint("cc1_out", (128, 2 * IC))
    cc2_in = dint("cc2_in", (128, DC, HALO))
    cc2_out = dint("cc2_out", (128, DC, HALO))
    cca_in = [dint(f"cca_in{l}", (128, DC)) for l in range(2)]
    cca_out = [dint(f"cca_out{l}", (128, DC)) for l in range(2)]

    dbg_h = {}
    def dtap(name):
        if name in dbg and name not in dbg_h:
            src = {"xn": sp_xn, "vs": sp_vs, "sg": sp_sg, "fw": sp_fw,
                   "pf": sp_pf, "bw": sp_bw, "pb": sp_pb, "xat": sp_xat,
                   "xl1": sp_xl1, "s": sp_s}.get(name)
            if src is None:
                return
            dbg_h[name] = (nc.dram_tensor("dbg_" + name, list(src.shape), F32,
                                          kind="ExternalOutput").ap(), src)

    for nm in dbg:
        dtap(nm)
    dbg_attn = {}
    if "attn0" in dbg:
        for nm, shape in [("qf0", (128, 4, CH)), ("kc0", (128, 4, CH)),
                          ("vc0", (128, 4, CH)), ("af0", (128, 4, CH))]:
            dbg_attn[nm] = nc.dram_tensor("dbg_" + nm, list(shape), BF16,
                                          kind="ExternalOutput").ap()
        dbg_attn["xat1"] = nc.dram_tensor("dbg_xat1", [128, DC, T], F32,
                                          kind="ExternalOutput").ap()
        dbg_attn["hn0"] = nc.dram_tensor("dbg_hn0", [128, DC, 1024], BF16,
                                         kind="ExternalOutput").ap()
        dbg_attn["h20"] = nc.dram_tensor("dbg_h20", [128, NBK, 1024], BF16,
                                         kind="ExternalOutput").ap()
        dbg_attn["rows0"] = nc.dram_tensor("dbg_rows0", [4, 1024], F32,
                                           kind="ExternalOutput").ap()
    if "attn0" in dbg:
        dbg_attn["blkb0"] = nc.dram_tensor("dbg_blkb0", [128, DC, 1025], F32,
                                           kind="ExternalOutput").ap()
    dbg_a1 = {}
    if "a1adbg" in dbg:
        dbg_a1["stat"] = nc.dram_tensor("dbg_stat", [128, 32], F32,
                                        kind="ExternalOutput").ap()
        dbg_a1["xntm"] = nc.dram_tensor("dbg_xntm", [128, 4, D], F32,
                                        kind="ExternalOutput").ap()
        dbg_a1["xtm"] = nc.dram_tensor("dbg_xtm", [128, 4, D], F32,
                                       kind="ExternalOutput").ap()
        dbg_attn["blkb0"] = nc.dram_tensor("dbg_blkb0", [128, DC, 1025], F32,
                                           kind="ExternalOutput").ap()

    ctx = ExitStack()
    with ctx:
        def sem(name):
            return ctx.enter_context(nc.semaphore(name))
        cTE, cVE, cAC = Ctr(sem("cTE")), Ctr(sem("cVE")), Ctr(sem("cAC"))
        cGP, cSY = Ctr(sem("cGP")), Ctr(sem("cSY"))
        cCC = Ctr(sem("cCC"))
        qS = Q([sem(f"qS{i}") for i in range(4)])
        qG = Q([sem(f"qG{i}") for i in range(3)])
        qC = Q([sem(f"qC{i}") for i in range(4)])

        block = ctx.enter_context(nc.Block())
        te, ve, ac = block.tensor, block.vector, block.scalar
        gp, sy = block.gpsimd, block.sync
        ECTR = {"te": cTE, "ve": cVE, "ac": cAC, "gp": cGP, "sy": cSY}
        EDEC = {"te": te, "ve": ve, "ac": ac, "gp": gp, "sy": sy}

        def barrier():
            def _dsy(e):
                qS.wait(e, qS.n)
            sy(_dsy)
            def _dgp(e):
                qG.wait(e, qG.n)
            gp(_dgp)
            def _dac(e):
                qC.wait(e, qC.n)
            ac(_dac)
            marks = {}
            def _ite(e):
                e.wait_ge(cVE.sem, cVE.n)
                cTE.inc(e.ldweights(id_bf[:, 0:4]))
            te(_ite)
            marks["te"] = cTE.n
            def _ive(e):
                e.wait_ge(cVE.sem, cVE.n)
                cVE.inc(e.tensor_copy(bsc[:, 0:1], bsc[:, 3:4]))
            ve(_ive)
            marks["ve"] = cVE.n
            def _iac(e):
                e.wait_ge(cAC.sem, cAC.n)
                cAC.inc(e.activation(bsc[:, 1:2], bsc[:, 3:4], Act.Copy,
                                     bias=0.0, scale=1.0))
            ac(_iac)
            marks["ac"] = cAC.n
            def _igp(e):
                e.wait_ge(cGP.sem, cGP.n)
                cGP.inc(e.tensor_copy(bsc[:, 2:3], bsc[:, 3:4]))
            gp(_igp)
            marks["gp"] = cGP.n
            def _isy(e):
                cSY.n += 1
                e.sem_inc(cSY.sem, 1)
            sy(_isy)
            marks["sy"] = cSY.n
            for name in EDEC:
                def _wait(e, me=name):
                    for o, tgt in marks.items():
                        if o != me:
                            e.wait_ge(ECTR[o].sem, tgt)
                EDEC[name](_wait)

        def sb(stack, name, shape, dt=F32):
            return stack.enter_context(nc.sbuf_tensor(name, list(shape), dt))

        mask_nS = sb(ctx, "mask_nS", (128, 4, 1024), BF16)
        mask_c0S = sb(ctx, "mask_c0S", (128, 4, 1024), BF16)
        id_bf = sb(ctx, "id_bfS", (128, 128), BF16)
        id_f32 = sb(ctx, "id_f32S", (128, 128))
        ones_row = sb(ctx, "ones_rowS", (1, 128))
        ones_col = sb(ctx, "ones_colS", (128, 1))
        eps_col = sb(ctx, "eps_colS", (128, 1))
        sel_first = sb(ctx, "sel_firstS", (128, 1))
        sel_second = sb(ctx, "sel_secondS", (128, 1))
        b_in = sb(ctx, "b_inS", (128, 32))
        beff = sb(ctx, "beffS", (128, 32))
        conv_w = sb(ctx, "conv_wS", (128, IC, 3))
        conv_b = sb(ctx, "conv_bS", (128, IC))
        b_delta = sb(ctx, "b_deltaS", (128, IC))
        nb_delta = sb(ctx, "nb_deltaS", (128, IC))
        b_out = sb(ctx, "b_outS", (128, DC))
        mgam = sb(ctx, "mgamS", (128, DC))
        mbeta = sb(ctx, "mbetaS", (128, DC))
        lvec = {}
        for l in range(2):
            for key, shape in [("b1p", (128, F1P // 128)), ("b2", (128, DC)),
                               ("lng", (128, DC)), ("lnb", (128, DC)),
                               ("qs", (1, DH)), ("ks", (1, DH))]:
                lvec[f"{key}{l}"] = sb(ctx, f"{key}S{l}", shape)
        bsc = sb(ctx, "bscS", (128, 4))
        bscb = sb(ctx, "bscbS", (128, 4), BF16)
        ccR1 = sb(ctx, "ccR1", (128, IC))
        ccR2 = sb(ctx, "ccR2", (128, IC))
        R1e = sb(ctx, "R1e", (128, IC))
        R2e = sb(ctx, "R2e", (128, IC))
        scA = sb(ctx, "scA", (128, DC))
        sideL1 = sb(ctx, "sideL1", (128, DC, 1))
        sideL2 = sb(ctx, "sideL2", (128, DC, 1))

        psW = ctx.enter_context(nc.psum_tensor("psW", [128, T], F32))
        psA = ctx.enter_context(nc.psum_tensor("psA", [128, 1024], F32))
        psT = ctx.enter_context(nc.psum_tensor("psT", [128, 1024], BF16))
        psR = ctx.enter_context(nc.psum_tensor("psR", [128, 512], F32))
        S512 = [psW[:, 512 * i:512 * (i + 1)] for i in range(4)] + \
               [psA[:, 0:512], psA[:, 512:1024]]
        S1024 = [psA[:, :], psW[:, 0:1024], psW[:, 1024:2048]]

        def psT3(f=128):
            return psT.ap().rearrange("p (c f) -> p c f", f=f)

        def psA3(f=128):
            return psA.ap().rearrange("p (c f) -> p c f", f=f)

        p0 = ExitStack()
        tb_lo = sb(p0, "tb_lo", (128, 20, 256))
        tb_hi = sb(p0, "tb_hi", (128, 20, 256))

        def _p0s(s):
            for nm, tile in [("mask_n", mask_nS), ("mask_c0", mask_c0S),
                             ("id_bf", id_bf), ("id_f32", id_f32),
                             ("ones_row", ones_row), ("ones_col", ones_col),
                             ("eps_col", eps_col),
                             ("sel_first", sel_first), ("sel_second", sel_second),
                             ("b_in_t", b_in), ("conv_w_t", conv_w),
                             ("conv_b_t", conv_b), ("b_delta_t", b_delta),
                             ("b_out_t", b_out), ("mgam_t", mgam),
                             ("mbeta_t", mbeta), ("tabs_hi", tb_hi),
                             ("tabs_lo", tb_lo)]:
                qS.dma(s, out=tile[:], in_=h_[nm][:])
            for l in range(2):
                for key, nm in [("b1p", f"b1p_t{l}"), ("b2", f"b2_t{l}"),
                                ("lng", f"lng_t{l}"), ("lnb", f"lnb_t{l}"),
                                ("qs", f"qs{l}"), ("ks", f"ks{l}")]:
                    qS.dma(s, out=lvec[f"{key}{l}"][:], in_=h_[nm][:])
        sy(_p0s)
        p0_ld = qS.n

        def _p0v(v):
            v.memset(bsc[:], 0.0)
            v.memset(bscb[:], 0.0)
            qS.wait(v, p0_ld)
            v.tensor_tensor(tb_hi[:], tb_hi[:], tb_lo[:], Alu.add)
            cVE.inc(v.tensor_scalar(nb_delta[:], b_delta[:], -1.0, None, Alu.mult))
        ve(_p0v)
        tabs_v = cVE.n

        def _p0s2(s):
            s.wait_ge(cVE.sem, tabs_v)
            qS.dma(s, out=sp_tab[:], in_=tb_hi[:])
        sy(_p0s2)
        tab_spill = qS.n
        p0.close()

        def _endprog():
            if dbg_h:
                def _tap(g):
                    for nm, (dst, src) in dbg_h.items():
                        qG.dma(g, out=dst[:], in_=src[:])
                    qG.wait(g, qG.n)
                gp(_tap)
            barrier()
            def _fin(s):
                qS.wait(s, qS.n)
            sy(_fin)

        a1 = ExitStack()
        w_inS = sb(a1, "w_inS", (128, DC, 2 * INNER), BF16)
        x_tm = sb(a1, "x_tm", (128, 4, D))
        xsq = sb(a1, "xsq", (128, 2, D))
        xn_tm = sb(a1, "xn_tm", (128, 4, D))
        xnm_tm = sb(a1, "xnm_tm", (128, 4, D), BF16)
        stat = sb(a1, "stat", (128, 16))
        st2 = sb(a1, "st2", (128, 16))
        xn_fm = sb(a1, "xn_fm", (128, DC, CH + 1))
        xnm_fm = sb(a1, "xnm_fm", (128, DC, CH), BF16)
        vals = sb(a1, "vals", (128, 2, CH + 2))
        vtail = sb(a1, "vtail", (128, IC, 2))
        sgb2 = [sb(a1, f"sgb2_{i}", (128, CH)) for i in range(2)]
        mbeta_bf = sb(a1, "mbeta_bf", (128, DC), BF16)
        brow_s = sb(a1, "brow_s", (1, 512))
        cb_eff = sb(a1, "cb_eff", (128, IC))
        nsel1 = sb(a1, "nsel1", (128, 1))
        sc1 = sb(a1, "sc1", (128, 1))
        hx = ExitStack()
        xh_fm = sb(hx, "xh_fm", (128, DC, HALO), BF16)
        xnmh_fm = sb(hx, "xnmh_fm", (128, DC, 128), BF16)

        def _a1g(g):
            qS.wait(g, tab_spill)
            wv = h_["w_in"].rearrange("(c p) f -> c p f", p=128)
            qG.dma(g, out=w_inS[:], in_=wv.transpose([1, 0, 2]))
        gp(_a1g)
        w_in_ld = qG.n

        def _mbv(v):
            qS.wait(v, p0_ld)
            cVE.inc(v.tensor_copy(mbeta_bf[:], mbeta[:]))
        ve(_mbv)
        mbv = cVE.n
        for nch in range(8):
            def _t(t, nch=nch):
                if nch == 0:
                    t.wait_ge(cVE.sem, mbv)
                    qG.wait(t, w_in_ld)
                ii = None
                for dc in range(DC):
                    ii = t.matmul(psR[0:1, :], mbeta_bf[:, dc:dc + 1],
                                  w_inS[:, dc, 512 * nch:512 * nch + 512],
                                  start=(dc == 0), stop=(dc == DC - 1))
                cTE.inc(ii)
            te(_t)
            mmn = cTE.n
            def _v(v, nch=nch, mmn=mmn):
                v.wait_ge(cTE.sem, mmn)
                if nch >= 1:
                    qS.wait(v, brow_mark[nch - 1])
                cVE.inc(v.tensor_copy(brow_s[:], psR[0:1, :]))
            if nch == 0:
                brow_mark = {}
            ve(_v)
            evn = cVE.n
            def _bs1(s, nch=nch, evn=evn):
                s.wait_ge(cVE.sem, evn)
                qS.dma(s,
                       out=sp_brow.rearrange("(a t) p -> a (t p)", a=8)[nch:nch + 1],
                       in_=brow_s[:])
            sy(_bs1)
            brow_mark[nch] = qS.n
            if nch < 7:
                te(lambda t, evn=evn: t.wait_ge(cVE.sem, evn))
        t1 = qS.n
        def _bs2(s):
            qS.wait(s, t1)
            qS.dma(s, out=beff[:], in_=sp_brow.transpose([1, 0]),
                               allow_slow_non_contiguous=True)
        sy(_bs2)
        t2 = qS.n
        def _bv(v):
            qS.wait(v, t2)
            cVE.inc(v.tensor_tensor(beff[:], beff[:], b_in[:], Alu.add))
            ii = None
            for dc in range(DC):
                ii = v.tensor_scalar(w_inS[:, dc, :], w_inS[:, dc, :],
                                     mgam[:, dc:dc + 1], None, Alu.mult)
            cVE.inc(ii)
        ve(_bv)
        def _cb(v):
            cVE.inc(v.tensor_tensor(cb_eff[:], conv_w[:, :, 0], conv_w[:, :, 1], Alu.add))
            v.wait_ge(cVE.sem, cVE.n)
            cVE.inc(v.tensor_tensor(cb_eff[:], cb_eff[:], conv_w[:, :, 2], Alu.add))
            v.wait_ge(cVE.sem, cVE.n)
            cVE.inc(v.tensor_tensor(cb_eff[:], cb_eff[:], beff[:, 0:IC], Alu.mult))
            v.wait_ge(cVE.sem, cVE.n)
            cVE.inc(v.tensor_tensor(cb_eff[:], cb_eff[:], conv_b[:], Alu.add))
            cVE.inc(v.tensor_scalar(nsel1[:], sel_first[:], -1.0, None, Alu.mult))
        ve(_cb)
        winS_ready = cVE.n

        def ln_stats(nts):
            for ts in range(nts):
                def _v(v, ts=ts):
                    ii = v.tensor_scalar(xn_tm[:, ts, :], x_tm[:, ts, :], 1.0, 0.0,
                                         Alu.mult, Alu.add,
                                         accum_out=stat[:, ts:ts + 1])
                    cVE.inc(ii)
                ve(_v)
                def _a(a, ts=ts):
                    if ts >= 2:
                        a.wait_ge(cAC.sem, sq_mark[ts - 2])
                    cAC.inc(a.activation(xsq[:, ts % 2, :], x_tm[:, ts, :],
                                         Act.Square,
                                         accum_out=stat[:, 4 + ts:5 + ts]))
                if ts == 0:
                    sq_mark = {}
                ac(_a)
                sq_mark[ts] = cAC.n
            acw, vew = cAC.n, cVE.n
            s_ = slice(0, nts)
            def _f1(v):
                v.wait_ge(cAC.sem, acw)
                v.wait_ge(cVE.sem, vew)
                cVE.inc(v.tensor_scalar(stat[:, 8:8 + nts], stat[:, s_], R1024,
                                        None, Alu.mult))
                v.wait_ge(cVE.sem, cVE.n)
                v.tensor_scalar(stat[:, 12:12 + nts], stat[:, 8:8 + nts], -1.0,
                                None, Alu.mult)
                cVE.inc(v.tensor_tensor(st2[:, 12:12 + nts], stat[:, 8:8 + nts],
                                        stat[:, 8:8 + nts], Alu.mult))
                v.wait_ge(cVE.sem, cVE.n)
                cVE.inc(v.scalar_tensor_tensor(st2[:, 0:nts], stat[:, 4:4 + nts],
                                               R1024, st2[:, 12:12 + nts],
                                               Alu.mult, Alu.subtract))
            ve(_f1)
            vw = cVE.n
            def _f2(a):
                a.wait_ge(cVE.sem, vw)
                cAC.inc(a.activation(st2[:, 4:4 + nts], st2[:, 0:nts], Act.Sqrt,
                                     bias=eps_col[:], scale=1.0))
            ac(_f2)
            aw = cAC.n
            def _f3(v):
                v.wait_ge(cAC.sem, aw)
                cVE.inc(v.reciprocal(st2[:, 4:4 + nts], st2[:, 4:4 + nts]))
                v.wait_ge(cVE.sem, cVE.n)
                cVE.inc(v.tensor_tensor(st2[:, 8:8 + nts], st2[:, 4:4 + nts],
                                        st2[:, 4:4 + nts], Alu.mult))
                v.wait_ge(cVE.sem, cVE.n)
                cVE.inc(v.tensor_tensor(st2[:, 8:8 + nts], st2[:, 0:nts],
                                        st2[:, 8:8 + nts], Alu.mult))
            ve(_f3)
            vw2 = cVE.n
            def _f4(a):
                a.wait_ge(cVE.sem, vw2)
                cAC.inc(a.activation(st2[:, 12:12 + nts], st2[:, 8:8 + nts], Act.Sqrt,
                                     bias=eps_col[:], scale=1.0))
            ac(_f4)
            aw2 = cAC.n
            def _f5(v):
                v.wait_ge(cAC.sem, aw2)
                cVE.inc(v.reciprocal(st2[:, 12:12 + nts], st2[:, 12:12 + nts]))
                v.wait_ge(cVE.sem, cVE.n)
                cVE.inc(v.tensor_tensor(st2[:, 8:8 + nts], st2[:, 4:4 + nts],
                                        st2[:, 12:12 + nts], Alu.mult))
            ve(_f5)
            return cVE.n

        def ln_apply(ts, want_xnm=True):
            def _v(v, ts=ts):
                ii = v.tensor_scalar(xn_tm[:, ts, :], x_tm[:, ts, :],
                                     stat[:, 12 + ts:13 + ts],
                                     st2[:, 4 + ts:5 + ts], Alu.add, Alu.mult)
                if want_xnm:
                    ii = v.tensor_scalar(xnm_tm[:, ts, :], x_tm[:, ts, :],
                                         stat[:, 12 + ts:13 + ts],
                                         st2[:, 8 + ts:9 + ts], Alu.add, Alu.mult)
                cVE.inc(ii)
            ve(_v)

        def _hs(s):
            qS.wait(s, p0_ld)
            for i, off in enumerate([3, 131, 259, 387]):
                qS.dma(s, out=x_tm[:, i, :], in_=h_["x_halo"][off:off + 128, :])
        sy(_hs)
        hld = qS.n
        ve(lambda v: qS.wait(v, hld))
        ac(lambda a: qS.wait(a, hld))
        fin = ln_stats(4)
        ve(lambda v, t=fin: v.wait_ge(cVE.sem, t))
        for ts in range(4):
            ln_apply(ts, want_xnm=False)
        apl = cVE.n

        te(lambda t, t0=apl: t.wait_ge(cVE.sem, t0))
        for i, off in enumerate([3, 131, 259, 387]):
            def _tr(t, i=i):
                ii = None
                for dc in range(DC):
                    ii = t.transpose(psA[:, dc * 128:dc * 128 + 128],
                                     xn_tm[:, i, dc * 128:dc * 128 + 128], id_f32[:])
                cTE.inc(ii)
            te(_tr)
            trn = cTE.n
            def _ev(v, off=off, trn=trn):
                v.wait_ge(cTE.sem, trn)
                cVE.inc(v.tensor_copy(xh_fm.ap()[:, :, off:off + 128], psA3()))
            ve(_ev)
            evn = cVE.n
            te(lambda t, evn=evn: t.wait_ge(cVE.sem, evn))
        trm = cTE.n
        def _hs2(s):
            s.wait_ge(cTE.sem, trm)
            qS.dma(s, out=x_tm[:, 0, :], in_=h_["x_halo"][388:516, :])
        sy(_hs2)
        hld2 = qS.n
        ve(lambda v: qS.wait(v, hld2))
        ac(lambda a: qS.wait(a, hld2))
        fin2 = ln_stats(1)
        ve(lambda v, t=fin2: v.wait_ge(cVE.sem, t))
        ln_apply(0, want_xnm=True)
        apl2 = cVE.n
        def _tr2(t):
            t.wait_ge(cVE.sem, apl2)
            for dc in range(DC):
                t.transpose(psA[:, dc * 128:dc * 128 + 128],
                            xn_tm[:, 0, dc * 128:dc * 128 + 128], id_f32[:])
            ii = None
            for dc in range(DC):
                ii = t.transpose(psT[:, dc * 128:dc * 128 + 128],
                                 xnm_tm[:, 0, dc * 128:dc * 128 + 128], id_bf[:])
            cTE.inc(ii)
        te(_tr2)
        tr2 = cTE.n
        def _ev2(v):
            v.wait_ge(cTE.sem, tr2)
            v.tensor_copy(xh_fm.ap()[:, :, 388:516], psA3())
            cVE.inc(v.tensor_copy(xnmh_fm.ap()[:, :, :], psT3()))
        ve(_ev2)
        xh_done = cVE.n
        def _hsp(s):
            s.wait_ge(cVE.sem, xh_done)
            qS.dma(s, out=sp_xh[:, :, 3:HALO], in_=xh_fm[:, :, 3:HALO])
        sy(_hsp)
        def _sd1(v):
            v.wait_ge(cVE.sem, xh_done)
            cVE.inc(v.tensor_scalar(sideL1[:], xh_fm[:, :, 515:516],
                                    sel_second[:], None, Alu.mult))
        ve(_sd1)

        te(lambda t: t.wait_ge(cVE.sem, max(winS_ready, xh_done)))
        hp_marks = {}
        for f in range(IC):
            def _t(t, f=f):
                ii = None
                for dc in range(DC):
                    ii = t.matmul(psR[:, 0:2], w_inS[:, dc, 128 * f:128 * f + 128],
                                  xnmh_fm[:, dc, 126:128],
                                  start=(dc == 0), stop=(dc == DC - 1))
                cTE.inc(ii)
            te(_t)
            mmn = cTE.n
            def _v(v, f=f, mmn=mmn):
                v.wait_ge(cTE.sem, mmn)
                v.tensor_scalar(vals[:, 0, 0:2], psR[:, 0:2],
                                sel_second[:], None, Alu.mult)
                cVE.inc(v.tensor_tensor(sc1[:], nsel1[:], beff[:, f:f + 1], Alu.mult))
                v.wait_ge(cVE.sem, cVE.n)
                cVE.inc(v.tensor_scalar(vtail[:, f, :], vals[:, 0, 0:2],
                                        sc1[:], None, Alu.add))
            ve(_v)
            evn = cVE.n
            te(lambda t, evn=evn: t.wait_ge(cVE.sem, evn))

        barrier()
        hx.close()
        vs16 = sb(a1, "vs16", (128, IC, CH))

        xr_own = h_["x_own"].rearrange("(a p) d -> a p d", p=128)
        conv_mark = {}
        vspill_mark = {}
        sgspill_mark = {}
        evac_mark = {}
        spxn_mark = {}
        winmm_mark = {}
        for c in range(NCH):
            def _xl(s, c=c):
                if c == 0:
                    s.wait_ge(cTE.sem, tr2)
                qS.dma(s, out=x_tm[:],
                                   in_=xr_own[4 * c:4 * c + 4].transpose([1, 0, 2]))
            sy(_xl)
            xld = qS.n
            ve(lambda v: qS.wait(v, xld))
            ac(lambda a: qS.wait(a, xld))
            if c >= 1:
                def _gx(v, c=c):
                    qS.wait(v, spxn_mark[c - 1])
                    v.wait_ge(cTE.sem, winmm_mark[c - 1])
                ve(_gx)
            fin = ln_stats(4)
            ve(lambda v, t=fin: v.wait_ge(cVE.sem, t))
            for ts in range(4):
                ln_apply(ts)
            aplv = cVE.n
            if dbg_a1 and c == 0:
                def _ta1(s, aplv=aplv):
                    s.wait_ge(cVE.sem, aplv)
                    qS.dma(s, out=dbg_a1["stat"][:, 0:16], in_=stat[:])
                    qS.dma(s, out=dbg_a1["stat"][:, 16:32], in_=st2[:])
                    qS.dma(s, out=dbg_a1["xntm"][:], in_=xn_tm[:])
                    qS.dma(s, out=dbg_a1["xtm"][:], in_=x_tm[:])
                sy(_ta1)
                _tm = qS.n
                ve(lambda v, m=_tm: qS.wait(v, m))
                ac(lambda a, m=_tm: qS.wait(a, m))
            te(lambda t, t0=aplv: t.wait_ge(cVE.sem, t0))
            for ts in range(4):
                def _tr(t, ts=ts):
                    for dc in range(DC):
                        t.transpose(psA[:, dc * 128:dc * 128 + 128],
                                    xn_tm[:, ts, dc * 128:dc * 128 + 128], id_f32[:])
                    ii = None
                    for dc in range(DC):
                        ii = t.transpose(psT[:, dc * 128:dc * 128 + 128],
                                        xnm_tm[:, ts, dc * 128:dc * 128 + 128], id_bf[:])
                    cTE.inc(ii)
                te(_tr)
                trn = cTE.n
                def _ev(v, ts=ts, trn=trn):
                    v.wait_ge(cTE.sem, trn)
                    v.tensor_copy(xn_fm.ap()[:, :, 1 + ts * 128:1 + ts * 128 + 128], psA3())
                    cVE.inc(v.tensor_copy(
                        xnm_fm.ap()[:, :, ts * 128:ts * 128 + 128], psT3()))
                ve(_ev)
                evn = cVE.n
                te(lambda t, evn=evn: t.wait_ge(cVE.sem, evn))
            fm_done = cVE.n
            def _spxn(s, c=c, t0=fm_done):
                s.wait_ge(cVE.sem, t0)
                qS.dma(s, out=sp_xn[:, :, c * CH:(c + 1) * CH],
                                   in_=xn_fm[:, :, 1:CH + 1])
            sy(_spxn)
            spxn_mark[c] = qS.n

            mm_mark2 = {}
            ts_mark = {}

            def emit_mm(f, c=c):
                slot = S512[f % 6]
                def _mm(t, f=f, slot=slot):
                    gk = (c, f - 6) if f >= 6 else (c - 1, f + 26)
                    if gk in evac_mark:
                        t.wait_ge(cAC.sem, evac_mark[gk])
                    ii = None
                    for dc in range(DC):
                        ii = t.matmul(slot, w_inS[:, dc, 128 * f:128 * f + 128],
                                      xnm_fm[:, dc, :],
                                      start=(dc == 0), stop=(dc == DC - 1))
                    cTE.inc(ii)
                te(_mm)
                mm_mark2[f] = cTE.n

            def emit_ev_pair(f0, c=c):
                def _ev(a, f0=f0):
                    for f in (f0, f0 + 1):
                        a.wait_ge(cTE.sem, mm_mark2[f])
                        gk = (c, f - 2) if f >= 2 else (c - 1, f + 14)
                        if gk in conv_mark:
                            a.wait_ge(cVE.sem, conv_mark[gk])
                        cAC.inc(a.activation(vals[:, f % 2, 2:CH + 2],
                                             S512[f % 6], Act.Copy,
                                             bias=0.0, scale=1.0))
                        evac_mark[(c, f)] = cAC.n
                ac(_ev)

            def emit_cv(f, c=c):
                vb = vals[:, f % 2, :]
                vs = vs16[:, f, :]
                def _cv(v, f=f, vb=vb, vs=vs):
                    v.wait_ge(cAC.sem, evac_mark[(c, f)])
                    gk = (c - 1, f)
                    if gk in vspill_mark:
                        qS.wait(v, vspill_mark[gk])
                    v.tensor_copy(vb[:, 0:2], vtail[:, f, :])
                    cVE.inc(v.tensor_scalar(vs[:], vb[:, 2:CH + 2],
                                            conv_w[:, f, 2:3], None, Alu.mult))
                    v.wait_ge(cVE.sem, cVE.n)
                    cVE.inc(v.scalar_tensor_tensor(vs[:], vb[:, 1:CH + 1],
                                                   conv_w[:, f, 1:2], vs[:],
                                                   Alu.mult, Alu.add))
                    v.wait_ge(cVE.sem, cVE.n)
                    cVE.inc(v.scalar_tensor_tensor(vs[:], vb[:, 0:CH],
                                                   conv_w[:, f, 0:1], vs[:],
                                                   Alu.mult, Alu.add))
                    cVE.inc(v.tensor_copy(vtail[:, f, :], vb[:, CH:CH + 2]))
                ve(_cv)
                conv_mark[(c, f)] = cVE.n

            def emit_si_batch(c=c):
                si_mark = {}
                def _si(a):
                    for f in range(IC):
                        a.wait_ge(cVE.sem, conv_mark[(c, f)])
                        cAC.inc(a.activation(vs16[:, f, :], vs16[:, f, :],
                                             Act.Silu,
                                             bias=cb_eff[:, f:f + 1], scale=1.0))
                        si_mark[f] = cAC.n
                ac(_si)
                def _sp(s):
                    for f in range(IC):
                        s.wait_ge(cAC.sem, si_mark[f])
                        qS.dma(s, out=sp_vs[:, f, c * CH:(c + 1) * CH],
                               in_=vs16[:, f, :])
                        vspill_mark[(c, f)] = qS.n
                sy(_sp)

            def emit_evg_pair(f0, c=c):
                spg_mark = {}
                def _evg(a, f0=f0):
                    for f in (f0, f0 + 1):
                        fg = f - IC
                        a.wait_ge(cTE.sem, mm_mark2[f])
                        gk = (c, fg - 2) if fg >= 2 else (c - 1, fg + 14)
                        if gk in sgspill_mark:
                            qS.wait(a, sgspill_mark[gk])
                        cAC.inc(a.activation(sgb2[fg % 2][:], S512[f % 6],
                                             Act.Sigmoid,
                                             bias=beff[:, f:f + 1], scale=1.0))
                        evac_mark[(c, f)] = cAC.n
                        spg_mark[f] = cAC.n
                ac(_evg)
                def _spg(s, f0=f0):
                    for f in (f0, f0 + 1):
                        fg = f - IC
                        s.wait_ge(cAC.sem, spg_mark[f])
                        qS.dma(s, out=sp_sg[:, fg, c * CH:(c + 1) * CH],
                               in_=sgb2[fg % 2][:])
                        sgspill_mark[(c, fg)] = qS.n
                sy(_spg)

            for s_ in range(17):
                f0 = 2 * s_
                if 0 <= f0 - 2 < IC:
                    emit_cv(f0 - 2)
                if 0 <= f0 - 1 < IC:
                    emit_cv(f0 - 1)
                if f0 < 32:
                    emit_mm(f0)
                    emit_mm(f0 + 1)
                    if f0 < IC:
                        emit_ev_pair(f0)
                    else:
                        emit_evg_pair(f0)
            emit_si_batch()
            winmm_mark[c] = cTE.n

        barrier()
        a1.close()
        if upto == "a1a2":
            _endprog()
            return nc

        b1 = ExitStack()
        vsbB = sb(b1, "vsbB", (128, IC, T), BF16)
        wdS = [sb(b1, f"wdS{i}", (128, IC, 128), BF16) for i in range(2)]
        delt = [sb(b1, f"delt{i}", (128, T)) for i in range(2)]
        signz = [sb(b1, f"signz{i}", (128, T)) for i in range(2)]
        vsf = [sb(b1, f"vsf{i}", (128, T)) for i in range(2)]
        bbuf = sb(b1, "bbuf", (128, T))
        zeros = sb(b1, "zerosB", (128, T))
        sout = [sb(b1, f"sout{i}", (128, T)) for i in range(4)]

        ve(lambda v: cVE.inc(v.memset(zeros[:], 0.0)))
        zdone = cVE.n

        def _b1g(g):
            for cq in range(4):
                qG.dma(g, out=vsbB[:, 4 * cq:4 * cq + 4, :],
                                   in_=sp_vs[:, 4 * cq:4 * cq + 4, :])
        gp(_b1g)
        vsb_ld = qG.n

        wdr = h_["w_delta"].rearrange("(ci p) co -> ci p co", p=128)
        ac_mark = {}
        scan_mark = {}
        spill_mark = {}
        mm_mark = {}
        wl_mark = {}

        def _wl0(g):
            qG.dma(g, out=wdS[0][:],
                   in_=wdr[:, :, 0:128].transpose([1, 0, 2]))
        gp(_wl0)
        wl_mark[0] = qG.n
        for ct in range(IC):
            if ct + 1 < IC:
                def _wl(g, ct=ct):
                    if ct >= 1:
                        g.wait_ge(cTE.sem, mm_mark[ct - 1])
                    qG.dma(g, out=wdS[(ct + 1) % 2][:],
                           in_=wdr[:, :, 128 * (ct + 1):128 * (ct + 1) + 128]
                               .transpose([1, 0, 2]))
                gp(_wl)
                wl_mark[ct + 1] = qG.n
            wl = wl_mark[ct]
            def _vl(a, ct=ct):
                if ct >= 2:
                    a.wait_ge(cVE.sem, scan_mark[ct - 2])
                qC.dma(a, out=vsf[ct % 2][:], in_=sp_vs[:, ct, :])
            ac(_vl)
            vl = qC.n

            if ct == 0:
                gscan_mark = {}
                ev_half = {}
            for h in range(2):
                def _mm(t, ct=ct, h=h, wl=wl):
                    if (ct, h) == (0, 0):
                        qG.wait(t, vsb_ld)
                    qG.wait(t, wl)
                    if ct >= 1:
                        t.wait_ge(cAC.sem, ev_half[(ct - 1, h)])
                    ii = None
                    for qr in range(2):
                        o = 1024 * h + 512 * qr
                        for ci in range(IC):
                            ii = t.matmul(psW[:, o:o + 512],
                                          wdS[ct % 2][:, ci, :],
                                          vsbB[:, ci, o:o + 512],
                                          start=(ci == 0), stop=(ci == IC - 1))
                    cTE.inc(ii)
                te(_mm)
                mm_mark[ct] = cTE.n
                mmn = cTE.n
                def _ev(a, ct=ct, h=h, mmn=mmn):
                    a.wait_ge(cTE.sem, mmn)
                    if h == 0 and ct >= 2:
                        a.wait_ge(cVE.sem, gscan_mark[ct - 2])
                    o = 1024 * h
                    a.activation(delt[ct % 2][:, o:o + 1024],
                                 psW[:, o:o + 1024], Act.Sigmoid,
                                 bias=b_delta[:, ct:ct + 1], scale=1.0)
                    cAC.inc(a.activation(signz[ct % 2][:, o:o + 1024],
                                         psW[:, o:o + 1024], Act.Sigmoid,
                                         bias=nb_delta[:, ct:ct + 1], scale=-1.0))
                ac(_ev)
                ev_half[(ct, h)] = cAC.n
            ac_mark[ct] = cAC.n
            acn = cAC.n
            def _sc(v, ct=ct, acn=acn, vl=vl):
                v.wait_ge(cAC.sem, acn)
                qC.wait(v, vl)
                if ct == 0:
                    v.wait_ge(cVE.sem, zdone)
                if ct >= 1:
                    qS.wait(v, spill_mark[ct - 1])
                cVE.inc(v.tensor_tensor(bbuf[:], signz[ct % 2][:],
                                        vsf[ct % 2][:], Alu.mult))
                v.wait_ge(cVE.sem, cVE.n)
                v.tensor_tensor_scan(sout[0][:], delt[ct % 2][:], bbuf[:], 0.0,
                                     Alu.mult, Alu.add)
                cVE.inc(v.tensor_tensor_scan(sout[2][:, ::-1], delt[ct % 2][:, ::-1],
                                             bbuf[:, ::-1], 0.0, Alu.mult, Alu.add))
                v.wait_ge(cVE.sem, cVE.n)
                v.tensor_copy(ccR1[:, ct:ct + 1], sout[0][:, T - 1:T])
                cVE.inc(v.tensor_copy(ccR2[:, ct:ct + 1], sout[2][:, 0:1]))
            ve(_sc)
            scan_mark[ct] = cVE.n
            scn = cVE.n
            def _gsc(v, ct=ct):
                v.tensor_tensor_scan(sout[1][:], delt[ct % 2][:], zeros[:], 1.0,
                                     Alu.mult, Alu.add)
                cVE.inc(v.tensor_tensor_scan(sout[3][:, ::-1], delt[ct % 2][:, ::-1],
                                             zeros[:], 1.0, Alu.mult, Alu.add))
            ve(_gsc)
            gscan_mark[ct] = cVE.n
            gscn = cVE.n
            def _sp(s, ct=ct, scn=scn, gscn=gscn):
                s.wait_ge(cVE.sem, gscn)
                qS.dma(s, out=sp_fw[:, ct, :], in_=sout[0][:])
                qS.dma(s, out=sp_pf[:, ct, :], in_=sout[1][:])
                qS.dma(s, out=sp_bw[:, ct, :], in_=sout[2][:])
                qS.dma(s, out=sp_pb[:, ct, :], in_=sout[3][:])
            sy(_sp)
            spill_mark[ct] = qS.n

        scnall = cVE.n
        def _c1v(v):
            v.wait_ge(cVE.sem, cVE.n)
            v.tensor_scalar(ccR1[:], ccR1[:], sel_first[:], None, Alu.mult)
            cVE.inc(v.tensor_scalar(ccR2[:], ccR2[:], sel_second[:], None, Alu.mult))
        ve(_c1v)
        c1v = cVE.n
        def _c1g(g):
            g.wait_ge(cVE.sem, c1v)
            qG.dma(g, out=cc1_in[:, 0:IC], in_=ccR1[:])
            qG.dma(g, out=cc1_in[:, IC:2 * IC], in_=ccR2[:])
            qG.wait(g, qG.n)
            cCC.inc(g.collective_compute("AllReduce", Alu.add, replica_groups=RG,
                                         ins=[cc1_in.opt()], outs=[cc1_out.opt()]))
        gp(_c1g)
        cc1_done = cCC.n

        barrier()
        b1.close()
        if upto == "a1b":
            _endprog()
            return nc

        def emit_attn(l, xin_sp, is_c2, cc2_wait=0, sub=6):
            lx = ExitStack()
            ltab = sb(lx, f"ltab{l}", (128, 20, 256))
            wqkvS = sb(lx, f"wqkvS{l}", (128, DC, 1536), BF16)
            woS = sb(lx, f"woS{l}", (128, 4, 1024), BF16)
            xi32 = sb(lx, f"xi32_{l}", (128, DC, CH + 1))
            xib = sb(lx, f"xib{l}", (128, DC, CH + 1), BF16)
            xatc = sb(lx, f"xatc{l}", (128, DC, CH))
            qk2 = [sb(lx, f"qk2{l}_{i}", (128, 1024)) for i in range(2)]
            sqb2 = [sb(lx, f"sqb2{l}_{i}", (128, 1024)) for i in range(2)]
            rot2 = [sb(lx, f"rot2{l}_{i}", (128, 1024)) for i in range(2)]
            qf2 = [sb(lx, f"qf2{l}_{i}", (128, 1024), BF16) for i in range(2)]
            ssn = sb(lx, f"ssn{l}", (128, 64))
            qfm = sb(lx, f"qfm{l}", (128, 4, CH), BF16)
            kc = [sb(lx, f"kc{l}_{i}", (128, 4, CH), BF16) for i in range(2)]
            vc = [sb(lx, f"vc{l}_{i}", (128, 4, CH), BF16) for i in range(2)]
            probs = [sb(lx, f"probs{l}_{i}", (128, 1024), BF16) for i in range(2)]
            probsN = [sb(lx, f"probsN{l}_{i}", (128, 1024), BF16) for i in range(2)]
            pT = sb(lx, f"pT{l}", (128, 2, 8, CH), BF16)
            attnfm = sb(lx, f"attnfm{l}", (128, 4, CH), BF16)
            smax = sb(lx, f"smax{l}", (128, 8))
            hbuf = sb(lx, f"hbuf{l}", (128, DC, HALO)) if is_c2 else None

            def _lg(g):
                wq = h_[f"wqkv{l}"].rearrange("(c p) f -> c p f", p=128)
                qG.dma(g, out=wqkvS[:], in_=wq.transpose([1, 0, 2]))
                wo = h_[f"wo{l}"].rearrange("(c p) f -> c p f", p=128)
                qG.dma(g, out=woS[:], in_=wo.transpose([1, 0, 2]))
            gp(_lg)
            w_ld = qG.n
            def _lt(a):
                qC.dma(a, out=ltab[:], in_=sp_tab[:])
                if is_c2:
                    a.wait_ge(cCC.sem, cc2_wait)
                    qC.dma(a, out=hbuf[:], in_=cc2_out[:])
            ac(_lt)
            tab_ld = qC.n

            def _qb(t):
                qC.wait(t, tab_ld)
                t.matmul(psR[:, 0:64], ones_row[:], lvec[f"qs{l}"][:],
                         start=True, stop=True)
                t.matmul(psR[:, 64:128], ones_row[:], lvec[f"ks{l}"][:],
                         start=True, stop=True)
                t.matmul(psR[:, 128:160], ones_row[:], lvec[f"qs{l}"][:, 32:64],
                         start=True, stop=True)
                t.matmul(psR[:, 160:192], ones_row[:], lvec[f"qs{l}"][:, 0:32],
                         start=True, stop=True)
                t.matmul(psR[:, 192:224], ones_row[:], lvec[f"ks{l}"][:, 32:64],
                         start=True, stop=True)
                cTE.inc(t.matmul(psR[:, 224:256], ones_row[:],
                                 lvec[f"ks{l}"][:, 0:32], start=True, stop=True))
            te(_qb)
            qbn = cTE.n
            def _qv(v):
                v.wait_ge(cTE.sem, qbn)
                ii = None
                for sl, ps in [(slice(0, 64), slice(0, 64)),
                               (slice(64, 128), slice(128, 192)),
                               (slice(128, 192), slice(64, 128)),
                               (slice(192, 256), slice(192, 256))]:
                    ii = v.tensor_tensor(
                        ltab.ap()[:, :, sl], ltab.ap()[:, :, sl],
                        psR.ap()[:, ps].unsqueeze(1).broadcast_to([128, 20, 64]),
                        Alu.mult)
                cVE.inc(ii)
                v.wait_ge(cVE.sem, cVE.n)
                cVE.inc(v.tensor_scalar(ltab.ap()[:, :, 0:128],
                                        ltab.ap()[:, :, 0:128], 8.0,
                                        None, Alu.mult))
            ve(_qv)
            ltab_done = cVE.n
            te(lambda t: t.wait_ge(cVE.sem, ltab_done))

            slot_g = {}
            psR_g = [(cVE, 0)]
            psT_g = [None]

            def _gw(e, gd):
                if gd is not None and gd[1]:
                    e.wait_ge(gd[0].sem, gd[1])

            q1_mark = {}
            q2r_mark = {}

            def emit_q1(ts, tag):
                slot = S1024[ts % 3]
                def _mmqk(t, ts=ts, slot=slot):
                    _gw(t, slot_g.get(ts % 3))
                    ii = None
                    for half in range(2):
                        for dc in range(DC):
                            off = (1 if dc < 4 else 0) + ts * 128
                            ii = t.matmul(slot[:, 512 * half:512 * half + 512],
                                          xib[:, dc, off:off + 128],
                                          wqkvS[:, dc, 512 * half:512 * half + 512],
                                          start=(dc == 0), stop=(dc == DC - 1))
                    cTE.inc(ii)
                te(_mmqk)
                def _mmv(t, ts=ts):
                    _gw(t, psR_g[0])
                    ii = None
                    for dc in range(DC):
                        off = (1 if dc < 4 else 0) + ts * 128
                        ii = t.matmul(psR[:], xib[:, dc, off:off + 128],
                                      wqkvS[:, dc, 1024:1536],
                                      start=(dc == 0), stop=(dc == DC - 1))
                    cTE.inc(ii)
                te(_mmv)
                q1_mark[(tag, ts)] = cTE.n

            def emit_q2(ts, tag, vdst):
                slot = S1024[ts % 3]
                base = 32 * (ts % 2)
                def _ev(a, ts=ts, slot=slot):
                    a.wait_ge(cTE.sem, q1_mark[(tag, ts)])
                    a.activation(qk2[ts % 2][:], slot, Act.Copy,
                                 bias=0.0, scale=1.0)
                    cAC.inc(a.activation(vdst[:, ts, :], psR[:], Act.Copy,
                                         bias=0.0, scale=1.0))
                ac(_ev)
                evn = cAC.n
                slot_g[ts % 3] = (cAC, evn)
                psR_g[0] = (cAC, evn)
                def _ss(v, ts=ts, evn=evn, base=base):
                    v.wait_ge(cAC.sem, evn)
                    cVE.inc(v.tensor_tensor(sqb2[ts % 2][:], qk2[ts % 2][:],
                                            qk2[ts % 2][:], Alu.mult))
                    v.wait_ge(cVE.sem, cVE.n)
                    cVE.inc(v.tensor_reduce(
                        ssn[:, base:base + 16],
                        sqb2[ts % 2].ap().rearrange("p (g d) -> p g d", d=64),
                        Ax.X, Alu.add))
                    v.wait_ge(cVE.sem, cVE.n)
                    cVE.inc(v.tensor_scalar(ssn[:, base:base + 16],
                                            ssn[:, base:base + 16], 1e-24,
                                            None, Alu.max))
                ve(_ss)
                ssv = cVE.n
                def _nr(a, ssv=ssv, base=base):
                    a.wait_ge(cVE.sem, ssv)
                    cAC.inc(a.activation(ssn[:, base + 16:base + 32],
                                         ssn[:, base:base + 16], Act.Sqrt,
                                         bias=0.0, scale=1.0))
                ac(_nr)
                q2r_mark[(tag, ts)] = cAC.n

            def emit_q3(ts, tag, sidx, want_q, kdst):
                base = 32 * (ts % 2)
                qk = qk2[ts % 2]
                sqb = sqb2[ts % 2]
                rot1 = rot2[ts % 2]
                qf = qf2[ts % 2]
                def _pr(v, ts=ts, sidx=sidx, base=base):
                    v.wait_ge(cAC.sem, q2r_mark[(tag, ts)])
                    cVE.inc(v.reciprocal(ssn[:, base + 16:base + 32],
                                         ssn[:, base + 16:base + 32]))
                    v.wait_ge(cVE.sem, cVE.n)
                    cVE.inc(v.tensor_tensor(
                        qk.ap().rearrange("p (g d) -> p g d", d=64),
                        qk.ap().rearrange("p (g d) -> p g d", d=64),
                        ssn.ap()[:, base + 16:base + 32].unsqueeze(2)
                           .broadcast_to([128, 16, 64]), Alu.mult))
                    v.wait_ge(cVE.sem, cVE.n)
                    v.tensor_tensor(
                        rot1.ap()[:, 0:512].rearrange("p (g d) -> p g d", d=64),
                        qk.ap()[:, 0:512].rearrange("p (g d) -> p g d", d=64),
                        ltab.ap()[:, sidx, 0:64].unsqueeze(1)
                            .broadcast_to([128, 8, 64]), Alu.mult)
                    v.tensor_tensor(
                        rot1.ap()[:, 512:1024].rearrange("p (g d) -> p g d", d=64),
                        qk.ap()[:, 512:1024].rearrange("p (g d) -> p g d", d=64),
                        ltab.ap()[:, sidx, 128:192].unsqueeze(1)
                            .broadcast_to([128, 8, 64]), Alu.mult)
                    ii = None
                    for lohi in range(2):
                        dsl = slice(32 * lohi, 32 * lohi + 32)
                        ssl = slice(32 * (1 - lohi), 32 * (1 - lohi) + 32)
                        tq = slice(64 + 32 * lohi, 64 + 32 * lohi + 32)
                        tk = slice(192 + 32 * lohi, 192 + 32 * lohi + 32)
                        v.tensor_tensor(
                            sqb.ap()[:, 0:512].rearrange("p (g d) -> p g d", d=64)[:, :, dsl],
                            qk.ap()[:, 0:512].rearrange("p (g d) -> p g d", d=64)[:, :, ssl],
                            ltab.ap()[:, sidx, tq].unsqueeze(1)
                                .broadcast_to([128, 8, 32]), Alu.mult)
                        ii = v.tensor_tensor(
                            sqb.ap()[:, 512:1024].rearrange("p (g d) -> p g d", d=64)[:, :, dsl],
                            qk.ap()[:, 512:1024].rearrange("p (g d) -> p g d", d=64)[:, :, ssl],
                            ltab.ap()[:, sidx, tk].unsqueeze(1)
                                .broadcast_to([128, 8, 32]), Alu.mult)
                    cVE.inc(ii)
                    v.wait_ge(cVE.sem, cVE.n)
                    cVE.inc(v.tensor_tensor(qf[:], rot1[:], sqb[:], Alu.add))
                ve(_pr)
                prn = cVE.n
                def _tr(t, prn=prn):
                    t.wait_ge(cVE.sem, prn)
                    _gw(t, psT_g[0])
                    ii = None
                    for j in range(8):
                        ii = t.transpose(psT[:, 128 * j:128 * j + 128],
                                         qf[:, 128 * j:128 * j + 128], id_bf[:])
                    cTE.inc(ii)
                te(_tr)
                trn = cTE.n
                def _te2(a, ts=ts, trn=trn):
                    a.wait_ge(cTE.sem, trn)
                    if want_q:
                        a.activation(
                            qfm.ap()[:, :, ts * 128:ts * 128 + 128],
                            psT.ap()[:, 0:512].rearrange("p (c f) -> p c f", f=128),
                            Act.Copy, bias=0.0, scale=1.0)
                    cAC.inc(a.activation(
                        kdst.ap()[:, :, ts * 128:ts * 128 + 128],
                        psT.ap()[:, 512:1024].rearrange("p (c f) -> p c f", f=128),
                        Act.Copy, bias=0.0, scale=1.0))
                ac(_te2)
                psT_g[0] = (cAC, cAC.n)

            def emit_qkv_group(sidx0, want_q, kdst, vdst, tag):
                for i in range(6):
                    if 1 <= i < 5:
                        emit_q2(i - 1, tag, vdst)
                    if i < 4:
                        emit_q1(i, tag)
                    if 2 <= i:
                        emit_q3(i - 2, tag, sidx0 + (i - 2), want_q, kdst)

            if is_c2:
                def _hc(v):
                    qC.wait(v, tab_ld)
                    ii = None
                    for dc in range(DC):
                        ii = v.tensor_copy(xib[:, dc, 0:HALO - 3],
                                           hbuf[:, dc, 3:HALO])
                    cVE.inc(ii)
                ve(_hc)
            else:
                def _hl(a):
                    qC.dma(a, out=xib[:, :, 0:CH + 1],
                                       in_=sp_xh[:, :, 3:HALO])
                ac(_hl)
                hldq = qC.n
                def _hc(v):
                    qC.wait(v, hldq)
                    cVE.inc(v.tensor_copy(bsc[:, 0:1], bsc[:, 3:4]))
                ve(_hc)
            hcast = cVE.n
            te(lambda t: t.wait_ge(cVE.sem, hcast))
            te(lambda t: qG.wait(t, w_ld))
            emit_qkv_group(0, False, kc[1], vc[1], -1)

            spx_mark = {}
            xat_done = 0
            for c in range(NCH):
                def _cl(a, c=c):
                    if c >= 1:
                        a.wait_ge(cVE.sem, xat_done)
                    if c == 0:
                        qC.dma(a, out=xi32[:, :, 1:CH + 1],
                                           in_=xin_sp[:, :, 0:CH])
                    else:
                        qC.dma(a, out=xi32[:, :, 0:CH + 1],
                                           in_=xin_sp[:, :, c * CH - 1:(c + 1) * CH])
                ac(_cl)
                cld = qC.n
                def _cc(v, c=c, cld=cld):
                    qC.wait(v, cld)
                    if c == 0:
                        cVE.inc(v.tensor_copy(xi32[:, :, 0:1],
                                              sideL2[:] if is_c2 else sideL1[:]))
                        v.wait_ge(cVE.sem, cVE.n)
                    cVE.inc(v.tensor_copy(xib[:], xi32[:]))
                ve(_cc)
                ccn = cVE.n
                te(lambda t, ccn=ccn: t.wait_ge(cVE.sem, ccn))
                emit_qkv_group(4 + 4 * c, True, kc[c % 2], vc[c % 2], c)
                kv_ready = cAC.n

                if sub < 2:
                    continue
                mask = mask_c0S if c == 0 else mask_nS
                sim_mark = {}
                sm_mark = {}
                e1_mark = {}
                nr_mark = {}
                tr_mark = {}
                pte_mark = {}
                pv_done = {}
                pve_mark = {}

                def emit_sim(kk, c=c, mask=mask):
                    hp, j = kk // 8, kk % 8
                    qs_, hh = j // 2, j % 2
                    slot = S1024[kk % 3]
                    def _sim(t, kk=kk, hp=hp, qs_=qs_, hh=hh, slot=slot,
                             kv_ready=kv_ready):
                        _gw(t, slot_g.get(kk % 3))
                        if kk < 3:
                            t.wait_ge(cAC.sem, kv_ready)
                        tp = (64 * hh, 0) if hh else None
                        t.matmul(slot[:, 0:512], id_bf[:],
                                 mask[:, qs_, 0:512], start=True, stop=False)
                        t.matmul(slot[:, 512:1024], id_bf[:],
                                 mask[:, qs_, 512:1024], start=True, stop=False)
                        base = 64 * hh
                        lhs = qfm[base:base + 64, hp,
                                  qs_ * 128:qs_ * 128 + 128]
                        t.matmul(slot[:, 0:512], lhs,
                                 kc[(c + 1) % 2][base:base + 64, hp, :],
                                 start=False, stop=True, tile_position=tp)
                        cTE.inc(t.matmul(slot[:, 512:1024], lhs,
                                         kc[c % 2][base:base + 64, hp, :],
                                         start=False, stop=True,
                                         tile_position=tp))
                    te(_sim)
                    sim_mark[kk] = cTE.n

                def emit_sm(kk):
                    slot = S1024[kk % 3]
                    s4 = 4 * (kk % 2)
                    def _sm(v, kk=kk, slot=slot, s4=s4):
                        v.wait_ge(cTE.sem, sim_mark[kk])
                        if kk - 2 in e1_mark:
                            v.wait_ge(cAC.sem, e1_mark[kk - 2])
                        cVE.inc(v.tensor_reduce(smax[:, s4:s4 + 1], slot, Ax.X,
                                                Alu.max, negate=True))
                    ve(_sm)
                    sm_mark[kk] = cVE.n

                def emit_e1_pair(k0):
                    def _e1(a, k0=k0):
                        for kk in (k0, k0 + 1):
                            s4 = 4 * (kk % 2)
                            a.wait_ge(cVE.sem, sm_mark[kk])
                            if kk - 2 in nr_mark:
                                a.wait_ge(cVE.sem, nr_mark[kk - 2])
                            cAC.inc(a.activation(probs[kk % 2][:],
                                                 S1024[kk % 3], Act.Exp,
                                                 bias=smax[:, s4:s4 + 1],
                                                 scale=1.0,
                                                 accum_out=smax[:, s4 + 1:s4 + 2]))
                            e1_mark[kk] = cAC.n
                            slot_g[kk % 3] = (cAC, cAC.n)
                    ac(_e1)

                def emit_nr(kk):
                    s4 = 4 * (kk % 2)
                    def _nr(v, kk=kk, s4=s4):
                        v.wait_ge(cAC.sem, e1_mark[kk])
                        if kk - 2 in tr_mark:
                            v.wait_ge(cTE.sem, tr_mark[kk - 2])
                        cVE.inc(v.reciprocal(smax[:, s4 + 2:s4 + 3],
                                             smax[:, s4 + 1:s4 + 2]))
                        v.wait_ge(cVE.sem, cVE.n)
                        cVE.inc(v.tensor_scalar(probsN[kk % 2][:],
                                                probs[kk % 2][:],
                                                smax[:, s4 + 2:s4 + 3],
                                                None, Alu.mult))
                    ve(_nr)
                    nr_mark[kk] = cVE.n

                def emit_tr(kk):
                    def _tr(t, kk=kk):
                        t.wait_ge(cVE.sem, nr_mark[kk])
                        _gw(t, psT_g[0])
                        ii = None
                        for j in range(8):
                            ii = t.transpose(psT[:, 128 * j:128 * j + 128],
                                             probsN[kk % 2][:, 128 * j:128 * j + 128],
                                             id_bf[:])
                        cTE.inc(ii)
                    te(_tr)
                    tr_mark[kk] = cTE.n

                def emit_pte(kk):
                    hp, j = kk // 8, kk % 8
                    qs_, hh = j // 2, j % 2
                    def _pte(v, kk=kk, hp=hp, qs_=qs_, hh=hh):
                        v.wait_ge(cTE.sem, tr_mark[kk])
                        if (hp - 1, hh) in pv_done:
                            v.wait_ge(cTE.sem, pv_done[(hp - 1, hh)])
                        cVE.inc(v.tensor_copy(
                            pT.ap()[:, hh, :, qs_ * 128:qs_ * 128 + 128],
                            psT3()))
                    ve(_pte)
                    pte_mark[kk] = cVE.n
                    psT_g[0] = (cVE, cVE.n)

                def emit_pv(hp, c=c):
                    for hh in range(2):
                        h = 2 * hp + hh
                        def _pv(t, h=h, hh=hh, hp=hp):
                            t.wait_ge(cVE.sem, pte_mark[8 * hp + 7])
                            _gw(t, psR_g[0])
                            ii = None
                            for kp in range(8):
                                vcb = vc[(c + 1) % 2] if kp < 4 else vc[c % 2]
                                ii = t.matmul(psR[0:64, :],
                                              vcb[:, kp % 4, 64 * h:64 * h + 64],
                                              pT[:, hh, kp, :],
                                              start=(kp == 0), stop=(kp == 7))
                            cTE.inc(ii)
                        te(_pv)
                        pv_done[(hp, hh)] = cTE.n
                        pvn = cTE.n
                        def _pve(v, hp=hp, hh=hh, pvn=pvn):
                            v.wait_ge(cTE.sem, pvn)
                            cVE.inc(v.tensor_copy(attnfm[64 * hh:64 * hh + 64, hp, :],
                                                  psR[0:64, :]))
                        ve(_pve)
                        pve_mark[(hp, hh)] = cVE.n
                        psR_g[0] = (cVE, cVE.n)

                if sub < 3:
                    for kk in range(32):
                        emit_sim(kk)
                    continue
                for g in range(18):
                    for kk in (2 * g - 4, 2 * g - 3):
                        if 0 <= kk < 32:
                            emit_nr(kk)
                            emit_tr(kk)
                            emit_pte(kk)
                            if sub >= 4 and kk % 8 == 7:
                                emit_pv(kk // 8)
                    if 1 <= g <= 16:
                        emit_sm(2 * g - 2)
                        emit_sm(2 * g - 1)
                        emit_e1_pair(2 * g - 2)
                    if g < 16:
                        emit_sim(2 * g)
                        emit_sim(2 * g + 1)

                af = cVE.n
                te(lambda t, af=af: t.wait_ge(cVE.sem, af))
                woe_mark = {}
                for dt in range(DC if sub >= 4.7 else 0):
                    slot = S512[dt % 6]
                    def _wo(t, dt=dt, slot=slot):
                        if dt >= 6:
                            t.wait_ge(cVE.sem, woe_mark[dt - 6])
                        ii = None
                        for ic in range(4):
                            ii = t.matmul(slot, woS[:, ic, dt * 128:dt * 128 + 128],
                                          attnfm[:, ic, :],
                                          start=(ic == 0), stop=(ic == 3))
                        cTE.inc(ii)
                    te(_wo)
                    won = cTE.n
                    def _woe(v, dt=dt, slot=slot, won=won, c=c):
                        v.wait_ge(cTE.sem, won)
                        if c >= 1:
                            qS.wait(v, spx_mark[c - 1])
                        cVE.inc(v.tensor_tensor(xatc[:, dt, :], slot,
                                                xi32[:, dt, 1:CH + 1], Alu.add))
                    ve(_woe)
                    woe_mark[dt] = cVE.n
                xat_done = cVE.n
                if sub < 5.5:
                    spx_mark[c] = qS.n
                    continue
                def _spx(s, c=c, t0=xat_done):
                    s.wait_ge(cVE.sem, t0)
                    qS.dma(s, out=sp_xat[:, :, c * CH:(c + 1) * CH],
                                       in_=xatc[:])
                sy(_spx)
                spx_mark[c] = qS.n
                if c == 0 and l == 0 and dbg_attn:
                    def _tapa(s):
                        s.wait_ge(cVE.sem, xat_done)
                        qS.dma(s, out=dbg_attn["qf0"][:], in_=qfm[:])
                        qS.dma(s, out=dbg_attn["kc0"][:], in_=kc[0][:])
                        qS.dma(s, out=dbg_attn["vc0"][:], in_=vc[0][:])
                        qS.dma(s, out=dbg_attn["af0"][:], in_=attnfm[:])
                    sy(_tapa)
                    _tm = qS.n
                    ve(lambda v, m=_tm: qS.wait(v, m))

            if sub < 6:
                barrier()
                lx.close()
                return cGP.n
            def _cav(v):
                v.wait_ge(cVE.sem, cVE.n)
                cVE.inc(v.tensor_scalar(
                    scA.ap().unsqueeze(2), xatc[:, :, CH - 1:CH],
                    sel_first[:], None, Alu.mult))
            ve(_cav)
            cav = cVE.n
            def _cag(g, l=l):
                g.wait_ge(cVE.sem, cav)
                qG.dma(g, out=cca_in[l][:], in_=scA[:])
                qG.wait(g, qG.n)
                cCC.inc(g.collective_compute("AllReduce", Alu.add,
                                             replica_groups=RG,
                                             ins=[cca_in[l].opt()],
                                             outs=[cca_out[l].opt()]))
            gp(_cag)
            cca_done = cCC.n
            barrier()
            lx.close()
            return cca_done

        def emit_mlp(l, is_c2, cca_done, fire_cc2):
            cc2_mark = {0: 0}
            bi_nonlocal = [0]
            mx = ExitStack()
            blkb = sb(mx, f"blkb{l}", (128, DC, 1025))
            hnorm = sb(mx, f"hnorm{l}", (128, DC, 1024), BF16)
            h2S = sb(mx, f"h2S{l}", (128, NBK, 1024), BF16)
            gel = sb(mx, f"gel{l}", (128, 1024))
            gb2 = [gel, sb(mx, f"gelb{l}", (128, 1024))]
            xsqm = sb(mx, f"xsqm{l}", (128, 1024))
            w1b = [sb(mx, f"w1b{l}_{i}", (128, DC, 256), BF16) for i in range(2)]
            w2b = [sb(mx, f"w2b{l}_{i}", (128, NBK, 128), BF16) for i in range(2)]
            srow = sb(mx, f"srow{l}", (1, 1024))
            sxrow = sb(mx, f"sxrow{l}", (1, 1024))
            murow = sb(mx, f"murow{l}", (1, 1024))
            rstrow = sb(mx, f"rstrow{l}", (1, 1024))
            ccb = sb(mx, f"ccb{l}", (128, DC, HALO)) if fire_cc2 else None
            gbuf = gel if is_c2 else None
            sbuf_s = sb(mx, f"sbuf_s{l}", (128, DC, 1024)) if is_c2 else None
            xfin = sb(mx, f"xfin{l}", (128, 4, D)) if is_c2 else None

            w1r = h_[f"w1p{l}"].rearrange("(c p) f -> c p f", p=128)
            w2r = h_[f"w2p{l}"].rearrange("(c p) f -> c p f", p=128)

            for bi, blk in enumerate([1, 0]):
                bi_nonlocal[0] = bi
                base = blk * 1024
                ve_now, te_now, sy_now = cVE.n, cTE.n, qS.n
                def _bl(a, blk=blk, base=base, ve_now=ve_now, te_now=te_now,
                        sy_now=sy_now):
                    if bi_nonlocal[0] == 1:
                        a.wait_ge(cVE.sem, ve_now)
                        a.wait_ge(cTE.sem, te_now)
                        qS.wait(a, sy_now)
                    if blk == 1:
                        qC.dma(a, out=blkb[:, :, 0:1025],
                                           in_=sp_xat[:, :, base - 1:base + 1024])
                    else:
                        qC.dma(a, out=blkb[:, :, 1:1025],
                                           in_=sp_xat[:, :, 0:1024])
                        a.wait_ge(cCC.sem, cca_done)
                        qC.dma(a, out=scA[:], in_=cca_out[l][:])
                ac(_bl)
                bld = qC.n
                def _bv(v, blk=blk, bld=bld):
                    qC.wait(v, bld)
                    if blk == 0:
                        cVE.inc(v.tensor_scalar(blkb[:, :, 0:1],
                                                scA.ap().unsqueeze(2),
                                                sel_second[:], None, Alu.mult))
                    else:
                        cVE.inc(v.tensor_copy(bsc[:, 0:1], bsc[:, 3:4]))
                ve(_bv)
                blv = cVE.n
                if dbg_attn and l == 0 and blk == 0:
                    def _tapb(s):
                        s.wait_ge(cVE.sem, blv)
                        qS.dma(s, out=dbg_attn["blkb0"][:], in_=blkb[:])
                    sy(_tapb)
                    _tm = qS.n
                    ve(lambda v, m=_tm: qS.wait(v, m))

                def shview(dc, h0, n):
                    off = (1 if dc < 4 else 0) + h0
                    return blkb[:, dc, off:off + n]

                te(lambda t, blv=blv: t.wait_ge(cVE.sem, blv))
                ac(lambda a, blv=blv: a.wait_ge(cVE.sem, blv))
                for half in range(2):
                    def _sx(t, half=half):
                        ii = None
                        for dc in range(DC):
                            ii = t.matmul(psR[0:1, :], ones_col[:],
                                          shview(dc, 512 * half, 512),
                                          start=(dc == 0), stop=(dc == DC - 1))
                        cTE.inc(ii)
                    te(_sx)
                    sxn = cTE.n
                    def _sxe(v, half=half, sxn=sxn):
                        v.wait_ge(cTE.sem, sxn)
                        cVE.inc(v.tensor_copy(sxrow[0:1, 512 * half:512 * half + 512],
                                              psR[0:1, :]))
                    ve(_sxe)
                    sxe = cVE.n
                    te(lambda t, sxe=sxe: t.wait_ge(cVE.sem, sxe))
                qm_mark = {}
                for half in range(2):
                    for dc in range(DC):
                        k = half * DC + dc
                        xq = xsqm[:, 512 * (k % 2):512 * (k % 2) + 512]
                        def _q(a, dc=dc, half=half, k=k, xq=xq):
                            if k >= 2:
                                a.wait_ge(cTE.sem, qm_mark[k - 2])
                            cAC.inc(a.activation(xq, shview(dc, 512 * half, 512),
                                                 Act.Square))
                        ac(_q)
                        qn = cAC.n
                        def _qm(t, dc=dc, qn=qn, xq=xq):
                            t.wait_ge(cAC.sem, qn)
                            cTE.inc(t.matmul(psR[0:1, :], ones_col[:],
                                             xq, start=(dc == 0),
                                             stop=(dc == DC - 1)))
                        te(_qm)
                        qm_mark[k] = cTE.n
                    qmn = cTE.n
                    def _qe(v, half=half, qmn=qmn):
                        v.wait_ge(cTE.sem, qmn)
                        cVE.inc(v.tensor_copy(srow[0:1, 512 * half:512 * half + 512],
                                              psR[0:1, :]))
                    ve(_qe)
                    qe = cVE.n
                    te(lambda t, qe=qe: t.wait_ge(cVE.sem, qe))
                stn = cVE.n
                def _st(v):
                    v.wait_ge(cVE.sem, cVE.n)
                    cVE.inc(v.tensor_scalar(murow[:], sxrow[:], R1024, None,
                                            Alu.mult))
                    v.wait_ge(cVE.sem, cVE.n)
                    cVE.inc(v.tensor_tensor(rstrow[:], murow[:], murow[:],
                                            Alu.mult))
                    v.wait_ge(cVE.sem, cVE.n)
                    cVE.inc(v.scalar_tensor_tensor(rstrow[:], srow[:], R1024,
                                                   rstrow[:], Alu.mult,
                                                   Alu.subtract))
                ve(_st)
                stv = cVE.n
                def _sa(a):
                    a.wait_ge(cVE.sem, stv)
                    cAC.inc(a.activation(rstrow[:], rstrow[:], Act.Sqrt,
                                         bias=eps_col[0:1, :], scale=1.0))
                ac(_sa)
                san = cAC.n
                def _sr(v):
                    v.wait_ge(cAC.sem, san)
                    cVE.inc(v.reciprocal(rstrow[:], rstrow[:]))
                ve(_sr)
                srn = cVE.n
                if dbg_attn and l == 0 and blk == 0:
                    def _taps2(s):
                        s.wait_ge(cVE.sem, srn)
                        qS.dma(s, out=dbg_attn["rows0"][0:1, :], in_=sxrow[:])
                        qS.dma(s, out=dbg_attn["rows0"][1:2, :], in_=srow[:])
                        qS.dma(s, out=dbg_attn["rows0"][2:3, :], in_=murow[:])
                        qS.dma(s, out=dbg_attn["rows0"][3:4, :], in_=rstrow[:])
                    sy(_taps2)
                    _tm = qS.n
                    ve(lambda v, m=_tm: qS.wait(v, m))
                for half in range(2):
                    def _bc(t, half=half, srn=srn):
                        t.wait_ge(cVE.sem, srn)
                        t.matmul(psR[:, :], ones_row[:],
                                 murow[0:1, 512 * half:512 * half + 512],
                                 start=True, stop=True)
                        cTE.inc(t.matmul(psA[:, 0:512], ones_row[:],
                                         rstrow[0:1, 512 * half:512 * half + 512],
                                         start=True, stop=True))
                    te(_bc)
                    bcn = cTE.n
                    for dc in range(DC):
                        def _ap(v, dc=dc, half=half, bcn=bcn):
                            v.wait_ge(cTE.sem, bcn)
                            cVE.inc(v.tensor_tensor(gel[:, 0:512],
                                                    shview(dc, 512 * half, 512),
                                                    psR[:], Alu.subtract))
                            v.wait_ge(cVE.sem, cVE.n)
                            cVE.inc(v.scalar_tensor_tensor(
                                gel[:, 512:1024], gel[:, 0:512],
                                lvec[f"lng{l}"][:, dc:dc + 1],
                                psA[:, 0:512], Alu.mult, Alu.mult))
                            v.wait_ge(cVE.sem, cVE.n)
                            cVE.inc(v.tensor_scalar(
                                hnorm[:, dc, 512 * half:512 * half + 512],
                                gel[:, 512:1024], lvec[f"lnb{l}"][:, dc:dc + 1],
                                None, Alu.add))
                        ve(_ap)
                    apn = cVE.n
                    te(lambda t, apn=apn: t.wait_ge(cVE.sem, apn))
                hn_done = cVE.n

                def _w1l(g, i=0):
                    qG.dma(g, out=w1b[0][:],
                                       in_=w1r[:, :, 0:256].transpose([1, 0, 2]))
                gp(_w1l)
                w1_mark = {0: qG.n}
                h2_mark = {}
                gl_mark = {}
                ama_mark = {}
                w1mm_mark = {}
                for i in range(NBK):
                    if i + 1 < NBK:
                        def _w1n(g, i=i):
                            if i >= 1:
                                g.wait_ge(cTE.sem, w1mm_mark[i - 1])
                            qG.dma(g,
                                out=w1b[(i + 1) % 2][:],
                                in_=w1r[:, :, 256 * (i + 1):256 * (i + 1) + 256]
                                    .transpose([1, 0, 2]))
                        gp(_w1n)
                        w1_mark[i + 1] = qG.n
                    sa = S1024[(2 * i) % 3]
                    sg_ = S1024[(2 * i + 1) % 3]
                    def _ama(t, i=i, sa=sa, hn_done=hn_done):
                        qG.wait(t, w1_mark[i])
                        if i == 0:
                            t.wait_ge(cVE.sem, hn_done)
                        if i >= 2:
                            t.wait_ge(cAC.sem, gl_mark[i - 2])
                        ii = None
                        for half in range(2):
                            for dc in range(DC):
                                ii = t.matmul(sa[:, 512 * half:512 * half + 512],
                                              w1b[i % 2][:, dc, 0:128],
                                              hnorm[:, dc, 512 * half:512 * half + 512],
                                              start=(dc == 0), stop=(dc == DC - 1))
                        cTE.inc(ii)
                    te(_ama)
                    ama_mark[i] = cTE.n
                    def _amg(t, i=i, sg_=sg_):
                        if i >= 1:
                            t.wait_ge(cVE.sem, h2_mark[i - 1])
                        ii = None
                        for half in range(2):
                            for dc in range(DC):
                                ii = t.matmul(sg_[:, 512 * half:512 * half + 512],
                                              w1b[i % 2][:, dc, 128:256],
                                              hnorm[:, dc, 512 * half:512 * half + 512],
                                              start=(dc == 0), stop=(dc == DC - 1))
                        cTE.inc(ii)
                    te(_amg)
                    w1mm_mark[i] = cTE.n
                    amn = cTE.n
                    def _gl(a, i=i, amn=amn, sg_=sg_):
                        a.wait_ge(cTE.sem, amn)
                        if i >= 2:
                            a.wait_ge(cVE.sem, h2_mark[i - 2])
                        cAC.inc(a.activation(gb2[i % 2][:], sg_, Act.Gelu,
                                             bias=lvec[f"b1p{l}"][:, 2 * i + 1:2 * i + 2],
                                             scale=1.0))
                    ac(_gl)
                    gl_mark[i] = cAC.n
                    gln = cAC.n
                    def _h2(v, i=i, gln=gln, sa=sa):
                        v.wait_ge(cAC.sem, gln)
                        cVE.inc(v.scalar_tensor_tensor(
                            h2S[:, i, :], sa,
                            lvec[f"b1p{l}"][:, 2 * i:2 * i + 1], gb2[i % 2][:],
                            Alu.add, Alu.mult))
                    ve(_h2)
                    h2_mark[i] = cVE.n
                h2_done = cVE.n
                if dbg_attn and l == 0 and blk == 0:
                    def _tapm(s):
                        s.wait_ge(cVE.sem, h2_done)
                        qS.dma(s, out=dbg_attn["hn0"][:], in_=hnorm[:])
                        qS.dma(s, out=dbg_attn["h20"][:], in_=h2S[:])
                    sy(_tapm)
                    _tm = qS.n
                    ve(lambda v, m=_tm: qS.wait(v, m))

                def _w2l(g):
                    qG.dma(g, out=w2b[0][:],
                                       in_=w2r[:, :, 0:128].transpose([1, 0, 2]))
                gp(_w2l)
                w2_mark = {0: qG.n}
                w2e_mark = {}
                for dt in range(DC):
                    if dt + 1 < DC:
                        def _w2n(g, dt=dt):
                            if dt >= 1:
                                g.wait_ge(cTE.sem, w2mm_mark[dt - 1])
                            qG.dma(g, 
                                out=w2b[(dt + 1) % 2][:],
                                in_=w2r[:, :, 128 * (dt + 1):128 * (dt + 1) + 128]
                                    .transpose([1, 0, 2]))
                        if dt == 0:
                            w2mm_mark = {}
                        gp(_w2n)
                        w2_mark[dt + 1] = qG.n
                    slot = S1024[dt % 3]
                    def _wm(t, dt=dt, slot=slot, h2_done=h2_done):
                        qG.wait(t, w2_mark[dt])
                        if dt == 0:
                            t.wait_ge(cVE.sem, h2_done)
                        if dt >= 3:
                            t.wait_ge(cVE.sem, w2e_mark[dt - 3])
                        ii = None
                        for half in range(2):
                            for hb in range(NBK):
                                ii = t.matmul(slot[:, 512 * half:512 * half + 512],
                                              w2b[dt % 2][:, hb, :],
                                              h2S[:, hb, 512 * half:512 * half + 512],
                                              start=(hb == 0), stop=(hb == NBK - 1))
                        cTE.inc(ii)
                    te(_wm)
                    w2mm_mark[dt] = cTE.n
                    wmn = cTE.n
                    def _we(v, dt=dt, slot=slot, wmn=wmn):
                        v.wait_ge(cTE.sem, wmn)
                        cVE.inc(v.scalar_tensor_tensor(
                            blkb[:, dt, 1:1025], slot,
                            lvec[f"b2{l}"][:, dt:dt + 1],
                            blkb[:, dt, 1:1025],
                            Alu.add, Alu.add))
                    ve(_we)
                    w2e_mark[dt] = cVE.n
                blk_done = cVE.n

                if not is_c2:
                    def _spb(s, base=base, blk_done=blk_done):
                        s.wait_ge(cVE.sem, blk_done)
                        qS.dma(s, out=sp_xl1[:, :, base:base + 1024],
                                           in_=blkb[:, :, 1:1025])
                    sy(_spb)
                    if fire_cc2 and blk == 1:
                        def _cc2v(v):
                            v.wait_ge(cVE.sem, cVE.n)
                            ii = None
                            for dc in range(DC):
                                ii = v.tensor_scalar(
                                    ccb[:, dc, :],
                                    blkb[:, dc, 1025 - HALO:1025],
                                    sel_first[:], None, Alu.mult)
                            cVE.inc(ii)
                        ve(_cc2v)
                        c2v = cVE.n
                        def _cc2g(g):
                            g.wait_ge(cVE.sem, c2v)
                            qG.dma(g, out=cc2_in[:], in_=ccb[:])
                            qG.wait(g, qG.n)
                            cCC.inc(g.collective_compute(
                                "AllReduce", Alu.add, replica_groups=RG,
                                ins=[cc2_in.opt()], outs=[cc2_out.opt()]))
                        gp(_cc2g)
                        cc2_mark[0] = cCC.n
                else:
                    def _sl(a, base=base, blk_done=blk_done):
                        a.wait_ge(cVE.sem, blk_done)
                        qC.dma(a, out=sbuf_s[:],
                                           in_=sp_s[:, :, base:base + 1024])
                    ac(_sl)
                    sld = qC.n
                    fin_mark = {}
                    for dc in range(DC):
                        def _gs(a, dc=dc, blk_done=blk_done):
                            a.wait_ge(cVE.sem, blk_done)
                            if dc >= 1:
                                a.wait_ge(cVE.sem, fin_mark[dc - 1])
                            cAC.inc(a.activation(gbuf[:],
                                                 blkb[:, dc, 1:1025],
                                                 Act.Silu))
                        ac(_gs)
                        gsn = cAC.n
                        def _gm(v, dc=dc, gsn=gsn, sld=sld):
                            v.wait_ge(cAC.sem, gsn)
                            qC.wait(v, sld)
                            cVE.inc(v.tensor_tensor(sbuf_s[:, dc, :], sbuf_s[:, dc, :],
                                                    gbuf[:], Alu.mult))
                        ve(_gm)
                        fin_mark[dc] = cVE.n
                    sg_done = cVE.n
                    out_mark = {}
                    qs_prev = qS.n
                    for ts in range(8):
                        def _xl(a, ts=ts, base=base, qs_prev=qs_prev):
                            if ts >= 4:
                                qS.wait(a, out_mark[ts - 4])
                            else:
                                qS.wait(a, qs_prev)
                            qC.dma(a, 
                                out=xfin[:, ts % 4, :],
                                in_=xr_own[(base + ts * 128) // 128])
                        ac(_xl)
                        xln = qC.n
                        def _ft(t, ts=ts, sg_done=sg_done):
                            t.wait_ge(cVE.sem, sg_done)
                            ii = None
                            for dc in range(DC):
                                ii = t.transpose(psA[:, dc * 128:dc * 128 + 128],
                                                 sbuf_s[:, dc, ts * 128:ts * 128 + 128],
                                                 id_f32[:])
                            cTE.inc(ii)
                        te(_ft)
                        ftn = cTE.n
                        def _fe(v, ts=ts, ftn=ftn, xln=xln):
                            v.wait_ge(cTE.sem, ftn)
                            qC.wait(v, xln)
                            cVE.inc(v.tensor_tensor(
                                xfin.ap()[:, ts % 4, :].rearrange("p (c f) -> p c f", f=128),
                                psA3(), xfin.ap()[:, ts % 4, :]
                                .rearrange("p (c f) -> p c f", f=128), Alu.add))
                        ve(_fe)
                        fen = cVE.n
                        te(lambda t, fen=fen: t.wait_ge(cVE.sem, fen))
                        def _fo(s, ts=ts, base=base, fen=fen):
                            s.wait_ge(cVE.sem, fen)
                            qS.dma(s, 
                                out=out_h[base + ts * 128:base + ts * 128 + 128, :],
                                in_=xfin[:, ts % 4, :])
                        sy(_fo)
                        out_mark[ts] = qS.n
            barrier()
            mx.close()
            return cc2_mark[0]

        def emit_c1():
            cx = ExitStack()
            woutS = sb(cx, "woutS", (128, IC, D), BF16)
            rbuf = [sb(cx, f"rbuf{i}", (128, 5, CH)) for i in range(2)]
            fused = sb(cx, "fusedC", (128, IC, CH), BF16)
            xnr = sb(cx, "xnrC", (128, DC, CH))
            sres = sb(cx, "sresC", (128, DC, CH))

            def _wg(g):
                wv = h_["w_out"].rearrange("(c p) f -> c p f", p=128)
                qG.dma(g, out=woutS[:], in_=wv.transpose([1, 0, 2]))
            gp(_wg)
            wg = qG.n
            def _r12(v):
                v.wait_ge(cCC.sem, cc1_done)
            ve(_r12)
            def _r12l(a):
                a.wait_ge(cCC.sem, cc1_done)
                qC.dma(a, out=R1e[:], in_=cc1_out[:, 0:IC])
                qC.dma(a, out=R2e[:], in_=cc1_out[:, IC:2 * IC])
            ac(_r12l)
            rld = qC.n
            def _r12v(v):
                qC.wait(v, rld)
                v.tensor_scalar(R1e[:], R1e[:], sel_second[:], None, Alu.mult)
                cVE.inc(v.tensor_scalar(R2e[:], R2e[:], sel_first[:], None, Alu.mult))
            ve(_r12v)
            r12 = cVE.n

            fuse_mark = {}
            ld_mark = {}
            spill_s_mark = {}
            prev_we_done = 0
            prev_wm_done = 0
            for c in range(NCH):
                c_wm_guard = prev_wm_done
                for ct in range(IC):
                    def _ld(a, c=c, ct=ct):
                        gk = (c, ct - 2) if ct >= 2 else (c - 1, ct + 14)
                        if gk in fuse_mark:
                            a.wait_ge(cVE.sem, fuse_mark[gk])
                        sl = slice(c * CH, (c + 1) * CH)
                        bf_ = rbuf[ct % 2]
                        qC.dma(a, out=bf_[:, 0, :], in_=sp_fw[:, ct, sl])
                        qC.dma(a, out=bf_[:, 1, :], in_=sp_pf[:, ct, sl])
                        qC.dma(a, out=bf_[:, 2, :], in_=sp_bw[:, ct, sl])
                        qC.dma(a, out=bf_[:, 3, :], in_=sp_pb[:, ct, sl])
                        qC.dma(a, out=bf_[:, 4, :], in_=sp_sg[:, ct, sl])
                    ac(_ld)
                    ld_mark[(c, ct)] = qC.n
                    ldn = qC.n
                    def _fu(v, c=c, ct=ct, ldn=ldn, gg=c_wm_guard):
                        qC.wait(v, ldn)
                        if ct == 0 and gg:
                            v.wait_ge(cTE.sem, gg)
                        if (c, ct) == (0, 0):
                            v.wait_ge(cVE.sem, r12)
                        bf_ = rbuf[ct % 2]
                        v.scalar_tensor_tensor(bf_[:, 0, :], bf_[:, 1, :],
                                               R1e[:, ct:ct + 1], bf_[:, 0, :],
                                               Alu.mult, Alu.add)
                        cVE.inc(v.scalar_tensor_tensor(bf_[:, 2, :], bf_[:, 3, :],
                                                       R2e[:, ct:ct + 1],
                                                       bf_[:, 2, :],
                                                       Alu.mult, Alu.add))
                        v.wait_ge(cVE.sem, cVE.n)
                        cVE.inc(v.tensor_tensor(bf_[:, 0, :], bf_[:, 0, :],
                                                bf_[:, 2, :], Alu.add))
                        v.wait_ge(cVE.sem, cVE.n)
                        cVE.inc(v.tensor_tensor(fused[:, ct, :], bf_[:, 0, :],
                                                bf_[:, 4, :], Alu.mult))
                    ve(_fu)
                    fuse_mark[(c, ct)] = cVE.n
                fu_done = cVE.n
                def _xr(a, c=c, fu_done=fu_done):
                    qC.dma(a, out=xnr[:],
                                       in_=sp_xn[:, :, c * CH:(c + 1) * CH])
                ac(_xr)
                xrn = qC.n
                we_mark = {}
                for dt in range(DC):
                    slot = S512[dt % 6]
                    def _wm(t, c=c, dt=dt, slot=slot, fu_done=fu_done, wg=wg):
                        if (c, dt) == (0, 0):
                            qG.wait(t, wg)
                        t.wait_ge(cVE.sem, fu_done)
                        if dt >= 6:
                            t.wait_ge(cVE.sem, we_mark[dt - 6])
                        elif c >= 1:
                            t.wait_ge(cVE.sem, prev_we_done)
                        ii = None
                        for ct in range(IC):
                            ii = t.matmul(slot, woutS[:, ct, dt * 128:dt * 128 + 128],
                                          fused[:, ct, :], start=(ct == 0),
                                          stop=(ct == IC - 1))
                        cTE.inc(ii)
                    te(_wm)
                    wmn = cTE.n
                    def _we(v, c=c, dt=dt, slot=slot, wmn=wmn, xrn=xrn):
                        v.wait_ge(cTE.sem, wmn)
                        qC.wait(v, xrn)
                        if c >= 1:
                            qS.wait(v, spill_s_mark[c - 1])
                        cVE.inc(v.scalar_tensor_tensor(sres[:, dt, :], slot,
                                                       b_out[:, dt:dt + 1],
                                                       xnr[:, dt, :],
                                                       Alu.add, Alu.add))
                    ve(_we)
                    we_mark[dt] = cVE.n
                sdone = cVE.n
                prev_we_done = sdone
                prev_wm_done = cTE.n
                def _ssp(s, c=c, sdone=sdone):
                    s.wait_ge(cVE.sem, sdone)
                    qS.dma(s, out=sp_s[:, :, c * CH:(c + 1) * CH],
                                       in_=sres[:])
                sy(_ssp)
                spill_s_mark[c] = qS.n
            barrier()
            cx.close()

        xr_own = h_["x_own"].rearrange("(a p) d -> a p d", p=128)

        _sub = {"l1s1": 1, "l1s2": 2, "l1s3": 3, "l1s4": 4,
                "l1s4a": 4.7, "l1s5": 5.5}.get(upto, 6)
        cca0 = emit_attn(0, sp_xn, False, sub=_sub)
        if upto in ("l1s1", "l1s2", "l1s3", "l1s4", "l1s4a", "l1s5"):
            _endprog()
            return nc
        if dbg_attn:
            def _tapx(g):
                qG.dma(g, out=dbg_attn["xat1"][:], in_=sp_xat[:])
                qG.wait(g, qG.n)
            gp(_tapx)
        if upto == "l1a":
            _endprog()
            return nc
        cc2m = emit_mlp(0, False, cca0, True)
        if upto == "l1m":
            _endprog()
            return nc
        emit_c1()
        if upto == "c1":
            _endprog()
            return nc
        def _sd2(a):
            a.wait_ge(cCC.sem, cc2m)
            qC.dma(a, out=sideL2[:], in_=cc2_out[:, :, 515:516],
                               allow_slow_non_contiguous=True)
        ac(_sd2)
        sd2 = qC.n
        def _sd2v(v):
            qC.wait(v, sd2)
            cVE.inc(v.tensor_scalar(sideL2[:], sideL2[:], sel_second[:],
                                    None, Alu.mult))
        ve(_sd2v)
        cca1 = emit_attn(1, sp_xl1, True, cc2_wait=cc2m)
        if upto == "l2a":
            _endprog()
            return nc
        emit_mlp(1, True, cca1, False)

        if dbg_h:
            def _tap(g):
                for nm, (dst, src) in dbg_h.items():
                    qG.dma(g, out=dst[:], in_=src[:])
                qG.wait(g, qG.n)
            gp(_tap)

        barrier()
        def _fin(s):
            qS.wait(s, qS.n)
        sy(_fin)

    return nc


_NC_CACHE = {}


def kernel(**inputs):
    maps = prepare_inputs(inputs)
    key = "main"
    if key not in _NC_CACHE:
        _NC_CACHE[key] = build_nc()
    nc = _NC_CACHE[key]
    res = run_bass_kernel_spmd(nc, maps, core_ids=list(range(8)))
    out = np.empty((4, 4096, 1024), np.float32)
    for c in range(8):
        b, h = c // 2, c % 2
        out[b, h * T:(h + 1) * T] = res.results[c]["out"]
    return out


# version 14
# speedup vs baseline: 1.1083x; 1.1083x over previous
import numpy as np
import ml_dtypes
from contextlib import ExitStack

import concourse.bass as bass
import concourse.mybir as mybir
from concourse.bass_utils import run_bass_kernel_spmd

F32 = mybir.dt.float32
BF16 = mybir.dt.bfloat16
Alu = mybir.AluOpType
Act = mybir.ActivationFunctionType
Ax = mybir.AxisListType
BF = ml_dtypes.bfloat16

D, DC = 1024, 8
INNER, IC = 2048, 16
T, CH, NCH = 2048, 512, 4
NH, DH = 8, 64
HID, HIDP = 2730, 2816
F1P = 5632
NBK = 22
HALO = 516
EPS = 1e-5
NEG = -1.0e30
R1024 = 1.0 / 1024.0
RG = [[0, 1], [2, 3], [4, 5], [6, 7]]



def _feat_tiles(v):
    v = np.asarray(v, np.float32)
    return np.ascontiguousarray(v.reshape(v.shape[0] // 128, 128).T)


def _rotary_tables(t0):
    n = 4096
    pos = np.clip(np.arange(t0 - 512, t0 + 2048, dtype=np.float64), 0, n - 1)
    half = np.arange(0, DH, 2, dtype=np.float64)
    inv_freq = 1.0 / 10000.0 ** (half / DH)
    fr = pos[:, None] * inv_freq[None, :]
    fr = np.concatenate([fr, fr], axis=1)
    base = (half + 0.4 * DH) / (1.4 * DH)
    sc = base[None, :] ** ((pos - n // 2) / 256.0)[:, None]
    sc = np.concatenate([sc, sc], axis=1)
    sign = np.where(np.arange(DH) < 32, -1.0, 1.0)[None, :]
    tab = np.concatenate([np.cos(fr) * sc, np.sin(fr) * sc * sign,
                          np.cos(fr) / sc, np.sin(fr) / sc * sign], axis=1)
    tab = tab.reshape(20, 128, 256).transpose(1, 0, 2)
    hi = tab.astype(np.float32)
    lo = (tab - hi.astype(np.float64)).astype(np.float32)
    return np.ascontiguousarray(hi), np.ascontiguousarray(lo)


def _masks(first_half):
    i = np.arange(CH)[:, None]
    j = np.arange(1024)[None, :]
    mn = np.where((j >= i) & (j <= i + 512), 0.0, NEG)
    mc0 = np.where((j >= i) & (j <= i + 512) & (j >= 512), 0.0, NEG) \
        if first_half else mn
    lay = lambda m: np.ascontiguousarray(
        m.reshape(4, 128, 1024).transpose(1, 0, 2).astype(BF))
    return lay(mn), lay(mc0)


def _pad_w1(w1, b1):
    a = np.zeros((D, HIDP), np.float32); a[:, :HID] = w1[:, :HID]
    g = np.zeros((D, HIDP), np.float32); g[:, :HID] = w1[:, HID:]
    ba = np.zeros(HIDP, np.float32); ba[:HID] = b1[:HID]
    bg = np.zeros(HIDP, np.float32); bg[:HID] = b1[HID:]
    w1p = np.empty((D, F1P), np.float32)
    b1p = np.empty(F1P, np.float32)
    for i in range(NBK):
        w1p[:, 256 * i:256 * i + 128] = a[:, 128 * i:128 * i + 128]
        w1p[:, 256 * i + 128:256 * i + 256] = g[:, 128 * i:128 * i + 128]
        b1p[256 * i:256 * i + 128] = ba[128 * i:128 * i + 128]
        b1p[256 * i + 128:256 * i + 256] = bg[128 * i:128 * i + 128]
    return (np.ascontiguousarray(w1p),
            np.ascontiguousarray(b1p.reshape(F1P // 128, 128).T))


def prepare_inputs(inputs):
    x = np.asarray(inputs["L0_attn_x"])
    sh = {}
    sh["w_in"] = np.ascontiguousarray(np.asarray(inputs["L0_attn_W_in"], np.float32))
    sh["w_delta"] = np.ascontiguousarray(np.asarray(inputs["L0_attn_W_delta"], np.float32))
    sh["w_out"] = np.ascontiguousarray(np.asarray(inputs["L0_attn_W_out"], np.float32))
    sh["b_in_t"] = _feat_tiles(inputs["L0_attn_b_in"])
    cw = np.asarray(inputs["L0_attn_conv_w"], np.float32)
    sh["conv_w_t"] = np.ascontiguousarray(cw.reshape(IC, 128, 3).transpose(1, 0, 2))
    sh["conv_b_t"] = _feat_tiles(inputs["L0_attn_conv_b"])
    sh["b_delta_t"] = _feat_tiles(inputs["L0_attn_b_delta"])
    sh["b_out_t"] = _feat_tiles(inputs["L0_attn_b_out"])
    sh["mgam_t"] = _feat_tiles(inputs["L0_attn_ln_gamma"])
    sh["mbeta_t"] = _feat_tiles(inputs["L0_attn_ln_beta"])
    for l in range(2):
        sh[f"wqkv{l}"] = np.ascontiguousarray(np.asarray(inputs["L0_ffn_wqkv"][l], np.float32))
        sh[f"wo{l}"] = np.ascontiguousarray(np.asarray(inputs["L0_ffn_wo"][l], np.float32))
        w1p, b1t = _pad_w1(np.asarray(inputs["L0_ffn_w1"][l], np.float32),
                           np.asarray(inputs["L0_ffn_b1"][l], np.float32))
        sh[f"w1p{l}"], sh[f"b1p_t{l}"] = w1p, b1t
        w2p = np.zeros((HIDP, D), np.float32)
        w2p[:HID] = np.asarray(inputs["L0_ffn_w2"][l], np.float32)
        sh[f"w2p{l}"] = w2p
        sh[f"b2_t{l}"] = _feat_tiles(inputs["L0_ffn_b2"][l])
        sh[f"lng_t{l}"] = _feat_tiles(inputs["L0_ffn_ln_g"][l])
        sh[f"lnb_t{l}"] = _feat_tiles(inputs["L0_ffn_ln_b"][l])
        sh[f"qs{l}"] = np.ascontiguousarray(
            np.asarray(inputs["L0_ffn_q_scale"][l], np.float32).reshape(1, DH))
        sh[f"ks{l}"] = np.ascontiguousarray(
            np.asarray(inputs["L0_ffn_k_scale"][l], np.float32).reshape(1, DH))
    sh["id_bf"] = np.eye(128, dtype=BF)
    sh["id_f32"] = np.eye(128, dtype=np.float32)
    sh["ones_row"] = np.ones((1, 128), np.float32)
    sh["ones_col"] = np.ones((128, 1), np.float32)
    sh["eps_col"] = np.full((128, 1), EPS, np.float32)

    maps = []
    for c in range(8):
        b, h = c // 2, c % 2
        t0 = h * T
        d = dict(sh)
        d["x_own"] = np.ascontiguousarray(x[b, t0:t0 + T])
        hs = t0 - HALO if h == 1 else 0
        d["x_halo"] = np.ascontiguousarray(x[b, hs:hs + HALO])
        d["tabs_hi"], d["tabs_lo"] = _rotary_tables(t0)
        d["mask_n"], d["mask_c0"] = _masks(h == 0)
        one = np.ones((128, 1), np.float32)
        zero = np.zeros((128, 1), np.float32)
        d["sel_first"] = one if h == 0 else zero
        d["sel_second"] = zero if h == 0 else one
        maps.append(d)
    return maps



class Ctr:
    def __init__(self, sem, step=1):
        self.sem, self.n, self.step = sem, 0, step

    def inc(self, instr, k=None):
        k = self.step if k is None else k
        self.n += k
        instr.then_inc(self.sem, k)
        return self.n

    def dma(self, eng, **kw):
        eng.wait_ge(self.sem, self.n)
        self.n += 16
        eng.dma_start(**kw).then_inc(self.sem, 16)
        return self.n


class Q:
    def __init__(self, sems):
        self.sems = list(sems)
        self.cnt = [0] * len(self.sems)
        self.i = 0
        self.n = 0
        self.snap = {}

    def dma(self, eng, **kw):
        i = self.i
        self.i = (i + 1) % len(self.sems)
        if self.cnt[i]:
            eng.wait_ge(self.sems[i], self.cnt[i])
        self.cnt[i] += 16
        eng.dma_start(**kw).then_inc(self.sems[i], 16)
        self.n += 1
        self.snap[self.n] = tuple(self.cnt)
        return self.n

    def wait(self, eng, mark):
        if mark <= 0:
            return
        for s, c in zip(self.sems, self.snap[mark]):
            if c:
                eng.wait_ge(s, c)


def build_nc(debug=(), upto="full", detect=False):
    nc = bass.Bass("TRN2", target_bir_lowering=False,
                   detect_race_conditions=detect)
    dbg = set(debug)

    h_ = {}
    def din(name, shape, dt=F32):
        h_[name] = nc.dram_tensor(name, list(shape), dt, kind="ExternalInput").ap()

    din("x_own", (T, D)); din("x_halo", (HALO, D))
    din("w_in", (D, 2 * INNER)); din("w_delta", (INNER, INNER)); din("w_out", (INNER, D))
    din("b_in_t", (128, 32)); din("conv_w_t", (128, IC, 3)); din("conv_b_t", (128, IC))
    din("b_delta_t", (128, IC)); din("b_out_t", (128, DC))
    din("mgam_t", (128, DC)); din("mbeta_t", (128, DC))
    for l in range(2):
        din(f"wqkv{l}", (D, 1536)); din(f"wo{l}", (512, D))
        din(f"w1p{l}", (D, F1P)); din(f"b1p_t{l}", (128, F1P // 128))
        din(f"w2p{l}", (HIDP, D)); din(f"b2_t{l}", (128, DC))
        din(f"lng_t{l}", (128, DC)); din(f"lnb_t{l}", (128, DC))
        din(f"qs{l}", (1, DH)); din(f"ks{l}", (1, DH))
    din("tabs_hi", (128, 20, 256)); din("tabs_lo", (128, 20, 256))
    din("mask_n", (128, 4, 1024), BF16); din("mask_c0", (128, 4, 1024), BF16)
    din("id_bf", (128, 128), BF16); din("id_f32", (128, 128))
    din("ones_row", (1, 128)); din("ones_col", (128, 1)); din("eps_col", (128, 1))
    din("sel_first", (128, 1)); din("sel_second", (128, 1))

    out_h = nc.dram_tensor("out", [T, D], F32, kind="ExternalOutput").ap()

    def dint(name, shape, dt=F32):
        return nc.dram_tensor(name, list(shape), dt).ap()

    sp_xn = dint("sp_xn", (128, DC, T))
    sp_vs = dint("sp_vs", (128, IC, T))
    sp_sg = dint("sp_sg", (128, IC, T))
    sp_fw = dint("sp_fw", (128, IC, T))
    sp_pf = dint("sp_pf", (128, IC, T))
    sp_bw = dint("sp_bw", (128, IC, T))
    sp_pb = dint("sp_pb", (128, IC, T))
    sp_xat = dint("sp_xat", (128, DC, T))
    sp_xl1 = dint("sp_xl1", (128, DC, T))
    sp_s = dint("sp_s", (128, DC, T))
    sp_xh = dint("sp_xh", (128, DC, HALO), BF16)
    sp_tab = dint("sp_tab", (128, 20, 256))
    sp_brow = dint("sp_brow", (32, 128))
    cc1_in = dint("cc1_in", (128, 2 * IC))
    cc1_out = dint("cc1_out", (128, 2 * IC))
    cc2_in = dint("cc2_in", (128, DC, HALO))
    cc2_out = dint("cc2_out", (128, DC, HALO))
    cca_in = [dint(f"cca_in{l}", (128, DC)) for l in range(2)]
    cca_out = [dint(f"cca_out{l}", (128, DC)) for l in range(2)]

    dbg_h = {}
    def dtap(name):
        if name in dbg and name not in dbg_h:
            src = {"xn": sp_xn, "vs": sp_vs, "sg": sp_sg, "fw": sp_fw,
                   "pf": sp_pf, "bw": sp_bw, "pb": sp_pb, "xat": sp_xat,
                   "xl1": sp_xl1, "s": sp_s}.get(name)
            if src is None:
                return
            dbg_h[name] = (nc.dram_tensor("dbg_" + name, list(src.shape), F32,
                                          kind="ExternalOutput").ap(), src)

    for nm in dbg:
        dtap(nm)
    dbg_attn = {}
    if "attn0" in dbg:
        for nm, shape in [("qf0", (128, 4, CH)), ("kc0", (128, 4, CH)),
                          ("vc0", (128, 4, CH)), ("af0", (128, 4, CH))]:
            dbg_attn[nm] = nc.dram_tensor("dbg_" + nm, list(shape), BF16,
                                          kind="ExternalOutput").ap()
        dbg_attn["xat1"] = nc.dram_tensor("dbg_xat1", [128, DC, T], F32,
                                          kind="ExternalOutput").ap()
        dbg_attn["hn0"] = nc.dram_tensor("dbg_hn0", [128, DC, 1024], BF16,
                                         kind="ExternalOutput").ap()
        dbg_attn["h20"] = nc.dram_tensor("dbg_h20", [128, NBK, 1024], BF16,
                                         kind="ExternalOutput").ap()
        dbg_attn["rows0"] = nc.dram_tensor("dbg_rows0", [4, 1024], F32,
                                           kind="ExternalOutput").ap()
    if "attn0" in dbg:
        dbg_attn["blkb0"] = nc.dram_tensor("dbg_blkb0", [128, DC, 1025], F32,
                                           kind="ExternalOutput").ap()
    dbg_a1 = {}
    if "a1adbg" in dbg:
        dbg_a1["stat"] = nc.dram_tensor("dbg_stat", [128, 32], F32,
                                        kind="ExternalOutput").ap()
        dbg_a1["xntm"] = nc.dram_tensor("dbg_xntm", [128, 4, D], F32,
                                        kind="ExternalOutput").ap()
        dbg_a1["xtm"] = nc.dram_tensor("dbg_xtm", [128, 4, D], F32,
                                       kind="ExternalOutput").ap()
        dbg_attn["blkb0"] = nc.dram_tensor("dbg_blkb0", [128, DC, 1025], F32,
                                           kind="ExternalOutput").ap()

    ctx = ExitStack()
    with ctx:
        def sem(name):
            return ctx.enter_context(nc.semaphore(name))
        cTE, cVE, cAC = Ctr(sem("cTE")), Ctr(sem("cVE")), Ctr(sem("cAC"))
        cGP, cSY = Ctr(sem("cGP")), Ctr(sem("cSY"))
        cCC = Ctr(sem("cCC"))
        qS = Q([sem(f"qS{i}") for i in range(4)])
        qG = Q([sem(f"qG{i}") for i in range(3)])
        qC = Q([sem(f"qC{i}") for i in range(4)])

        block = ctx.enter_context(nc.Block())
        te, ve, ac = block.tensor, block.vector, block.scalar
        gp, sy = block.gpsimd, block.sync
        ECTR = {"te": cTE, "ve": cVE, "ac": cAC, "gp": cGP, "sy": cSY}
        EDEC = {"te": te, "ve": ve, "ac": ac, "gp": gp, "sy": sy}

        def barrier():
            def _dsy(e):
                qS.wait(e, qS.n)
            sy(_dsy)
            def _dgp(e):
                qG.wait(e, qG.n)
            gp(_dgp)
            def _dac(e):
                qC.wait(e, qC.n)
            ac(_dac)
            marks = {}
            def _ite(e):
                e.wait_ge(cVE.sem, cVE.n)
                cTE.inc(e.ldweights(id_bf[:, 0:4]))
            te(_ite)
            marks["te"] = cTE.n
            def _ive(e):
                e.wait_ge(cVE.sem, cVE.n)
                cVE.inc(e.tensor_copy(bsc[:, 0:1], bsc[:, 3:4]))
            ve(_ive)
            marks["ve"] = cVE.n
            def _iac(e):
                e.wait_ge(cAC.sem, cAC.n)
                cAC.inc(e.activation(bsc[:, 1:2], bsc[:, 3:4], Act.Copy,
                                     bias=0.0, scale=1.0))
            ac(_iac)
            marks["ac"] = cAC.n
            def _igp(e):
                e.wait_ge(cGP.sem, cGP.n)
                cGP.inc(e.tensor_copy(bsc[:, 2:3], bsc[:, 3:4]))
            gp(_igp)
            marks["gp"] = cGP.n
            def _isy(e):
                cSY.n += 1
                e.sem_inc(cSY.sem, 1)
            sy(_isy)
            marks["sy"] = cSY.n
            for name in EDEC:
                def _wait(e, me=name):
                    for o, tgt in marks.items():
                        if o != me:
                            e.wait_ge(ECTR[o].sem, tgt)
                EDEC[name](_wait)

        def sb(stack, name, shape, dt=F32):
            return stack.enter_context(nc.sbuf_tensor(name, list(shape), dt))

        mask_nS = sb(ctx, "mask_nS", (128, 4, 1024), BF16)
        mask_c0S = sb(ctx, "mask_c0S", (128, 4, 1024), BF16)
        id_bf = sb(ctx, "id_bfS", (128, 128), BF16)
        id_f32 = sb(ctx, "id_f32S", (128, 128))
        ones_row = sb(ctx, "ones_rowS", (1, 128))
        ones_col = sb(ctx, "ones_colS", (128, 1))
        eps_col = sb(ctx, "eps_colS", (128, 1))
        sel_first = sb(ctx, "sel_firstS", (128, 1))
        sel_second = sb(ctx, "sel_secondS", (128, 1))
        b_in = sb(ctx, "b_inS", (128, 32))
        beff = sb(ctx, "beffS", (128, 32))
        conv_w = sb(ctx, "conv_wS", (128, IC, 3))
        conv_b = sb(ctx, "conv_bS", (128, IC))
        b_delta = sb(ctx, "b_deltaS", (128, IC))
        nb_delta = sb(ctx, "nb_deltaS", (128, IC))
        b_out = sb(ctx, "b_outS", (128, DC))
        mgam = sb(ctx, "mgamS", (128, DC))
        mbeta = sb(ctx, "mbetaS", (128, DC))
        lvec = {}
        for l in range(2):
            for key, shape in [("b1p", (128, F1P // 128)), ("b2", (128, DC)),
                               ("lng", (128, DC)), ("lnb", (128, DC)),
                               ("qs", (1, DH)), ("ks", (1, DH))]:
                lvec[f"{key}{l}"] = sb(ctx, f"{key}S{l}", shape)
        bsc = sb(ctx, "bscS", (128, 4))
        bscb = sb(ctx, "bscbS", (128, 4), BF16)
        ccR1 = sb(ctx, "ccR1", (128, IC))
        ccR2 = sb(ctx, "ccR2", (128, IC))
        R1e = sb(ctx, "R1e", (128, IC))
        R2e = sb(ctx, "R2e", (128, IC))
        scA = sb(ctx, "scA", (128, DC))
        sideL1 = sb(ctx, "sideL1", (128, DC, 1))
        sideL2 = sb(ctx, "sideL2", (128, DC, 1))

        psW = ctx.enter_context(nc.psum_tensor("psW", [128, T], F32))
        psA = ctx.enter_context(nc.psum_tensor("psA", [128, 1024], F32))
        psT = ctx.enter_context(nc.psum_tensor("psT", [128, 1024], BF16))
        psR = ctx.enter_context(nc.psum_tensor("psR", [128, 512], F32))
        S512 = [psW[:, 512 * i:512 * (i + 1)] for i in range(4)] + \
               [psA[:, 0:512], psA[:, 512:1024]]
        S1024 = [psA[:, :], psW[:, 0:1024], psW[:, 1024:2048]]

        def psT3(f=128):
            return psT.ap().rearrange("p (c f) -> p c f", f=f)

        def psA3(f=128):
            return psA.ap().rearrange("p (c f) -> p c f", f=f)

        p0 = ExitStack()
        tb_lo = sb(p0, "tb_lo", (128, 20, 256))
        tb_hi = sb(p0, "tb_hi", (128, 20, 256))

        def _p0s(s):
            for nm, tile in [("mask_n", mask_nS), ("mask_c0", mask_c0S),
                             ("id_bf", id_bf), ("id_f32", id_f32),
                             ("ones_row", ones_row), ("ones_col", ones_col),
                             ("eps_col", eps_col),
                             ("sel_first", sel_first), ("sel_second", sel_second),
                             ("b_in_t", b_in), ("conv_w_t", conv_w),
                             ("conv_b_t", conv_b), ("b_delta_t", b_delta),
                             ("b_out_t", b_out), ("mgam_t", mgam),
                             ("mbeta_t", mbeta), ("tabs_hi", tb_hi),
                             ("tabs_lo", tb_lo)]:
                qS.dma(s, out=tile[:], in_=h_[nm][:])
            for l in range(2):
                for key, nm in [("b1p", f"b1p_t{l}"), ("b2", f"b2_t{l}"),
                                ("lng", f"lng_t{l}"), ("lnb", f"lnb_t{l}"),
                                ("qs", f"qs{l}"), ("ks", f"ks{l}")]:
                    qS.dma(s, out=lvec[f"{key}{l}"][:], in_=h_[nm][:])
        sy(_p0s)
        p0_ld = qS.n

        def _p0v(v):
            v.memset(bsc[:], 0.0)
            v.memset(bscb[:], 0.0)
            qS.wait(v, p0_ld)
            v.tensor_tensor(tb_hi[:], tb_hi[:], tb_lo[:], Alu.add)
            cVE.inc(v.tensor_scalar(nb_delta[:], b_delta[:], -1.0, None, Alu.mult))
        ve(_p0v)
        tabs_v = cVE.n

        def _p0s2(s):
            s.wait_ge(cVE.sem, tabs_v)
            qS.dma(s, out=sp_tab[:], in_=tb_hi[:])
        sy(_p0s2)
        tab_spill = qS.n
        p0.close()

        def _endprog():
            if dbg_h:
                def _tap(g):
                    for nm, (dst, src) in dbg_h.items():
                        qG.dma(g, out=dst[:], in_=src[:])
                    qG.wait(g, qG.n)
                gp(_tap)
            barrier()
            def _fin(s):
                qS.wait(s, qS.n)
            sy(_fin)

        a1 = ExitStack()
        w_inS = sb(a1, "w_inS", (128, DC, 2 * INNER), BF16)
        x_tm = sb(a1, "x_tm", (128, 4, D))
        xsq = sb(a1, "xsq", (128, 2, D))
        xn_tm = sb(a1, "xn_tm", (128, 4, D))
        xnm_tm = sb(a1, "xnm_tm", (128, 4, D), BF16)
        stat = sb(a1, "stat", (128, 16))
        st2 = sb(a1, "st2", (128, 16))
        xn_fm = sb(a1, "xn_fm", (128, DC, CH + 1))
        xnm_fm = sb(a1, "xnm_fm", (128, DC, CH), BF16)
        vals = sb(a1, "vals", (128, 2, CH + 2))
        vtail = sb(a1, "vtail", (128, IC, 2))
        sgb2 = [sb(a1, f"sgb2_{i}", (128, CH)) for i in range(2)]
        mbeta_bf = sb(a1, "mbeta_bf", (128, DC), BF16)
        brow_s = sb(a1, "brow_s", (1, 512))
        cb_eff = sb(a1, "cb_eff", (128, IC))
        nsel1 = sb(a1, "nsel1", (128, 1))
        sc1 = sb(a1, "sc1", (128, 1))
        hx = ExitStack()
        xh_fm = sb(hx, "xh_fm", (128, DC, HALO), BF16)
        xnmh_fm = sb(hx, "xnmh_fm", (128, DC, 128), BF16)

        def _a1g(g):
            qS.wait(g, tab_spill)
            wv = h_["w_in"].rearrange("(c p) f -> c p f", p=128)
            qG.dma(g, out=w_inS[:], in_=wv.transpose([1, 0, 2]))
        gp(_a1g)
        w_in_ld = qG.n

        def _mbv(v):
            qS.wait(v, p0_ld)
            cVE.inc(v.tensor_copy(mbeta_bf[:], mbeta[:]))
        ve(_mbv)
        mbv = cVE.n
        for nch in range(8):
            def _t(t, nch=nch):
                if nch == 0:
                    t.wait_ge(cVE.sem, mbv)
                    qG.wait(t, w_in_ld)
                ii = None
                for dc in range(DC):
                    ii = t.matmul(psR[0:1, :], mbeta_bf[:, dc:dc + 1],
                                  w_inS[:, dc, 512 * nch:512 * nch + 512],
                                  start=(dc == 0), stop=(dc == DC - 1))
                cTE.inc(ii)
            te(_t)
            mmn = cTE.n
            def _v(v, nch=nch, mmn=mmn):
                v.wait_ge(cTE.sem, mmn)
                if nch >= 1:
                    qS.wait(v, brow_mark[nch - 1])
                cVE.inc(v.tensor_copy(brow_s[:], psR[0:1, :]))
            if nch == 0:
                brow_mark = {}
            ve(_v)
            evn = cVE.n
            def _bs1(s, nch=nch, evn=evn):
                s.wait_ge(cVE.sem, evn)
                qS.dma(s,
                       out=sp_brow.rearrange("(a t) p -> a (t p)", a=8)[nch:nch + 1],
                       in_=brow_s[:])
            sy(_bs1)
            brow_mark[nch] = qS.n
            if nch < 7:
                te(lambda t, evn=evn: t.wait_ge(cVE.sem, evn))
        t1 = qS.n
        def _bs2(s):
            qS.wait(s, t1)
            qS.dma(s, out=beff[:], in_=sp_brow.transpose([1, 0]),
                               allow_slow_non_contiguous=True)
        sy(_bs2)
        t2 = qS.n
        def _bv(v):
            qS.wait(v, t2)
            cVE.inc(v.tensor_tensor(beff[:], beff[:], b_in[:], Alu.add))
            ii = None
            for dc in range(DC):
                ii = v.tensor_scalar(w_inS[:, dc, :], w_inS[:, dc, :],
                                     mgam[:, dc:dc + 1], None, Alu.mult)
            cVE.inc(ii)
        ve(_bv)
        def _cb(v):
            cVE.inc(v.tensor_tensor(cb_eff[:], conv_w[:, :, 0], conv_w[:, :, 1], Alu.add))
            v.wait_ge(cVE.sem, cVE.n)
            cVE.inc(v.tensor_tensor(cb_eff[:], cb_eff[:], conv_w[:, :, 2], Alu.add))
            v.wait_ge(cVE.sem, cVE.n)
            cVE.inc(v.tensor_tensor(cb_eff[:], cb_eff[:], beff[:, 0:IC], Alu.mult))
            v.wait_ge(cVE.sem, cVE.n)
            cVE.inc(v.tensor_tensor(cb_eff[:], cb_eff[:], conv_b[:], Alu.add))
            cVE.inc(v.tensor_scalar(nsel1[:], sel_first[:], -1.0, None, Alu.mult))
        ve(_cb)
        winS_ready = cVE.n

        def ln_stats(nts):
            for ts in range(nts):
                def _v(v, ts=ts):
                    ii = v.tensor_scalar(xn_tm[:, ts, :], x_tm[:, ts, :], 1.0, 0.0,
                                         Alu.mult, Alu.add,
                                         accum_out=stat[:, ts:ts + 1])
                    cVE.inc(ii)
                ve(_v)
                def _a(a, ts=ts):
                    if ts >= 2:
                        a.wait_ge(cAC.sem, sq_mark[ts - 2])
                    cAC.inc(a.activation(xsq[:, ts % 2, :], x_tm[:, ts, :],
                                         Act.Square,
                                         accum_out=stat[:, 4 + ts:5 + ts]))
                if ts == 0:
                    sq_mark = {}
                ac(_a)
                sq_mark[ts] = cAC.n
            acw, vew = cAC.n, cVE.n
            s_ = slice(0, nts)
            def _f1(v):
                v.wait_ge(cAC.sem, acw)
                v.wait_ge(cVE.sem, vew)
                cVE.inc(v.tensor_scalar(stat[:, 8:8 + nts], stat[:, s_], R1024,
                                        None, Alu.mult))
                v.wait_ge(cVE.sem, cVE.n)
                v.tensor_scalar(stat[:, 12:12 + nts], stat[:, 8:8 + nts], -1.0,
                                None, Alu.mult)
                cVE.inc(v.tensor_tensor(st2[:, 12:12 + nts], stat[:, 8:8 + nts],
                                        stat[:, 8:8 + nts], Alu.mult))
                v.wait_ge(cVE.sem, cVE.n)
                cVE.inc(v.scalar_tensor_tensor(st2[:, 0:nts], stat[:, 4:4 + nts],
                                               R1024, st2[:, 12:12 + nts],
                                               Alu.mult, Alu.subtract))
            ve(_f1)
            vw = cVE.n
            def _f2(a):
                a.wait_ge(cVE.sem, vw)
                cAC.inc(a.activation(st2[:, 4:4 + nts], st2[:, 0:nts], Act.Sqrt,
                                     bias=eps_col[:], scale=1.0))
            ac(_f2)
            aw = cAC.n
            def _f3(v):
                v.wait_ge(cAC.sem, aw)
                cVE.inc(v.reciprocal(st2[:, 4:4 + nts], st2[:, 4:4 + nts]))
                v.wait_ge(cVE.sem, cVE.n)
                cVE.inc(v.tensor_tensor(st2[:, 8:8 + nts], st2[:, 4:4 + nts],
                                        st2[:, 4:4 + nts], Alu.mult))
                v.wait_ge(cVE.sem, cVE.n)
                cVE.inc(v.tensor_tensor(st2[:, 8:8 + nts], st2[:, 0:nts],
                                        st2[:, 8:8 + nts], Alu.mult))
            ve(_f3)
            vw2 = cVE.n
            def _f4(a):
                a.wait_ge(cVE.sem, vw2)
                cAC.inc(a.activation(st2[:, 12:12 + nts], st2[:, 8:8 + nts], Act.Sqrt,
                                     bias=eps_col[:], scale=1.0))
            ac(_f4)
            aw2 = cAC.n
            def _f5(v):
                v.wait_ge(cAC.sem, aw2)
                cVE.inc(v.reciprocal(st2[:, 12:12 + nts], st2[:, 12:12 + nts]))
                v.wait_ge(cVE.sem, cVE.n)
                cVE.inc(v.tensor_tensor(st2[:, 8:8 + nts], st2[:, 4:4 + nts],
                                        st2[:, 12:12 + nts], Alu.mult))
            ve(_f5)
            return cVE.n

        def ln_apply(ts, want_xnm=True):
            def _v(v, ts=ts):
                ii = v.tensor_scalar(xn_tm[:, ts, :], x_tm[:, ts, :],
                                     stat[:, 12 + ts:13 + ts],
                                     st2[:, 4 + ts:5 + ts], Alu.add, Alu.mult)
                if want_xnm:
                    ii = v.tensor_scalar(xnm_tm[:, ts, :], x_tm[:, ts, :],
                                         stat[:, 12 + ts:13 + ts],
                                         st2[:, 8 + ts:9 + ts], Alu.add, Alu.mult)
                cVE.inc(ii)
            ve(_v)

        def _hs(s):
            qS.wait(s, p0_ld)
            for i, off in enumerate([3, 131, 259, 387]):
                qS.dma(s, out=x_tm[:, i, :], in_=h_["x_halo"][off:off + 128, :])
        sy(_hs)
        hld = qS.n
        ve(lambda v: qS.wait(v, hld))
        ac(lambda a: qS.wait(a, hld))
        fin = ln_stats(4)
        ve(lambda v, t=fin: v.wait_ge(cVE.sem, t))
        for ts in range(4):
            ln_apply(ts, want_xnm=False)
        apl = cVE.n

        te(lambda t, t0=apl: t.wait_ge(cVE.sem, t0))
        for i, off in enumerate([3, 131, 259, 387]):
            def _tr(t, i=i):
                ii = None
                for dc in range(DC):
                    ii = t.transpose(psA[:, dc * 128:dc * 128 + 128],
                                     xn_tm[:, i, dc * 128:dc * 128 + 128], id_f32[:])
                cTE.inc(ii)
            te(_tr)
            trn = cTE.n
            def _ev(v, off=off, trn=trn):
                v.wait_ge(cTE.sem, trn)
                cVE.inc(v.tensor_copy(xh_fm.ap()[:, :, off:off + 128], psA3()))
            ve(_ev)
            evn = cVE.n
            te(lambda t, evn=evn: t.wait_ge(cVE.sem, evn))
        trm = cTE.n
        def _hs2(s):
            s.wait_ge(cTE.sem, trm)
            qS.dma(s, out=x_tm[:, 0, :], in_=h_["x_halo"][388:516, :])
        sy(_hs2)
        hld2 = qS.n
        ve(lambda v: qS.wait(v, hld2))
        ac(lambda a: qS.wait(a, hld2))
        fin2 = ln_stats(1)
        ve(lambda v, t=fin2: v.wait_ge(cVE.sem, t))
        ln_apply(0, want_xnm=True)
        apl2 = cVE.n
        def _tr2(t):
            t.wait_ge(cVE.sem, apl2)
            for dc in range(DC):
                t.transpose(psA[:, dc * 128:dc * 128 + 128],
                            xn_tm[:, 0, dc * 128:dc * 128 + 128], id_f32[:])
            ii = None
            for dc in range(DC):
                ii = t.transpose(psT[:, dc * 128:dc * 128 + 128],
                                 xnm_tm[:, 0, dc * 128:dc * 128 + 128], id_bf[:])
            cTE.inc(ii)
        te(_tr2)
        tr2 = cTE.n
        def _ev2(v):
            v.wait_ge(cTE.sem, tr2)
            v.tensor_copy(xh_fm.ap()[:, :, 388:516], psA3())
            cVE.inc(v.tensor_copy(xnmh_fm.ap()[:, :, :], psT3()))
        ve(_ev2)
        xh_done = cVE.n
        def _hsp(s):
            s.wait_ge(cVE.sem, xh_done)
            qS.dma(s, out=sp_xh[:, :, 3:HALO], in_=xh_fm[:, :, 3:HALO])
        sy(_hsp)
        def _sd1(v):
            v.wait_ge(cVE.sem, xh_done)
            cVE.inc(v.tensor_scalar(sideL1[:], xh_fm[:, :, 515:516],
                                    sel_second[:], None, Alu.mult))
        ve(_sd1)

        te(lambda t: t.wait_ge(cVE.sem, max(winS_ready, xh_done)))
        hp_marks = {}
        for f in range(IC):
            def _t(t, f=f):
                ii = None
                for dc in range(DC):
                    ii = t.matmul(psR[:, 0:2], w_inS[:, dc, 128 * f:128 * f + 128],
                                  xnmh_fm[:, dc, 126:128],
                                  start=(dc == 0), stop=(dc == DC - 1))
                cTE.inc(ii)
            te(_t)
            mmn = cTE.n
            def _v(v, f=f, mmn=mmn):
                v.wait_ge(cTE.sem, mmn)
                v.tensor_scalar(vals[:, 0, 0:2], psR[:, 0:2],
                                sel_second[:], None, Alu.mult)
                cVE.inc(v.tensor_tensor(sc1[:], nsel1[:], beff[:, f:f + 1], Alu.mult))
                v.wait_ge(cVE.sem, cVE.n)
                cVE.inc(v.tensor_scalar(vtail[:, f, :], vals[:, 0, 0:2],
                                        sc1[:], None, Alu.add))
            ve(_v)
            evn = cVE.n
            te(lambda t, evn=evn: t.wait_ge(cVE.sem, evn))

        barrier()
        hx.close()
        vs16 = sb(a1, "vs16", (128, IC, CH))

        xr_own = h_["x_own"].rearrange("(a p) d -> a p d", p=128)
        conv_mark = {}
        vspill_mark = {}
        sgspill_mark = {}
        evac_mark = {}
        spxn_mark = {}
        winmm_mark = {}
        for c in range(NCH):
            def _xl(s, c=c):
                if c == 0:
                    s.wait_ge(cTE.sem, tr2)
                qS.dma(s, out=x_tm[:],
                                   in_=xr_own[4 * c:4 * c + 4].transpose([1, 0, 2]))
            sy(_xl)
            xld = qS.n
            ve(lambda v: qS.wait(v, xld))
            ac(lambda a: qS.wait(a, xld))
            if c >= 1:
                def _gx(v, c=c):
                    qS.wait(v, spxn_mark[c - 1])
                    v.wait_ge(cTE.sem, winmm_mark[c - 1])
                ve(_gx)
            fin = ln_stats(4)
            ve(lambda v, t=fin: v.wait_ge(cVE.sem, t))
            for ts in range(4):
                ln_apply(ts)
            aplv = cVE.n
            if dbg_a1 and c == 0:
                def _ta1(s, aplv=aplv):
                    s.wait_ge(cVE.sem, aplv)
                    qS.dma(s, out=dbg_a1["stat"][:, 0:16], in_=stat[:])
                    qS.dma(s, out=dbg_a1["stat"][:, 16:32], in_=st2[:])
                    qS.dma(s, out=dbg_a1["xntm"][:], in_=xn_tm[:])
                    qS.dma(s, out=dbg_a1["xtm"][:], in_=x_tm[:])
                sy(_ta1)
                _tm = qS.n
                ve(lambda v, m=_tm: qS.wait(v, m))
                ac(lambda a, m=_tm: qS.wait(a, m))
            te(lambda t, t0=aplv: t.wait_ge(cVE.sem, t0))
            for ts in range(4):
                def _tr(t, ts=ts):
                    for dc in range(DC):
                        t.transpose(psA[:, dc * 128:dc * 128 + 128],
                                    xn_tm[:, ts, dc * 128:dc * 128 + 128], id_f32[:])
                    ii = None
                    for dc in range(DC):
                        ii = t.transpose(psT[:, dc * 128:dc * 128 + 128],
                                        xnm_tm[:, ts, dc * 128:dc * 128 + 128], id_bf[:])
                    cTE.inc(ii)
                te(_tr)
                trn = cTE.n
                def _ev(v, ts=ts, trn=trn):
                    v.wait_ge(cTE.sem, trn)
                    v.tensor_copy(xn_fm.ap()[:, :, 1 + ts * 128:1 + ts * 128 + 128], psA3())
                    cVE.inc(v.tensor_copy(
                        xnm_fm.ap()[:, :, ts * 128:ts * 128 + 128], psT3()))
                ve(_ev)
                evn = cVE.n
                te(lambda t, evn=evn: t.wait_ge(cVE.sem, evn))
            fm_done = cVE.n
            def _spxn(s, c=c, t0=fm_done):
                s.wait_ge(cVE.sem, t0)
                qS.dma(s, out=sp_xn[:, :, c * CH:(c + 1) * CH],
                                   in_=xn_fm[:, :, 1:CH + 1])
            sy(_spxn)
            spxn_mark[c] = qS.n

            mm_mark2 = {}
            ts_mark = {}

            def emit_mm(f, c=c):
                slot = S512[f % 6]
                def _mm(t, f=f, slot=slot):
                    gk = (c, f - 6) if f >= 6 else (c - 1, f + 26)
                    if gk in evac_mark:
                        t.wait_ge(cAC.sem, evac_mark[gk])
                    ii = None
                    for dc in range(DC):
                        ii = t.matmul(slot, w_inS[:, dc, 128 * f:128 * f + 128],
                                      xnm_fm[:, dc, :],
                                      start=(dc == 0), stop=(dc == DC - 1))
                    cTE.inc(ii)
                te(_mm)
                mm_mark2[f] = cTE.n

            def emit_ev_pair(f0, c=c):
                def _ev(a, f0=f0):
                    for f in (f0, f0 + 1):
                        a.wait_ge(cTE.sem, mm_mark2[f])
                        gk = (c, f - 2) if f >= 2 else (c - 1, f + 14)
                        if gk in conv_mark:
                            a.wait_ge(cVE.sem, conv_mark[gk])
                        cAC.inc(a.activation(vals[:, f % 2, 2:CH + 2],
                                             S512[f % 6], Act.Copy,
                                             bias=0.0, scale=1.0))
                        evac_mark[(c, f)] = cAC.n
                ac(_ev)

            def emit_cv(f, c=c):
                vb = vals[:, f % 2, :]
                vs = vs16[:, f, :]
                def _cv(v, f=f, vb=vb, vs=vs):
                    v.wait_ge(cAC.sem, evac_mark[(c, f)])
                    gk = (c - 1, f)
                    if gk in vspill_mark:
                        qS.wait(v, vspill_mark[gk])
                    v.tensor_copy(vb[:, 0:2], vtail[:, f, :])
                    cVE.inc(v.tensor_scalar(vs[:], vb[:, 2:CH + 2],
                                            conv_w[:, f, 2:3], None, Alu.mult))
                    v.wait_ge(cVE.sem, cVE.n)
                    cVE.inc(v.scalar_tensor_tensor(vs[:], vb[:, 1:CH + 1],
                                                   conv_w[:, f, 1:2], vs[:],
                                                   Alu.mult, Alu.add))
                    v.wait_ge(cVE.sem, cVE.n)
                    cVE.inc(v.scalar_tensor_tensor(vs[:], vb[:, 0:CH],
                                                   conv_w[:, f, 0:1], vs[:],
                                                   Alu.mult, Alu.add))
                    cVE.inc(v.tensor_copy(vtail[:, f, :], vb[:, CH:CH + 2]))
                ve(_cv)
                conv_mark[(c, f)] = cVE.n

            def emit_si_batch(c=c):
                si_mark = {}
                def _si(a):
                    for f in range(IC):
                        a.wait_ge(cVE.sem, conv_mark[(c, f)])
                        cAC.inc(a.activation(vs16[:, f, :], vs16[:, f, :],
                                             Act.Silu,
                                             bias=cb_eff[:, f:f + 1], scale=1.0))
                        si_mark[f] = cAC.n
                ac(_si)
                def _sp(s):
                    for f in range(IC):
                        s.wait_ge(cAC.sem, si_mark[f])
                        qS.dma(s, out=sp_vs[:, f, c * CH:(c + 1) * CH],
                               in_=vs16[:, f, :])
                        vspill_mark[(c, f)] = qS.n
                sy(_sp)

            def emit_evg_pair(f0, c=c):
                spg_mark = {}
                def _evg(a, f0=f0):
                    for f in (f0, f0 + 1):
                        fg = f - IC
                        a.wait_ge(cTE.sem, mm_mark2[f])
                        gk = (c, fg - 2) if fg >= 2 else (c - 1, fg + 14)
                        if gk in sgspill_mark:
                            qS.wait(a, sgspill_mark[gk])
                        cAC.inc(a.activation(sgb2[fg % 2][:], S512[f % 6],
                                             Act.Sigmoid,
                                             bias=beff[:, f:f + 1], scale=1.0))
                        evac_mark[(c, f)] = cAC.n
                        spg_mark[f] = cAC.n
                ac(_evg)
                def _spg(s, f0=f0):
                    for f in (f0, f0 + 1):
                        fg = f - IC
                        s.wait_ge(cAC.sem, spg_mark[f])
                        qS.dma(s, out=sp_sg[:, fg, c * CH:(c + 1) * CH],
                               in_=sgb2[fg % 2][:])
                        sgspill_mark[(c, fg)] = qS.n
                sy(_spg)

            for s_ in range(17):
                f0 = 2 * s_
                if 0 <= f0 - 2 < IC:
                    emit_cv(f0 - 2)
                if 0 <= f0 - 1 < IC:
                    emit_cv(f0 - 1)
                if f0 < 32:
                    emit_mm(f0)
                    emit_mm(f0 + 1)
                    if f0 < IC:
                        emit_ev_pair(f0)
                    else:
                        emit_evg_pair(f0)
            emit_si_batch()
            winmm_mark[c] = cTE.n

        barrier()
        a1.close()
        if upto == "a1a2":
            _endprog()
            return nc

        b1 = ExitStack()
        vsbB = sb(b1, "vsbB", (128, IC, T), BF16)
        wdS = [sb(b1, f"wdS{i}", (128, IC, 128), BF16) for i in range(2)]
        delt = [sb(b1, f"delt{i}", (128, T)) for i in range(2)]
        signz = [sb(b1, f"signz{i}", (128, T)) for i in range(2)]
        vsf = [sb(b1, f"vsf{i}", (128, T)) for i in range(2)]
        bbuf = sb(b1, "bbuf", (128, T))
        zeros = sb(b1, "zerosB", (128, T))
        sout = [sb(b1, f"sout{i}", (128, T)) for i in range(4)]

        ve(lambda v: cVE.inc(v.memset(zeros[:], 0.0)))
        zdone = cVE.n

        def _b1g(g):
            for cq in range(4):
                qG.dma(g, out=vsbB[:, 4 * cq:4 * cq + 4, :],
                                   in_=sp_vs[:, 4 * cq:4 * cq + 4, :])
        gp(_b1g)
        vsb_ld = qG.n

        wdr = h_["w_delta"].rearrange("(ci p) co -> ci p co", p=128)
        ac_mark = {}
        scan_mark = {}
        spill_mark = {}
        mm_mark = {}
        wl_mark = {}

        def _wl0(g):
            qG.dma(g, out=wdS[0][:],
                   in_=wdr[:, :, 0:128].transpose([1, 0, 2]))
        gp(_wl0)
        wl_mark[0] = qG.n
        for ct in range(IC):
            if ct + 1 < IC:
                def _wl(g, ct=ct):
                    if ct >= 1:
                        g.wait_ge(cTE.sem, mm_mark[ct - 1])
                    qG.dma(g, out=wdS[(ct + 1) % 2][:],
                           in_=wdr[:, :, 128 * (ct + 1):128 * (ct + 1) + 128]
                               .transpose([1, 0, 2]))
                gp(_wl)
                wl_mark[ct + 1] = qG.n
            wl = wl_mark[ct]
            def _vl(a, ct=ct):
                if ct >= 2:
                    a.wait_ge(cVE.sem, scan_mark[ct - 2])
                qC.dma(a, out=vsf[ct % 2][:], in_=sp_vs[:, ct, :])
            ac(_vl)
            vl = qC.n

            if ct == 0:
                gscan_mark = {}
                ev_half = {}
            for h in range(2):
                def _mm(t, ct=ct, h=h, wl=wl):
                    if (ct, h) == (0, 0):
                        qG.wait(t, vsb_ld)
                    qG.wait(t, wl)
                    if ct >= 1:
                        t.wait_ge(cAC.sem, ev_half[(ct - 1, h)])
                    ii = None
                    for qr in range(2):
                        o = 1024 * h + 512 * qr
                        for ci in range(IC):
                            ii = t.matmul(psW[:, o:o + 512],
                                          wdS[ct % 2][:, ci, :],
                                          vsbB[:, ci, o:o + 512],
                                          start=(ci == 0), stop=(ci == IC - 1))
                    cTE.inc(ii)
                te(_mm)
                mm_mark[ct] = cTE.n
                mmn = cTE.n
                def _ev(a, ct=ct, h=h, mmn=mmn):
                    a.wait_ge(cTE.sem, mmn)
                    if h == 0 and ct >= 2:
                        a.wait_ge(cVE.sem, gscan_mark[ct - 2])
                    o = 1024 * h
                    a.activation(delt[ct % 2][:, o:o + 1024],
                                 psW[:, o:o + 1024], Act.Sigmoid,
                                 bias=b_delta[:, ct:ct + 1], scale=1.0)
                    cAC.inc(a.activation(signz[ct % 2][:, o:o + 1024],
                                         psW[:, o:o + 1024], Act.Sigmoid,
                                         bias=nb_delta[:, ct:ct + 1], scale=-1.0))
                ac(_ev)
                ev_half[(ct, h)] = cAC.n
            ac_mark[ct] = cAC.n
            acn = cAC.n
            def _sc(v, ct=ct, acn=acn, vl=vl):
                v.wait_ge(cAC.sem, acn)
                qC.wait(v, vl)
                if ct == 0:
                    v.wait_ge(cVE.sem, zdone)
                if ct >= 1:
                    qS.wait(v, spill_mark[ct - 1])
                cVE.inc(v.tensor_tensor(bbuf[:], signz[ct % 2][:],
                                        vsf[ct % 2][:], Alu.mult))
                v.wait_ge(cVE.sem, cVE.n)
                v.tensor_tensor_scan(sout[0][:], delt[ct % 2][:], bbuf[:], 0.0,
                                     Alu.mult, Alu.add)
                cVE.inc(v.tensor_tensor_scan(sout[2][:, ::-1], delt[ct % 2][:, ::-1],
                                             bbuf[:, ::-1], 0.0, Alu.mult, Alu.add))
                v.wait_ge(cVE.sem, cVE.n)
                v.tensor_copy(ccR1[:, ct:ct + 1], sout[0][:, T - 1:T])
                cVE.inc(v.tensor_copy(ccR2[:, ct:ct + 1], sout[2][:, 0:1]))
            ve(_sc)
            scan_mark[ct] = cVE.n
            scn = cVE.n
            def _gsc(v, ct=ct):
                v.tensor_tensor_scan(sout[1][:], delt[ct % 2][:], zeros[:], 1.0,
                                     Alu.mult, Alu.add)
                cVE.inc(v.tensor_tensor_scan(sout[3][:, ::-1], delt[ct % 2][:, ::-1],
                                             zeros[:], 1.0, Alu.mult, Alu.add))
            ve(_gsc)
            gscan_mark[ct] = cVE.n
            gscn = cVE.n
            def _sp(s, ct=ct, scn=scn, gscn=gscn):
                s.wait_ge(cVE.sem, gscn)
                qS.dma(s, out=sp_fw[:, ct, :], in_=sout[0][:])
                qS.dma(s, out=sp_pf[:, ct, :], in_=sout[1][:])
                qS.dma(s, out=sp_bw[:, ct, :], in_=sout[2][:])
                qS.dma(s, out=sp_pb[:, ct, :], in_=sout[3][:])
            sy(_sp)
            spill_mark[ct] = qS.n

        scnall = cVE.n
        def _c1v(v):
            v.wait_ge(cVE.sem, cVE.n)
            v.tensor_scalar(ccR1[:], ccR1[:], sel_first[:], None, Alu.mult)
            cVE.inc(v.tensor_scalar(ccR2[:], ccR2[:], sel_second[:], None, Alu.mult))
        ve(_c1v)
        c1v = cVE.n
        def _c1g(g):
            g.wait_ge(cVE.sem, c1v)
            qG.dma(g, out=cc1_in[:, 0:IC], in_=ccR1[:])
            qG.dma(g, out=cc1_in[:, IC:2 * IC], in_=ccR2[:])
            qG.wait(g, qG.n)
            cCC.inc(g.collective_compute("AllReduce", Alu.add, replica_groups=RG,
                                         ins=[cc1_in.opt()], outs=[cc1_out.opt()]))
        gp(_c1g)
        cc1_done = cCC.n

        barrier()
        b1.close()
        if upto == "a1b":
            _endprog()
            return nc

        def emit_attn(l, xin_sp, is_c2, cc2_wait=0, sub=6):
            lx = ExitStack()
            ltab = sb(lx, f"ltab{l}", (128, 20, 256))
            wqkvS = sb(lx, f"wqkvS{l}", (128, DC, 1536), BF16)
            woS = sb(lx, f"woS{l}", (128, 4, 1024), BF16)
            xi32 = sb(lx, f"xi32_{l}", (128, DC, CH + 1))
            xib = sb(lx, f"xib{l}", (128, DC, CH + 1), BF16)
            xatc = sb(lx, f"xatc{l}", (128, DC, CH))
            qk2 = [sb(lx, f"qk2{l}_{i}", (128, 1024)) for i in range(2)]
            sqb2 = [sb(lx, f"sqb2{l}_{i}", (128, 1024)) for i in range(2)]
            rot2 = [sb(lx, f"rot2{l}_{i}", (128, 1024)) for i in range(2)]
            qf2 = [sb(lx, f"qf2{l}_{i}", (128, 1024), BF16) for i in range(2)]
            ssn = sb(lx, f"ssn{l}", (128, 64))
            qfm = sb(lx, f"qfm{l}", (128, 4, CH), BF16)
            kc = [sb(lx, f"kc{l}_{i}", (128, 4, CH), BF16) for i in range(2)]
            vc = [sb(lx, f"vc{l}_{i}", (128, 4, CH), BF16) for i in range(2)]
            probs = [sb(lx, f"probs{l}_{i}", (128, 1024), BF16) for i in range(2)]
            probsN = [sb(lx, f"probsN{l}_{i}", (128, 1024), BF16) for i in range(2)]
            pT = sb(lx, f"pT{l}", (128, 2, 8, CH), BF16)
            attnfm = sb(lx, f"attnfm{l}", (128, 4, CH), BF16)
            smax = sb(lx, f"smax{l}", (128, 8))
            hbuf = sb(lx, f"hbuf{l}", (128, DC, HALO)) if is_c2 else None

            def _lg(g):
                wq = h_[f"wqkv{l}"].rearrange("(c p) f -> c p f", p=128)
                qG.dma(g, out=wqkvS[:], in_=wq.transpose([1, 0, 2]))
                wo = h_[f"wo{l}"].rearrange("(c p) f -> c p f", p=128)
                qG.dma(g, out=woS[:], in_=wo.transpose([1, 0, 2]))
            gp(_lg)
            w_ld = qG.n
            def _lt(a):
                qC.dma(a, out=ltab[:], in_=sp_tab[:])
                if is_c2:
                    a.wait_ge(cCC.sem, cc2_wait)
                    qC.dma(a, out=hbuf[:], in_=cc2_out[:])
            ac(_lt)
            tab_ld = qC.n

            def _qb(t):
                qC.wait(t, tab_ld)
                t.matmul(psR[:, 0:64], ones_row[:], lvec[f"qs{l}"][:],
                         start=True, stop=True)
                t.matmul(psR[:, 64:128], ones_row[:], lvec[f"ks{l}"][:],
                         start=True, stop=True)
                t.matmul(psR[:, 128:160], ones_row[:], lvec[f"qs{l}"][:, 32:64],
                         start=True, stop=True)
                t.matmul(psR[:, 160:192], ones_row[:], lvec[f"qs{l}"][:, 0:32],
                         start=True, stop=True)
                t.matmul(psR[:, 192:224], ones_row[:], lvec[f"ks{l}"][:, 32:64],
                         start=True, stop=True)
                cTE.inc(t.matmul(psR[:, 224:256], ones_row[:],
                                 lvec[f"ks{l}"][:, 0:32], start=True, stop=True))
            te(_qb)
            qbn = cTE.n
            def _qv(v):
                v.wait_ge(cTE.sem, qbn)
                ii = None
                for sl, ps in [(slice(0, 64), slice(0, 64)),
                               (slice(64, 128), slice(128, 192)),
                               (slice(128, 192), slice(64, 128)),
                               (slice(192, 256), slice(192, 256))]:
                    ii = v.tensor_tensor(
                        ltab.ap()[:, :, sl], ltab.ap()[:, :, sl],
                        psR.ap()[:, ps].unsqueeze(1).broadcast_to([128, 20, 64]),
                        Alu.mult)
                cVE.inc(ii)
                v.wait_ge(cVE.sem, cVE.n)
                cVE.inc(v.tensor_scalar(ltab.ap()[:, :, 0:128],
                                        ltab.ap()[:, :, 0:128], 8.0,
                                        None, Alu.mult))
            ve(_qv)
            ltab_done = cVE.n
            te(lambda t: t.wait_ge(cVE.sem, ltab_done))

            slot_g = {}
            psR_g = [(cVE, 0)]
            psT_g = [None]

            def _gw(e, gd):
                if gd is not None and gd[1]:
                    e.wait_ge(gd[0].sem, gd[1])

            q1_mark = {}
            q2r_mark = {}

            def emit_q1(ts, tag):
                slot = S1024[ts % 3]
                def _mmqk(t, ts=ts, slot=slot):
                    _gw(t, slot_g.get(ts % 3))
                    ii = None
                    for half in range(2):
                        for dc in range(DC):
                            off = (1 if dc < 4 else 0) + ts * 128
                            ii = t.matmul(slot[:, 512 * half:512 * half + 512],
                                          xib[:, dc, off:off + 128],
                                          wqkvS[:, dc, 512 * half:512 * half + 512],
                                          start=(dc == 0), stop=(dc == DC - 1))
                    cTE.inc(ii)
                te(_mmqk)
                def _mmv(t, ts=ts):
                    _gw(t, psR_g[0])
                    ii = None
                    for dc in range(DC):
                        off = (1 if dc < 4 else 0) + ts * 128
                        ii = t.matmul(psR[:], xib[:, dc, off:off + 128],
                                      wqkvS[:, dc, 1024:1536],
                                      start=(dc == 0), stop=(dc == DC - 1))
                    cTE.inc(ii)
                te(_mmv)
                q1_mark[(tag, ts)] = cTE.n

            def emit_q2(ts, tag, vdst):
                slot = S1024[ts % 3]
                base = 32 * (ts % 2)
                def _ev(a, ts=ts, slot=slot):
                    a.wait_ge(cTE.sem, q1_mark[(tag, ts)])
                    a.activation(qk2[ts % 2][:], slot, Act.Copy,
                                 bias=0.0, scale=1.0)
                    cAC.inc(a.activation(vdst[:, ts, :], psR[:], Act.Copy,
                                         bias=0.0, scale=1.0))
                ac(_ev)
                evn = cAC.n
                slot_g[ts % 3] = (cAC, evn)
                psR_g[0] = (cAC, evn)
                def _ss(v, ts=ts, evn=evn, base=base):
                    v.wait_ge(cAC.sem, evn)
                    cVE.inc(v.tensor_tensor(sqb2[ts % 2][:], qk2[ts % 2][:],
                                            qk2[ts % 2][:], Alu.mult))
                    v.wait_ge(cVE.sem, cVE.n)
                    cVE.inc(v.tensor_reduce(
                        ssn[:, base:base + 16],
                        sqb2[ts % 2].ap().rearrange("p (g d) -> p g d", d=64),
                        Ax.X, Alu.add))
                    v.wait_ge(cVE.sem, cVE.n)
                    cVE.inc(v.tensor_scalar(ssn[:, base:base + 16],
                                            ssn[:, base:base + 16], 1e-24,
                                            None, Alu.max))
                ve(_ss)
                ssv = cVE.n
                def _nr(a, ssv=ssv, base=base):
                    a.wait_ge(cVE.sem, ssv)
                    cAC.inc(a.activation(ssn[:, base + 16:base + 32],
                                         ssn[:, base:base + 16], Act.Sqrt,
                                         bias=0.0, scale=1.0))
                ac(_nr)
                q2r_mark[(tag, ts)] = cAC.n

            def emit_q3(ts, tag, sidx, want_q, kdst):
                base = 32 * (ts % 2)
                qk = qk2[ts % 2]
                sqb = sqb2[ts % 2]
                rot1 = rot2[ts % 2]
                qf = qf2[ts % 2]
                def _pr(v, ts=ts, sidx=sidx, base=base):
                    v.wait_ge(cAC.sem, q2r_mark[(tag, ts)])
                    cVE.inc(v.reciprocal(ssn[:, base + 16:base + 32],
                                         ssn[:, base + 16:base + 32]))
                    v.wait_ge(cVE.sem, cVE.n)
                    cVE.inc(v.tensor_tensor(
                        qk.ap().rearrange("p (g d) -> p g d", d=64),
                        qk.ap().rearrange("p (g d) -> p g d", d=64),
                        ssn.ap()[:, base + 16:base + 32].unsqueeze(2)
                           .broadcast_to([128, 16, 64]), Alu.mult))
                    v.wait_ge(cVE.sem, cVE.n)
                    v.tensor_tensor(
                        rot1.ap()[:, 0:512].rearrange("p (g d) -> p g d", d=64),
                        qk.ap()[:, 0:512].rearrange("p (g d) -> p g d", d=64),
                        ltab.ap()[:, sidx, 0:64].unsqueeze(1)
                            .broadcast_to([128, 8, 64]), Alu.mult)
                    v.tensor_tensor(
                        rot1.ap()[:, 512:1024].rearrange("p (g d) -> p g d", d=64),
                        qk.ap()[:, 512:1024].rearrange("p (g d) -> p g d", d=64),
                        ltab.ap()[:, sidx, 128:192].unsqueeze(1)
                            .broadcast_to([128, 8, 64]), Alu.mult)
                    ii = None
                    for lohi in range(2):
                        dsl = slice(32 * lohi, 32 * lohi + 32)
                        ssl = slice(32 * (1 - lohi), 32 * (1 - lohi) + 32)
                        tq = slice(64 + 32 * lohi, 64 + 32 * lohi + 32)
                        tk = slice(192 + 32 * lohi, 192 + 32 * lohi + 32)
                        v.tensor_tensor(
                            sqb.ap()[:, 0:512].rearrange("p (g d) -> p g d", d=64)[:, :, dsl],
                            qk.ap()[:, 0:512].rearrange("p (g d) -> p g d", d=64)[:, :, ssl],
                            ltab.ap()[:, sidx, tq].unsqueeze(1)
                                .broadcast_to([128, 8, 32]), Alu.mult)
                        ii = v.tensor_tensor(
                            sqb.ap()[:, 512:1024].rearrange("p (g d) -> p g d", d=64)[:, :, dsl],
                            qk.ap()[:, 512:1024].rearrange("p (g d) -> p g d", d=64)[:, :, ssl],
                            ltab.ap()[:, sidx, tk].unsqueeze(1)
                                .broadcast_to([128, 8, 32]), Alu.mult)
                    cVE.inc(ii)
                    v.wait_ge(cVE.sem, cVE.n)
                    cVE.inc(v.tensor_tensor(qf[:], rot1[:], sqb[:], Alu.add))
                ve(_pr)
                prn = cVE.n
                def _tr(t, prn=prn):
                    t.wait_ge(cVE.sem, prn)
                    _gw(t, psT_g[0])
                    ii = None
                    for j in range(8):
                        ii = t.transpose(psT[:, 128 * j:128 * j + 128],
                                         qf[:, 128 * j:128 * j + 128], id_bf[:])
                    cTE.inc(ii)
                te(_tr)
                trn = cTE.n
                def _te2(a, ts=ts, trn=trn):
                    a.wait_ge(cTE.sem, trn)
                    if want_q:
                        a.activation(
                            qfm.ap()[:, :, ts * 128:ts * 128 + 128],
                            psT.ap()[:, 0:512].rearrange("p (c f) -> p c f", f=128),
                            Act.Copy, bias=0.0, scale=1.0)
                    cAC.inc(a.activation(
                        kdst.ap()[:, :, ts * 128:ts * 128 + 128],
                        psT.ap()[:, 512:1024].rearrange("p (c f) -> p c f", f=128),
                        Act.Copy, bias=0.0, scale=1.0))
                ac(_te2)
                psT_g[0] = (cAC, cAC.n)

            def emit_qkv_group(sidx0, want_q, kdst, vdst, tag):
                for i in range(6):
                    if 1 <= i < 5:
                        emit_q2(i - 1, tag, vdst)
                    if i < 4:
                        emit_q1(i, tag)
                    if 2 <= i:
                        emit_q3(i - 2, tag, sidx0 + (i - 2), want_q, kdst)

            if is_c2:
                def _hc(v):
                    qC.wait(v, tab_ld)
                    ii = None
                    for dc in range(DC):
                        ii = v.tensor_copy(xib[:, dc, 0:HALO - 3],
                                           hbuf[:, dc, 3:HALO])
                    cVE.inc(ii)
                ve(_hc)
            else:
                def _hl(a):
                    qC.dma(a, out=xib[:, :, 0:CH + 1],
                                       in_=sp_xh[:, :, 3:HALO])
                ac(_hl)
                hldq = qC.n
                def _hc(v):
                    qC.wait(v, hldq)
                    cVE.inc(v.tensor_copy(bsc[:, 0:1], bsc[:, 3:4]))
                ve(_hc)
            hcast = cVE.n
            te(lambda t: t.wait_ge(cVE.sem, hcast))
            te(lambda t: qG.wait(t, w_ld))
            emit_qkv_group(0, False, kc[1], vc[1], -1)

            spx_mark = {}
            xat_done = 0
            for c in range(NCH):
                def _cl(a, c=c):
                    if c >= 1:
                        a.wait_ge(cVE.sem, xat_done)
                    if c == 0:
                        qC.dma(a, out=xi32[:, :, 1:CH + 1],
                                           in_=xin_sp[:, :, 0:CH])
                    else:
                        qC.dma(a, out=xi32[:, :, 0:CH + 1],
                                           in_=xin_sp[:, :, c * CH - 1:(c + 1) * CH])
                ac(_cl)
                cld = qC.n
                def _cc(v, c=c, cld=cld):
                    qC.wait(v, cld)
                    if c == 0:
                        cVE.inc(v.tensor_copy(xi32[:, :, 0:1],
                                              sideL2[:] if is_c2 else sideL1[:]))
                        v.wait_ge(cVE.sem, cVE.n)
                    cVE.inc(v.tensor_copy(xib[:], xi32[:]))
                ve(_cc)
                ccn = cVE.n
                te(lambda t, ccn=ccn: t.wait_ge(cVE.sem, ccn))
                emit_qkv_group(4 + 4 * c, True, kc[c % 2], vc[c % 2], c)
                kv_ready = cAC.n

                if sub < 2:
                    continue
                mask = mask_c0S if c == 0 else mask_nS
                sim_mark = {}
                sm_mark = {}
                e1_mark = {}
                nr_mark = {}
                tr_mark = {}
                pte_mark = {}
                pv_done = {}
                pve_mark = {}

                def emit_sim(kk, c=c, mask=mask):
                    hp, j = kk // 8, kk % 8
                    qs_, hh = j // 2, j % 2
                    slot = S1024[kk % 3]
                    def _sim(t, kk=kk, hp=hp, qs_=qs_, hh=hh, slot=slot,
                             kv_ready=kv_ready):
                        _gw(t, slot_g.get(kk % 3))
                        if kk < 3:
                            t.wait_ge(cAC.sem, kv_ready)
                        tp = (64 * hh, 0) if hh else None
                        t.matmul(slot[:, 0:512], id_bf[:],
                                 mask[:, qs_, 0:512], start=True, stop=False)
                        t.matmul(slot[:, 512:1024], id_bf[:],
                                 mask[:, qs_, 512:1024], start=True, stop=False)
                        base = 64 * hh
                        lhs = qfm[base:base + 64, hp,
                                  qs_ * 128:qs_ * 128 + 128]
                        t.matmul(slot[:, 0:512], lhs,
                                 kc[(c + 1) % 2][base:base + 64, hp, :],
                                 start=False, stop=True, tile_position=tp)
                        cTE.inc(t.matmul(slot[:, 512:1024], lhs,
                                         kc[c % 2][base:base + 64, hp, :],
                                         start=False, stop=True,
                                         tile_position=tp))
                    te(_sim)
                    sim_mark[kk] = cTE.n

                def emit_sm(kk):
                    slot = S1024[kk % 3]
                    s4 = 4 * (kk % 2)
                    def _sm(v, kk=kk, slot=slot, s4=s4):
                        v.wait_ge(cTE.sem, sim_mark[kk])
                        if kk - 2 in e1_mark:
                            v.wait_ge(cAC.sem, e1_mark[kk - 2])
                        cVE.inc(v.tensor_reduce(smax[:, s4:s4 + 1], slot, Ax.X,
                                                Alu.max, negate=True))
                    ve(_sm)
                    sm_mark[kk] = cVE.n

                def emit_e1_pair(kk):
                    def _e1(a, kk=kk):
                        s4 = 4 * (kk % 2)
                        a.wait_ge(cVE.sem, sm_mark[kk])
                        if kk - 2 in nr_mark:
                            a.wait_ge(cVE.sem, nr_mark[kk - 2])
                        cAC.inc(a.activation(probs[kk % 2][:],
                                             S1024[kk % 3], Act.Exp,
                                             bias=smax[:, s4:s4 + 1],
                                             scale=1.0,
                                             accum_out=smax[:, s4 + 1:s4 + 2]))
                        e1_mark[kk] = cAC.n
                        slot_g[kk % 3] = (cAC, cAC.n)
                    ac(_e1)

                def emit_nr(kk):
                    s4 = 4 * (kk % 2)
                    def _nr(v, kk=kk, s4=s4):
                        v.wait_ge(cAC.sem, e1_mark[kk])
                        if kk - 2 in tr_mark:
                            v.wait_ge(cTE.sem, tr_mark[kk - 2])
                        cVE.inc(v.reciprocal(smax[:, s4 + 2:s4 + 3],
                                             smax[:, s4 + 1:s4 + 2]))
                        v.wait_ge(cVE.sem, cVE.n)
                        cVE.inc(v.tensor_scalar(probsN[kk % 2][:],
                                                probs[kk % 2][:],
                                                smax[:, s4 + 2:s4 + 3],
                                                None, Alu.mult))
                    ve(_nr)
                    nr_mark[kk] = cVE.n

                def emit_tr(kk):
                    def _tr(t, kk=kk):
                        t.wait_ge(cVE.sem, nr_mark[kk])
                        _gw(t, psT_g[0])
                        ii = None
                        for j in range(8):
                            ii = t.transpose(psT[:, 128 * j:128 * j + 128],
                                             probsN[kk % 2][:, 128 * j:128 * j + 128],
                                             id_bf[:])
                        cTE.inc(ii)
                    te(_tr)
                    tr_mark[kk] = cTE.n

                def emit_pte(kk):
                    hp, j = kk // 8, kk % 8
                    qs_, hh = j // 2, j % 2
                    def _pte(v, kk=kk, hp=hp, qs_=qs_, hh=hh):
                        v.wait_ge(cTE.sem, tr_mark[kk])
                        if (hp - 1, hh) in pv_done:
                            v.wait_ge(cTE.sem, pv_done[(hp - 1, hh)])
                        cVE.inc(v.tensor_copy(
                            pT.ap()[:, hh, :, qs_ * 128:qs_ * 128 + 128],
                            psT3()))
                    ve(_pte)
                    pte_mark[kk] = cVE.n
                    psT_g[0] = (cVE, cVE.n)

                def emit_pv(hp, c=c):
                    for hh in range(2):
                        h = 2 * hp + hh
                        def _pv(t, h=h, hh=hh, hp=hp):
                            t.wait_ge(cVE.sem, pte_mark[8 * hp + 7])
                            _gw(t, psR_g[0])
                            ii = None
                            for kp in range(8):
                                vcb = vc[(c + 1) % 2] if kp < 4 else vc[c % 2]
                                ii = t.matmul(psR[0:64, :],
                                              vcb[:, kp % 4, 64 * h:64 * h + 64],
                                              pT[:, hh, kp, :],
                                              start=(kp == 0), stop=(kp == 7))
                            cTE.inc(ii)
                        te(_pv)
                        pv_done[(hp, hh)] = cTE.n
                        pvn = cTE.n
                        def _pve(v, hp=hp, hh=hh, pvn=pvn):
                            v.wait_ge(cTE.sem, pvn)
                            cVE.inc(v.tensor_copy(attnfm[64 * hh:64 * hh + 64, hp, :],
                                                  psR[0:64, :]))
                        ve(_pve)
                        pve_mark[(hp, hh)] = cVE.n
                        psR_g[0] = (cVE, cVE.n)

                if sub < 3:
                    for kk in range(32):
                        emit_sim(kk)
                    continue
                for kk in range(35):
                    if kk < 32:
                        emit_sim(kk)
                    if 0 <= kk - 1 < 32:
                        emit_sm(kk - 1)
                        emit_e1_pair(kk - 1)
                    if 0 <= kk - 3 < 32:
                        emit_pte(kk - 3)
                        if sub >= 4 and (kk - 3) % 8 == 7:
                            emit_pv((kk - 3) // 8)
                    if 0 <= kk - 2 < 32:
                        emit_nr(kk - 2)
                        emit_tr(kk - 2)

                af = cVE.n
                te(lambda t, af=af: t.wait_ge(cVE.sem, af))
                woe_mark = {}
                for dt in range(DC if sub >= 4.7 else 0):
                    slot = S512[dt % 6]
                    def _wo(t, dt=dt, slot=slot):
                        if dt >= 6:
                            t.wait_ge(cVE.sem, woe_mark[dt - 6])
                        ii = None
                        for ic in range(4):
                            ii = t.matmul(slot, woS[:, ic, dt * 128:dt * 128 + 128],
                                          attnfm[:, ic, :],
                                          start=(ic == 0), stop=(ic == 3))
                        cTE.inc(ii)
                    te(_wo)
                    won = cTE.n
                    def _woe(v, dt=dt, slot=slot, won=won, c=c):
                        v.wait_ge(cTE.sem, won)
                        if c >= 1:
                            qS.wait(v, spx_mark[c - 1])
                        cVE.inc(v.tensor_tensor(xatc[:, dt, :], slot,
                                                xi32[:, dt, 1:CH + 1], Alu.add))
                    ve(_woe)
                    woe_mark[dt] = cVE.n
                xat_done = cVE.n
                if sub < 5.5:
                    spx_mark[c] = qS.n
                    continue
                def _spx(s, c=c, t0=xat_done):
                    s.wait_ge(cVE.sem, t0)
                    qS.dma(s, out=sp_xat[:, :, c * CH:(c + 1) * CH],
                                       in_=xatc[:])
                sy(_spx)
                spx_mark[c] = qS.n
                if c == 0 and l == 0 and dbg_attn:
                    def _tapa(s):
                        s.wait_ge(cVE.sem, xat_done)
                        qS.dma(s, out=dbg_attn["qf0"][:], in_=qfm[:])
                        qS.dma(s, out=dbg_attn["kc0"][:], in_=kc[0][:])
                        qS.dma(s, out=dbg_attn["vc0"][:], in_=vc[0][:])
                        qS.dma(s, out=dbg_attn["af0"][:], in_=attnfm[:])
                    sy(_tapa)
                    _tm = qS.n
                    ve(lambda v, m=_tm: qS.wait(v, m))

            if sub < 6:
                barrier()
                lx.close()
                return cGP.n
            def _cav(v):
                v.wait_ge(cVE.sem, cVE.n)
                cVE.inc(v.tensor_scalar(
                    scA.ap().unsqueeze(2), xatc[:, :, CH - 1:CH],
                    sel_first[:], None, Alu.mult))
            ve(_cav)
            cav = cVE.n
            def _cag(g, l=l):
                g.wait_ge(cVE.sem, cav)
                qG.dma(g, out=cca_in[l][:], in_=scA[:])
                qG.wait(g, qG.n)
                cCC.inc(g.collective_compute("AllReduce", Alu.add,
                                             replica_groups=RG,
                                             ins=[cca_in[l].opt()],
                                             outs=[cca_out[l].opt()]))
            gp(_cag)
            cca_done = cCC.n
            barrier()
            lx.close()
            return cca_done

        def emit_mlp(l, is_c2, cca_done, fire_cc2):
            cc2_mark = {0: 0}
            bi_nonlocal = [0]
            mx = ExitStack()
            blkb = sb(mx, f"blkb{l}", (128, DC, 1025))
            hnorm = sb(mx, f"hnorm{l}", (128, DC, 1024), BF16)
            h2S = sb(mx, f"h2S{l}", (128, NBK, 1024), BF16)
            gel = sb(mx, f"gel{l}", (128, 1024))
            gb2 = [gel, sb(mx, f"gelb{l}", (128, 1024))]
            xsqm = sb(mx, f"xsqm{l}", (128, 1024))
            w1b = [sb(mx, f"w1b{l}_{i}", (128, DC, 256), BF16) for i in range(2)]
            w2b = [sb(mx, f"w2b{l}_{i}", (128, NBK, 128), BF16) for i in range(2)]
            srow = sb(mx, f"srow{l}", (1, 1024))
            sxrow = sb(mx, f"sxrow{l}", (1, 1024))
            murow = sb(mx, f"murow{l}", (1, 1024))
            rstrow = sb(mx, f"rstrow{l}", (1, 1024))
            ccb = sb(mx, f"ccb{l}", (128, DC, HALO)) if fire_cc2 else None
            gbuf = gel if is_c2 else None
            sbuf_s = sb(mx, f"sbuf_s{l}", (128, DC, 1024)) if is_c2 else None
            xfin = sb(mx, f"xfin{l}", (128, 4, D)) if is_c2 else None

            w1r = h_[f"w1p{l}"].rearrange("(c p) f -> c p f", p=128)
            w2r = h_[f"w2p{l}"].rearrange("(c p) f -> c p f", p=128)

            for bi, blk in enumerate([1, 0]):
                bi_nonlocal[0] = bi
                base = blk * 1024
                ve_now, te_now, sy_now = cVE.n, cTE.n, qS.n
                def _bl(a, blk=blk, base=base, ve_now=ve_now, te_now=te_now,
                        sy_now=sy_now):
                    if bi_nonlocal[0] == 1:
                        a.wait_ge(cVE.sem, ve_now)
                        a.wait_ge(cTE.sem, te_now)
                        qS.wait(a, sy_now)
                    if blk == 1:
                        qC.dma(a, out=blkb[:, :, 0:1025],
                                           in_=sp_xat[:, :, base - 1:base + 1024])
                    else:
                        qC.dma(a, out=blkb[:, :, 1:1025],
                                           in_=sp_xat[:, :, 0:1024])
                        a.wait_ge(cCC.sem, cca_done)
                        qC.dma(a, out=scA[:], in_=cca_out[l][:])
                ac(_bl)
                bld = qC.n
                def _bv(v, blk=blk, bld=bld):
                    qC.wait(v, bld)
                    if blk == 0:
                        cVE.inc(v.tensor_scalar(blkb[:, :, 0:1],
                                                scA.ap().unsqueeze(2),
                                                sel_second[:], None, Alu.mult))
                    else:
                        cVE.inc(v.tensor_copy(bsc[:, 0:1], bsc[:, 3:4]))
                ve(_bv)
                blv = cVE.n
                if dbg_attn and l == 0 and blk == 0:
                    def _tapb(s):
                        s.wait_ge(cVE.sem, blv)
                        qS.dma(s, out=dbg_attn["blkb0"][:], in_=blkb[:])
                    sy(_tapb)
                    _tm = qS.n
                    ve(lambda v, m=_tm: qS.wait(v, m))

                def shview(dc, h0, n):
                    off = (1 if dc < 4 else 0) + h0
                    return blkb[:, dc, off:off + n]

                te(lambda t, blv=blv: t.wait_ge(cVE.sem, blv))
                ac(lambda a, blv=blv: a.wait_ge(cVE.sem, blv))
                for half in range(2):
                    def _sx(t, half=half):
                        ii = None
                        for dc in range(DC):
                            ii = t.matmul(psR[0:1, :], ones_col[:],
                                          shview(dc, 512 * half, 512),
                                          start=(dc == 0), stop=(dc == DC - 1))
                        cTE.inc(ii)
                    te(_sx)
                    sxn = cTE.n
                    def _sxe(v, half=half, sxn=sxn):
                        v.wait_ge(cTE.sem, sxn)
                        cVE.inc(v.tensor_copy(sxrow[0:1, 512 * half:512 * half + 512],
                                              psR[0:1, :]))
                    ve(_sxe)
                    sxe = cVE.n
                    te(lambda t, sxe=sxe: t.wait_ge(cVE.sem, sxe))
                qm_mark = {}
                for half in range(2):
                    for dc in range(DC):
                        k = half * DC + dc
                        xq = xsqm[:, 512 * (k % 2):512 * (k % 2) + 512]
                        def _q(a, dc=dc, half=half, k=k, xq=xq):
                            if k >= 2:
                                a.wait_ge(cTE.sem, qm_mark[k - 2])
                            cAC.inc(a.activation(xq, shview(dc, 512 * half, 512),
                                                 Act.Square))
                        ac(_q)
                        qn = cAC.n
                        def _qm(t, dc=dc, qn=qn, xq=xq):
                            t.wait_ge(cAC.sem, qn)
                            cTE.inc(t.matmul(psR[0:1, :], ones_col[:],
                                             xq, start=(dc == 0),
                                             stop=(dc == DC - 1)))
                        te(_qm)
                        qm_mark[k] = cTE.n
                    qmn = cTE.n
                    def _qe(v, half=half, qmn=qmn):
                        v.wait_ge(cTE.sem, qmn)
                        cVE.inc(v.tensor_copy(srow[0:1, 512 * half:512 * half + 512],
                                              psR[0:1, :]))
                    ve(_qe)
                    qe = cVE.n
                    te(lambda t, qe=qe: t.wait_ge(cVE.sem, qe))
                stn = cVE.n
                def _st(v):
                    v.wait_ge(cVE.sem, cVE.n)
                    cVE.inc(v.tensor_scalar(murow[:], sxrow[:], R1024, None,
                                            Alu.mult))
                    v.wait_ge(cVE.sem, cVE.n)
                    cVE.inc(v.tensor_tensor(rstrow[:], murow[:], murow[:],
                                            Alu.mult))
                    v.wait_ge(cVE.sem, cVE.n)
                    cVE.inc(v.scalar_tensor_tensor(rstrow[:], srow[:], R1024,
                                                   rstrow[:], Alu.mult,
                                                   Alu.subtract))
                ve(_st)
                stv = cVE.n
                def _sa(a):
                    a.wait_ge(cVE.sem, stv)
                    cAC.inc(a.activation(rstrow[:], rstrow[:], Act.Sqrt,
                                         bias=eps_col[0:1, :], scale=1.0))
                ac(_sa)
                san = cAC.n
                def _sr(v):
                    v.wait_ge(cAC.sem, san)
                    cVE.inc(v.reciprocal(rstrow[:], rstrow[:]))
                ve(_sr)
                srn = cVE.n
                if dbg_attn and l == 0 and blk == 0:
                    def _taps2(s):
                        s.wait_ge(cVE.sem, srn)
                        qS.dma(s, out=dbg_attn["rows0"][0:1, :], in_=sxrow[:])
                        qS.dma(s, out=dbg_attn["rows0"][1:2, :], in_=srow[:])
                        qS.dma(s, out=dbg_attn["rows0"][2:3, :], in_=murow[:])
                        qS.dma(s, out=dbg_attn["rows0"][3:4, :], in_=rstrow[:])
                    sy(_taps2)
                    _tm = qS.n
                    ve(lambda v, m=_tm: qS.wait(v, m))
                for half in range(2):
                    def _bc(t, half=half, srn=srn):
                        t.wait_ge(cVE.sem, srn)
                        t.matmul(psR[:, :], ones_row[:],
                                 murow[0:1, 512 * half:512 * half + 512],
                                 start=True, stop=True)
                        cTE.inc(t.matmul(psA[:, 0:512], ones_row[:],
                                         rstrow[0:1, 512 * half:512 * half + 512],
                                         start=True, stop=True))
                    te(_bc)
                    bcn = cTE.n
                    for dc in range(DC):
                        def _ap(v, dc=dc, half=half, bcn=bcn):
                            v.wait_ge(cTE.sem, bcn)
                            cVE.inc(v.tensor_tensor(gel[:, 0:512],
                                                    shview(dc, 512 * half, 512),
                                                    psR[:], Alu.subtract))
                            v.wait_ge(cVE.sem, cVE.n)
                            cVE.inc(v.scalar_tensor_tensor(
                                gel[:, 512:1024], gel[:, 0:512],
                                lvec[f"lng{l}"][:, dc:dc + 1],
                                psA[:, 0:512], Alu.mult, Alu.mult))
                            v.wait_ge(cVE.sem, cVE.n)
                            cVE.inc(v.tensor_scalar(
                                hnorm[:, dc, 512 * half:512 * half + 512],
                                gel[:, 512:1024], lvec[f"lnb{l}"][:, dc:dc + 1],
                                None, Alu.add))
                        ve(_ap)
                    apn = cVE.n
                    te(lambda t, apn=apn: t.wait_ge(cVE.sem, apn))
                hn_done = cVE.n

                def _w1l(g, i=0):
                    qG.dma(g, out=w1b[0][:],
                                       in_=w1r[:, :, 0:256].transpose([1, 0, 2]))
                gp(_w1l)
                w1_mark = {0: qG.n}
                h2_mark = {}
                gl_mark = {}
                ama_mark = {}
                w1mm_mark = {}
                for i in range(NBK):
                    if i + 1 < NBK:
                        def _w1n(g, i=i):
                            if i >= 1:
                                g.wait_ge(cTE.sem, w1mm_mark[i - 1])
                            qG.dma(g,
                                out=w1b[(i + 1) % 2][:],
                                in_=w1r[:, :, 256 * (i + 1):256 * (i + 1) + 256]
                                    .transpose([1, 0, 2]))
                        gp(_w1n)
                        w1_mark[i + 1] = qG.n
                    sa = S1024[(2 * i) % 3]
                    sg_ = S1024[(2 * i + 1) % 3]
                    def _ama(t, i=i, sa=sa, hn_done=hn_done):
                        qG.wait(t, w1_mark[i])
                        if i == 0:
                            t.wait_ge(cVE.sem, hn_done)
                        if i >= 2:
                            t.wait_ge(cAC.sem, gl_mark[i - 2])
                        ii = None
                        for half in range(2):
                            for dc in range(DC):
                                ii = t.matmul(sa[:, 512 * half:512 * half + 512],
                                              w1b[i % 2][:, dc, 0:128],
                                              hnorm[:, dc, 512 * half:512 * half + 512],
                                              start=(dc == 0), stop=(dc == DC - 1))
                        cTE.inc(ii)
                    te(_ama)
                    ama_mark[i] = cTE.n
                    def _amg(t, i=i, sg_=sg_):
                        if i >= 1:
                            t.wait_ge(cVE.sem, h2_mark[i - 1])
                        ii = None
                        for half in range(2):
                            for dc in range(DC):
                                ii = t.matmul(sg_[:, 512 * half:512 * half + 512],
                                              w1b[i % 2][:, dc, 128:256],
                                              hnorm[:, dc, 512 * half:512 * half + 512],
                                              start=(dc == 0), stop=(dc == DC - 1))
                        cTE.inc(ii)
                    te(_amg)
                    w1mm_mark[i] = cTE.n
                    amn = cTE.n
                    def _gl(a, i=i, amn=amn, sg_=sg_):
                        a.wait_ge(cTE.sem, amn)
                        if i >= 2:
                            a.wait_ge(cVE.sem, h2_mark[i - 2])
                        cAC.inc(a.activation(gb2[i % 2][:], sg_, Act.Gelu,
                                             bias=lvec[f"b1p{l}"][:, 2 * i + 1:2 * i + 2],
                                             scale=1.0))
                    ac(_gl)
                    gl_mark[i] = cAC.n
                    gln = cAC.n
                    def _h2(v, i=i, gln=gln, sa=sa):
                        v.wait_ge(cAC.sem, gln)
                        cVE.inc(v.scalar_tensor_tensor(
                            h2S[:, i, :], sa,
                            lvec[f"b1p{l}"][:, 2 * i:2 * i + 1], gb2[i % 2][:],
                            Alu.add, Alu.mult))
                    ve(_h2)
                    h2_mark[i] = cVE.n
                h2_done = cVE.n
                if dbg_attn and l == 0 and blk == 0:
                    def _tapm(s):
                        s.wait_ge(cVE.sem, h2_done)
                        qS.dma(s, out=dbg_attn["hn0"][:], in_=hnorm[:])
                        qS.dma(s, out=dbg_attn["h20"][:], in_=h2S[:])
                    sy(_tapm)
                    _tm = qS.n
                    ve(lambda v, m=_tm: qS.wait(v, m))

                def _w2l(g):
                    qG.dma(g, out=w2b[0][:],
                                       in_=w2r[:, :, 0:128].transpose([1, 0, 2]))
                gp(_w2l)
                w2_mark = {0: qG.n}
                w2e_mark = {}
                for dt in range(DC):
                    if dt + 1 < DC:
                        def _w2n(g, dt=dt):
                            if dt >= 1:
                                g.wait_ge(cTE.sem, w2mm_mark[dt - 1])
                            qG.dma(g, 
                                out=w2b[(dt + 1) % 2][:],
                                in_=w2r[:, :, 128 * (dt + 1):128 * (dt + 1) + 128]
                                    .transpose([1, 0, 2]))
                        if dt == 0:
                            w2mm_mark = {}
                        gp(_w2n)
                        w2_mark[dt + 1] = qG.n
                    slot = S1024[dt % 3]
                    def _wm(t, dt=dt, slot=slot, h2_done=h2_done):
                        qG.wait(t, w2_mark[dt])
                        if dt == 0:
                            t.wait_ge(cVE.sem, h2_done)
                        if dt >= 3:
                            t.wait_ge(cVE.sem, w2e_mark[dt - 3])
                        ii = None
                        for half in range(2):
                            for hb in range(NBK):
                                ii = t.matmul(slot[:, 512 * half:512 * half + 512],
                                              w2b[dt % 2][:, hb, :],
                                              h2S[:, hb, 512 * half:512 * half + 512],
                                              start=(hb == 0), stop=(hb == NBK - 1))
                        cTE.inc(ii)
                    te(_wm)
                    w2mm_mark[dt] = cTE.n
                    wmn = cTE.n
                    def _we(v, dt=dt, slot=slot, wmn=wmn):
                        v.wait_ge(cTE.sem, wmn)
                        cVE.inc(v.scalar_tensor_tensor(
                            blkb[:, dt, 1:1025], slot,
                            lvec[f"b2{l}"][:, dt:dt + 1],
                            blkb[:, dt, 1:1025],
                            Alu.add, Alu.add))
                    ve(_we)
                    w2e_mark[dt] = cVE.n
                blk_done = cVE.n

                if not is_c2:
                    def _spb(s, base=base, blk_done=blk_done):
                        s.wait_ge(cVE.sem, blk_done)
                        qS.dma(s, out=sp_xl1[:, :, base:base + 1024],
                                           in_=blkb[:, :, 1:1025])
                    sy(_spb)
                    if fire_cc2 and blk == 1:
                        def _cc2v(v):
                            v.wait_ge(cVE.sem, cVE.n)
                            ii = None
                            for dc in range(DC):
                                ii = v.tensor_scalar(
                                    ccb[:, dc, :],
                                    blkb[:, dc, 1025 - HALO:1025],
                                    sel_first[:], None, Alu.mult)
                            cVE.inc(ii)
                        ve(_cc2v)
                        c2v = cVE.n
                        def _cc2g(g):
                            g.wait_ge(cVE.sem, c2v)
                            qG.dma(g, out=cc2_in[:], in_=ccb[:])
                            qG.wait(g, qG.n)
                            cCC.inc(g.collective_compute(
                                "AllReduce", Alu.add, replica_groups=RG,
                                ins=[cc2_in.opt()], outs=[cc2_out.opt()]))
                        gp(_cc2g)
                        cc2_mark[0] = cCC.n
                else:
                    def _sl(a, base=base, blk_done=blk_done):
                        a.wait_ge(cVE.sem, blk_done)
                        qC.dma(a, out=sbuf_s[:],
                                           in_=sp_s[:, :, base:base + 1024])
                    ac(_sl)
                    sld = qC.n
                    fin_mark = {}
                    for dc in range(DC):
                        def _gs(a, dc=dc, blk_done=blk_done):
                            a.wait_ge(cVE.sem, blk_done)
                            if dc >= 1:
                                a.wait_ge(cVE.sem, fin_mark[dc - 1])
                            cAC.inc(a.activation(gbuf[:],
                                                 blkb[:, dc, 1:1025],
                                                 Act.Silu))
                        ac(_gs)
                        gsn = cAC.n
                        def _gm(v, dc=dc, gsn=gsn, sld=sld):
                            v.wait_ge(cAC.sem, gsn)
                            qC.wait(v, sld)
                            cVE.inc(v.tensor_tensor(sbuf_s[:, dc, :], sbuf_s[:, dc, :],
                                                    gbuf[:], Alu.mult))
                        ve(_gm)
                        fin_mark[dc] = cVE.n
                    sg_done = cVE.n
                    out_mark = {}
                    qs_prev = qS.n
                    for ts in range(8):
                        def _xl(a, ts=ts, base=base, qs_prev=qs_prev):
                            if ts >= 4:
                                qS.wait(a, out_mark[ts - 4])
                            else:
                                qS.wait(a, qs_prev)
                            qC.dma(a, 
                                out=xfin[:, ts % 4, :],
                                in_=xr_own[(base + ts * 128) // 128])
                        ac(_xl)
                        xln = qC.n
                        def _ft(t, ts=ts, sg_done=sg_done):
                            t.wait_ge(cVE.sem, sg_done)
                            ii = None
                            for dc in range(DC):
                                ii = t.transpose(psA[:, dc * 128:dc * 128 + 128],
                                                 sbuf_s[:, dc, ts * 128:ts * 128 + 128],
                                                 id_f32[:])
                            cTE.inc(ii)
                        te(_ft)
                        ftn = cTE.n
                        def _fe(v, ts=ts, ftn=ftn, xln=xln):
                            v.wait_ge(cTE.sem, ftn)
                            qC.wait(v, xln)
                            cVE.inc(v.tensor_tensor(
                                xfin.ap()[:, ts % 4, :].rearrange("p (c f) -> p c f", f=128),
                                psA3(), xfin.ap()[:, ts % 4, :]
                                .rearrange("p (c f) -> p c f", f=128), Alu.add))
                        ve(_fe)
                        fen = cVE.n
                        te(lambda t, fen=fen: t.wait_ge(cVE.sem, fen))
                        def _fo(s, ts=ts, base=base, fen=fen):
                            s.wait_ge(cVE.sem, fen)
                            qS.dma(s, 
                                out=out_h[base + ts * 128:base + ts * 128 + 128, :],
                                in_=xfin[:, ts % 4, :])
                        sy(_fo)
                        out_mark[ts] = qS.n
            barrier()
            mx.close()
            return cc2_mark[0]

        def emit_c1():
            cx = ExitStack()
            woutS = sb(cx, "woutS", (128, IC, D), BF16)
            rbuf = [sb(cx, f"rbuf{i}", (128, 5, CH)) for i in range(2)]
            fused = sb(cx, "fusedC", (128, IC, CH), BF16)
            xnr = sb(cx, "xnrC", (128, DC, CH))
            sres = sb(cx, "sresC", (128, DC, CH))

            def _wg(g):
                wv = h_["w_out"].rearrange("(c p) f -> c p f", p=128)
                qG.dma(g, out=woutS[:], in_=wv.transpose([1, 0, 2]))
            gp(_wg)
            wg = qG.n
            def _r12(v):
                v.wait_ge(cCC.sem, cc1_done)
            ve(_r12)
            def _r12l(a):
                a.wait_ge(cCC.sem, cc1_done)
                qC.dma(a, out=R1e[:], in_=cc1_out[:, 0:IC])
                qC.dma(a, out=R2e[:], in_=cc1_out[:, IC:2 * IC])
            ac(_r12l)
            rld = qC.n
            def _r12v(v):
                qC.wait(v, rld)
                v.tensor_scalar(R1e[:], R1e[:], sel_second[:], None, Alu.mult)
                cVE.inc(v.tensor_scalar(R2e[:], R2e[:], sel_first[:], None, Alu.mult))
            ve(_r12v)
            r12 = cVE.n

            fuse_mark = {}
            ld_mark = {}
            spill_s_mark = {}
            prev_we_done = 0
            prev_wm_done = 0
            for c in range(NCH):
                c_wm_guard = prev_wm_done
                for ct in range(IC):
                    def _ld(a, c=c, ct=ct):
                        gk = (c, ct - 2) if ct >= 2 else (c - 1, ct + 14)
                        if gk in fuse_mark:
                            a.wait_ge(cVE.sem, fuse_mark[gk])
                        sl = slice(c * CH, (c + 1) * CH)
                        bf_ = rbuf[ct % 2]
                        qC.dma(a, out=bf_[:, 0, :], in_=sp_fw[:, ct, sl])
                        qC.dma(a, out=bf_[:, 1, :], in_=sp_pf[:, ct, sl])
                        qC.dma(a, out=bf_[:, 2, :], in_=sp_bw[:, ct, sl])
                        qC.dma(a, out=bf_[:, 3, :], in_=sp_pb[:, ct, sl])
                        qC.dma(a, out=bf_[:, 4, :], in_=sp_sg[:, ct, sl])
                    ac(_ld)
                    ld_mark[(c, ct)] = qC.n
                    ldn = qC.n
                    def _fu(v, c=c, ct=ct, ldn=ldn, gg=c_wm_guard):
                        qC.wait(v, ldn)
                        if ct == 0 and gg:
                            v.wait_ge(cTE.sem, gg)
                        if (c, ct) == (0, 0):
                            v.wait_ge(cVE.sem, r12)
                        bf_ = rbuf[ct % 2]
                        v.scalar_tensor_tensor(bf_[:, 0, :], bf_[:, 1, :],
                                               R1e[:, ct:ct + 1], bf_[:, 0, :],
                                               Alu.mult, Alu.add)
                        cVE.inc(v.scalar_tensor_tensor(bf_[:, 2, :], bf_[:, 3, :],
                                                       R2e[:, ct:ct + 1],
                                                       bf_[:, 2, :],
                                                       Alu.mult, Alu.add))
                        v.wait_ge(cVE.sem, cVE.n)
                        cVE.inc(v.tensor_tensor(bf_[:, 0, :], bf_[:, 0, :],
                                                bf_[:, 2, :], Alu.add))
                        v.wait_ge(cVE.sem, cVE.n)
                        cVE.inc(v.tensor_tensor(fused[:, ct, :], bf_[:, 0, :],
                                                bf_[:, 4, :], Alu.mult))
                    ve(_fu)
                    fuse_mark[(c, ct)] = cVE.n
                fu_done = cVE.n
                def _xr(a, c=c, fu_done=fu_done):
                    qC.dma(a, out=xnr[:],
                                       in_=sp_xn[:, :, c * CH:(c + 1) * CH])
                ac(_xr)
                xrn = qC.n
                we_mark = {}
                for dt in range(DC):
                    slot = S512[dt % 6]
                    def _wm(t, c=c, dt=dt, slot=slot, fu_done=fu_done, wg=wg):
                        if (c, dt) == (0, 0):
                            qG.wait(t, wg)
                        t.wait_ge(cVE.sem, fu_done)
                        if dt >= 6:
                            t.wait_ge(cVE.sem, we_mark[dt - 6])
                        elif c >= 1:
                            t.wait_ge(cVE.sem, prev_we_done)
                        ii = None
                        for ct in range(IC):
                            ii = t.matmul(slot, woutS[:, ct, dt * 128:dt * 128 + 128],
                                          fused[:, ct, :], start=(ct == 0),
                                          stop=(ct == IC - 1))
                        cTE.inc(ii)
                    te(_wm)
                    wmn = cTE.n
                    def _we(v, c=c, dt=dt, slot=slot, wmn=wmn, xrn=xrn):
                        v.wait_ge(cTE.sem, wmn)
                        qC.wait(v, xrn)
                        if c >= 1:
                            qS.wait(v, spill_s_mark[c - 1])
                        cVE.inc(v.scalar_tensor_tensor(sres[:, dt, :], slot,
                                                       b_out[:, dt:dt + 1],
                                                       xnr[:, dt, :],
                                                       Alu.add, Alu.add))
                    ve(_we)
                    we_mark[dt] = cVE.n
                sdone = cVE.n
                prev_we_done = sdone
                prev_wm_done = cTE.n
                def _ssp(s, c=c, sdone=sdone):
                    s.wait_ge(cVE.sem, sdone)
                    qS.dma(s, out=sp_s[:, :, c * CH:(c + 1) * CH],
                                       in_=sres[:])
                sy(_ssp)
                spill_s_mark[c] = qS.n
            barrier()
            cx.close()

        xr_own = h_["x_own"].rearrange("(a p) d -> a p d", p=128)

        _sub = {"l1s1": 1, "l1s2": 2, "l1s3": 3, "l1s4": 4,
                "l1s4a": 4.7, "l1s5": 5.5}.get(upto, 6)
        cca0 = emit_attn(0, sp_xn, False, sub=_sub)
        if upto in ("l1s1", "l1s2", "l1s3", "l1s4", "l1s4a", "l1s5"):
            _endprog()
            return nc
        if dbg_attn:
            def _tapx(g):
                qG.dma(g, out=dbg_attn["xat1"][:], in_=sp_xat[:])
                qG.wait(g, qG.n)
            gp(_tapx)
        if upto == "l1a":
            _endprog()
            return nc
        cc2m = emit_mlp(0, False, cca0, True)
        if upto == "l1m":
            _endprog()
            return nc
        emit_c1()
        if upto == "c1":
            _endprog()
            return nc
        def _sd2(a):
            a.wait_ge(cCC.sem, cc2m)
            qC.dma(a, out=sideL2[:], in_=cc2_out[:, :, 515:516],
                               allow_slow_non_contiguous=True)
        ac(_sd2)
        sd2 = qC.n
        def _sd2v(v):
            qC.wait(v, sd2)
            cVE.inc(v.tensor_scalar(sideL2[:], sideL2[:], sel_second[:],
                                    None, Alu.mult))
        ve(_sd2v)
        cca1 = emit_attn(1, sp_xl1, True, cc2_wait=cc2m)
        if upto == "l2a":
            _endprog()
            return nc
        emit_mlp(1, True, cca1, False)

        if dbg_h:
            def _tap(g):
                for nm, (dst, src) in dbg_h.items():
                    qG.dma(g, out=dst[:], in_=src[:])
                qG.wait(g, qG.n)
            gp(_tap)

        barrier()
        def _fin(s):
            qS.wait(s, qS.n)
        sy(_fin)

    return nc


_NC_CACHE = {}


def kernel(**inputs):
    maps = prepare_inputs(inputs)
    key = "main"
    if key not in _NC_CACHE:
        _NC_CACHE[key] = build_nc()
    nc = _NC_CACHE[key]
    res = run_bass_kernel_spmd(nc, maps, core_ids=list(range(8)))
    out = np.empty((4, 4096, 1024), np.float32)
    for c in range(8):
        b, h = c // 2, c % 2
        out[b, h * T:(h + 1) * T] = res.results[c]["out"]
    return out


# version 15
# speedup vs baseline: 1.1108x; 1.0023x over previous
import numpy as np
import ml_dtypes
from contextlib import ExitStack

import concourse.bass as bass
import concourse.mybir as mybir
from concourse.bass_utils import run_bass_kernel_spmd

F32 = mybir.dt.float32
BF16 = mybir.dt.bfloat16
Alu = mybir.AluOpType
Act = mybir.ActivationFunctionType
Ax = mybir.AxisListType
BF = ml_dtypes.bfloat16

D, DC = 1024, 8
INNER, IC = 2048, 16
T, CH, NCH = 2048, 512, 4
NH, DH = 8, 64
HID, HIDP = 2730, 2816
F1P = 5632
NBK = 22
HALO = 516
EPS = 1e-5
NEG = -1.0e30
R1024 = 1.0 / 1024.0
RG = [[0, 1], [2, 3], [4, 5], [6, 7]]



def _feat_tiles(v):
    v = np.asarray(v, np.float32)
    return np.ascontiguousarray(v.reshape(v.shape[0] // 128, 128).T)


def _rotary_tables(t0):
    n = 4096
    pos = np.clip(np.arange(t0 - 512, t0 + 2048, dtype=np.float64), 0, n - 1)
    half = np.arange(0, DH, 2, dtype=np.float64)
    inv_freq = 1.0 / 10000.0 ** (half / DH)
    fr = pos[:, None] * inv_freq[None, :]
    fr = np.concatenate([fr, fr], axis=1)
    base = (half + 0.4 * DH) / (1.4 * DH)
    sc = base[None, :] ** ((pos - n // 2) / 256.0)[:, None]
    sc = np.concatenate([sc, sc], axis=1)
    sign = np.where(np.arange(DH) < 32, -1.0, 1.0)[None, :]
    tab = np.concatenate([np.cos(fr) * sc, np.sin(fr) * sc * sign,
                          np.cos(fr) / sc, np.sin(fr) / sc * sign], axis=1)
    tab = tab.reshape(20, 128, 256).transpose(1, 0, 2)
    hi = tab.astype(np.float32)
    lo = (tab - hi.astype(np.float64)).astype(np.float32)
    return np.ascontiguousarray(hi), np.ascontiguousarray(lo)


def _masks(first_half):
    i = np.arange(CH)[:, None]
    j = np.arange(1024)[None, :]
    mn = np.where((j >= i) & (j <= i + 512), 0.0, NEG)
    mc0 = np.where((j >= i) & (j <= i + 512) & (j >= 512), 0.0, NEG) \
        if first_half else mn
    lay = lambda m: np.ascontiguousarray(
        m.reshape(4, 128, 1024).transpose(1, 0, 2).astype(BF))
    return lay(mn), lay(mc0)


def _pad_w1(w1, b1):
    a = np.zeros((D, HIDP), np.float32); a[:, :HID] = w1[:, :HID]
    g = np.zeros((D, HIDP), np.float32); g[:, :HID] = w1[:, HID:]
    ba = np.zeros(HIDP, np.float32); ba[:HID] = b1[:HID]
    bg = np.zeros(HIDP, np.float32); bg[:HID] = b1[HID:]
    w1p = np.empty((D, F1P), np.float32)
    b1p = np.empty(F1P, np.float32)
    for i in range(NBK):
        w1p[:, 256 * i:256 * i + 128] = a[:, 128 * i:128 * i + 128]
        w1p[:, 256 * i + 128:256 * i + 256] = g[:, 128 * i:128 * i + 128]
        b1p[256 * i:256 * i + 128] = ba[128 * i:128 * i + 128]
        b1p[256 * i + 128:256 * i + 256] = bg[128 * i:128 * i + 128]
    return (np.ascontiguousarray(w1p),
            np.ascontiguousarray(b1p.reshape(F1P // 128, 128).T))


def prepare_inputs(inputs):
    x = np.asarray(inputs["L0_attn_x"])
    sh = {}
    sh["w_in"] = np.ascontiguousarray(np.asarray(inputs["L0_attn_W_in"], np.float32))
    sh["w_delta"] = np.ascontiguousarray(np.asarray(inputs["L0_attn_W_delta"], np.float32))
    sh["w_out"] = np.ascontiguousarray(np.asarray(inputs["L0_attn_W_out"], np.float32))
    sh["b_in_t"] = _feat_tiles(inputs["L0_attn_b_in"])
    cw = np.asarray(inputs["L0_attn_conv_w"], np.float32)
    sh["conv_w_t"] = np.ascontiguousarray(cw.reshape(IC, 128, 3).transpose(1, 0, 2))
    sh["conv_b_t"] = _feat_tiles(inputs["L0_attn_conv_b"])
    sh["b_delta_t"] = _feat_tiles(inputs["L0_attn_b_delta"])
    sh["b_out_t"] = _feat_tiles(inputs["L0_attn_b_out"])
    sh["mgam_t"] = _feat_tiles(inputs["L0_attn_ln_gamma"])
    sh["mbeta_t"] = _feat_tiles(inputs["L0_attn_ln_beta"])
    for l in range(2):
        sh[f"wqkv{l}"] = np.ascontiguousarray(np.asarray(inputs["L0_ffn_wqkv"][l], np.float32))
        sh[f"wo{l}"] = np.ascontiguousarray(np.asarray(inputs["L0_ffn_wo"][l], np.float32))
        w1p, b1t = _pad_w1(np.asarray(inputs["L0_ffn_w1"][l], np.float32),
                           np.asarray(inputs["L0_ffn_b1"][l], np.float32))
        sh[f"w1p{l}"], sh[f"b1p_t{l}"] = w1p, b1t
        w2p = np.zeros((HIDP, D), np.float32)
        w2p[:HID] = np.asarray(inputs["L0_ffn_w2"][l], np.float32)
        sh[f"w2p{l}"] = w2p
        sh[f"b2_t{l}"] = _feat_tiles(inputs["L0_ffn_b2"][l])
        sh[f"lng_t{l}"] = _feat_tiles(inputs["L0_ffn_ln_g"][l])
        sh[f"lnb_t{l}"] = _feat_tiles(inputs["L0_ffn_ln_b"][l])
        sh[f"qs{l}"] = np.ascontiguousarray(
            np.asarray(inputs["L0_ffn_q_scale"][l], np.float32).reshape(1, DH))
        sh[f"ks{l}"] = np.ascontiguousarray(
            np.asarray(inputs["L0_ffn_k_scale"][l], np.float32).reshape(1, DH))
    sh["id_bf"] = np.eye(128, dtype=BF)
    sh["id_f32"] = np.eye(128, dtype=np.float32)
    sh["ones_row"] = np.ones((1, 128), np.float32)
    sh["ones_col"] = np.ones((128, 1), np.float32)
    sh["eps_col"] = np.full((128, 1), EPS, np.float32)

    maps = []
    for c in range(8):
        b, h = c // 2, c % 2
        t0 = h * T
        d = dict(sh)
        d["x_own"] = np.ascontiguousarray(x[b, t0:t0 + T])
        hs = t0 - HALO if h == 1 else 0
        d["x_halo"] = np.ascontiguousarray(x[b, hs:hs + HALO])
        d["tabs_hi"], d["tabs_lo"] = _rotary_tables(t0)
        d["mask_n"], d["mask_c0"] = _masks(h == 0)
        one = np.ones((128, 1), np.float32)
        zero = np.zeros((128, 1), np.float32)
        d["sel_first"] = one if h == 0 else zero
        d["sel_second"] = zero if h == 0 else one
        maps.append(d)
    return maps



class Ctr:
    def __init__(self, sem, step=1):
        self.sem, self.n, self.step = sem, 0, step

    def inc(self, instr, k=None):
        k = self.step if k is None else k
        self.n += k
        instr.then_inc(self.sem, k)
        return self.n

    def dma(self, eng, **kw):
        eng.wait_ge(self.sem, self.n)
        self.n += 16
        eng.dma_start(**kw).then_inc(self.sem, 16)
        return self.n


class Q:
    def __init__(self, sems):
        self.sems = list(sems)
        self.cnt = [0] * len(self.sems)
        self.i = 0
        self.n = 0
        self.snap = {}

    def dma(self, eng, **kw):
        i = self.i
        self.i = (i + 1) % len(self.sems)
        if self.cnt[i]:
            eng.wait_ge(self.sems[i], self.cnt[i])
        self.cnt[i] += 16
        eng.dma_start(**kw).then_inc(self.sems[i], 16)
        self.n += 1
        self.snap[self.n] = tuple(self.cnt)
        return self.n

    def wait(self, eng, mark):
        if mark <= 0:
            return
        for s, c in zip(self.sems, self.snap[mark]):
            if c:
                eng.wait_ge(s, c)


def build_nc(debug=(), upto="full", detect=False):
    nc = bass.Bass("TRN2", target_bir_lowering=False,
                   detect_race_conditions=detect)
    dbg = set(debug)

    h_ = {}
    def din(name, shape, dt=F32):
        h_[name] = nc.dram_tensor(name, list(shape), dt, kind="ExternalInput").ap()

    din("x_own", (T, D)); din("x_halo", (HALO, D))
    din("w_in", (D, 2 * INNER)); din("w_delta", (INNER, INNER)); din("w_out", (INNER, D))
    din("b_in_t", (128, 32)); din("conv_w_t", (128, IC, 3)); din("conv_b_t", (128, IC))
    din("b_delta_t", (128, IC)); din("b_out_t", (128, DC))
    din("mgam_t", (128, DC)); din("mbeta_t", (128, DC))
    for l in range(2):
        din(f"wqkv{l}", (D, 1536)); din(f"wo{l}", (512, D))
        din(f"w1p{l}", (D, F1P)); din(f"b1p_t{l}", (128, F1P // 128))
        din(f"w2p{l}", (HIDP, D)); din(f"b2_t{l}", (128, DC))
        din(f"lng_t{l}", (128, DC)); din(f"lnb_t{l}", (128, DC))
        din(f"qs{l}", (1, DH)); din(f"ks{l}", (1, DH))
    din("tabs_hi", (128, 20, 256)); din("tabs_lo", (128, 20, 256))
    din("mask_n", (128, 4, 1024), BF16); din("mask_c0", (128, 4, 1024), BF16)
    din("id_bf", (128, 128), BF16); din("id_f32", (128, 128))
    din("ones_row", (1, 128)); din("ones_col", (128, 1)); din("eps_col", (128, 1))
    din("sel_first", (128, 1)); din("sel_second", (128, 1))

    out_h = nc.dram_tensor("out", [T, D], F32, kind="ExternalOutput").ap()

    def dint(name, shape, dt=F32):
        return nc.dram_tensor(name, list(shape), dt).ap()

    sp_xn = dint("sp_xn", (128, DC, T))
    sp_vs = dint("sp_vs", (128, IC, T))
    sp_sg = dint("sp_sg", (128, IC, T))
    sp_fw = dint("sp_fw", (128, IC, T))
    sp_pf = dint("sp_pf", (128, IC, T))
    sp_bw = dint("sp_bw", (128, IC, T))
    sp_pb = dint("sp_pb", (128, IC, T))
    sp_xat = dint("sp_xat", (128, DC, T))
    sp_xl1 = dint("sp_xl1", (128, DC, T))
    sp_s = dint("sp_s", (128, DC, T))
    sp_xh = dint("sp_xh", (128, DC, HALO), BF16)
    sp_tab = dint("sp_tab", (128, 20, 256))
    sp_brow = dint("sp_brow", (32, 128))
    cc1_in = dint("cc1_in", (128, 2 * IC))
    cc1_out = dint("cc1_out", (128, 2 * IC))
    cc2_in = dint("cc2_in", (128, DC, HALO))
    cc2_out = dint("cc2_out", (128, DC, HALO))
    cca_in = [dint(f"cca_in{l}", (128, DC)) for l in range(2)]
    cca_out = [dint(f"cca_out{l}", (128, DC)) for l in range(2)]

    dbg_h = {}
    def dtap(name):
        if name in dbg and name not in dbg_h:
            src = {"xn": sp_xn, "vs": sp_vs, "sg": sp_sg, "fw": sp_fw,
                   "pf": sp_pf, "bw": sp_bw, "pb": sp_pb, "xat": sp_xat,
                   "xl1": sp_xl1, "s": sp_s}.get(name)
            if src is None:
                return
            dbg_h[name] = (nc.dram_tensor("dbg_" + name, list(src.shape), F32,
                                          kind="ExternalOutput").ap(), src)

    for nm in dbg:
        dtap(nm)
    dbg_attn = {}
    if "attn0" in dbg:
        for nm, shape in [("qf0", (128, 4, CH)), ("kc0", (128, 4, CH)),
                          ("vc0", (128, 4, CH)), ("af0", (128, 4, CH))]:
            dbg_attn[nm] = nc.dram_tensor("dbg_" + nm, list(shape), BF16,
                                          kind="ExternalOutput").ap()
        dbg_attn["xat1"] = nc.dram_tensor("dbg_xat1", [128, DC, T], F32,
                                          kind="ExternalOutput").ap()
        dbg_attn["hn0"] = nc.dram_tensor("dbg_hn0", [128, DC, 1024], BF16,
                                         kind="ExternalOutput").ap()
        dbg_attn["h20"] = nc.dram_tensor("dbg_h20", [128, NBK, 1024], BF16,
                                         kind="ExternalOutput").ap()
        dbg_attn["rows0"] = nc.dram_tensor("dbg_rows0", [4, 1024], F32,
                                           kind="ExternalOutput").ap()
    if "attn0" in dbg:
        dbg_attn["blkb0"] = nc.dram_tensor("dbg_blkb0", [128, DC, 1025], F32,
                                           kind="ExternalOutput").ap()
    dbg_a1 = {}
    if "a1adbg" in dbg:
        dbg_a1["stat"] = nc.dram_tensor("dbg_stat", [128, 32], F32,
                                        kind="ExternalOutput").ap()
        dbg_a1["xntm"] = nc.dram_tensor("dbg_xntm", [128, 4, D], F32,
                                        kind="ExternalOutput").ap()
        dbg_a1["xtm"] = nc.dram_tensor("dbg_xtm", [128, 4, D], F32,
                                       kind="ExternalOutput").ap()
        dbg_attn["blkb0"] = nc.dram_tensor("dbg_blkb0", [128, DC, 1025], F32,
                                           kind="ExternalOutput").ap()

    ctx = ExitStack()
    with ctx:
        def sem(name):
            return ctx.enter_context(nc.semaphore(name))
        cTE, cVE, cAC = Ctr(sem("cTE")), Ctr(sem("cVE")), Ctr(sem("cAC"))
        cGP, cSY = Ctr(sem("cGP")), Ctr(sem("cSY"))
        cCC = Ctr(sem("cCC"))
        qS = Q([sem(f"qS{i}") for i in range(4)])
        qG = Q([sem(f"qG{i}") for i in range(3)])
        qC = Q([sem(f"qC{i}") for i in range(4)])

        block = ctx.enter_context(nc.Block())
        te, ve, ac = block.tensor, block.vector, block.scalar
        gp, sy = block.gpsimd, block.sync
        ECTR = {"te": cTE, "ve": cVE, "ac": cAC, "gp": cGP, "sy": cSY}
        EDEC = {"te": te, "ve": ve, "ac": ac, "gp": gp, "sy": sy}

        def barrier():
            def _dsy(e):
                qS.wait(e, qS.n)
            sy(_dsy)
            def _dgp(e):
                qG.wait(e, qG.n)
            gp(_dgp)
            def _dac(e):
                qC.wait(e, qC.n)
            ac(_dac)
            marks = {}
            def _ite(e):
                e.wait_ge(cVE.sem, cVE.n)
                cTE.inc(e.ldweights(id_bf[:, 0:4]))
            te(_ite)
            marks["te"] = cTE.n
            def _ive(e):
                e.wait_ge(cVE.sem, cVE.n)
                cVE.inc(e.tensor_copy(bsc[:, 0:1], bsc[:, 3:4]))
            ve(_ive)
            marks["ve"] = cVE.n
            def _iac(e):
                e.wait_ge(cAC.sem, cAC.n)
                cAC.inc(e.activation(bsc[:, 1:2], bsc[:, 3:4], Act.Copy,
                                     bias=0.0, scale=1.0))
            ac(_iac)
            marks["ac"] = cAC.n
            def _igp(e):
                e.wait_ge(cGP.sem, cGP.n)
                cGP.inc(e.tensor_copy(bsc[:, 2:3], bsc[:, 3:4]))
            gp(_igp)
            marks["gp"] = cGP.n
            def _isy(e):
                cSY.n += 1
                e.sem_inc(cSY.sem, 1)
            sy(_isy)
            marks["sy"] = cSY.n
            for name in EDEC:
                def _wait(e, me=name):
                    for o, tgt in marks.items():
                        if o != me:
                            e.wait_ge(ECTR[o].sem, tgt)
                EDEC[name](_wait)

        def sb(stack, name, shape, dt=F32):
            return stack.enter_context(nc.sbuf_tensor(name, list(shape), dt))

        mask_nS = sb(ctx, "mask_nS", (128, 4, 1024), BF16)
        mask_c0S = sb(ctx, "mask_c0S", (128, 4, 1024), BF16)
        id_bf = sb(ctx, "id_bfS", (128, 128), BF16)
        id_f32 = sb(ctx, "id_f32S", (128, 128))
        ones_row = sb(ctx, "ones_rowS", (1, 128))
        ones_col = sb(ctx, "ones_colS", (128, 1))
        eps_col = sb(ctx, "eps_colS", (128, 1))
        sel_first = sb(ctx, "sel_firstS", (128, 1))
        sel_second = sb(ctx, "sel_secondS", (128, 1))
        b_in = sb(ctx, "b_inS", (128, 32))
        beff = sb(ctx, "beffS", (128, 32))
        conv_w = sb(ctx, "conv_wS", (128, IC, 3))
        conv_b = sb(ctx, "conv_bS", (128, IC))
        b_delta = sb(ctx, "b_deltaS", (128, IC))
        nb_delta = sb(ctx, "nb_deltaS", (128, IC))
        b_out = sb(ctx, "b_outS", (128, DC))
        mgam = sb(ctx, "mgamS", (128, DC))
        mbeta = sb(ctx, "mbetaS", (128, DC))
        lvec = {}
        for l in range(2):
            for key, shape in [("b1p", (128, F1P // 128)), ("b2", (128, DC)),
                               ("lng", (128, DC)), ("lnb", (128, DC)),
                               ("qs", (1, DH)), ("ks", (1, DH))]:
                lvec[f"{key}{l}"] = sb(ctx, f"{key}S{l}", shape)
        bsc = sb(ctx, "bscS", (128, 4))
        bscb = sb(ctx, "bscbS", (128, 4), BF16)
        ccR1 = sb(ctx, "ccR1", (128, IC))
        ccR2 = sb(ctx, "ccR2", (128, IC))
        R1e = sb(ctx, "R1e", (128, IC))
        R2e = sb(ctx, "R2e", (128, IC))
        scA = sb(ctx, "scA", (128, DC))
        sideL1 = sb(ctx, "sideL1", (128, DC, 1))
        sideL2 = sb(ctx, "sideL2", (128, DC, 1))

        psW = ctx.enter_context(nc.psum_tensor("psW", [128, T], F32))
        psA = ctx.enter_context(nc.psum_tensor("psA", [128, 1024], F32))
        psT = ctx.enter_context(nc.psum_tensor("psT", [128, 1024], BF16))
        psR = ctx.enter_context(nc.psum_tensor("psR", [128, 512], F32))
        S512 = [psW[:, 512 * i:512 * (i + 1)] for i in range(4)] + \
               [psA[:, 0:512], psA[:, 512:1024]]
        S1024 = [psA[:, :], psW[:, 0:1024], psW[:, 1024:2048]]

        def psT3(f=128):
            return psT.ap().rearrange("p (c f) -> p c f", f=f)

        def psA3(f=128):
            return psA.ap().rearrange("p (c f) -> p c f", f=f)

        p0 = ExitStack()
        tb_lo = sb(p0, "tb_lo", (128, 20, 256))
        tb_hi = sb(p0, "tb_hi", (128, 20, 256))

        def _p0s(s):
            for nm, tile in [("mask_n", mask_nS), ("mask_c0", mask_c0S),
                             ("id_bf", id_bf), ("id_f32", id_f32),
                             ("ones_row", ones_row), ("ones_col", ones_col),
                             ("eps_col", eps_col),
                             ("sel_first", sel_first), ("sel_second", sel_second),
                             ("b_in_t", b_in), ("conv_w_t", conv_w),
                             ("conv_b_t", conv_b), ("b_delta_t", b_delta),
                             ("b_out_t", b_out), ("mgam_t", mgam),
                             ("mbeta_t", mbeta), ("tabs_hi", tb_hi),
                             ("tabs_lo", tb_lo)]:
                qS.dma(s, out=tile[:], in_=h_[nm][:])
            for l in range(2):
                for key, nm in [("b1p", f"b1p_t{l}"), ("b2", f"b2_t{l}"),
                                ("lng", f"lng_t{l}"), ("lnb", f"lnb_t{l}"),
                                ("qs", f"qs{l}"), ("ks", f"ks{l}")]:
                    qS.dma(s, out=lvec[f"{key}{l}"][:], in_=h_[nm][:])
        sy(_p0s)
        p0_ld = qS.n

        def _p0v(v):
            v.memset(bsc[:], 0.0)
            v.memset(bscb[:], 0.0)
            qS.wait(v, p0_ld)
            v.tensor_tensor(tb_hi[:], tb_hi[:], tb_lo[:], Alu.add)
            cVE.inc(v.tensor_scalar(nb_delta[:], b_delta[:], -1.0, None, Alu.mult))
        ve(_p0v)
        tabs_v = cVE.n

        def _p0s2(s):
            s.wait_ge(cVE.sem, tabs_v)
            qS.dma(s, out=sp_tab[:], in_=tb_hi[:])
        sy(_p0s2)
        tab_spill = qS.n
        p0.close()

        def _endprog():
            if dbg_h:
                def _tap(g):
                    for nm, (dst, src) in dbg_h.items():
                        qG.dma(g, out=dst[:], in_=src[:])
                    qG.wait(g, qG.n)
                gp(_tap)
            barrier()
            def _fin(s):
                qS.wait(s, qS.n)
            sy(_fin)

        a1 = ExitStack()
        w_inS = sb(a1, "w_inS", (128, DC, 2 * INNER), BF16)
        x_tm = sb(a1, "x_tm", (128, 4, D))
        xsq = sb(a1, "xsq", (128, 2, D))
        xn_tm = sb(a1, "xn_tm", (128, 4, D))
        xnm_tm = sb(a1, "xnm_tm", (128, 4, D), BF16)
        stat = sb(a1, "stat", (128, 16))
        st2 = sb(a1, "st2", (128, 16))
        xn_fm = sb(a1, "xn_fm", (128, DC, CH + 1))
        xnm_fm = sb(a1, "xnm_fm", (128, DC, CH), BF16)
        vals = sb(a1, "vals", (128, 2, CH + 2))
        vtail = sb(a1, "vtail", (128, IC, 2))
        sgb2 = [sb(a1, f"sgb2_{i}", (128, CH)) for i in range(2)]
        mbeta_bf = sb(a1, "mbeta_bf", (128, DC), BF16)
        brow_s = sb(a1, "brow_s", (1, 512))
        cb_eff = sb(a1, "cb_eff", (128, IC))
        nsel1 = sb(a1, "nsel1", (128, 1))
        sc1 = sb(a1, "sc1", (128, 1))
        hx = ExitStack()
        xh_fm = sb(hx, "xh_fm", (128, DC, HALO), BF16)
        xnmh_fm = sb(hx, "xnmh_fm", (128, DC, 128), BF16)

        def _a1g(g):
            qS.wait(g, tab_spill)
            wv = h_["w_in"].rearrange("(c p) f -> c p f", p=128)
            qG.dma(g, out=w_inS[:], in_=wv.transpose([1, 0, 2]))
        gp(_a1g)
        w_in_ld = qG.n

        def _mbv(v):
            qS.wait(v, p0_ld)
            cVE.inc(v.tensor_copy(mbeta_bf[:], mbeta[:]))
        ve(_mbv)
        mbv = cVE.n
        for nch in range(8):
            def _t(t, nch=nch):
                if nch == 0:
                    t.wait_ge(cVE.sem, mbv)
                    qG.wait(t, w_in_ld)
                ii = None
                for dc in range(DC):
                    ii = t.matmul(psR[0:1, :], mbeta_bf[:, dc:dc + 1],
                                  w_inS[:, dc, 512 * nch:512 * nch + 512],
                                  start=(dc == 0), stop=(dc == DC - 1))
                cTE.inc(ii)
            te(_t)
            mmn = cTE.n
            def _v(v, nch=nch, mmn=mmn):
                v.wait_ge(cTE.sem, mmn)
                if nch >= 1:
                    qS.wait(v, brow_mark[nch - 1])
                cVE.inc(v.tensor_copy(brow_s[:], psR[0:1, :]))
            if nch == 0:
                brow_mark = {}
            ve(_v)
            evn = cVE.n
            def _bs1(s, nch=nch, evn=evn):
                s.wait_ge(cVE.sem, evn)
                qS.dma(s,
                       out=sp_brow.rearrange("(a t) p -> a (t p)", a=8)[nch:nch + 1],
                       in_=brow_s[:])
            sy(_bs1)
            brow_mark[nch] = qS.n
            if nch < 7:
                te(lambda t, evn=evn: t.wait_ge(cVE.sem, evn))
        t1 = qS.n
        def _bs2(s):
            qS.wait(s, t1)
            qS.dma(s, out=beff[:], in_=sp_brow.transpose([1, 0]),
                               allow_slow_non_contiguous=True)
        sy(_bs2)
        t2 = qS.n
        def _bv(v):
            qS.wait(v, t2)
            cVE.inc(v.tensor_tensor(beff[:], beff[:], b_in[:], Alu.add))
            ii = None
            for dc in range(DC):
                ii = v.tensor_scalar(w_inS[:, dc, :], w_inS[:, dc, :],
                                     mgam[:, dc:dc + 1], None, Alu.mult)
            cVE.inc(ii)
        ve(_bv)
        def _cb(v):
            cVE.inc(v.tensor_tensor(cb_eff[:], conv_w[:, :, 0], conv_w[:, :, 1], Alu.add))
            v.wait_ge(cVE.sem, cVE.n)
            cVE.inc(v.tensor_tensor(cb_eff[:], cb_eff[:], conv_w[:, :, 2], Alu.add))
            v.wait_ge(cVE.sem, cVE.n)
            cVE.inc(v.tensor_tensor(cb_eff[:], cb_eff[:], beff[:, 0:IC], Alu.mult))
            v.wait_ge(cVE.sem, cVE.n)
            cVE.inc(v.tensor_tensor(cb_eff[:], cb_eff[:], conv_b[:], Alu.add))
            cVE.inc(v.tensor_scalar(nsel1[:], sel_first[:], -1.0, None, Alu.mult))
        ve(_cb)
        winS_ready = cVE.n

        def ln_stats(nts):
            for ts in range(nts):
                def _v(v, ts=ts):
                    ii = v.tensor_scalar(xn_tm[:, ts, :], x_tm[:, ts, :], 1.0, 0.0,
                                         Alu.mult, Alu.add,
                                         accum_out=stat[:, ts:ts + 1])
                    cVE.inc(ii)
                ve(_v)
                def _a(a, ts=ts):
                    if ts >= 2:
                        a.wait_ge(cAC.sem, sq_mark[ts - 2])
                    cAC.inc(a.activation(xsq[:, ts % 2, :], x_tm[:, ts, :],
                                         Act.Square,
                                         accum_out=stat[:, 4 + ts:5 + ts]))
                if ts == 0:
                    sq_mark = {}
                ac(_a)
                sq_mark[ts] = cAC.n
            acw, vew = cAC.n, cVE.n
            s_ = slice(0, nts)
            def _f1(v):
                v.wait_ge(cAC.sem, acw)
                v.wait_ge(cVE.sem, vew)
                cVE.inc(v.tensor_scalar(stat[:, 8:8 + nts], stat[:, s_], R1024,
                                        None, Alu.mult))
                v.wait_ge(cVE.sem, cVE.n)
                v.tensor_scalar(stat[:, 12:12 + nts], stat[:, 8:8 + nts], -1.0,
                                None, Alu.mult)
                cVE.inc(v.tensor_tensor(st2[:, 12:12 + nts], stat[:, 8:8 + nts],
                                        stat[:, 8:8 + nts], Alu.mult))
                v.wait_ge(cVE.sem, cVE.n)
                cVE.inc(v.scalar_tensor_tensor(st2[:, 0:nts], stat[:, 4:4 + nts],
                                               R1024, st2[:, 12:12 + nts],
                                               Alu.mult, Alu.subtract))
            ve(_f1)
            vw = cVE.n
            def _f2(a):
                a.wait_ge(cVE.sem, vw)
                cAC.inc(a.activation(st2[:, 4:4 + nts], st2[:, 0:nts], Act.Sqrt,
                                     bias=eps_col[:], scale=1.0))
            ac(_f2)
            aw = cAC.n
            def _f3(v):
                v.wait_ge(cAC.sem, aw)
                cVE.inc(v.reciprocal(st2[:, 4:4 + nts], st2[:, 4:4 + nts]))
                v.wait_ge(cVE.sem, cVE.n)
                cVE.inc(v.tensor_tensor(st2[:, 8:8 + nts], st2[:, 4:4 + nts],
                                        st2[:, 4:4 + nts], Alu.mult))
                v.wait_ge(cVE.sem, cVE.n)
                cVE.inc(v.tensor_tensor(st2[:, 8:8 + nts], st2[:, 0:nts],
                                        st2[:, 8:8 + nts], Alu.mult))
            ve(_f3)
            vw2 = cVE.n
            def _f4(a):
                a.wait_ge(cVE.sem, vw2)
                cAC.inc(a.activation(st2[:, 12:12 + nts], st2[:, 8:8 + nts], Act.Sqrt,
                                     bias=eps_col[:], scale=1.0))
            ac(_f4)
            aw2 = cAC.n
            def _f5(v):
                v.wait_ge(cAC.sem, aw2)
                cVE.inc(v.reciprocal(st2[:, 12:12 + nts], st2[:, 12:12 + nts]))
                v.wait_ge(cVE.sem, cVE.n)
                cVE.inc(v.tensor_tensor(st2[:, 8:8 + nts], st2[:, 4:4 + nts],
                                        st2[:, 12:12 + nts], Alu.mult))
            ve(_f5)
            return cVE.n

        def ln_apply(ts, want_xnm=True):
            def _v(v, ts=ts):
                ii = v.tensor_scalar(xn_tm[:, ts, :], x_tm[:, ts, :],
                                     stat[:, 12 + ts:13 + ts],
                                     st2[:, 4 + ts:5 + ts], Alu.add, Alu.mult)
                if want_xnm:
                    ii = v.tensor_scalar(xnm_tm[:, ts, :], x_tm[:, ts, :],
                                         stat[:, 12 + ts:13 + ts],
                                         st2[:, 8 + ts:9 + ts], Alu.add, Alu.mult)
                cVE.inc(ii)
            ve(_v)

        def _hs(s):
            qS.wait(s, p0_ld)
            for i, off in enumerate([3, 131, 259, 387]):
                qS.dma(s, out=x_tm[:, i, :], in_=h_["x_halo"][off:off + 128, :])
        sy(_hs)
        hld = qS.n
        ve(lambda v: qS.wait(v, hld))
        ac(lambda a: qS.wait(a, hld))
        fin = ln_stats(4)
        ve(lambda v, t=fin: v.wait_ge(cVE.sem, t))
        for ts in range(4):
            ln_apply(ts, want_xnm=False)
        apl = cVE.n

        te(lambda t, t0=apl: t.wait_ge(cVE.sem, t0))
        for i, off in enumerate([3, 131, 259, 387]):
            def _tr(t, i=i):
                ii = None
                for dc in range(DC):
                    ii = t.transpose(psA[:, dc * 128:dc * 128 + 128],
                                     xn_tm[:, i, dc * 128:dc * 128 + 128], id_f32[:])
                cTE.inc(ii)
            te(_tr)
            trn = cTE.n
            def _ev(v, off=off, trn=trn):
                v.wait_ge(cTE.sem, trn)
                cVE.inc(v.tensor_copy(xh_fm.ap()[:, :, off:off + 128], psA3()))
            ve(_ev)
            evn = cVE.n
            te(lambda t, evn=evn: t.wait_ge(cVE.sem, evn))
        trm = cTE.n
        def _hs2(s):
            s.wait_ge(cTE.sem, trm)
            qS.dma(s, out=x_tm[:, 0, :], in_=h_["x_halo"][388:516, :])
        sy(_hs2)
        hld2 = qS.n
        ve(lambda v: qS.wait(v, hld2))
        ac(lambda a: qS.wait(a, hld2))
        fin2 = ln_stats(1)
        ve(lambda v, t=fin2: v.wait_ge(cVE.sem, t))
        ln_apply(0, want_xnm=True)
        apl2 = cVE.n
        def _tr2(t):
            t.wait_ge(cVE.sem, apl2)
            for dc in range(DC):
                t.transpose(psA[:, dc * 128:dc * 128 + 128],
                            xn_tm[:, 0, dc * 128:dc * 128 + 128], id_f32[:])
            ii = None
            for dc in range(DC):
                ii = t.transpose(psT[:, dc * 128:dc * 128 + 128],
                                 xnm_tm[:, 0, dc * 128:dc * 128 + 128], id_bf[:])
            cTE.inc(ii)
        te(_tr2)
        tr2 = cTE.n
        def _ev2(v):
            v.wait_ge(cTE.sem, tr2)
            v.tensor_copy(xh_fm.ap()[:, :, 388:516], psA3())
            cVE.inc(v.tensor_copy(xnmh_fm.ap()[:, :, :], psT3()))
        ve(_ev2)
        xh_done = cVE.n
        def _hsp(s):
            s.wait_ge(cVE.sem, xh_done)
            qS.dma(s, out=sp_xh[:, :, 3:HALO], in_=xh_fm[:, :, 3:HALO])
        sy(_hsp)
        def _sd1(v):
            v.wait_ge(cVE.sem, xh_done)
            cVE.inc(v.tensor_scalar(sideL1[:], xh_fm[:, :, 515:516],
                                    sel_second[:], None, Alu.mult))
        ve(_sd1)

        te(lambda t: t.wait_ge(cVE.sem, max(winS_ready, xh_done)))
        hp_marks = {}
        for f in range(IC):
            def _t(t, f=f):
                ii = None
                for dc in range(DC):
                    ii = t.matmul(psR[:, 0:2], w_inS[:, dc, 128 * f:128 * f + 128],
                                  xnmh_fm[:, dc, 126:128],
                                  start=(dc == 0), stop=(dc == DC - 1))
                cTE.inc(ii)
            te(_t)
            mmn = cTE.n
            def _v(v, f=f, mmn=mmn):
                v.wait_ge(cTE.sem, mmn)
                v.tensor_scalar(vals[:, 0, 0:2], psR[:, 0:2],
                                sel_second[:], None, Alu.mult)
                cVE.inc(v.tensor_tensor(sc1[:], nsel1[:], beff[:, f:f + 1], Alu.mult))
                v.wait_ge(cVE.sem, cVE.n)
                cVE.inc(v.tensor_scalar(vtail[:, f, :], vals[:, 0, 0:2],
                                        sc1[:], None, Alu.add))
            ve(_v)
            evn = cVE.n
            te(lambda t, evn=evn: t.wait_ge(cVE.sem, evn))

        barrier()
        hx.close()
        vs16 = sb(a1, "vs16", (128, IC, CH))

        xr_own = h_["x_own"].rearrange("(a p) d -> a p d", p=128)
        conv_mark = {}
        vspill_mark = {}
        sgspill_mark = {}
        evac_mark = {}
        spxn_mark = {}
        winmm_mark = {}
        for c in range(NCH):
            def _xl(s, c=c):
                if c == 0:
                    s.wait_ge(cTE.sem, tr2)
                qS.dma(s, out=x_tm[:],
                                   in_=xr_own[4 * c:4 * c + 4].transpose([1, 0, 2]))
            sy(_xl)
            xld = qS.n
            ve(lambda v: qS.wait(v, xld))
            ac(lambda a: qS.wait(a, xld))
            if c >= 1:
                def _gx(v, c=c):
                    qS.wait(v, spxn_mark[c - 1])
                    v.wait_ge(cTE.sem, winmm_mark[c - 1])
                ve(_gx)
            fin = ln_stats(4)
            ve(lambda v, t=fin: v.wait_ge(cVE.sem, t))
            for ts in range(4):
                ln_apply(ts)
            aplv = cVE.n
            if dbg_a1 and c == 0:
                def _ta1(s, aplv=aplv):
                    s.wait_ge(cVE.sem, aplv)
                    qS.dma(s, out=dbg_a1["stat"][:, 0:16], in_=stat[:])
                    qS.dma(s, out=dbg_a1["stat"][:, 16:32], in_=st2[:])
                    qS.dma(s, out=dbg_a1["xntm"][:], in_=xn_tm[:])
                    qS.dma(s, out=dbg_a1["xtm"][:], in_=x_tm[:])
                sy(_ta1)
                _tm = qS.n
                ve(lambda v, m=_tm: qS.wait(v, m))
                ac(lambda a, m=_tm: qS.wait(a, m))
            te(lambda t, t0=aplv: t.wait_ge(cVE.sem, t0))
            for ts in range(4):
                def _tr(t, ts=ts):
                    for dc in range(DC):
                        t.transpose(psA[:, dc * 128:dc * 128 + 128],
                                    xn_tm[:, ts, dc * 128:dc * 128 + 128], id_f32[:])
                    ii = None
                    for dc in range(DC):
                        ii = t.transpose(psT[:, dc * 128:dc * 128 + 128],
                                        xnm_tm[:, ts, dc * 128:dc * 128 + 128], id_bf[:])
                    cTE.inc(ii)
                te(_tr)
                trn = cTE.n
                def _ev(v, ts=ts, trn=trn):
                    v.wait_ge(cTE.sem, trn)
                    v.tensor_copy(xn_fm.ap()[:, :, 1 + ts * 128:1 + ts * 128 + 128], psA3())
                    cVE.inc(v.tensor_copy(
                        xnm_fm.ap()[:, :, ts * 128:ts * 128 + 128], psT3()))
                ve(_ev)
                evn = cVE.n
                te(lambda t, evn=evn: t.wait_ge(cVE.sem, evn))
            fm_done = cVE.n
            def _spxn(s, c=c, t0=fm_done):
                s.wait_ge(cVE.sem, t0)
                qS.dma(s, out=sp_xn[:, :, c * CH:(c + 1) * CH],
                                   in_=xn_fm[:, :, 1:CH + 1])
            sy(_spxn)
            spxn_mark[c] = qS.n

            mm_mark2 = {}
            ts_mark = {}

            def emit_mm(f, c=c):
                slot = S512[f % 6]
                def _mm(t, f=f, slot=slot):
                    gk = (c, f - 6) if f >= 6 else (c - 1, f + 26)
                    if gk in evac_mark:
                        t.wait_ge(cAC.sem, evac_mark[gk])
                    ii = None
                    for dc in range(DC):
                        ii = t.matmul(slot, w_inS[:, dc, 128 * f:128 * f + 128],
                                      xnm_fm[:, dc, :],
                                      start=(dc == 0), stop=(dc == DC - 1))
                    cTE.inc(ii)
                te(_mm)
                mm_mark2[f] = cTE.n

            def emit_ev_pair(f0, c=c):
                def _ev(a, f0=f0):
                    for f in (f0, f0 + 1):
                        a.wait_ge(cTE.sem, mm_mark2[f])
                        gk = (c, f - 2) if f >= 2 else (c - 1, f + 14)
                        if gk in conv_mark:
                            a.wait_ge(cVE.sem, conv_mark[gk])
                        cAC.inc(a.activation(vals[:, f % 2, 2:CH + 2],
                                             S512[f % 6], Act.Copy,
                                             bias=0.0, scale=1.0))
                        evac_mark[(c, f)] = cAC.n
                ac(_ev)

            def emit_cv(f, c=c):
                vb = vals[:, f % 2, :]
                vs = vs16[:, f, :]
                def _cv(v, f=f, vb=vb, vs=vs):
                    v.wait_ge(cAC.sem, evac_mark[(c, f)])
                    gk = (c - 1, f)
                    if gk in vspill_mark:
                        qS.wait(v, vspill_mark[gk])
                    v.tensor_copy(vb[:, 0:2], vtail[:, f, :])
                    cVE.inc(v.tensor_scalar(vs[:], vb[:, 2:CH + 2],
                                            conv_w[:, f, 2:3], None, Alu.mult))
                    v.wait_ge(cVE.sem, cVE.n)
                    cVE.inc(v.scalar_tensor_tensor(vs[:], vb[:, 1:CH + 1],
                                                   conv_w[:, f, 1:2], vs[:],
                                                   Alu.mult, Alu.add))
                    v.wait_ge(cVE.sem, cVE.n)
                    cVE.inc(v.scalar_tensor_tensor(vs[:], vb[:, 0:CH],
                                                   conv_w[:, f, 0:1], vs[:],
                                                   Alu.mult, Alu.add))
                    cVE.inc(v.tensor_copy(vtail[:, f, :], vb[:, CH:CH + 2]))
                ve(_cv)
                conv_mark[(c, f)] = cVE.n

            def emit_si_batch(c=c):
                si_mark = {}
                def _si(a):
                    for f in range(IC):
                        a.wait_ge(cVE.sem, conv_mark[(c, f)])
                        cAC.inc(a.activation(vs16[:, f, :], vs16[:, f, :],
                                             Act.Silu,
                                             bias=cb_eff[:, f:f + 1], scale=1.0))
                        si_mark[f] = cAC.n
                ac(_si)
                def _sp(s):
                    for f in range(IC):
                        s.wait_ge(cAC.sem, si_mark[f])
                        qS.dma(s, out=sp_vs[:, f, c * CH:(c + 1) * CH],
                               in_=vs16[:, f, :])
                        vspill_mark[(c, f)] = qS.n
                sy(_sp)

            def emit_evg_pair(f0, c=c):
                spg_mark = {}
                def _evg(a, f0=f0):
                    for f in (f0, f0 + 1):
                        fg = f - IC
                        a.wait_ge(cTE.sem, mm_mark2[f])
                        gk = (c, fg - 2) if fg >= 2 else (c - 1, fg + 14)
                        if gk in sgspill_mark:
                            qS.wait(a, sgspill_mark[gk])
                        cAC.inc(a.activation(sgb2[fg % 2][:], S512[f % 6],
                                             Act.Sigmoid,
                                             bias=beff[:, f:f + 1], scale=1.0))
                        evac_mark[(c, f)] = cAC.n
                        spg_mark[f] = cAC.n
                ac(_evg)
                def _spg(s, f0=f0):
                    for f in (f0, f0 + 1):
                        fg = f - IC
                        s.wait_ge(cAC.sem, spg_mark[f])
                        qS.dma(s, out=sp_sg[:, fg, c * CH:(c + 1) * CH],
                               in_=sgb2[fg % 2][:])
                        sgspill_mark[(c, fg)] = qS.n
                sy(_spg)

            for s_ in range(17):
                f0 = 2 * s_
                if 0 <= f0 - 2 < IC:
                    emit_cv(f0 - 2)
                if 0 <= f0 - 1 < IC:
                    emit_cv(f0 - 1)
                if f0 < 32:
                    emit_mm(f0)
                    emit_mm(f0 + 1)
                    if f0 < IC:
                        emit_ev_pair(f0)
                    else:
                        emit_evg_pair(f0)
            emit_si_batch()
            winmm_mark[c] = cTE.n

        barrier()
        a1.close()
        if upto == "a1a2":
            _endprog()
            return nc

        b1 = ExitStack()
        vsbB = sb(b1, "vsbB", (128, IC, T), BF16)
        wdS = [sb(b1, f"wdS{i}", (128, IC, 128), BF16) for i in range(2)]
        delt = [sb(b1, f"delt{i}", (128, T)) for i in range(2)]
        signz = [sb(b1, f"signz{i}", (128, T)) for i in range(2)]
        vsf = [sb(b1, f"vsf{i}", (128, T)) for i in range(2)]
        bbuf = sb(b1, "bbuf", (128, T))
        zeros = sb(b1, "zerosB", (128, T))
        sout = [sb(b1, f"sout{i}", (128, T)) for i in range(4)]

        ve(lambda v: cVE.inc(v.memset(zeros[:], 0.0)))
        zdone = cVE.n

        def _b1g(g):
            for cq in range(4):
                qG.dma(g, out=vsbB[:, 4 * cq:4 * cq + 4, :],
                                   in_=sp_vs[:, 4 * cq:4 * cq + 4, :])
        gp(_b1g)
        vsb_ld = qG.n

        wdr = h_["w_delta"].rearrange("(ci p) co -> ci p co", p=128)
        ac_mark = {}
        scan_mark = {}
        spill_mark = {}
        mm_mark = {}
        wl_mark = {}

        def _wl0(g):
            qG.dma(g, out=wdS[0][:],
                   in_=wdr[:, :, 0:128].transpose([1, 0, 2]))
        gp(_wl0)
        wl_mark[0] = qG.n
        for ct in range(IC):
            if ct + 1 < IC:
                def _wl(g, ct=ct):
                    if ct >= 1:
                        g.wait_ge(cTE.sem, mm_mark[ct - 1])
                    qG.dma(g, out=wdS[(ct + 1) % 2][:],
                           in_=wdr[:, :, 128 * (ct + 1):128 * (ct + 1) + 128]
                               .transpose([1, 0, 2]))
                gp(_wl)
                wl_mark[ct + 1] = qG.n
            wl = wl_mark[ct]
            def _vl(a, ct=ct):
                if ct >= 2:
                    a.wait_ge(cVE.sem, scan_mark[ct - 2])
                qC.dma(a, out=vsf[ct % 2][:], in_=sp_vs[:, ct, :])
            ac(_vl)
            vl = qC.n

            if ct == 0:
                gscan_mark = {}
                ev_half = {}
            for h in range(2):
                def _mm(t, ct=ct, h=h, wl=wl):
                    if (ct, h) == (0, 0):
                        qG.wait(t, vsb_ld)
                    qG.wait(t, wl)
                    if ct >= 1:
                        t.wait_ge(cAC.sem, ev_half[(ct - 1, h)])
                    ii = None
                    for qr in range(2):
                        o = 1024 * h + 512 * qr
                        for ci in range(IC):
                            ii = t.matmul(psW[:, o:o + 512],
                                          wdS[ct % 2][:, ci, :],
                                          vsbB[:, ci, o:o + 512],
                                          start=(ci == 0), stop=(ci == IC - 1))
                    cTE.inc(ii)
                te(_mm)
                mm_mark[ct] = cTE.n
                mmn = cTE.n
                def _ev(a, ct=ct, h=h, mmn=mmn):
                    a.wait_ge(cTE.sem, mmn)
                    if h == 0 and ct >= 2:
                        a.wait_ge(cVE.sem, gscan_mark[ct - 2])
                    o = 1024 * h
                    a.activation(delt[ct % 2][:, o:o + 1024],
                                 psW[:, o:o + 1024], Act.Sigmoid,
                                 bias=b_delta[:, ct:ct + 1], scale=1.0)
                    cAC.inc(a.activation(signz[ct % 2][:, o:o + 1024],
                                         psW[:, o:o + 1024], Act.Sigmoid,
                                         bias=nb_delta[:, ct:ct + 1], scale=-1.0))
                ac(_ev)
                ev_half[(ct, h)] = cAC.n
            ac_mark[ct] = cAC.n
            acn = cAC.n
            def _sc(v, ct=ct, acn=acn, vl=vl):
                v.wait_ge(cAC.sem, acn)
                qC.wait(v, vl)
                if ct == 0:
                    v.wait_ge(cVE.sem, zdone)
                if ct >= 1:
                    qS.wait(v, spill_mark[ct - 1])
                cVE.inc(v.tensor_tensor(bbuf[:], signz[ct % 2][:],
                                        vsf[ct % 2][:], Alu.mult))
                v.wait_ge(cVE.sem, cVE.n)
                v.tensor_tensor_scan(sout[0][:], delt[ct % 2][:], bbuf[:], 0.0,
                                     Alu.mult, Alu.add)
                cVE.inc(v.tensor_tensor_scan(sout[2][:, ::-1], delt[ct % 2][:, ::-1],
                                             bbuf[:, ::-1], 0.0, Alu.mult, Alu.add))
                v.wait_ge(cVE.sem, cVE.n)
                v.tensor_copy(ccR1[:, ct:ct + 1], sout[0][:, T - 1:T])
                cVE.inc(v.tensor_copy(ccR2[:, ct:ct + 1], sout[2][:, 0:1]))
            ve(_sc)
            scan_mark[ct] = cVE.n
            scn = cVE.n
            def _gsc(v, ct=ct):
                v.tensor_tensor_scan(sout[1][:], delt[ct % 2][:], zeros[:], 1.0,
                                     Alu.mult, Alu.add)
                cVE.inc(v.tensor_tensor_scan(sout[3][:, ::-1], delt[ct % 2][:, ::-1],
                                             zeros[:], 1.0, Alu.mult, Alu.add))
            ve(_gsc)
            gscan_mark[ct] = cVE.n
            gscn = cVE.n
            def _sp(s, ct=ct, scn=scn, gscn=gscn):
                s.wait_ge(cVE.sem, gscn)
                qS.dma(s, out=sp_fw[:, ct, :], in_=sout[0][:])
                qS.dma(s, out=sp_pf[:, ct, :], in_=sout[1][:])
                qS.dma(s, out=sp_bw[:, ct, :], in_=sout[2][:])
                qS.dma(s, out=sp_pb[:, ct, :], in_=sout[3][:])
            sy(_sp)
            spill_mark[ct] = qS.n

        scnall = cVE.n
        def _c1v(v):
            v.wait_ge(cVE.sem, cVE.n)
            v.tensor_scalar(ccR1[:], ccR1[:], sel_first[:], None, Alu.mult)
            cVE.inc(v.tensor_scalar(ccR2[:], ccR2[:], sel_second[:], None, Alu.mult))
        ve(_c1v)
        c1v = cVE.n
        def _c1g(g):
            g.wait_ge(cVE.sem, c1v)
            qG.dma(g, out=cc1_in[:, 0:IC], in_=ccR1[:])
            qG.dma(g, out=cc1_in[:, IC:2 * IC], in_=ccR2[:])
            qG.wait(g, qG.n)
            cCC.inc(g.collective_compute("AllReduce", Alu.add, replica_groups=RG,
                                         ins=[cc1_in.opt()], outs=[cc1_out.opt()]))
        gp(_c1g)
        cc1_done = cCC.n

        barrier()
        b1.close()
        if upto == "a1b":
            _endprog()
            return nc

        def emit_attn(l, xin_sp, is_c2, cc2_wait=0, sub=6):
            lx = ExitStack()
            ltab = sb(lx, f"ltab{l}", (128, 20, 256))
            wqkvS = sb(lx, f"wqkvS{l}", (128, DC, 1536), BF16)
            woS = sb(lx, f"woS{l}", (128, 4, 1024), BF16)
            xi32 = sb(lx, f"xi32_{l}", (128, DC, CH + 1))
            xib = sb(lx, f"xib{l}", (128, DC, CH + 1), BF16)
            xatc = sb(lx, f"xatc{l}", (128, DC, CH))
            qk2 = [sb(lx, f"qk2{l}_{i}", (128, 1024)) for i in range(2)]
            sqb2 = [sb(lx, f"sqb2{l}_{i}", (128, 1024)) for i in range(2)]
            rot2 = [sb(lx, f"rot2{l}_{i}", (128, 1024)) for i in range(2)]
            qf2 = [sb(lx, f"qf2{l}_{i}", (128, 1024), BF16) for i in range(2)]
            ssn = sb(lx, f"ssn{l}", (128, 64))
            qfm = sb(lx, f"qfm{l}", (128, 4, CH), BF16)
            kc = [sb(lx, f"kc{l}_{i}", (128, 4, CH), BF16) for i in range(2)]
            vc = [sb(lx, f"vc{l}_{i}", (128, 4, CH), BF16) for i in range(2)]
            probs = [sb(lx, f"probs{l}_{i}", (128, 1024), BF16) for i in range(2)]
            probsN = [sb(lx, f"probsN{l}_{i}", (128, 1024), BF16) for i in range(2)]
            pT = sb(lx, f"pT{l}", (128, 2, 8, CH), BF16)
            attnfm = sb(lx, f"attnfm{l}", (128, 4, CH), BF16)
            smax = sb(lx, f"smax{l}", (128, 8))
            hbuf = sb(lx, f"hbuf{l}", (128, DC, HALO)) if is_c2 else None

            def _lg(g):
                wq = h_[f"wqkv{l}"].rearrange("(c p) f -> c p f", p=128)
                qG.dma(g, out=wqkvS[:], in_=wq.transpose([1, 0, 2]))
                wo = h_[f"wo{l}"].rearrange("(c p) f -> c p f", p=128)
                qG.dma(g, out=woS[:], in_=wo.transpose([1, 0, 2]))
            gp(_lg)
            w_ld = qG.n
            def _lt(a):
                qC.dma(a, out=ltab[:], in_=sp_tab[:])
                if is_c2:
                    a.wait_ge(cCC.sem, cc2_wait)
                    qC.dma(a, out=hbuf[:], in_=cc2_out[:])
            ac(_lt)
            tab_ld = qC.n

            def _qb(t):
                qC.wait(t, tab_ld)
                t.matmul(psR[:, 0:64], ones_row[:], lvec[f"qs{l}"][:],
                         start=True, stop=True)
                t.matmul(psR[:, 64:128], ones_row[:], lvec[f"ks{l}"][:],
                         start=True, stop=True)
                t.matmul(psR[:, 128:160], ones_row[:], lvec[f"qs{l}"][:, 32:64],
                         start=True, stop=True)
                t.matmul(psR[:, 160:192], ones_row[:], lvec[f"qs{l}"][:, 0:32],
                         start=True, stop=True)
                t.matmul(psR[:, 192:224], ones_row[:], lvec[f"ks{l}"][:, 32:64],
                         start=True, stop=True)
                cTE.inc(t.matmul(psR[:, 224:256], ones_row[:],
                                 lvec[f"ks{l}"][:, 0:32], start=True, stop=True))
            te(_qb)
            qbn = cTE.n
            def _qv(v):
                v.wait_ge(cTE.sem, qbn)
                ii = None
                for sl, ps in [(slice(0, 64), slice(0, 64)),
                               (slice(64, 128), slice(128, 192)),
                               (slice(128, 192), slice(64, 128)),
                               (slice(192, 256), slice(192, 256))]:
                    ii = v.tensor_tensor(
                        ltab.ap()[:, :, sl], ltab.ap()[:, :, sl],
                        psR.ap()[:, ps].unsqueeze(1).broadcast_to([128, 20, 64]),
                        Alu.mult)
                cVE.inc(ii)
                v.wait_ge(cVE.sem, cVE.n)
                cVE.inc(v.tensor_scalar(ltab.ap()[:, :, 0:128],
                                        ltab.ap()[:, :, 0:128], 8.0,
                                        None, Alu.mult))
            ve(_qv)
            ltab_done = cVE.n
            te(lambda t: t.wait_ge(cVE.sem, ltab_done))

            slot_g = {}
            psR_g = [(cVE, 0)]
            psT_g = [None]

            def _gw(e, gd):
                if gd is not None and gd[1]:
                    e.wait_ge(gd[0].sem, gd[1])

            q1_mark = {}
            q2r_mark = {}

            def emit_q1(ts, tag):
                slot = S1024[ts % 3]
                def _mmqk(t, ts=ts, slot=slot):
                    _gw(t, slot_g.get(ts % 3))
                    ii = None
                    for half in range(2):
                        for dc in range(DC):
                            off = (1 if dc < 4 else 0) + ts * 128
                            ii = t.matmul(slot[:, 512 * half:512 * half + 512],
                                          xib[:, dc, off:off + 128],
                                          wqkvS[:, dc, 512 * half:512 * half + 512],
                                          start=(dc == 0), stop=(dc == DC - 1))
                    cTE.inc(ii)
                te(_mmqk)
                def _mmv(t, ts=ts):
                    _gw(t, psR_g[0])
                    ii = None
                    for dc in range(DC):
                        off = (1 if dc < 4 else 0) + ts * 128
                        ii = t.matmul(psR[:], xib[:, dc, off:off + 128],
                                      wqkvS[:, dc, 1024:1536],
                                      start=(dc == 0), stop=(dc == DC - 1))
                    cTE.inc(ii)
                te(_mmv)
                q1_mark[(tag, ts)] = cTE.n

            def emit_q2(ts, tag, vdst):
                slot = S1024[ts % 3]
                base = 32 * (ts % 2)
                def _ev(a, ts=ts, slot=slot):
                    a.wait_ge(cTE.sem, q1_mark[(tag, ts)])
                    a.activation(qk2[ts % 2][:], slot, Act.Copy,
                                 bias=0.0, scale=1.0)
                    cAC.inc(a.activation(vdst[:, ts, :], psR[:], Act.Copy,
                                         bias=0.0, scale=1.0))
                ac(_ev)
                evn = cAC.n
                slot_g[ts % 3] = (cAC, evn)
                psR_g[0] = (cAC, evn)
                def _ss(v, ts=ts, evn=evn, base=base):
                    v.wait_ge(cAC.sem, evn)
                    cVE.inc(v.tensor_tensor(sqb2[ts % 2][:], qk2[ts % 2][:],
                                            qk2[ts % 2][:], Alu.mult))
                    v.wait_ge(cVE.sem, cVE.n)
                    cVE.inc(v.tensor_reduce(
                        ssn[:, base:base + 16],
                        sqb2[ts % 2].ap().rearrange("p (g d) -> p g d", d=64),
                        Ax.X, Alu.add))
                    v.wait_ge(cVE.sem, cVE.n)
                    cVE.inc(v.tensor_scalar(ssn[:, base:base + 16],
                                            ssn[:, base:base + 16], 1e-24,
                                            None, Alu.max))
                ve(_ss)
                ssv = cVE.n
                def _nr(a, ssv=ssv, base=base):
                    a.wait_ge(cVE.sem, ssv)
                    cAC.inc(a.activation(ssn[:, base + 16:base + 32],
                                         ssn[:, base:base + 16], Act.Sqrt,
                                         bias=0.0, scale=1.0))
                ac(_nr)
                q2r_mark[(tag, ts)] = cAC.n

            def emit_q3(ts, tag, sidx, want_q, kdst):
                base = 32 * (ts % 2)
                qk = qk2[ts % 2]
                sqb = sqb2[ts % 2]
                rot1 = rot2[ts % 2]
                qf = qf2[ts % 2]
                def _pr(v, ts=ts, sidx=sidx, base=base):
                    v.wait_ge(cAC.sem, q2r_mark[(tag, ts)])
                    cVE.inc(v.reciprocal(ssn[:, base + 16:base + 32],
                                         ssn[:, base + 16:base + 32]))
                    v.wait_ge(cVE.sem, cVE.n)
                    cVE.inc(v.tensor_tensor(
                        qk.ap().rearrange("p (g d) -> p g d", d=64),
                        qk.ap().rearrange("p (g d) -> p g d", d=64),
                        ssn.ap()[:, base + 16:base + 32].unsqueeze(2)
                           .broadcast_to([128, 16, 64]), Alu.mult))
                    v.wait_ge(cVE.sem, cVE.n)
                    v.tensor_tensor(
                        rot1.ap()[:, 0:512].rearrange("p (g d) -> p g d", d=64),
                        qk.ap()[:, 0:512].rearrange("p (g d) -> p g d", d=64),
                        ltab.ap()[:, sidx, 0:64].unsqueeze(1)
                            .broadcast_to([128, 8, 64]), Alu.mult)
                    v.tensor_tensor(
                        rot1.ap()[:, 512:1024].rearrange("p (g d) -> p g d", d=64),
                        qk.ap()[:, 512:1024].rearrange("p (g d) -> p g d", d=64),
                        ltab.ap()[:, sidx, 128:192].unsqueeze(1)
                            .broadcast_to([128, 8, 64]), Alu.mult)
                    ii = None
                    for lohi in range(2):
                        dsl = slice(32 * lohi, 32 * lohi + 32)
                        ssl = slice(32 * (1 - lohi), 32 * (1 - lohi) + 32)
                        tq = slice(64 + 32 * lohi, 64 + 32 * lohi + 32)
                        tk = slice(192 + 32 * lohi, 192 + 32 * lohi + 32)
                        v.tensor_tensor(
                            sqb.ap()[:, 0:512].rearrange("p (g d) -> p g d", d=64)[:, :, dsl],
                            qk.ap()[:, 0:512].rearrange("p (g d) -> p g d", d=64)[:, :, ssl],
                            ltab.ap()[:, sidx, tq].unsqueeze(1)
                                .broadcast_to([128, 8, 32]), Alu.mult)
                        ii = v.tensor_tensor(
                            sqb.ap()[:, 512:1024].rearrange("p (g d) -> p g d", d=64)[:, :, dsl],
                            qk.ap()[:, 512:1024].rearrange("p (g d) -> p g d", d=64)[:, :, ssl],
                            ltab.ap()[:, sidx, tk].unsqueeze(1)
                                .broadcast_to([128, 8, 32]), Alu.mult)
                    cVE.inc(ii)
                    v.wait_ge(cVE.sem, cVE.n)
                    cVE.inc(v.tensor_tensor(qf[:], rot1[:], sqb[:], Alu.add))
                ve(_pr)
                prn = cVE.n
                def _tr(t, prn=prn):
                    t.wait_ge(cVE.sem, prn)
                    _gw(t, psT_g[0])
                    ii = None
                    for j in range(8):
                        ii = t.transpose(psT[:, 128 * j:128 * j + 128],
                                         qf[:, 128 * j:128 * j + 128], id_bf[:])
                    cTE.inc(ii)
                te(_tr)
                trn = cTE.n
                def _te2(a, ts=ts, trn=trn):
                    a.wait_ge(cTE.sem, trn)
                    if want_q:
                        a.activation(
                            qfm.ap()[:, :, ts * 128:ts * 128 + 128],
                            psT.ap()[:, 0:512].rearrange("p (c f) -> p c f", f=128),
                            Act.Copy, bias=0.0, scale=1.0)
                    cAC.inc(a.activation(
                        kdst.ap()[:, :, ts * 128:ts * 128 + 128],
                        psT.ap()[:, 512:1024].rearrange("p (c f) -> p c f", f=128),
                        Act.Copy, bias=0.0, scale=1.0))
                ac(_te2)
                psT_g[0] = (cAC, cAC.n)

            def emit_qkv_group(sidx0, want_q, kdst, vdst, tag):
                for i in range(6):
                    if 1 <= i < 5:
                        emit_q2(i - 1, tag, vdst)
                    if i < 4:
                        emit_q1(i, tag)
                    if 2 <= i:
                        emit_q3(i - 2, tag, sidx0 + (i - 2), want_q, kdst)

            if is_c2:
                def _hc(v):
                    qC.wait(v, tab_ld)
                    ii = None
                    for dc in range(DC):
                        ii = v.tensor_copy(xib[:, dc, 0:HALO - 3],
                                           hbuf[:, dc, 3:HALO])
                    cVE.inc(ii)
                ve(_hc)
            else:
                def _hl(a):
                    qC.dma(a, out=xib[:, :, 0:CH + 1],
                                       in_=sp_xh[:, :, 3:HALO])
                ac(_hl)
                hldq = qC.n
                def _hc(v):
                    qC.wait(v, hldq)
                    cVE.inc(v.tensor_copy(bsc[:, 0:1], bsc[:, 3:4]))
                ve(_hc)
            hcast = cVE.n
            te(lambda t: t.wait_ge(cVE.sem, hcast))
            te(lambda t: qG.wait(t, w_ld))
            emit_qkv_group(0, False, kc[1], vc[1], -1)

            spx_mark = {}
            xat_done = 0
            for c in range(NCH):
                def _cl(a, c=c):
                    if c >= 1:
                        a.wait_ge(cVE.sem, xat_done)
                    if c == 0:
                        qC.dma(a, out=xi32[:, :, 1:CH + 1],
                                           in_=xin_sp[:, :, 0:CH])
                    else:
                        qC.dma(a, out=xi32[:, :, 0:CH + 1],
                                           in_=xin_sp[:, :, c * CH - 1:(c + 1) * CH])
                ac(_cl)
                cld = qC.n
                def _cc(v, c=c, cld=cld):
                    qC.wait(v, cld)
                    if c == 0:
                        cVE.inc(v.tensor_copy(xi32[:, :, 0:1],
                                              sideL2[:] if is_c2 else sideL1[:]))
                        v.wait_ge(cVE.sem, cVE.n)
                    cVE.inc(v.tensor_copy(xib[:], xi32[:]))
                ve(_cc)
                ccn = cVE.n
                te(lambda t, ccn=ccn: t.wait_ge(cVE.sem, ccn))
                emit_qkv_group(4 + 4 * c, True, kc[c % 2], vc[c % 2], c)
                kv_ready = cAC.n

                if sub < 2:
                    continue
                mask = mask_c0S if c == 0 else mask_nS
                sim_mark = {}
                sm_mark = {}
                e1_mark = {}
                nr_mark = {}
                tr_mark = {}
                pte_mark = {}
                pv_done = {}
                pve_mark = {}

                def emit_sim(kk, c=c, mask=mask):
                    hp, j = kk // 8, kk % 8
                    qs_, hh = j // 2, j % 2
                    slot = S1024[kk % 3]
                    def _sim(t, kk=kk, hp=hp, qs_=qs_, hh=hh, slot=slot,
                             kv_ready=kv_ready):
                        _gw(t, slot_g.get(kk % 3))
                        if kk < 3:
                            t.wait_ge(cAC.sem, kv_ready)
                        tp = (64 * hh, 0) if hh else None
                        t.matmul(slot[:, 0:512], id_bf[:],
                                 mask[:, qs_, 0:512], start=True, stop=False)
                        t.matmul(slot[:, 512:1024], id_bf[:],
                                 mask[:, qs_, 512:1024], start=True, stop=False)
                        base = 64 * hh
                        lhs = qfm[base:base + 64, hp,
                                  qs_ * 128:qs_ * 128 + 128]
                        t.matmul(slot[:, 0:512], lhs,
                                 kc[(c + 1) % 2][base:base + 64, hp, :],
                                 start=False, stop=True, tile_position=tp)
                        cTE.inc(t.matmul(slot[:, 512:1024], lhs,
                                         kc[c % 2][base:base + 64, hp, :],
                                         start=False, stop=True,
                                         tile_position=tp))
                    te(_sim)
                    sim_mark[kk] = cTE.n

                def emit_sm(kk):
                    slot = S1024[kk % 3]
                    s4 = 4 * (kk % 2)
                    def _sm(v, kk=kk, slot=slot, s4=s4):
                        v.wait_ge(cTE.sem, sim_mark[kk])
                        if kk - 2 in e1_mark:
                            v.wait_ge(cAC.sem, e1_mark[kk - 2])
                        cVE.inc(v.tensor_reduce(smax[:, s4:s4 + 1], slot, Ax.X,
                                                Alu.max, negate=True))
                    ve(_sm)
                    sm_mark[kk] = cVE.n

                def emit_e1_pair(kk):
                    def _e1(a, kk=kk):
                        s4 = 4 * (kk % 2)
                        a.wait_ge(cVE.sem, sm_mark[kk])
                        if kk - 2 in nr_mark:
                            a.wait_ge(cVE.sem, nr_mark[kk - 2])
                        cAC.inc(a.activation(probs[kk % 2][:],
                                             S1024[kk % 3], Act.Exp,
                                             bias=smax[:, s4:s4 + 1],
                                             scale=1.0,
                                             accum_out=smax[:, s4 + 1:s4 + 2]))
                        e1_mark[kk] = cAC.n
                        slot_g[kk % 3] = (cAC, cAC.n)
                    ac(_e1)

                def emit_nr(kk):
                    s4 = 4 * (kk % 2)
                    def _nr(v, kk=kk, s4=s4):
                        v.wait_ge(cAC.sem, e1_mark[kk])
                        if kk - 2 in tr_mark:
                            v.wait_ge(cTE.sem, tr_mark[kk - 2])
                        cVE.inc(v.reciprocal(smax[:, s4 + 2:s4 + 3],
                                             smax[:, s4 + 1:s4 + 2]))
                        v.wait_ge(cVE.sem, cVE.n)
                        cVE.inc(v.tensor_scalar(probsN[kk % 2][:],
                                                probs[kk % 2][:],
                                                smax[:, s4 + 2:s4 + 3],
                                                None, Alu.mult))
                    ve(_nr)
                    nr_mark[kk] = cVE.n

                def emit_tr(kk):
                    def _tr(t, kk=kk):
                        t.wait_ge(cVE.sem, nr_mark[kk])
                        _gw(t, psT_g[0])
                        ii = None
                        for j in range(8):
                            ii = t.transpose(psT[:, 128 * j:128 * j + 128],
                                             probsN[kk % 2][:, 128 * j:128 * j + 128],
                                             id_bf[:])
                        cTE.inc(ii)
                    te(_tr)
                    tr_mark[kk] = cTE.n

                def emit_pte(kk):
                    hp, j = kk // 8, kk % 8
                    qs_, hh = j // 2, j % 2
                    def _pte(v, kk=kk, hp=hp, qs_=qs_, hh=hh):
                        v.wait_ge(cTE.sem, tr_mark[kk])
                        if (hp - 1, hh) in pv_done:
                            v.wait_ge(cTE.sem, pv_done[(hp - 1, hh)])
                        cVE.inc(v.tensor_copy(
                            pT.ap()[:, hh, :, qs_ * 128:qs_ * 128 + 128],
                            psT3()))
                    ve(_pte)
                    pte_mark[kk] = cVE.n
                    psT_g[0] = (cVE, cVE.n)

                def emit_pv(hp, c=c):
                    for hh in range(2):
                        h = 2 * hp + hh
                        def _pv(t, h=h, hh=hh, hp=hp):
                            t.wait_ge(cVE.sem, pte_mark[8 * hp + 7])
                            _gw(t, psR_g[0])
                            ii = None
                            for kp in range(8):
                                vcb = vc[(c + 1) % 2] if kp < 4 else vc[c % 2]
                                ii = t.matmul(psR[0:64, :],
                                              vcb[:, kp % 4, 64 * h:64 * h + 64],
                                              pT[:, hh, kp, :],
                                              start=(kp == 0), stop=(kp == 7))
                            cTE.inc(ii)
                        te(_pv)
                        pv_done[(hp, hh)] = cTE.n
                        pvn = cTE.n
                        def _pve(v, hp=hp, hh=hh, pvn=pvn):
                            v.wait_ge(cTE.sem, pvn)
                            cVE.inc(v.tensor_copy(attnfm[64 * hh:64 * hh + 64, hp, :],
                                                  psR[0:64, :]))
                        ve(_pve)
                        pve_mark[(hp, hh)] = cVE.n
                        psR_g[0] = (cVE, cVE.n)

                if sub < 3:
                    for kk in range(32):
                        emit_sim(kk)
                    continue
                for kk in range(35):
                    if kk < 32:
                        emit_sim(kk)
                    if 0 <= kk - 1 < 32:
                        emit_sm(kk - 1)
                        emit_e1_pair(kk - 1)
                    if 0 <= kk - 3 < 32:
                        emit_pte(kk - 3)
                        if sub >= 4 and (kk - 3) % 8 == 7:
                            emit_pv((kk - 3) // 8)
                    if 0 <= kk - 2 < 32:
                        emit_nr(kk - 2)
                        emit_tr(kk - 2)

                af = cVE.n
                te(lambda t, af=af: t.wait_ge(cVE.sem, af))
                woe_mark = {}
                for dt in range(DC if sub >= 4.7 else 0):
                    slot = S512[dt % 6]
                    def _wo(t, dt=dt, slot=slot):
                        if dt >= 6:
                            t.wait_ge(cVE.sem, woe_mark[dt - 6])
                        ii = None
                        for ic in range(4):
                            ii = t.matmul(slot, woS[:, ic, dt * 128:dt * 128 + 128],
                                          attnfm[:, ic, :],
                                          start=(ic == 0), stop=(ic == 3))
                        cTE.inc(ii)
                    te(_wo)
                    won = cTE.n
                    def _woe(v, dt=dt, slot=slot, won=won, c=c):
                        v.wait_ge(cTE.sem, won)
                        if c >= 1:
                            qS.wait(v, spx_mark[c - 1])
                        cVE.inc(v.tensor_tensor(xatc[:, dt, :], slot,
                                                xi32[:, dt, 1:CH + 1], Alu.add))
                    ve(_woe)
                    woe_mark[dt] = cVE.n
                xat_done = cVE.n
                if sub < 5.5:
                    spx_mark[c] = qS.n
                    continue
                def _spx(s, c=c, t0=xat_done):
                    s.wait_ge(cVE.sem, t0)
                    qS.dma(s, out=sp_xat[:, :, c * CH:(c + 1) * CH],
                                       in_=xatc[:])
                sy(_spx)
                spx_mark[c] = qS.n
                if c == 0 and l == 0 and dbg_attn:
                    def _tapa(s):
                        s.wait_ge(cVE.sem, xat_done)
                        qS.dma(s, out=dbg_attn["qf0"][:], in_=qfm[:])
                        qS.dma(s, out=dbg_attn["kc0"][:], in_=kc[0][:])
                        qS.dma(s, out=dbg_attn["vc0"][:], in_=vc[0][:])
                        qS.dma(s, out=dbg_attn["af0"][:], in_=attnfm[:])
                    sy(_tapa)
                    _tm = qS.n
                    ve(lambda v, m=_tm: qS.wait(v, m))

            if sub < 6:
                barrier()
                lx.close()
                return cGP.n
            def _cav(v):
                v.wait_ge(cVE.sem, cVE.n)
                cVE.inc(v.tensor_scalar(
                    scA.ap().unsqueeze(2), xatc[:, :, CH - 1:CH],
                    sel_first[:], None, Alu.mult))
            ve(_cav)
            cav = cVE.n
            def _cag(g, l=l):
                g.wait_ge(cVE.sem, cav)
                qG.dma(g, out=cca_in[l][:], in_=scA[:])
                qG.wait(g, qG.n)
                cCC.inc(g.collective_compute("AllReduce", Alu.add,
                                             replica_groups=RG,
                                             ins=[cca_in[l].opt()],
                                             outs=[cca_out[l].opt()]))
            gp(_cag)
            cca_done = cCC.n
            barrier()
            lx.close()
            return cca_done

        def emit_mlp(l, is_c2, cca_done, fire_cc2):
            cc2_mark = {0: 0}
            bi_nonlocal = [0]
            mx = ExitStack()
            blkb = sb(mx, f"blkb{l}", (128, DC, 1025))
            hnorm = sb(mx, f"hnorm{l}", (128, DC, 1024), BF16)
            h2S = sb(mx, f"h2S{l}", (128, NBK, 1024), BF16)
            gel = sb(mx, f"gel{l}", (128, 1024))
            gb2 = [gel, sb(mx, f"gelb{l}", (128, 1024))]
            xsqm = sb(mx, f"xsqm{l}", (128, 1024))
            w1b = [sb(mx, f"w1b{l}_{i}", (128, DC, 256), BF16) for i in range(2)]
            w2b = [sb(mx, f"w2b{l}_{i}", (128, NBK, 128), BF16) for i in range(2)]
            srow = sb(mx, f"srow{l}", (1, 1024))
            sxrow = sb(mx, f"sxrow{l}", (1, 1024))
            murow = sb(mx, f"murow{l}", (1, 1024))
            rstrow = sb(mx, f"rstrow{l}", (1, 1024))
            ccb = sb(mx, f"ccb{l}", (128, DC, HALO)) if fire_cc2 else None
            gbuf = gel if is_c2 else None
            sbuf_s = sb(mx, f"sbuf_s{l}", (128, DC, 1024)) if is_c2 else None
            xfin = sb(mx, f"xfin{l}", (128, 4, D)) if is_c2 else None

            w1r = h_[f"w1p{l}"].rearrange("(c p) f -> c p f", p=128)
            w2r = h_[f"w2p{l}"].rearrange("(c p) f -> c p f", p=128)

            for bi, blk in enumerate([1, 0]):
                bi_nonlocal[0] = bi
                base = blk * 1024
                ve_now, te_now, sy_now = cVE.n, cTE.n, qS.n
                def _bl(a, blk=blk, base=base, ve_now=ve_now, te_now=te_now,
                        sy_now=sy_now):
                    if bi_nonlocal[0] == 1:
                        a.wait_ge(cVE.sem, ve_now)
                        a.wait_ge(cTE.sem, te_now)
                        qS.wait(a, sy_now)
                    if blk == 1:
                        qC.dma(a, out=blkb[:, :, 0:1025],
                                           in_=sp_xat[:, :, base - 1:base + 1024])
                    else:
                        qC.dma(a, out=blkb[:, :, 1:1025],
                                           in_=sp_xat[:, :, 0:1024])
                        a.wait_ge(cCC.sem, cca_done)
                        qC.dma(a, out=scA[:], in_=cca_out[l][:])
                ac(_bl)
                bld = qC.n
                def _bv(v, blk=blk, bld=bld):
                    qC.wait(v, bld)
                    if blk == 0:
                        cVE.inc(v.tensor_scalar(blkb[:, :, 0:1],
                                                scA.ap().unsqueeze(2),
                                                sel_second[:], None, Alu.mult))
                    else:
                        cVE.inc(v.tensor_copy(bsc[:, 0:1], bsc[:, 3:4]))
                ve(_bv)
                blv = cVE.n
                if dbg_attn and l == 0 and blk == 0:
                    def _tapb(s):
                        s.wait_ge(cVE.sem, blv)
                        qS.dma(s, out=dbg_attn["blkb0"][:], in_=blkb[:])
                    sy(_tapb)
                    _tm = qS.n
                    ve(lambda v, m=_tm: qS.wait(v, m))

                def shview(dc, h0, n):
                    off = (1 if dc < 4 else 0) + h0
                    return blkb[:, dc, off:off + n]

                te(lambda t, blv=blv: t.wait_ge(cVE.sem, blv))
                ac(lambda a, blv=blv: a.wait_ge(cVE.sem, blv))
                for half in range(2):
                    def _sx(t, half=half):
                        ii = None
                        for dc in range(DC):
                            ii = t.matmul(psR[0:1, :], ones_col[:],
                                          shview(dc, 512 * half, 512),
                                          start=(dc == 0), stop=(dc == DC - 1))
                        cTE.inc(ii)
                    te(_sx)
                    sxn = cTE.n
                    def _sxe(v, half=half, sxn=sxn):
                        v.wait_ge(cTE.sem, sxn)
                        cVE.inc(v.tensor_copy(sxrow[0:1, 512 * half:512 * half + 512],
                                              psR[0:1, :]))
                    ve(_sxe)
                    sxe = cVE.n
                    te(lambda t, sxe=sxe: t.wait_ge(cVE.sem, sxe))
                qm_mark = {}
                for half in range(2):
                    for dc in range(DC):
                        k = half * DC + dc
                        xq = xsqm[:, 512 * (k % 2):512 * (k % 2) + 512]
                        def _q(a, dc=dc, half=half, k=k, xq=xq):
                            if k >= 2:
                                a.wait_ge(cTE.sem, qm_mark[k - 2])
                            cAC.inc(a.activation(xq, shview(dc, 512 * half, 512),
                                                 Act.Square))
                        ac(_q)
                        qn = cAC.n
                        def _qm(t, dc=dc, qn=qn, xq=xq):
                            t.wait_ge(cAC.sem, qn)
                            cTE.inc(t.matmul(psR[0:1, :], ones_col[:],
                                             xq, start=(dc == 0),
                                             stop=(dc == DC - 1)))
                        te(_qm)
                        qm_mark[k] = cTE.n
                    qmn = cTE.n
                    def _qe(v, half=half, qmn=qmn):
                        v.wait_ge(cTE.sem, qmn)
                        cVE.inc(v.tensor_copy(srow[0:1, 512 * half:512 * half + 512],
                                              psR[0:1, :]))
                    ve(_qe)
                    qe = cVE.n
                    te(lambda t, qe=qe: t.wait_ge(cVE.sem, qe))
                stn = cVE.n
                def _st(v):
                    v.wait_ge(cVE.sem, cVE.n)
                    cVE.inc(v.tensor_scalar(murow[:], sxrow[:], R1024, None,
                                            Alu.mult))
                    v.wait_ge(cVE.sem, cVE.n)
                    cVE.inc(v.tensor_tensor(rstrow[:], murow[:], murow[:],
                                            Alu.mult))
                    v.wait_ge(cVE.sem, cVE.n)
                    cVE.inc(v.scalar_tensor_tensor(rstrow[:], srow[:], R1024,
                                                   rstrow[:], Alu.mult,
                                                   Alu.subtract))
                ve(_st)
                stv = cVE.n
                def _sa(a):
                    a.wait_ge(cVE.sem, stv)
                    cAC.inc(a.activation(rstrow[:], rstrow[:], Act.Sqrt,
                                         bias=eps_col[0:1, :], scale=1.0))
                ac(_sa)
                san = cAC.n
                def _sr(v):
                    v.wait_ge(cAC.sem, san)
                    cVE.inc(v.reciprocal(rstrow[:], rstrow[:]))
                ve(_sr)
                srn = cVE.n
                if dbg_attn and l == 0 and blk == 0:
                    def _taps2(s):
                        s.wait_ge(cVE.sem, srn)
                        qS.dma(s, out=dbg_attn["rows0"][0:1, :], in_=sxrow[:])
                        qS.dma(s, out=dbg_attn["rows0"][1:2, :], in_=srow[:])
                        qS.dma(s, out=dbg_attn["rows0"][2:3, :], in_=murow[:])
                        qS.dma(s, out=dbg_attn["rows0"][3:4, :], in_=rstrow[:])
                    sy(_taps2)
                    _tm = qS.n
                    ve(lambda v, m=_tm: qS.wait(v, m))
                for half in range(2):
                    def _bc(t, half=half, srn=srn):
                        t.wait_ge(cVE.sem, srn)
                        t.matmul(psR[:, :], ones_row[:],
                                 murow[0:1, 512 * half:512 * half + 512],
                                 start=True, stop=True)
                        cTE.inc(t.matmul(psA[:, 0:512], ones_row[:],
                                         rstrow[0:1, 512 * half:512 * half + 512],
                                         start=True, stop=True))
                    te(_bc)
                    bcn = cTE.n
                    for dc in range(DC):
                        def _ap(v, dc=dc, half=half, bcn=bcn):
                            v.wait_ge(cTE.sem, bcn)
                            cVE.inc(v.tensor_tensor(gel[:, 0:512],
                                                    shview(dc, 512 * half, 512),
                                                    psR[:], Alu.subtract))
                            v.wait_ge(cVE.sem, cVE.n)
                            cVE.inc(v.scalar_tensor_tensor(
                                gel[:, 512:1024], gel[:, 0:512],
                                lvec[f"lng{l}"][:, dc:dc + 1],
                                psA[:, 0:512], Alu.mult, Alu.mult))
                            v.wait_ge(cVE.sem, cVE.n)
                            cVE.inc(v.tensor_scalar(
                                hnorm[:, dc, 512 * half:512 * half + 512],
                                gel[:, 512:1024], lvec[f"lnb{l}"][:, dc:dc + 1],
                                None, Alu.add))
                        ve(_ap)
                    apn = cVE.n
                    te(lambda t, apn=apn: t.wait_ge(cVE.sem, apn))
                hn_done = cVE.n

                def _w1l(g, i=0):
                    qG.dma(g, out=w1b[0][:],
                                       in_=w1r[:, :, 0:256].transpose([1, 0, 2]))
                gp(_w1l)
                w1_mark = {0: qG.n}
                h2_mark = {}
                gl_mark = {}
                ama_mark = {}
                w1mm_mark = {}
                for i in range(NBK):
                    if i + 1 < NBK:
                        def _w1n(g, i=i):
                            if i >= 1:
                                g.wait_ge(cTE.sem, w1mm_mark[i - 1])
                            qG.dma(g,
                                out=w1b[(i + 1) % 2][:],
                                in_=w1r[:, :, 256 * (i + 1):256 * (i + 1) + 256]
                                    .transpose([1, 0, 2]))
                        gp(_w1n)
                        w1_mark[i + 1] = qG.n
                    sa = S1024[(2 * i) % 3]
                    sg_ = S1024[(2 * i + 1) % 3]
                    def _ama(t, i=i, sa=sa, hn_done=hn_done):
                        qG.wait(t, w1_mark[i])
                        if i == 0:
                            t.wait_ge(cVE.sem, hn_done)
                        if i >= 2:
                            t.wait_ge(cAC.sem, gl_mark[i - 2])
                        ii = None
                        for half in range(2):
                            for dc in range(DC):
                                ii = t.matmul(sa[:, 512 * half:512 * half + 512],
                                              w1b[i % 2][:, dc, 0:128],
                                              hnorm[:, dc, 512 * half:512 * half + 512],
                                              start=(dc == 0), stop=(dc == DC - 1))
                        cTE.inc(ii)
                    te(_ama)
                    ama_mark[i] = cTE.n
                    def _amg(t, i=i, sg_=sg_):
                        if i >= 1:
                            t.wait_ge(cVE.sem, h2_mark[i - 1])
                        ii = None
                        for half in range(2):
                            for dc in range(DC):
                                ii = t.matmul(sg_[:, 512 * half:512 * half + 512],
                                              w1b[i % 2][:, dc, 128:256],
                                              hnorm[:, dc, 512 * half:512 * half + 512],
                                              start=(dc == 0), stop=(dc == DC - 1))
                        cTE.inc(ii)
                    te(_amg)
                    w1mm_mark[i] = cTE.n
                    amn = cTE.n
                    def _gl(a, i=i, amn=amn, sg_=sg_):
                        a.wait_ge(cTE.sem, amn)
                        if i >= 2:
                            a.wait_ge(cVE.sem, h2_mark[i - 2])
                        cAC.inc(a.activation(gb2[i % 2][:], sg_, Act.Gelu,
                                             bias=lvec[f"b1p{l}"][:, 2 * i + 1:2 * i + 2],
                                             scale=1.0))
                    ac(_gl)
                    gl_mark[i] = cAC.n
                    gln = cAC.n
                    def _h2(v, i=i, gln=gln, sa=sa):
                        v.wait_ge(cAC.sem, gln)
                        cVE.inc(v.scalar_tensor_tensor(
                            h2S[:, i, :], sa,
                            lvec[f"b1p{l}"][:, 2 * i:2 * i + 1], gb2[i % 2][:],
                            Alu.add, Alu.mult))
                    ve(_h2)
                    h2_mark[i] = cVE.n
                h2_done = cVE.n
                if dbg_attn and l == 0 and blk == 0:
                    def _tapm(s):
                        s.wait_ge(cVE.sem, h2_done)
                        qS.dma(s, out=dbg_attn["hn0"][:], in_=hnorm[:])
                        qS.dma(s, out=dbg_attn["h20"][:], in_=h2S[:])
                    sy(_tapm)
                    _tm = qS.n
                    ve(lambda v, m=_tm: qS.wait(v, m))

                def _w2l(g):
                    qG.dma(g, out=w2b[0][:],
                                       in_=w2r[:, :, 0:128].transpose([1, 0, 2]))
                gp(_w2l)
                w2_mark = {0: qG.n}
                w2e_mark = {}
                for dt in range(DC):
                    if dt + 1 < DC:
                        def _w2n(g, dt=dt):
                            if dt >= 1:
                                g.wait_ge(cTE.sem, w2mm_mark[dt - 1])
                            qG.dma(g, 
                                out=w2b[(dt + 1) % 2][:],
                                in_=w2r[:, :, 128 * (dt + 1):128 * (dt + 1) + 128]
                                    .transpose([1, 0, 2]))
                        if dt == 0:
                            w2mm_mark = {}
                        gp(_w2n)
                        w2_mark[dt + 1] = qG.n
                    slot = S1024[dt % 3]
                    def _wm(t, dt=dt, slot=slot, h2_done=h2_done):
                        qG.wait(t, w2_mark[dt])
                        if dt == 0:
                            t.wait_ge(cVE.sem, h2_done)
                        if dt >= 3:
                            t.wait_ge(cVE.sem, w2e_mark[dt - 3])
                        ii = None
                        for half in range(2):
                            for hb in range(NBK):
                                ii = t.matmul(slot[:, 512 * half:512 * half + 512],
                                              w2b[dt % 2][:, hb, :],
                                              h2S[:, hb, 512 * half:512 * half + 512],
                                              start=(hb == 0), stop=(hb == NBK - 1))
                        cTE.inc(ii)
                    te(_wm)
                    w2mm_mark[dt] = cTE.n
                    wmn = cTE.n
                    def _we(v, dt=dt, slot=slot, wmn=wmn):
                        v.wait_ge(cTE.sem, wmn)
                        cVE.inc(v.scalar_tensor_tensor(
                            blkb[:, dt, 1:1025], slot,
                            lvec[f"b2{l}"][:, dt:dt + 1],
                            blkb[:, dt, 1:1025],
                            Alu.add, Alu.add))
                    ve(_we)
                    w2e_mark[dt] = cVE.n
                blk_done = cVE.n

                if not is_c2:
                    def _spb(s, base=base, blk_done=blk_done):
                        s.wait_ge(cVE.sem, blk_done)
                        qS.dma(s, out=sp_xl1[:, :, base:base + 1024],
                                           in_=blkb[:, :, 1:1025])
                    sy(_spb)
                    if fire_cc2 and blk == 1:
                        def _cc2v(v):
                            v.wait_ge(cVE.sem, cVE.n)
                            ii = None
                            for dc in range(DC):
                                ii = v.tensor_scalar(
                                    ccb[:, dc, :],
                                    blkb[:, dc, 1025 - HALO:1025],
                                    sel_first[:], None, Alu.mult)
                            cVE.inc(ii)
                        ve(_cc2v)
                        c2v = cVE.n
                        def _cc2g(g):
                            g.wait_ge(cVE.sem, c2v)
                            qG.dma(g, out=cc2_in[:], in_=ccb[:])
                            qG.wait(g, qG.n)
                            cCC.inc(g.collective_compute(
                                "AllReduce", Alu.add, replica_groups=RG,
                                ins=[cc2_in.opt()], outs=[cc2_out.opt()]))
                        gp(_cc2g)
                        cc2_mark[0] = cCC.n
                else:
                    def _sl(a, base=base, blk_done=blk_done):
                        a.wait_ge(cVE.sem, blk_done)
                        qC.dma(a, out=sbuf_s[:],
                                           in_=sp_s[:, :, base:base + 1024])
                    ac(_sl)
                    sld = qC.n
                    fin_mark = {}
                    for dc in range(DC):
                        def _gs(a, dc=dc, blk_done=blk_done):
                            a.wait_ge(cVE.sem, blk_done)
                            if dc >= 1:
                                a.wait_ge(cVE.sem, fin_mark[dc - 1])
                            cAC.inc(a.activation(gbuf[:],
                                                 blkb[:, dc, 1:1025],
                                                 Act.Silu))
                        ac(_gs)
                        gsn = cAC.n
                        def _gm(v, dc=dc, gsn=gsn, sld=sld):
                            v.wait_ge(cAC.sem, gsn)
                            qC.wait(v, sld)
                            cVE.inc(v.tensor_tensor(sbuf_s[:, dc, :], sbuf_s[:, dc, :],
                                                    gbuf[:], Alu.mult))
                        ve(_gm)
                        fin_mark[dc] = cVE.n
                    sg_done = cVE.n
                    out_mark = {}
                    qs_prev = qS.n
                    for ts in range(8):
                        def _xl(a, ts=ts, base=base, qs_prev=qs_prev):
                            if ts >= 4:
                                qS.wait(a, out_mark[ts - 4])
                            else:
                                qS.wait(a, qs_prev)
                            qC.dma(a, 
                                out=xfin[:, ts % 4, :],
                                in_=xr_own[(base + ts * 128) // 128])
                        ac(_xl)
                        xln = qC.n
                        def _ft(t, ts=ts, sg_done=sg_done):
                            t.wait_ge(cVE.sem, sg_done)
                            ii = None
                            for dc in range(DC):
                                ii = t.transpose(psA[:, dc * 128:dc * 128 + 128],
                                                 sbuf_s[:, dc, ts * 128:ts * 128 + 128],
                                                 id_f32[:])
                            cTE.inc(ii)
                        te(_ft)
                        ftn = cTE.n
                        def _fe(v, ts=ts, ftn=ftn, xln=xln):
                            v.wait_ge(cTE.sem, ftn)
                            qC.wait(v, xln)
                            cVE.inc(v.tensor_tensor(
                                xfin.ap()[:, ts % 4, :].rearrange("p (c f) -> p c f", f=128),
                                psA3(), xfin.ap()[:, ts % 4, :]
                                .rearrange("p (c f) -> p c f", f=128), Alu.add))
                        ve(_fe)
                        fen = cVE.n
                        te(lambda t, fen=fen: t.wait_ge(cVE.sem, fen))
                        def _fo(s, ts=ts, base=base, fen=fen):
                            s.wait_ge(cVE.sem, fen)
                            qS.dma(s, 
                                out=out_h[base + ts * 128:base + ts * 128 + 128, :],
                                in_=xfin[:, ts % 4, :])
                        sy(_fo)
                        out_mark[ts] = qS.n
            barrier()
            mx.close()
            return cc2_mark[0]

        def emit_c1():
            cx = ExitStack()
            woutS = sb(cx, "woutS", (128, IC, D), BF16)
            rbuf = [sb(cx, f"rbuf{i}", (128, 5, CH)) for i in range(2)]
            fused2 = [sb(cx, f"fusedC{i}", (128, IC, CH), BF16) for i in range(2)]
            xnr2 = [sb(cx, f"xnrC{i}", (128, DC, CH)) for i in range(2)]
            sres = sb(cx, "sresC", (128, DC, CH))

            def _wg(g):
                wv = h_["w_out"].rearrange("(c p) f -> c p f", p=128)
                qG.dma(g, out=woutS[:], in_=wv.transpose([1, 0, 2]))
            gp(_wg)
            wg = qG.n
            def _r12(v):
                v.wait_ge(cCC.sem, cc1_done)
            ve(_r12)
            def _r12l(a):
                a.wait_ge(cCC.sem, cc1_done)
                qC.dma(a, out=R1e[:], in_=cc1_out[:, 0:IC])
                qC.dma(a, out=R2e[:], in_=cc1_out[:, IC:2 * IC])
            ac(_r12l)
            rld = qC.n
            def _r12v(v):
                qC.wait(v, rld)
                v.tensor_scalar(R1e[:], R1e[:], sel_second[:], None, Alu.mult)
                cVE.inc(v.tensor_scalar(R2e[:], R2e[:], sel_first[:], None, Alu.mult))
            ve(_r12v)
            r12 = cVE.n

            fuse_mark = {}
            spill_s_mark = {}
            we_mark = {}
            wm_done = {}
            we_done = {}
            xr_mark = {}

            def emit_ld_fu(c, ct):
                def _ld(a, c=c, ct=ct):
                    gk = (c, ct - 2) if ct >= 2 else (c - 1, ct + 14)
                    if gk in fuse_mark:
                        a.wait_ge(cVE.sem, fuse_mark[gk])
                    sl = slice(c * CH, (c + 1) * CH)
                    bf_ = rbuf[ct % 2]
                    qC.dma(a, out=bf_[:, 0, :], in_=sp_fw[:, ct, sl])
                    qC.dma(a, out=bf_[:, 1, :], in_=sp_pf[:, ct, sl])
                    qC.dma(a, out=bf_[:, 2, :], in_=sp_bw[:, ct, sl])
                    qC.dma(a, out=bf_[:, 3, :], in_=sp_pb[:, ct, sl])
                    qC.dma(a, out=bf_[:, 4, :], in_=sp_sg[:, ct, sl])
                ac(_ld)
                ldn = qC.n
                def _fu(v, c=c, ct=ct, ldn=ldn):
                    qC.wait(v, ldn)
                    if ct == 0 and c - 2 in wm_done:
                        v.wait_ge(cTE.sem, wm_done[c - 2])
                    if (c, ct) == (0, 0):
                        v.wait_ge(cVE.sem, r12)
                    bf_ = rbuf[ct % 2]
                    v.scalar_tensor_tensor(bf_[:, 0, :], bf_[:, 1, :],
                                           R1e[:, ct:ct + 1], bf_[:, 0, :],
                                           Alu.mult, Alu.add)
                    cVE.inc(v.scalar_tensor_tensor(bf_[:, 2, :], bf_[:, 3, :],
                                                   R2e[:, ct:ct + 1],
                                                   bf_[:, 2, :],
                                                   Alu.mult, Alu.add))
                    v.wait_ge(cVE.sem, cVE.n)
                    cVE.inc(v.tensor_tensor(bf_[:, 0, :], bf_[:, 0, :],
                                            bf_[:, 2, :], Alu.add))
                    v.wait_ge(cVE.sem, cVE.n)
                    cVE.inc(v.tensor_tensor(fused2[c % 2][:, ct, :], bf_[:, 0, :],
                                            bf_[:, 4, :], Alu.mult))
                ve(_fu)
                fuse_mark[(c, ct)] = cVE.n

            def emit_xr(c):
                def _xr(a, c=c):
                    if c - 2 in we_done:
                        a.wait_ge(cVE.sem, we_done[c - 2])
                    qC.dma(a, out=xnr2[c % 2][:],
                           in_=sp_xn[:, :, c * CH:(c + 1) * CH])
                ac(_xr)
                xr_mark[c] = qC.n

            def emit_wout_dt(c, dt):
                slot = S512[dt % 6]
                def _wm(t, c=c, dt=dt, slot=slot, wg=wg):
                    if (c, dt) == (0, 0):
                        qG.wait(t, wg)
                    t.wait_ge(cVE.sem, fuse_mark[(c, IC - 1)])
                    if dt >= 6:
                        t.wait_ge(cVE.sem, we_mark[(c, dt - 6)])
                    elif c >= 1:
                        t.wait_ge(cVE.sem, we_done[c - 1])
                    ii = None
                    for ct in range(IC):
                        ii = t.matmul(slot, woutS[:, ct, dt * 128:dt * 128 + 128],
                                      fused2[c % 2][:, ct, :], start=(ct == 0),
                                      stop=(ct == IC - 1))
                    cTE.inc(ii)
                te(_wm)
                wmn = cTE.n
                def _we(v, c=c, dt=dt, slot=slot, wmn=wmn):
                    v.wait_ge(cTE.sem, wmn)
                    qC.wait(v, xr_mark[c])
                    if c >= 1:
                        qS.wait(v, spill_s_mark[c - 1])
                    cVE.inc(v.scalar_tensor_tensor(sres[:, dt, :], slot,
                                                   b_out[:, dt:dt + 1],
                                                   xnr2[c % 2][:, dt, :],
                                                   Alu.add, Alu.add))
                ve(_we)
                we_mark[(c, dt)] = cVE.n
                if dt == DC - 1:
                    we_done[c] = cVE.n
                    wm_done[c] = cTE.n
                    sdone = cVE.n
                    def _ssp(s, c=c, sdone=sdone):
                        s.wait_ge(cVE.sem, sdone)
                        qS.dma(s, out=sp_s[:, :, c * CH:(c + 1) * CH],
                               in_=sres[:])
                    sy(_ssp)
                    spill_s_mark[c] = qS.n

            for c in range(NCH):
                for ct in range(IC):
                    emit_ld_fu(c, ct)
                    if ct == 0:
                        emit_xr(c)
                    if c >= 1 and ct % 2 == 1:
                        emit_wout_dt(c - 1, ct // 2)
            for dt in range(DC):
                emit_wout_dt(NCH - 1, dt)
            barrier()
            cx.close()

        xr_own = h_["x_own"].rearrange("(a p) d -> a p d", p=128)

        _sub = {"l1s1": 1, "l1s2": 2, "l1s3": 3, "l1s4": 4,
                "l1s4a": 4.7, "l1s5": 5.5}.get(upto, 6)
        cca0 = emit_attn(0, sp_xn, False, sub=_sub)
        if upto in ("l1s1", "l1s2", "l1s3", "l1s4", "l1s4a", "l1s5"):
            _endprog()
            return nc
        if dbg_attn:
            def _tapx(g):
                qG.dma(g, out=dbg_attn["xat1"][:], in_=sp_xat[:])
                qG.wait(g, qG.n)
            gp(_tapx)
        if upto == "l1a":
            _endprog()
            return nc
        cc2m = emit_mlp(0, False, cca0, True)
        if upto == "l1m":
            _endprog()
            return nc
        emit_c1()
        if upto == "c1":
            _endprog()
            return nc
        def _sd2(a):
            a.wait_ge(cCC.sem, cc2m)
            qC.dma(a, out=sideL2[:], in_=cc2_out[:, :, 515:516],
                               allow_slow_non_contiguous=True)
        ac(_sd2)
        sd2 = qC.n
        def _sd2v(v):
            qC.wait(v, sd2)
            cVE.inc(v.tensor_scalar(sideL2[:], sideL2[:], sel_second[:],
                                    None, Alu.mult))
        ve(_sd2v)
        cca1 = emit_attn(1, sp_xl1, True, cc2_wait=cc2m)
        if upto == "l2a":
            _endprog()
            return nc
        emit_mlp(1, True, cca1, False)

        if dbg_h:
            def _tap(g):
                for nm, (dst, src) in dbg_h.items():
                    qG.dma(g, out=dst[:], in_=src[:])
                qG.wait(g, qG.n)
            gp(_tap)

        barrier()
        def _fin(s):
            qS.wait(s, qS.n)
        sy(_fin)

    return nc


_NC_CACHE = {}


def kernel(**inputs):
    maps = prepare_inputs(inputs)
    key = "main"
    if key not in _NC_CACHE:
        _NC_CACHE[key] = build_nc()
    nc = _NC_CACHE[key]
    res = run_bass_kernel_spmd(nc, maps, core_ids=list(range(8)))
    out = np.empty((4, 4096, 1024), np.float32)
    for c in range(8):
        b, h = c // 2, c % 2
        out[b, h * T:(h + 1) * T] = res.results[c]["out"]
    return out


# version 17 (final)
# speedup vs baseline: 1.1295x; 1.0168x over previous
import numpy as np
import ml_dtypes
from contextlib import ExitStack

import concourse.bass as bass
import concourse.mybir as mybir
from concourse.bass_utils import run_bass_kernel_spmd

F32 = mybir.dt.float32
BF16 = mybir.dt.bfloat16
Alu = mybir.AluOpType
Act = mybir.ActivationFunctionType
Ax = mybir.AxisListType
BF = ml_dtypes.bfloat16

D, DC = 1024, 8
INNER, IC = 2048, 16
T, CH, NCH = 2048, 512, 4
NH, DH = 8, 64
HID, HIDP = 2730, 2816
F1P = 5632
NBK = 22
HALO = 516
EPS = 1e-5
NEG = -1.0e30
R1024 = 1.0 / 1024.0
RG = [[0, 1], [2, 3], [4, 5], [6, 7]]



def _feat_tiles(v):
    v = np.asarray(v, np.float32)
    return np.ascontiguousarray(v.reshape(v.shape[0] // 128, 128).T)


def _rotary_tables(t0):
    n = 4096
    pos = np.clip(np.arange(t0 - 512, t0 + 2048, dtype=np.float64), 0, n - 1)
    half = np.arange(0, DH, 2, dtype=np.float64)
    inv_freq = 1.0 / 10000.0 ** (half / DH)
    fr = pos[:, None] * inv_freq[None, :]
    fr = np.concatenate([fr, fr], axis=1)
    base = (half + 0.4 * DH) / (1.4 * DH)
    sc = base[None, :] ** ((pos - n // 2) / 256.0)[:, None]
    sc = np.concatenate([sc, sc], axis=1)
    sign = np.where(np.arange(DH) < 32, -1.0, 1.0)[None, :]
    tab = np.concatenate([np.cos(fr) * sc, np.sin(fr) * sc * sign,
                          np.cos(fr) / sc, np.sin(fr) / sc * sign], axis=1)
    tab = tab.reshape(20, 128, 256).transpose(1, 0, 2)
    hi = tab.astype(np.float32)
    lo = (tab - hi.astype(np.float64)).astype(np.float32)
    return np.ascontiguousarray(hi), np.ascontiguousarray(lo)


def _masks(first_half):
    i = np.arange(CH)[:, None]
    j = np.arange(1024)[None, :]
    mn = np.where((j >= i) & (j <= i + 512), 0.0, NEG)
    mc0 = np.where((j >= i) & (j <= i + 512) & (j >= 512), 0.0, NEG) \
        if first_half else mn
    lay = lambda m: np.ascontiguousarray(
        m.reshape(4, 128, 1024).transpose(1, 0, 2).astype(BF))
    return lay(mn), lay(mc0)


def _pad_w1(w1, b1):
    a = np.zeros((D, HIDP), np.float32); a[:, :HID] = w1[:, :HID]
    g = np.zeros((D, HIDP), np.float32); g[:, :HID] = w1[:, HID:]
    ba = np.zeros(HIDP, np.float32); ba[:HID] = b1[:HID]
    bg = np.zeros(HIDP, np.float32); bg[:HID] = b1[HID:]
    w1p = np.empty((D, F1P), np.float32)
    b1p = np.empty(F1P, np.float32)
    for i in range(NBK):
        w1p[:, 256 * i:256 * i + 128] = a[:, 128 * i:128 * i + 128]
        w1p[:, 256 * i + 128:256 * i + 256] = g[:, 128 * i:128 * i + 128]
        b1p[256 * i:256 * i + 128] = ba[128 * i:128 * i + 128]
        b1p[256 * i + 128:256 * i + 256] = bg[128 * i:128 * i + 128]
    return (np.ascontiguousarray(w1p),
            np.ascontiguousarray(b1p.reshape(F1P // 128, 128).T))


def prepare_inputs(inputs):
    x = np.asarray(inputs["L0_attn_x"])
    sh = {}
    sh["w_in"] = np.ascontiguousarray(np.asarray(inputs["L0_attn_W_in"], np.float32))
    sh["w_delta"] = np.ascontiguousarray(np.asarray(inputs["L0_attn_W_delta"], np.float32))
    sh["w_out"] = np.ascontiguousarray(np.asarray(inputs["L0_attn_W_out"], np.float32))
    sh["b_in_t"] = _feat_tiles(inputs["L0_attn_b_in"])
    cw = np.asarray(inputs["L0_attn_conv_w"], np.float32)
    sh["conv_w_t"] = np.ascontiguousarray(cw.reshape(IC, 128, 3).transpose(1, 0, 2))
    sh["conv_b_t"] = _feat_tiles(inputs["L0_attn_conv_b"])
    sh["b_delta_t"] = _feat_tiles(inputs["L0_attn_b_delta"])
    sh["b_out_t"] = _feat_tiles(inputs["L0_attn_b_out"])
    sh["mgam_t"] = _feat_tiles(inputs["L0_attn_ln_gamma"])
    sh["mbeta_t"] = _feat_tiles(inputs["L0_attn_ln_beta"])
    for l in range(2):
        sh[f"wqkv{l}"] = np.ascontiguousarray(np.asarray(inputs["L0_ffn_wqkv"][l], np.float32))
        sh[f"wo{l}"] = np.ascontiguousarray(np.asarray(inputs["L0_ffn_wo"][l], np.float32))
        w1p, b1t = _pad_w1(np.asarray(inputs["L0_ffn_w1"][l], np.float32),
                           np.asarray(inputs["L0_ffn_b1"][l], np.float32))
        sh[f"w1p{l}"], sh[f"b1p_t{l}"] = w1p, b1t
        w2p = np.zeros((HIDP, D), np.float32)
        w2p[:HID] = np.asarray(inputs["L0_ffn_w2"][l], np.float32)
        sh[f"w2p{l}"] = w2p
        sh[f"b2_t{l}"] = _feat_tiles(inputs["L0_ffn_b2"][l])
        sh[f"lng_t{l}"] = _feat_tiles(inputs["L0_ffn_ln_g"][l])
        sh[f"lnb_t{l}"] = _feat_tiles(inputs["L0_ffn_ln_b"][l])
        sh[f"qs{l}"] = np.ascontiguousarray(
            np.asarray(inputs["L0_ffn_q_scale"][l], np.float32).reshape(1, DH))
        sh[f"ks{l}"] = np.ascontiguousarray(
            np.asarray(inputs["L0_ffn_k_scale"][l], np.float32).reshape(1, DH))
    sh["id_bf"] = np.eye(128, dtype=BF)
    sh["id_f32"] = np.eye(128, dtype=np.float32)
    sh["ones_row"] = np.ones((1, 128), np.float32)
    sh["ones_col"] = np.ones((128, 1), np.float32)
    sh["eps_col"] = np.full((128, 1), EPS, np.float32)

    maps = []
    for c in range(8):
        b, h = c // 2, c % 2
        t0 = h * T
        d = dict(sh)
        d["x_own"] = np.ascontiguousarray(x[b, t0:t0 + T])
        hs = t0 - HALO if h == 1 else 0
        d["x_halo"] = np.ascontiguousarray(x[b, hs:hs + HALO])
        d["tabs_hi"], d["tabs_lo"] = _rotary_tables(t0)
        d["mask_n"], d["mask_c0"] = _masks(h == 0)
        one = np.ones((128, 1), np.float32)
        zero = np.zeros((128, 1), np.float32)
        d["sel_first"] = one if h == 0 else zero
        d["sel_second"] = zero if h == 0 else one
        maps.append(d)
    return maps



class Ctr:
    def __init__(self, sem, step=1):
        self.sem, self.n, self.step = sem, 0, step

    def inc(self, instr, k=None):
        k = self.step if k is None else k
        self.n += k
        instr.then_inc(self.sem, k)
        return self.n

    def dma(self, eng, **kw):
        eng.wait_ge(self.sem, self.n)
        self.n += 16
        eng.dma_start(**kw).then_inc(self.sem, 16)
        return self.n


class Q:
    def __init__(self, sems):
        self.sems = list(sems)
        self.cnt = [0] * len(self.sems)
        self.i = 0
        self.n = 0
        self.snap = {}

    def dma(self, eng, **kw):
        i = self.i
        self.i = (i + 1) % len(self.sems)
        if self.cnt[i]:
            eng.wait_ge(self.sems[i], self.cnt[i])
        self.cnt[i] += 16
        eng.dma_start(**kw).then_inc(self.sems[i], 16)
        self.n += 1
        self.snap[self.n] = tuple(self.cnt)
        return self.n

    def wait(self, eng, mark):
        if mark <= 0:
            return
        for s, c in zip(self.sems, self.snap[mark]):
            if c:
                eng.wait_ge(s, c)


def build_nc(debug=(), upto="full", detect=False):
    nc = bass.Bass("TRN2", target_bir_lowering=False,
                   detect_race_conditions=detect)
    dbg = set(debug)

    h_ = {}
    def din(name, shape, dt=F32):
        h_[name] = nc.dram_tensor(name, list(shape), dt, kind="ExternalInput").ap()

    din("x_own", (T, D)); din("x_halo", (HALO, D))
    din("w_in", (D, 2 * INNER)); din("w_delta", (INNER, INNER)); din("w_out", (INNER, D))
    din("b_in_t", (128, 32)); din("conv_w_t", (128, IC, 3)); din("conv_b_t", (128, IC))
    din("b_delta_t", (128, IC)); din("b_out_t", (128, DC))
    din("mgam_t", (128, DC)); din("mbeta_t", (128, DC))
    for l in range(2):
        din(f"wqkv{l}", (D, 1536)); din(f"wo{l}", (512, D))
        din(f"w1p{l}", (D, F1P)); din(f"b1p_t{l}", (128, F1P // 128))
        din(f"w2p{l}", (HIDP, D)); din(f"b2_t{l}", (128, DC))
        din(f"lng_t{l}", (128, DC)); din(f"lnb_t{l}", (128, DC))
        din(f"qs{l}", (1, DH)); din(f"ks{l}", (1, DH))
    din("tabs_hi", (128, 20, 256)); din("tabs_lo", (128, 20, 256))
    din("mask_n", (128, 4, 1024), BF16); din("mask_c0", (128, 4, 1024), BF16)
    din("id_bf", (128, 128), BF16); din("id_f32", (128, 128))
    din("ones_row", (1, 128)); din("ones_col", (128, 1)); din("eps_col", (128, 1))
    din("sel_first", (128, 1)); din("sel_second", (128, 1))

    out_h = nc.dram_tensor("out", [T, D], F32, kind="ExternalOutput").ap()

    def dint(name, shape, dt=F32):
        return nc.dram_tensor(name, list(shape), dt).ap()

    sp_xn = dint("sp_xn", (128, DC, T))
    sp_vs = dint("sp_vs", (128, IC, T))
    sp_sg = dint("sp_sg", (128, IC, T))
    sp_fw = dint("sp_fw", (128, IC, T))
    sp_pf = dint("sp_pf", (128, IC, T))
    sp_bw = dint("sp_bw", (128, IC, T))
    sp_pb = dint("sp_pb", (128, IC, T))
    sp_xat = dint("sp_xat", (128, DC, T))
    sp_xl1 = dint("sp_xl1", (128, DC, T))
    sp_s = dint("sp_s", (128, DC, T))
    sp_xh = dint("sp_xh", (128, DC, HALO), BF16)
    sp_tab = dint("sp_tab", (128, 20, 256))
    sp_brow = dint("sp_brow", (32, 128))
    cc1_in = dint("cc1_in", (128, 2 * IC))
    cc1_out = dint("cc1_out", (128, 2 * IC))
    cc2_in = dint("cc2_in", (128, DC, HALO))
    cc2_out = dint("cc2_out", (128, DC, HALO))
    cca_in = [dint(f"cca_in{l}", (128, DC)) for l in range(2)]
    cca_out = [dint(f"cca_out{l}", (128, DC)) for l in range(2)]

    dbg_h = {}
    def dtap(name):
        if name in dbg and name not in dbg_h:
            src = {"xn": sp_xn, "vs": sp_vs, "sg": sp_sg, "fw": sp_fw,
                   "pf": sp_pf, "bw": sp_bw, "pb": sp_pb, "xat": sp_xat,
                   "xl1": sp_xl1, "s": sp_s}.get(name)
            if src is None:
                return
            dbg_h[name] = (nc.dram_tensor("dbg_" + name, list(src.shape), F32,
                                          kind="ExternalOutput").ap(), src)

    for nm in dbg:
        dtap(nm)
    dbg_attn = {}
    if "attn0" in dbg:
        for nm, shape in [("qf0", (128, 4, CH)), ("kc0", (128, 4, CH)),
                          ("vc0", (128, 4, CH)), ("af0", (128, 4, CH))]:
            dbg_attn[nm] = nc.dram_tensor("dbg_" + nm, list(shape), BF16,
                                          kind="ExternalOutput").ap()
        dbg_attn["xat1"] = nc.dram_tensor("dbg_xat1", [128, DC, T], F32,
                                          kind="ExternalOutput").ap()
        dbg_attn["hn0"] = nc.dram_tensor("dbg_hn0", [128, DC, 1024], BF16,
                                         kind="ExternalOutput").ap()
        dbg_attn["h20"] = nc.dram_tensor("dbg_h20", [128, NBK, 1024], BF16,
                                         kind="ExternalOutput").ap()
        dbg_attn["rows0"] = nc.dram_tensor("dbg_rows0", [4, 1024], F32,
                                           kind="ExternalOutput").ap()
    if "attn0" in dbg:
        dbg_attn["blkb0"] = nc.dram_tensor("dbg_blkb0", [128, DC, 1025], F32,
                                           kind="ExternalOutput").ap()
    dbg_a1 = {}
    if "a1adbg" in dbg:
        dbg_a1["stat"] = nc.dram_tensor("dbg_stat", [128, 32], F32,
                                        kind="ExternalOutput").ap()
        dbg_a1["xntm"] = nc.dram_tensor("dbg_xntm", [128, 4, D], F32,
                                        kind="ExternalOutput").ap()
        dbg_a1["xtm"] = nc.dram_tensor("dbg_xtm", [128, 4, D], F32,
                                       kind="ExternalOutput").ap()
        dbg_attn["blkb0"] = nc.dram_tensor("dbg_blkb0", [128, DC, 1025], F32,
                                           kind="ExternalOutput").ap()

    ctx = ExitStack()
    with ctx:
        def sem(name):
            return ctx.enter_context(nc.semaphore(name))
        cTE, cVE, cAC = Ctr(sem("cTE")), Ctr(sem("cVE")), Ctr(sem("cAC"))
        cGP, cSY = Ctr(sem("cGP")), Ctr(sem("cSY"))
        cCC = Ctr(sem("cCC"))
        qS = Q([sem(f"qS{i}") for i in range(4)])
        qG = Q([sem(f"qG{i}") for i in range(4)])
        qC = Q([sem(f"qC{i}") for i in range(4)])

        block = ctx.enter_context(nc.Block())
        te, ve, ac = block.tensor, block.vector, block.scalar
        gp, sy = block.gpsimd, block.sync
        ECTR = {"te": cTE, "ve": cVE, "ac": cAC, "gp": cGP, "sy": cSY}
        EDEC = {"te": te, "ve": ve, "ac": ac, "gp": gp, "sy": sy}

        def barrier():
            def _dsy(e):
                qS.wait(e, qS.n)
            sy(_dsy)
            def _dgp(e):
                qG.wait(e, qG.n)
            gp(_dgp)
            def _dac(e):
                qC.wait(e, qC.n)
            ac(_dac)
            marks = {}
            def _ite(e):
                e.wait_ge(cVE.sem, cVE.n)
                cTE.inc(e.ldweights(id_bf[:, 0:4]))
            te(_ite)
            marks["te"] = cTE.n
            def _ive(e):
                e.wait_ge(cVE.sem, cVE.n)
                cVE.inc(e.tensor_copy(bsc[:, 0:1], bsc[:, 3:4]))
            ve(_ive)
            marks["ve"] = cVE.n
            def _iac(e):
                e.wait_ge(cAC.sem, cAC.n)
                cAC.inc(e.activation(bsc[:, 1:2], bsc[:, 3:4], Act.Copy,
                                     bias=0.0, scale=1.0))
            ac(_iac)
            marks["ac"] = cAC.n
            def _igp(e):
                e.wait_ge(cGP.sem, cGP.n)
                cGP.inc(e.tensor_copy(bsc[:, 2:3], bsc[:, 3:4]))
            gp(_igp)
            marks["gp"] = cGP.n
            def _isy(e):
                cSY.n += 1
                e.sem_inc(cSY.sem, 1)
            sy(_isy)
            marks["sy"] = cSY.n
            for name in EDEC:
                def _wait(e, me=name):
                    for o, tgt in marks.items():
                        if o != me:
                            e.wait_ge(ECTR[o].sem, tgt)
                EDEC[name](_wait)

        def sb(stack, name, shape, dt=F32):
            return stack.enter_context(nc.sbuf_tensor(name, list(shape), dt))

        mask_nS = sb(ctx, "mask_nS", (128, 4, 1024), BF16)
        mask_c0S = sb(ctx, "mask_c0S", (128, 4, 1024), BF16)
        id_bf = sb(ctx, "id_bfS", (128, 128), BF16)
        id_f32 = sb(ctx, "id_f32S", (128, 128))
        ones_row = sb(ctx, "ones_rowS", (1, 128))
        ones_col = sb(ctx, "ones_colS", (128, 1))
        eps_col = sb(ctx, "eps_colS", (128, 1))
        sel_first = sb(ctx, "sel_firstS", (128, 1))
        sel_second = sb(ctx, "sel_secondS", (128, 1))
        b_in = sb(ctx, "b_inS", (128, 32))
        beff = sb(ctx, "beffS", (128, 32))
        conv_w = sb(ctx, "conv_wS", (128, IC, 3))
        conv_b = sb(ctx, "conv_bS", (128, IC))
        b_delta = sb(ctx, "b_deltaS", (128, IC))
        nb_delta = sb(ctx, "nb_deltaS", (128, IC))
        b_out = sb(ctx, "b_outS", (128, DC))
        mgam = sb(ctx, "mgamS", (128, DC))
        mbeta = sb(ctx, "mbetaS", (128, DC))
        lvec = {}
        for l in range(2):
            for key, shape in [("b1p", (128, F1P // 128)), ("b2", (128, DC)),
                               ("lng", (128, DC)), ("lnb", (128, DC)),
                               ("qs", (1, DH)), ("ks", (1, DH))]:
                lvec[f"{key}{l}"] = sb(ctx, f"{key}S{l}", shape)
        bsc = sb(ctx, "bscS", (128, 4))
        bscb = sb(ctx, "bscbS", (128, 4), BF16)
        ccR1 = sb(ctx, "ccR1", (128, IC))
        ccR2 = sb(ctx, "ccR2", (128, IC))
        R1e = sb(ctx, "R1e", (128, IC))
        R2e = sb(ctx, "R2e", (128, IC))
        scA = sb(ctx, "scA", (128, DC))
        sideL1 = sb(ctx, "sideL1", (128, DC, 1))
        sideL2 = sb(ctx, "sideL2", (128, DC, 1))

        psW = ctx.enter_context(nc.psum_tensor("psW", [128, T], F32))
        psA = ctx.enter_context(nc.psum_tensor("psA", [128, 1024], F32))
        psT = ctx.enter_context(nc.psum_tensor("psT", [128, 1024], BF16))
        psR = ctx.enter_context(nc.psum_tensor("psR", [128, 512], F32))
        S512 = [psW[:, 512 * i:512 * (i + 1)] for i in range(4)] + \
               [psA[:, 0:512], psA[:, 512:1024]]
        S1024 = [psA[:, :], psW[:, 0:1024], psW[:, 1024:2048]]

        def psT3(f=128):
            return psT.ap().rearrange("p (c f) -> p c f", f=f)

        def psA3(f=128):
            return psA.ap().rearrange("p (c f) -> p c f", f=f)

        p0 = ExitStack()
        tb_lo = sb(p0, "tb_lo", (128, 20, 256))
        tb_hi = sb(p0, "tb_hi", (128, 20, 256))

        def _p0s(s):
            for nm, tile in [("mask_n", mask_nS), ("mask_c0", mask_c0S),
                             ("id_bf", id_bf), ("id_f32", id_f32),
                             ("ones_row", ones_row), ("ones_col", ones_col),
                             ("eps_col", eps_col),
                             ("sel_first", sel_first), ("sel_second", sel_second),
                             ("b_in_t", b_in), ("conv_w_t", conv_w),
                             ("conv_b_t", conv_b), ("b_delta_t", b_delta),
                             ("b_out_t", b_out), ("mgam_t", mgam),
                             ("mbeta_t", mbeta), ("tabs_hi", tb_hi),
                             ("tabs_lo", tb_lo)]:
                qS.dma(s, out=tile[:], in_=h_[nm][:])
            for l in range(2):
                for key, nm in [("b1p", f"b1p_t{l}"), ("b2", f"b2_t{l}"),
                                ("lng", f"lng_t{l}"), ("lnb", f"lnb_t{l}"),
                                ("qs", f"qs{l}"), ("ks", f"ks{l}")]:
                    qS.dma(s, out=lvec[f"{key}{l}"][:], in_=h_[nm][:])
        sy(_p0s)
        p0_ld = qS.n

        def _p0v(v):
            v.memset(bsc[:], 0.0)
            v.memset(bscb[:], 0.0)
            qS.wait(v, p0_ld)
            v.tensor_tensor(tb_hi[:], tb_hi[:], tb_lo[:], Alu.add)
            cVE.inc(v.tensor_scalar(nb_delta[:], b_delta[:], -1.0, None, Alu.mult))
        ve(_p0v)
        tabs_v = cVE.n

        def _p0s2(s):
            s.wait_ge(cVE.sem, tabs_v)
            qS.dma(s, out=sp_tab[:], in_=tb_hi[:])
        sy(_p0s2)
        tab_spill = qS.n
        p0.close()

        def _endprog():
            if dbg_h:
                def _tap(g):
                    for nm, (dst, src) in dbg_h.items():
                        qG.dma(g, out=dst[:], in_=src[:])
                    qG.wait(g, qG.n)
                gp(_tap)
            barrier()
            def _fin(s):
                qS.wait(s, qS.n)
            sy(_fin)

        a1 = ExitStack()
        w_inS = sb(a1, "w_inS", (128, DC, 2 * INNER), BF16)
        x_tm = sb(a1, "x_tm", (128, 4, D))
        xsq = sb(a1, "xsq", (128, 2, D))
        xn_tm = sb(a1, "xn_tm", (128, 4, D))
        xnm_tm = sb(a1, "xnm_tm", (128, 4, D), BF16)
        stat = sb(a1, "stat", (128, 16))
        st2 = sb(a1, "st2", (128, 16))
        xn_fm = sb(a1, "xn_fm", (128, DC, CH + 1))
        xnm_fm = sb(a1, "xnm_fm", (128, DC, CH), BF16)
        vals = sb(a1, "vals", (128, 2, CH + 2))
        vtail = sb(a1, "vtail", (128, IC, 2))
        sgb2 = [sb(a1, f"sgb2_{i}", (128, CH)) for i in range(2)]
        mbeta_bf = sb(a1, "mbeta_bf", (128, DC), BF16)
        brow_s = sb(a1, "brow_s", (1, 512))
        cb_eff = sb(a1, "cb_eff", (128, IC))
        nsel1 = sb(a1, "nsel1", (128, 1))
        sc1 = sb(a1, "sc1", (128, 1))
        hx = ExitStack()
        xh_fm = sb(hx, "xh_fm", (128, DC, HALO), BF16)
        xnmh_fm = sb(hx, "xnmh_fm", (128, DC, 128), BF16)

        def _a1g(g):
            qS.wait(g, tab_spill)
            wv = h_["w_in"].rearrange("(c p) f -> c p f", p=128)
            qG.dma(g, out=w_inS[:], in_=wv.transpose([1, 0, 2]))
        gp(_a1g)
        w_in_ld = qG.n

        def _mbv(v):
            qS.wait(v, p0_ld)
            cVE.inc(v.tensor_copy(mbeta_bf[:], mbeta[:]))
        ve(_mbv)
        mbv = cVE.n
        for nch in range(8):
            def _t(t, nch=nch):
                if nch == 0:
                    t.wait_ge(cVE.sem, mbv)
                    qG.wait(t, w_in_ld)
                ii = None
                for dc in range(DC):
                    ii = t.matmul(psR[0:1, :], mbeta_bf[:, dc:dc + 1],
                                  w_inS[:, dc, 512 * nch:512 * nch + 512],
                                  start=(dc == 0), stop=(dc == DC - 1))
                cTE.inc(ii)
            te(_t)
            mmn = cTE.n
            def _v(v, nch=nch, mmn=mmn):
                v.wait_ge(cTE.sem, mmn)
                if nch >= 1:
                    qS.wait(v, brow_mark[nch - 1])
                cVE.inc(v.tensor_copy(brow_s[:], psR[0:1, :]))
            if nch == 0:
                brow_mark = {}
            ve(_v)
            evn = cVE.n
            def _bs1(s, nch=nch, evn=evn):
                s.wait_ge(cVE.sem, evn)
                qS.dma(s,
                       out=sp_brow.rearrange("(a t) p -> a (t p)", a=8)[nch:nch + 1],
                       in_=brow_s[:])
            sy(_bs1)
            brow_mark[nch] = qS.n
            if nch < 7:
                te(lambda t, evn=evn: t.wait_ge(cVE.sem, evn))
        t1 = qS.n
        def _bs2(s):
            qS.wait(s, t1)
            qS.dma(s, out=beff[:], in_=sp_brow.transpose([1, 0]),
                               allow_slow_non_contiguous=True)
        sy(_bs2)
        t2 = qS.n
        def _bv(v):
            qS.wait(v, t2)
            cVE.inc(v.tensor_tensor(beff[:], beff[:], b_in[:], Alu.add))
            ii = None
            for dc in range(DC):
                ii = v.tensor_scalar(w_inS[:, dc, :], w_inS[:, dc, :],
                                     mgam[:, dc:dc + 1], None, Alu.mult)
            cVE.inc(ii)
        ve(_bv)
        def _cb(v):
            cVE.inc(v.tensor_tensor(cb_eff[:], conv_w[:, :, 0], conv_w[:, :, 1], Alu.add))
            v.wait_ge(cVE.sem, cVE.n)
            cVE.inc(v.tensor_tensor(cb_eff[:], cb_eff[:], conv_w[:, :, 2], Alu.add))
            v.wait_ge(cVE.sem, cVE.n)
            cVE.inc(v.tensor_tensor(cb_eff[:], cb_eff[:], beff[:, 0:IC], Alu.mult))
            v.wait_ge(cVE.sem, cVE.n)
            cVE.inc(v.tensor_tensor(cb_eff[:], cb_eff[:], conv_b[:], Alu.add))
            cVE.inc(v.tensor_scalar(nsel1[:], sel_first[:], -1.0, None, Alu.mult))
        ve(_cb)
        winS_ready = cVE.n

        def ln_stats(nts):
            for ts in range(nts):
                def _v(v, ts=ts):
                    ii = v.tensor_scalar(xn_tm[:, ts, :], x_tm[:, ts, :], 1.0, 0.0,
                                         Alu.mult, Alu.add,
                                         accum_out=stat[:, ts:ts + 1])
                    cVE.inc(ii)
                ve(_v)
                def _a(a, ts=ts):
                    if ts >= 2:
                        a.wait_ge(cAC.sem, sq_mark[ts - 2])
                    cAC.inc(a.activation(xsq[:, ts % 2, :], x_tm[:, ts, :],
                                         Act.Square,
                                         accum_out=stat[:, 4 + ts:5 + ts]))
                if ts == 0:
                    sq_mark = {}
                ac(_a)
                sq_mark[ts] = cAC.n
            acw, vew = cAC.n, cVE.n
            s_ = slice(0, nts)
            def _f1(v):
                v.wait_ge(cAC.sem, acw)
                v.wait_ge(cVE.sem, vew)
                cVE.inc(v.tensor_scalar(stat[:, 8:8 + nts], stat[:, s_], R1024,
                                        None, Alu.mult))
                v.wait_ge(cVE.sem, cVE.n)
                v.tensor_scalar(stat[:, 12:12 + nts], stat[:, 8:8 + nts], -1.0,
                                None, Alu.mult)
                cVE.inc(v.tensor_tensor(st2[:, 12:12 + nts], stat[:, 8:8 + nts],
                                        stat[:, 8:8 + nts], Alu.mult))
                v.wait_ge(cVE.sem, cVE.n)
                cVE.inc(v.scalar_tensor_tensor(st2[:, 0:nts], stat[:, 4:4 + nts],
                                               R1024, st2[:, 12:12 + nts],
                                               Alu.mult, Alu.subtract))
            ve(_f1)
            vw = cVE.n
            def _f2(a):
                a.wait_ge(cVE.sem, vw)
                cAC.inc(a.activation(st2[:, 4:4 + nts], st2[:, 0:nts], Act.Sqrt,
                                     bias=eps_col[:], scale=1.0))
            ac(_f2)
            aw = cAC.n
            def _f3(v):
                v.wait_ge(cAC.sem, aw)
                cVE.inc(v.reciprocal(st2[:, 4:4 + nts], st2[:, 4:4 + nts]))
                v.wait_ge(cVE.sem, cVE.n)
                cVE.inc(v.tensor_tensor(st2[:, 8:8 + nts], st2[:, 4:4 + nts],
                                        st2[:, 4:4 + nts], Alu.mult))
                v.wait_ge(cVE.sem, cVE.n)
                cVE.inc(v.tensor_tensor(st2[:, 8:8 + nts], st2[:, 0:nts],
                                        st2[:, 8:8 + nts], Alu.mult))
            ve(_f3)
            vw2 = cVE.n
            def _f4(a):
                a.wait_ge(cVE.sem, vw2)
                cAC.inc(a.activation(st2[:, 12:12 + nts], st2[:, 8:8 + nts], Act.Sqrt,
                                     bias=eps_col[:], scale=1.0))
            ac(_f4)
            aw2 = cAC.n
            def _f5(v):
                v.wait_ge(cAC.sem, aw2)
                cVE.inc(v.reciprocal(st2[:, 12:12 + nts], st2[:, 12:12 + nts]))
                v.wait_ge(cVE.sem, cVE.n)
                cVE.inc(v.tensor_tensor(st2[:, 8:8 + nts], st2[:, 4:4 + nts],
                                        st2[:, 12:12 + nts], Alu.mult))
            ve(_f5)
            return cVE.n

        def ln_apply(ts, want_xnm=True):
            def _v(v, ts=ts):
                ii = v.tensor_scalar(xn_tm[:, ts, :], x_tm[:, ts, :],
                                     stat[:, 12 + ts:13 + ts],
                                     st2[:, 4 + ts:5 + ts], Alu.add, Alu.mult)
                if want_xnm:
                    ii = v.tensor_scalar(xnm_tm[:, ts, :], x_tm[:, ts, :],
                                         stat[:, 12 + ts:13 + ts],
                                         st2[:, 8 + ts:9 + ts], Alu.add, Alu.mult)
                cVE.inc(ii)
            ve(_v)

        def _hs(s):
            qS.wait(s, p0_ld)
            for i, off in enumerate([3, 131, 259, 387]):
                qS.dma(s, out=x_tm[:, i, :], in_=h_["x_halo"][off:off + 128, :])
        sy(_hs)
        hld = qS.n
        ve(lambda v: qS.wait(v, hld))
        ac(lambda a: qS.wait(a, hld))
        fin = ln_stats(4)
        ve(lambda v, t=fin: v.wait_ge(cVE.sem, t))
        for ts in range(4):
            ln_apply(ts, want_xnm=False)
        apl = cVE.n

        te(lambda t, t0=apl: t.wait_ge(cVE.sem, t0))
        for i, off in enumerate([3, 131, 259, 387]):
            def _tr(t, i=i):
                ii = None
                for dc in range(DC):
                    ii = t.transpose(psA[:, dc * 128:dc * 128 + 128],
                                     xn_tm[:, i, dc * 128:dc * 128 + 128], id_f32[:])
                cTE.inc(ii)
            te(_tr)
            trn = cTE.n
            def _ev(v, off=off, trn=trn):
                v.wait_ge(cTE.sem, trn)
                cVE.inc(v.tensor_copy(xh_fm.ap()[:, :, off:off + 128], psA3()))
            ve(_ev)
            evn = cVE.n
            te(lambda t, evn=evn: t.wait_ge(cVE.sem, evn))
        trm = cTE.n
        def _hs2(s):
            s.wait_ge(cTE.sem, trm)
            qS.dma(s, out=x_tm[:, 0, :], in_=h_["x_halo"][388:516, :])
        sy(_hs2)
        hld2 = qS.n
        ve(lambda v: qS.wait(v, hld2))
        ac(lambda a: qS.wait(a, hld2))
        fin2 = ln_stats(1)
        ve(lambda v, t=fin2: v.wait_ge(cVE.sem, t))
        ln_apply(0, want_xnm=True)
        apl2 = cVE.n
        def _tr2(t):
            t.wait_ge(cVE.sem, apl2)
            for dc in range(DC):
                t.transpose(psA[:, dc * 128:dc * 128 + 128],
                            xn_tm[:, 0, dc * 128:dc * 128 + 128], id_f32[:])
            ii = None
            for dc in range(DC):
                ii = t.transpose(psT[:, dc * 128:dc * 128 + 128],
                                 xnm_tm[:, 0, dc * 128:dc * 128 + 128], id_bf[:])
            cTE.inc(ii)
        te(_tr2)
        tr2 = cTE.n
        def _ev2(v):
            v.wait_ge(cTE.sem, tr2)
            v.tensor_copy(xh_fm.ap()[:, :, 388:516], psA3())
            cVE.inc(v.tensor_copy(xnmh_fm.ap()[:, :, :], psT3()))
        ve(_ev2)
        xh_done = cVE.n
        def _hsp(s):
            s.wait_ge(cVE.sem, xh_done)
            qS.dma(s, out=sp_xh[:, :, 3:HALO], in_=xh_fm[:, :, 3:HALO])
        sy(_hsp)
        def _sd1(v):
            v.wait_ge(cVE.sem, xh_done)
            cVE.inc(v.tensor_scalar(sideL1[:], xh_fm[:, :, 515:516],
                                    sel_second[:], None, Alu.mult))
        ve(_sd1)

        te(lambda t: t.wait_ge(cVE.sem, max(winS_ready, xh_done)))
        hp_marks = {}
        for f in range(IC):
            def _t(t, f=f):
                ii = None
                for dc in range(DC):
                    ii = t.matmul(psR[:, 0:2], w_inS[:, dc, 128 * f:128 * f + 128],
                                  xnmh_fm[:, dc, 126:128],
                                  start=(dc == 0), stop=(dc == DC - 1))
                cTE.inc(ii)
            te(_t)
            mmn = cTE.n
            def _v(v, f=f, mmn=mmn):
                v.wait_ge(cTE.sem, mmn)
                v.tensor_scalar(vals[:, 0, 0:2], psR[:, 0:2],
                                sel_second[:], None, Alu.mult)
                cVE.inc(v.tensor_tensor(sc1[:], nsel1[:], beff[:, f:f + 1], Alu.mult))
                v.wait_ge(cVE.sem, cVE.n)
                cVE.inc(v.tensor_scalar(vtail[:, f, :], vals[:, 0, 0:2],
                                        sc1[:], None, Alu.add))
            ve(_v)
            evn = cVE.n
            te(lambda t, evn=evn: t.wait_ge(cVE.sem, evn))

        barrier()
        hx.close()
        vs16 = sb(a1, "vs16", (128, IC, CH))

        xr_own = h_["x_own"].rearrange("(a p) d -> a p d", p=128)
        conv_mark = {}
        vspill_mark = {}
        sgspill_mark = {}
        evac_mark = {}
        spxn_mark = {}
        winmm_mark = {}
        for c in range(NCH):
            def _xl(s, c=c):
                if c == 0:
                    s.wait_ge(cTE.sem, tr2)
                qS.dma(s, out=x_tm[:],
                                   in_=xr_own[4 * c:4 * c + 4].transpose([1, 0, 2]))
            sy(_xl)
            xld = qS.n
            ve(lambda v: qS.wait(v, xld))
            ac(lambda a: qS.wait(a, xld))
            if c >= 1:
                def _gx(v, c=c):
                    qS.wait(v, spxn_mark[c - 1])
                    v.wait_ge(cTE.sem, winmm_mark[c - 1])
                ve(_gx)
            fin = ln_stats(4)
            ve(lambda v, t=fin: v.wait_ge(cVE.sem, t))
            for ts in range(4):
                ln_apply(ts)
            aplv = cVE.n
            if dbg_a1 and c == 0:
                def _ta1(s, aplv=aplv):
                    s.wait_ge(cVE.sem, aplv)
                    qS.dma(s, out=dbg_a1["stat"][:, 0:16], in_=stat[:])
                    qS.dma(s, out=dbg_a1["stat"][:, 16:32], in_=st2[:])
                    qS.dma(s, out=dbg_a1["xntm"][:], in_=xn_tm[:])
                    qS.dma(s, out=dbg_a1["xtm"][:], in_=x_tm[:])
                sy(_ta1)
                _tm = qS.n
                ve(lambda v, m=_tm: qS.wait(v, m))
                ac(lambda a, m=_tm: qS.wait(a, m))
            te(lambda t, t0=aplv: t.wait_ge(cVE.sem, t0))
            for ts in range(4):
                def _tr(t, ts=ts):
                    for dc in range(DC):
                        t.transpose(psA[:, dc * 128:dc * 128 + 128],
                                    xn_tm[:, ts, dc * 128:dc * 128 + 128], id_f32[:])
                    ii = None
                    for dc in range(DC):
                        ii = t.transpose(psT[:, dc * 128:dc * 128 + 128],
                                        xnm_tm[:, ts, dc * 128:dc * 128 + 128], id_bf[:])
                    cTE.inc(ii)
                te(_tr)
                trn = cTE.n
                def _ev(v, ts=ts, trn=trn):
                    v.wait_ge(cTE.sem, trn)
                    v.tensor_copy(xn_fm.ap()[:, :, 1 + ts * 128:1 + ts * 128 + 128], psA3())
                    cVE.inc(v.tensor_copy(
                        xnm_fm.ap()[:, :, ts * 128:ts * 128 + 128], psT3()))
                ve(_ev)
                evn = cVE.n
                te(lambda t, evn=evn: t.wait_ge(cVE.sem, evn))
            fm_done = cVE.n
            def _spxn(s, c=c, t0=fm_done):
                s.wait_ge(cVE.sem, t0)
                qS.dma(s, out=sp_xn[:, :, c * CH:(c + 1) * CH],
                                   in_=xn_fm[:, :, 1:CH + 1])
            sy(_spxn)
            spxn_mark[c] = qS.n

            mm_mark2 = {}
            ts_mark = {}

            def emit_mm(f, c=c):
                slot = S512[f % 6]
                def _mm(t, f=f, slot=slot):
                    gk = (c, f - 6) if f >= 6 else (c - 1, f + 26)
                    if gk in evac_mark:
                        t.wait_ge(cAC.sem, evac_mark[gk])
                    ii = None
                    for dc in range(DC):
                        ii = t.matmul(slot, w_inS[:, dc, 128 * f:128 * f + 128],
                                      xnm_fm[:, dc, :],
                                      start=(dc == 0), stop=(dc == DC - 1))
                    cTE.inc(ii)
                te(_mm)
                mm_mark2[f] = cTE.n

            def emit_ev_pair(f0, c=c):
                def _ev(a, f0=f0):
                    for f in (f0, f0 + 1):
                        a.wait_ge(cTE.sem, mm_mark2[f])
                        gk = (c, f - 2) if f >= 2 else (c - 1, f + 14)
                        if gk in conv_mark:
                            a.wait_ge(cVE.sem, conv_mark[gk])
                        cAC.inc(a.activation(vals[:, f % 2, 2:CH + 2],
                                             S512[f % 6], Act.Copy,
                                             bias=0.0, scale=1.0))
                        evac_mark[(c, f)] = cAC.n
                ac(_ev)

            def emit_cv(f, c=c):
                vb = vals[:, f % 2, :]
                vs = vs16[:, f, :]
                def _cv(v, f=f, vb=vb, vs=vs):
                    v.wait_ge(cAC.sem, evac_mark[(c, f)])
                    gk = (c - 1, f)
                    if gk in vspill_mark:
                        qS.wait(v, vspill_mark[gk])
                    v.tensor_copy(vb[:, 0:2], vtail[:, f, :])
                    cVE.inc(v.tensor_scalar(vs[:], vb[:, 2:CH + 2],
                                            conv_w[:, f, 2:3], None, Alu.mult))
                    v.wait_ge(cVE.sem, cVE.n)
                    cVE.inc(v.scalar_tensor_tensor(vs[:], vb[:, 1:CH + 1],
                                                   conv_w[:, f, 1:2], vs[:],
                                                   Alu.mult, Alu.add))
                    v.wait_ge(cVE.sem, cVE.n)
                    cVE.inc(v.scalar_tensor_tensor(vs[:], vb[:, 0:CH],
                                                   conv_w[:, f, 0:1], vs[:],
                                                   Alu.mult, Alu.add))
                    cVE.inc(v.tensor_copy(vtail[:, f, :], vb[:, CH:CH + 2]))
                ve(_cv)
                conv_mark[(c, f)] = cVE.n

            def emit_si_batch(c=c):
                si_mark = {}
                def _si(a):
                    for f in range(IC):
                        a.wait_ge(cVE.sem, conv_mark[(c, f)])
                        cAC.inc(a.activation(vs16[:, f, :], vs16[:, f, :],
                                             Act.Silu,
                                             bias=cb_eff[:, f:f + 1], scale=1.0))
                        si_mark[f] = cAC.n
                ac(_si)
                def _sp(s):
                    for f in range(IC):
                        s.wait_ge(cAC.sem, si_mark[f])
                        qS.dma(s, out=sp_vs[:, f, c * CH:(c + 1) * CH],
                               in_=vs16[:, f, :])
                        vspill_mark[(c, f)] = qS.n
                sy(_sp)

            def emit_evg_pair(f0, c=c):
                spg_mark = {}
                def _evg(a, f0=f0):
                    for f in (f0, f0 + 1):
                        fg = f - IC
                        a.wait_ge(cTE.sem, mm_mark2[f])
                        gk = (c, fg - 2) if fg >= 2 else (c - 1, fg + 14)
                        if gk in sgspill_mark:
                            qS.wait(a, sgspill_mark[gk])
                        cAC.inc(a.activation(sgb2[fg % 2][:], S512[f % 6],
                                             Act.Sigmoid,
                                             bias=beff[:, f:f + 1], scale=1.0))
                        evac_mark[(c, f)] = cAC.n
                        spg_mark[f] = cAC.n
                ac(_evg)
                def _spg(s, f0=f0):
                    for f in (f0, f0 + 1):
                        fg = f - IC
                        s.wait_ge(cAC.sem, spg_mark[f])
                        qS.dma(s, out=sp_sg[:, fg, c * CH:(c + 1) * CH],
                               in_=sgb2[fg % 2][:])
                        sgspill_mark[(c, fg)] = qS.n
                sy(_spg)

            for s_ in range(17):
                f0 = 2 * s_
                if 0 <= f0 - 2 < IC:
                    emit_cv(f0 - 2)
                if 0 <= f0 - 1 < IC:
                    emit_cv(f0 - 1)
                if f0 < 32:
                    emit_mm(f0)
                    emit_mm(f0 + 1)
                    if f0 < IC:
                        emit_ev_pair(f0)
                    else:
                        emit_evg_pair(f0)
            emit_si_batch()
            winmm_mark[c] = cTE.n

        barrier()
        a1.close()
        if upto == "a1a2":
            _endprog()
            return nc

        b1 = ExitStack()
        vsbB = sb(b1, "vsbB", (128, IC, T), BF16)
        wdS = [sb(b1, f"wdS{i}", (128, IC, 128), BF16) for i in range(2)]
        delt = [sb(b1, f"delt{i}", (128, T)) for i in range(2)]
        signz = [sb(b1, f"signz{i}", (128, T)) for i in range(2)]
        vsf = [sb(b1, f"vsf{i}", (128, T)) for i in range(2)]
        bbuf = sb(b1, "bbuf", (128, T))
        zeros = sb(b1, "zerosB", (128, T))
        sout = [sb(b1, f"sout{i}", (128, T)) for i in range(4)]

        ve(lambda v: cVE.inc(v.memset(zeros[:], 0.0)))
        zdone = cVE.n

        def _b1g(g):
            for cq in range(4):
                qG.dma(g, out=vsbB[:, 4 * cq:4 * cq + 4, :],
                                   in_=sp_vs[:, 4 * cq:4 * cq + 4, :])
        gp(_b1g)
        vsb_ld = qG.n

        wdr = h_["w_delta"].rearrange("(ci p) co -> ci p co", p=128)
        ac_mark = {}
        scan_mark = {}
        spill_mark = {}
        mm_mark = {}
        wl_mark = {}

        def _wl0(g):
            qG.dma(g, out=wdS[0][:],
                   in_=wdr[:, :, 0:128].transpose([1, 0, 2]))
        gp(_wl0)
        wl_mark[0] = qG.n
        for ct in range(IC):
            if ct + 1 < IC:
                def _wl(g, ct=ct):
                    if ct >= 1:
                        g.wait_ge(cTE.sem, mm_mark[ct - 1])
                    qG.dma(g, out=wdS[(ct + 1) % 2][:],
                           in_=wdr[:, :, 128 * (ct + 1):128 * (ct + 1) + 128]
                               .transpose([1, 0, 2]))
                gp(_wl)
                wl_mark[ct + 1] = qG.n
            wl = wl_mark[ct]
            def _vl(a, ct=ct):
                if ct >= 2:
                    a.wait_ge(cVE.sem, scan_mark[ct - 2])
                qC.dma(a, out=vsf[ct % 2][:], in_=sp_vs[:, ct, :])
            ac(_vl)
            vl = qC.n

            if ct == 0:
                gscan_mark = {}
                ev_half = {}
            for h in range(2):
                def _mm(t, ct=ct, h=h, wl=wl):
                    if (ct, h) == (0, 0):
                        qG.wait(t, vsb_ld)
                    qG.wait(t, wl)
                    if ct >= 1:
                        t.wait_ge(cAC.sem, ev_half[(ct - 1, h)])
                    ii = None
                    for qr in range(2):
                        o = 1024 * h + 512 * qr
                        for ci in range(IC):
                            ii = t.matmul(psW[:, o:o + 512],
                                          wdS[ct % 2][:, ci, :],
                                          vsbB[:, ci, o:o + 512],
                                          start=(ci == 0), stop=(ci == IC - 1))
                    cTE.inc(ii)
                te(_mm)
                mm_mark[ct] = cTE.n
                mmn = cTE.n
                def _ev(a, ct=ct, h=h, mmn=mmn):
                    a.wait_ge(cTE.sem, mmn)
                    if h == 0 and ct >= 2:
                        a.wait_ge(cVE.sem, gscan_mark[ct - 2])
                    o = 1024 * h
                    a.activation(delt[ct % 2][:, o:o + 1024],
                                 psW[:, o:o + 1024], Act.Sigmoid,
                                 bias=b_delta[:, ct:ct + 1], scale=1.0)
                    cAC.inc(a.activation(signz[ct % 2][:, o:o + 1024],
                                         psW[:, o:o + 1024], Act.Sigmoid,
                                         bias=nb_delta[:, ct:ct + 1], scale=-1.0))
                ac(_ev)
                ev_half[(ct, h)] = cAC.n
            ac_mark[ct] = cAC.n
            acn = cAC.n
            def _sc(v, ct=ct, acn=acn, vl=vl):
                v.wait_ge(cAC.sem, acn)
                qC.wait(v, vl)
                if ct == 0:
                    v.wait_ge(cVE.sem, zdone)
                if ct >= 1:
                    qS.wait(v, spill_mark[ct - 1])
                cVE.inc(v.tensor_tensor(bbuf[:], signz[ct % 2][:],
                                        vsf[ct % 2][:], Alu.mult))
                v.wait_ge(cVE.sem, cVE.n)
                v.tensor_tensor_scan(sout[0][:], delt[ct % 2][:], bbuf[:], 0.0,
                                     Alu.mult, Alu.add)
                cVE.inc(v.tensor_tensor_scan(sout[2][:, ::-1], delt[ct % 2][:, ::-1],
                                             bbuf[:, ::-1], 0.0, Alu.mult, Alu.add))
                v.wait_ge(cVE.sem, cVE.n)
                v.tensor_copy(ccR1[:, ct:ct + 1], sout[0][:, T - 1:T])
                cVE.inc(v.tensor_copy(ccR2[:, ct:ct + 1], sout[2][:, 0:1]))
            ve(_sc)
            scan_mark[ct] = cVE.n
            scn = cVE.n
            def _gsc(v, ct=ct):
                v.tensor_tensor_scan(sout[1][:], delt[ct % 2][:], zeros[:], 1.0,
                                     Alu.mult, Alu.add)
                cVE.inc(v.tensor_tensor_scan(sout[3][:, ::-1], delt[ct % 2][:, ::-1],
                                             zeros[:], 1.0, Alu.mult, Alu.add))
            ve(_gsc)
            gscan_mark[ct] = cVE.n
            gscn = cVE.n
            def _sp(s, ct=ct, scn=scn, gscn=gscn):
                s.wait_ge(cVE.sem, gscn)
                qS.dma(s, out=sp_fw[:, ct, :], in_=sout[0][:])
                qS.dma(s, out=sp_pf[:, ct, :], in_=sout[1][:])
                qS.dma(s, out=sp_bw[:, ct, :], in_=sout[2][:])
                qS.dma(s, out=sp_pb[:, ct, :], in_=sout[3][:])
            sy(_sp)
            spill_mark[ct] = qS.n

        scnall = cVE.n
        def _c1v(v):
            v.wait_ge(cVE.sem, cVE.n)
            v.tensor_scalar(ccR1[:], ccR1[:], sel_first[:], None, Alu.mult)
            cVE.inc(v.tensor_scalar(ccR2[:], ccR2[:], sel_second[:], None, Alu.mult))
        ve(_c1v)
        c1v = cVE.n
        def _c1g(g):
            g.wait_ge(cVE.sem, c1v)
            qG.dma(g, out=cc1_in[:, 0:IC], in_=ccR1[:])
            qG.dma(g, out=cc1_in[:, IC:2 * IC], in_=ccR2[:])
            qG.wait(g, qG.n)
            cCC.inc(g.collective_compute("AllReduce", Alu.add, replica_groups=RG,
                                         ins=[cc1_in.opt()], outs=[cc1_out.opt()]))
        gp(_c1g)
        cc1_done = cCC.n

        barrier()
        b1.close()
        if upto == "a1b":
            _endprog()
            return nc

        def emit_attn(l, xin_sp, is_c2, cc2_wait=0, sub=6):
            lx = ExitStack()
            ltab = sb(lx, f"ltab{l}", (128, 20, 256))
            wqkvS = sb(lx, f"wqkvS{l}", (128, DC, 1536), BF16)
            woS = sb(lx, f"woS{l}", (128, 4, 1024), BF16)
            xi32 = sb(lx, f"xi32_{l}", (128, DC, CH + 1))
            xib = sb(lx, f"xib{l}", (128, DC, CH + 1), BF16)
            xatc = sb(lx, f"xatc{l}", (128, DC, CH))
            qk2 = [sb(lx, f"qk2{l}_{i}", (128, 1024)) for i in range(2)]
            sqb2 = [sb(lx, f"sqb2{l}_{i}", (128, 1024)) for i in range(2)]
            rot2 = [sb(lx, f"rot2{l}_{i}", (128, 1024)) for i in range(2)]
            qf2 = [sb(lx, f"qf2{l}_{i}", (128, 1024), BF16) for i in range(2)]
            ssn = sb(lx, f"ssn{l}", (128, 64))
            qfm = sb(lx, f"qfm{l}", (128, 4, CH), BF16)
            kc = [sb(lx, f"kc{l}_{i}", (128, 4, CH), BF16) for i in range(2)]
            vc = [sb(lx, f"vc{l}_{i}", (128, 4, CH), BF16) for i in range(2)]
            probs = [sb(lx, f"probs{l}_{i}", (128, 1024), BF16) for i in range(2)]
            probsN = [sb(lx, f"probsN{l}_{i}", (128, 1024), BF16) for i in range(2)]
            pT = sb(lx, f"pT{l}", (128, 2, 8, CH), BF16)
            attnfm = sb(lx, f"attnfm{l}", (128, 4, CH), BF16)
            smax = sb(lx, f"smax{l}", (128, 8))
            hbuf = sb(lx, f"hbuf{l}", (128, DC, HALO)) if is_c2 else None

            def _lg(g):
                wq = h_[f"wqkv{l}"].rearrange("(c p) f -> c p f", p=128)
                qG.dma(g, out=wqkvS[:], in_=wq.transpose([1, 0, 2]))
                wo = h_[f"wo{l}"].rearrange("(c p) f -> c p f", p=128)
                qG.dma(g, out=woS[:], in_=wo.transpose([1, 0, 2]))
            gp(_lg)
            w_ld = qG.n
            def _lt(a):
                qC.dma(a, out=ltab[:], in_=sp_tab[:])
                if is_c2:
                    a.wait_ge(cCC.sem, cc2_wait)
                    qC.dma(a, out=hbuf[:], in_=cc2_out[:])
            ac(_lt)
            tab_ld = qC.n

            def _qb(t):
                qC.wait(t, tab_ld)
                t.matmul(psR[:, 0:64], ones_row[:], lvec[f"qs{l}"][:],
                         start=True, stop=True)
                t.matmul(psR[:, 64:128], ones_row[:], lvec[f"ks{l}"][:],
                         start=True, stop=True)
                t.matmul(psR[:, 128:160], ones_row[:], lvec[f"qs{l}"][:, 32:64],
                         start=True, stop=True)
                t.matmul(psR[:, 160:192], ones_row[:], lvec[f"qs{l}"][:, 0:32],
                         start=True, stop=True)
                t.matmul(psR[:, 192:224], ones_row[:], lvec[f"ks{l}"][:, 32:64],
                         start=True, stop=True)
                cTE.inc(t.matmul(psR[:, 224:256], ones_row[:],
                                 lvec[f"ks{l}"][:, 0:32], start=True, stop=True))
            te(_qb)
            qbn = cTE.n
            def _qv(v):
                v.wait_ge(cTE.sem, qbn)
                ii = None
                for sl, ps in [(slice(0, 64), slice(0, 64)),
                               (slice(64, 128), slice(128, 192)),
                               (slice(128, 192), slice(64, 128)),
                               (slice(192, 256), slice(192, 256))]:
                    ii = v.tensor_tensor(
                        ltab.ap()[:, :, sl], ltab.ap()[:, :, sl],
                        psR.ap()[:, ps].unsqueeze(1).broadcast_to([128, 20, 64]),
                        Alu.mult)
                cVE.inc(ii)
                v.wait_ge(cVE.sem, cVE.n)
                cVE.inc(v.tensor_scalar(ltab.ap()[:, :, 0:128],
                                        ltab.ap()[:, :, 0:128], 8.0,
                                        None, Alu.mult))
            ve(_qv)
            ltab_done = cVE.n
            te(lambda t: t.wait_ge(cVE.sem, ltab_done))

            slot_g = {}
            psR_g = [(cVE, 0)]
            psT_g = [None]

            def _gw(e, gd):
                if gd is not None and gd[1]:
                    e.wait_ge(gd[0].sem, gd[1])

            q1_mark = {}
            q2r_mark = {}

            def emit_q1(ts, tag):
                slot = S1024[ts % 3]
                def _mmqk(t, ts=ts, slot=slot):
                    _gw(t, slot_g.get(ts % 3))
                    ii = None
                    for half in range(2):
                        for dc in range(DC):
                            off = (1 if dc < 4 else 0) + ts * 128
                            ii = t.matmul(slot[:, 512 * half:512 * half + 512],
                                          xib[:, dc, off:off + 128],
                                          wqkvS[:, dc, 512 * half:512 * half + 512],
                                          start=(dc == 0), stop=(dc == DC - 1))
                    cTE.inc(ii)
                te(_mmqk)
                def _mmv(t, ts=ts):
                    _gw(t, psR_g[0])
                    ii = None
                    for dc in range(DC):
                        off = (1 if dc < 4 else 0) + ts * 128
                        ii = t.matmul(psR[:], xib[:, dc, off:off + 128],
                                      wqkvS[:, dc, 1024:1536],
                                      start=(dc == 0), stop=(dc == DC - 1))
                    cTE.inc(ii)
                te(_mmv)
                q1_mark[(tag, ts)] = cTE.n

            def emit_q2(ts, tag, vdst):
                slot = S1024[ts % 3]
                base = 32 * (ts % 2)
                def _ev(a, ts=ts, slot=slot):
                    a.wait_ge(cTE.sem, q1_mark[(tag, ts)])
                    a.activation(qk2[ts % 2][:], slot, Act.Copy,
                                 bias=0.0, scale=1.0)
                    cAC.inc(a.activation(vdst[:, ts, :], psR[:], Act.Copy,
                                         bias=0.0, scale=1.0))
                ac(_ev)
                evn = cAC.n
                slot_g[ts % 3] = (cAC, evn)
                psR_g[0] = (cAC, evn)
                def _ss(v, ts=ts, evn=evn, base=base):
                    v.wait_ge(cAC.sem, evn)
                    cVE.inc(v.tensor_tensor(sqb2[ts % 2][:], qk2[ts % 2][:],
                                            qk2[ts % 2][:], Alu.mult))
                    v.wait_ge(cVE.sem, cVE.n)
                    cVE.inc(v.tensor_reduce(
                        ssn[:, base:base + 16],
                        sqb2[ts % 2].ap().rearrange("p (g d) -> p g d", d=64),
                        Ax.X, Alu.add))
                    v.wait_ge(cVE.sem, cVE.n)
                    cVE.inc(v.tensor_scalar(ssn[:, base:base + 16],
                                            ssn[:, base:base + 16], 1e-24,
                                            None, Alu.max))
                ve(_ss)
                ssv = cVE.n
                def _nr(a, ssv=ssv, base=base):
                    a.wait_ge(cVE.sem, ssv)
                    cAC.inc(a.activation(ssn[:, base + 16:base + 32],
                                         ssn[:, base:base + 16], Act.Sqrt,
                                         bias=0.0, scale=1.0))
                ac(_nr)
                q2r_mark[(tag, ts)] = cAC.n

            def emit_q3(ts, tag, sidx, want_q, kdst):
                base = 32 * (ts % 2)
                qk = qk2[ts % 2]
                sqb = sqb2[ts % 2]
                rot1 = rot2[ts % 2]
                qf = qf2[ts % 2]
                def _pr(v, ts=ts, sidx=sidx, base=base):
                    v.wait_ge(cAC.sem, q2r_mark[(tag, ts)])
                    cVE.inc(v.reciprocal(ssn[:, base + 16:base + 32],
                                         ssn[:, base + 16:base + 32]))
                    v.wait_ge(cVE.sem, cVE.n)
                    cVE.inc(v.tensor_tensor(
                        qk.ap().rearrange("p (g d) -> p g d", d=64),
                        qk.ap().rearrange("p (g d) -> p g d", d=64),
                        ssn.ap()[:, base + 16:base + 32].unsqueeze(2)
                           .broadcast_to([128, 16, 64]), Alu.mult))
                    v.wait_ge(cVE.sem, cVE.n)
                    v.tensor_tensor(
                        rot1.ap()[:, 0:512].rearrange("p (g d) -> p g d", d=64),
                        qk.ap()[:, 0:512].rearrange("p (g d) -> p g d", d=64),
                        ltab.ap()[:, sidx, 0:64].unsqueeze(1)
                            .broadcast_to([128, 8, 64]), Alu.mult)
                    v.tensor_tensor(
                        rot1.ap()[:, 512:1024].rearrange("p (g d) -> p g d", d=64),
                        qk.ap()[:, 512:1024].rearrange("p (g d) -> p g d", d=64),
                        ltab.ap()[:, sidx, 128:192].unsqueeze(1)
                            .broadcast_to([128, 8, 64]), Alu.mult)
                    ii = None
                    for lohi in range(2):
                        dsl = slice(32 * lohi, 32 * lohi + 32)
                        ssl = slice(32 * (1 - lohi), 32 * (1 - lohi) + 32)
                        tq = slice(64 + 32 * lohi, 64 + 32 * lohi + 32)
                        tk = slice(192 + 32 * lohi, 192 + 32 * lohi + 32)
                        v.tensor_tensor(
                            sqb.ap()[:, 0:512].rearrange("p (g d) -> p g d", d=64)[:, :, dsl],
                            qk.ap()[:, 0:512].rearrange("p (g d) -> p g d", d=64)[:, :, ssl],
                            ltab.ap()[:, sidx, tq].unsqueeze(1)
                                .broadcast_to([128, 8, 32]), Alu.mult)
                        ii = v.tensor_tensor(
                            sqb.ap()[:, 512:1024].rearrange("p (g d) -> p g d", d=64)[:, :, dsl],
                            qk.ap()[:, 512:1024].rearrange("p (g d) -> p g d", d=64)[:, :, ssl],
                            ltab.ap()[:, sidx, tk].unsqueeze(1)
                                .broadcast_to([128, 8, 32]), Alu.mult)
                    cVE.inc(ii)
                    v.wait_ge(cVE.sem, cVE.n)
                    cVE.inc(v.tensor_tensor(qf[:], rot1[:], sqb[:], Alu.add))
                ve(_pr)
                prn = cVE.n
                def _tr(t, prn=prn):
                    t.wait_ge(cVE.sem, prn)
                    _gw(t, psT_g[0])
                    ii = None
                    for j in range(8):
                        ii = t.transpose(psT[:, 128 * j:128 * j + 128],
                                         qf[:, 128 * j:128 * j + 128], id_bf[:])
                    cTE.inc(ii)
                te(_tr)
                trn = cTE.n
                def _te2(a, ts=ts, trn=trn):
                    a.wait_ge(cTE.sem, trn)
                    if want_q:
                        a.activation(
                            qfm.ap()[:, :, ts * 128:ts * 128 + 128],
                            psT.ap()[:, 0:512].rearrange("p (c f) -> p c f", f=128),
                            Act.Copy, bias=0.0, scale=1.0)
                    cAC.inc(a.activation(
                        kdst.ap()[:, :, ts * 128:ts * 128 + 128],
                        psT.ap()[:, 512:1024].rearrange("p (c f) -> p c f", f=128),
                        Act.Copy, bias=0.0, scale=1.0))
                ac(_te2)
                psT_g[0] = (cAC, cAC.n)

            def emit_qkv_group(sidx0, want_q, kdst, vdst, tag):
                for i in range(6):
                    if 1 <= i < 5:
                        emit_q2(i - 1, tag, vdst)
                    if i < 4:
                        emit_q1(i, tag)
                    if 2 <= i:
                        emit_q3(i - 2, tag, sidx0 + (i - 2), want_q, kdst)

            if is_c2:
                def _hc(v):
                    qC.wait(v, tab_ld)
                    ii = None
                    for dc in range(DC):
                        ii = v.tensor_copy(xib[:, dc, 0:HALO - 3],
                                           hbuf[:, dc, 3:HALO])
                    cVE.inc(ii)
                ve(_hc)
            else:
                def _hl(a):
                    qC.dma(a, out=xib[:, :, 0:CH + 1],
                                       in_=sp_xh[:, :, 3:HALO])
                ac(_hl)
                hldq = qC.n
                def _hc(v):
                    qC.wait(v, hldq)
                    cVE.inc(v.tensor_copy(bsc[:, 0:1], bsc[:, 3:4]))
                ve(_hc)
            hcast = cVE.n
            te(lambda t: t.wait_ge(cVE.sem, hcast))
            te(lambda t: qG.wait(t, w_ld))
            emit_qkv_group(0, False, kc[1], vc[1], -1)

            spx_mark = {}
            xat_done = 0
            for c in range(NCH):
                def _cl(a, c=c):
                    if c >= 1:
                        a.wait_ge(cVE.sem, xat_done)
                    if c == 0:
                        qC.dma(a, out=xi32[:, :, 1:CH + 1],
                                           in_=xin_sp[:, :, 0:CH])
                    else:
                        qC.dma(a, out=xi32[:, :, 0:CH + 1],
                                           in_=xin_sp[:, :, c * CH - 1:(c + 1) * CH])
                ac(_cl)
                cld = qC.n
                def _cc(v, c=c, cld=cld):
                    qC.wait(v, cld)
                    if c == 0:
                        cVE.inc(v.tensor_copy(xi32[:, :, 0:1],
                                              sideL2[:] if is_c2 else sideL1[:]))
                        v.wait_ge(cVE.sem, cVE.n)
                    cVE.inc(v.tensor_copy(xib[:], xi32[:]))
                ve(_cc)
                ccn = cVE.n
                te(lambda t, ccn=ccn: t.wait_ge(cVE.sem, ccn))
                emit_qkv_group(4 + 4 * c, True, kc[c % 2], vc[c % 2], c)
                kv_ready = cAC.n

                if sub < 2:
                    continue
                mask = mask_c0S if c == 0 else mask_nS
                sim_mark = {}
                sm_mark = {}
                e1_mark = {}
                nr_mark = {}
                tr_mark = {}
                pte_mark = {}
                pv_done = {}
                pve_mark = {}

                def emit_sim(kk, c=c, mask=mask):
                    hp, j = kk // 8, kk % 8
                    qs_, hh = j // 2, j % 2
                    slot = S1024[kk % 3]
                    def _sim(t, kk=kk, hp=hp, qs_=qs_, hh=hh, slot=slot,
                             kv_ready=kv_ready):
                        _gw(t, slot_g.get(kk % 3))
                        if kk < 3:
                            t.wait_ge(cAC.sem, kv_ready)
                        tp = (64 * hh, 0) if hh else None
                        t.matmul(slot[:, 0:512], id_bf[:],
                                 mask[:, qs_, 0:512], start=True, stop=False)
                        t.matmul(slot[:, 512:1024], id_bf[:],
                                 mask[:, qs_, 512:1024], start=True, stop=False)
                        base = 64 * hh
                        lhs = qfm[base:base + 64, hp,
                                  qs_ * 128:qs_ * 128 + 128]
                        t.matmul(slot[:, 0:512], lhs,
                                 kc[(c + 1) % 2][base:base + 64, hp, :],
                                 start=False, stop=True, tile_position=tp)
                        cTE.inc(t.matmul(slot[:, 512:1024], lhs,
                                         kc[c % 2][base:base + 64, hp, :],
                                         start=False, stop=True,
                                         tile_position=tp))
                    te(_sim)
                    sim_mark[kk] = cTE.n

                def emit_sm(kk):
                    slot = S1024[kk % 3]
                    s4 = 4 * (kk % 2)
                    def _sm(v, kk=kk, slot=slot, s4=s4):
                        v.wait_ge(cTE.sem, sim_mark[kk])
                        if kk - 2 in e1_mark:
                            v.wait_ge(cAC.sem, e1_mark[kk - 2])
                        cVE.inc(v.tensor_reduce(smax[:, s4:s4 + 1], slot, Ax.X,
                                                Alu.max, negate=True))
                    ve(_sm)
                    sm_mark[kk] = cVE.n

                def emit_e1_pair(kk):
                    def _e1(a, kk=kk):
                        s4 = 4 * (kk % 2)
                        a.wait_ge(cVE.sem, sm_mark[kk])
                        if kk - 2 in nr_mark:
                            a.wait_ge(cVE.sem, nr_mark[kk - 2])
                        cAC.inc(a.activation(probs[kk % 2][:],
                                             S1024[kk % 3], Act.Exp,
                                             bias=smax[:, s4:s4 + 1],
                                             scale=1.0,
                                             accum_out=smax[:, s4 + 1:s4 + 2]))
                        e1_mark[kk] = cAC.n
                        slot_g[kk % 3] = (cAC, cAC.n)
                    ac(_e1)

                def emit_nr(kk):
                    s4 = 4 * (kk % 2)
                    def _nr(v, kk=kk, s4=s4):
                        v.wait_ge(cAC.sem, e1_mark[kk])
                        if kk - 2 in tr_mark:
                            v.wait_ge(cTE.sem, tr_mark[kk - 2])
                        cVE.inc(v.reciprocal(smax[:, s4 + 2:s4 + 3],
                                             smax[:, s4 + 1:s4 + 2]))
                        v.wait_ge(cVE.sem, cVE.n)
                        cVE.inc(v.tensor_scalar(probsN[kk % 2][:],
                                                probs[kk % 2][:],
                                                smax[:, s4 + 2:s4 + 3],
                                                None, Alu.mult))
                    ve(_nr)
                    nr_mark[kk] = cVE.n

                def emit_tr(kk):
                    def _tr(t, kk=kk):
                        t.wait_ge(cVE.sem, nr_mark[kk])
                        _gw(t, psT_g[0])
                        ii = None
                        for j in range(8):
                            ii = t.transpose(psT[:, 128 * j:128 * j + 128],
                                             probsN[kk % 2][:, 128 * j:128 * j + 128],
                                             id_bf[:])
                        cTE.inc(ii)
                    te(_tr)
                    tr_mark[kk] = cTE.n

                def emit_pte(kk):
                    hp, j = kk // 8, kk % 8
                    qs_, hh = j // 2, j % 2
                    def _pte(v, kk=kk, hp=hp, qs_=qs_, hh=hh):
                        v.wait_ge(cTE.sem, tr_mark[kk])
                        if (hp - 1, hh) in pv_done:
                            v.wait_ge(cTE.sem, pv_done[(hp - 1, hh)])
                        cVE.inc(v.tensor_copy(
                            pT.ap()[:, hh, :, qs_ * 128:qs_ * 128 + 128],
                            psT3()))
                    ve(_pte)
                    pte_mark[kk] = cVE.n
                    psT_g[0] = (cVE, cVE.n)

                def emit_pv(hp, c=c):
                    for hh in range(2):
                        h = 2 * hp + hh
                        def _pv(t, h=h, hh=hh, hp=hp):
                            t.wait_ge(cVE.sem, pte_mark[8 * hp + 7])
                            _gw(t, psR_g[0])
                            ii = None
                            for kp in range(8):
                                vcb = vc[(c + 1) % 2] if kp < 4 else vc[c % 2]
                                ii = t.matmul(psR[0:64, :],
                                              vcb[:, kp % 4, 64 * h:64 * h + 64],
                                              pT[:, hh, kp, :],
                                              start=(kp == 0), stop=(kp == 7))
                            cTE.inc(ii)
                        te(_pv)
                        pv_done[(hp, hh)] = cTE.n
                        pvn = cTE.n
                        def _pve(v, hp=hp, hh=hh, pvn=pvn):
                            v.wait_ge(cTE.sem, pvn)
                            cVE.inc(v.tensor_copy(attnfm[64 * hh:64 * hh + 64, hp, :],
                                                  psR[0:64, :]))
                        ve(_pve)
                        pve_mark[(hp, hh)] = cVE.n
                        psR_g[0] = (cVE, cVE.n)

                if sub < 3:
                    for kk in range(32):
                        emit_sim(kk)
                    continue
                for kk in range(35):
                    if kk < 32:
                        emit_sim(kk)
                    if 0 <= kk - 1 < 32:
                        emit_sm(kk - 1)
                        emit_e1_pair(kk - 1)
                    if 0 <= kk - 3 < 32:
                        emit_pte(kk - 3)
                        if sub >= 4 and (kk - 3) % 8 == 7:
                            emit_pv((kk - 3) // 8)
                    if 0 <= kk - 2 < 32:
                        emit_nr(kk - 2)
                        emit_tr(kk - 2)

                af = cVE.n
                te(lambda t, af=af: t.wait_ge(cVE.sem, af))
                woe_mark = {}
                for dt in range(DC if sub >= 4.7 else 0):
                    slot = S512[dt % 6]
                    def _wo(t, dt=dt, slot=slot):
                        if dt >= 6:
                            t.wait_ge(cVE.sem, woe_mark[dt - 6])
                        ii = None
                        for ic in range(4):
                            ii = t.matmul(slot, woS[:, ic, dt * 128:dt * 128 + 128],
                                          attnfm[:, ic, :],
                                          start=(ic == 0), stop=(ic == 3))
                        cTE.inc(ii)
                    te(_wo)
                    won = cTE.n
                    def _woe(v, dt=dt, slot=slot, won=won, c=c):
                        v.wait_ge(cTE.sem, won)
                        if c >= 1:
                            qS.wait(v, spx_mark[c - 1])
                        cVE.inc(v.tensor_tensor(xatc[:, dt, :], slot,
                                                xi32[:, dt, 1:CH + 1], Alu.add))
                    ve(_woe)
                    woe_mark[dt] = cVE.n
                xat_done = cVE.n
                if sub < 5.5:
                    spx_mark[c] = qS.n
                    continue
                def _spx(s, c=c, t0=xat_done):
                    s.wait_ge(cVE.sem, t0)
                    qS.dma(s, out=sp_xat[:, :, c * CH:(c + 1) * CH],
                                       in_=xatc[:])
                sy(_spx)
                spx_mark[c] = qS.n
                if c == 0 and l == 0 and dbg_attn:
                    def _tapa(s):
                        s.wait_ge(cVE.sem, xat_done)
                        qS.dma(s, out=dbg_attn["qf0"][:], in_=qfm[:])
                        qS.dma(s, out=dbg_attn["kc0"][:], in_=kc[0][:])
                        qS.dma(s, out=dbg_attn["vc0"][:], in_=vc[0][:])
                        qS.dma(s, out=dbg_attn["af0"][:], in_=attnfm[:])
                    sy(_tapa)
                    _tm = qS.n
                    ve(lambda v, m=_tm: qS.wait(v, m))

            if sub < 6:
                barrier()
                lx.close()
                return cGP.n
            def _cav(v):
                v.wait_ge(cVE.sem, cVE.n)
                cVE.inc(v.tensor_scalar(
                    scA.ap().unsqueeze(2), xatc[:, :, CH - 1:CH],
                    sel_first[:], None, Alu.mult))
            ve(_cav)
            cav = cVE.n
            def _cag(g, l=l):
                g.wait_ge(cVE.sem, cav)
                qG.dma(g, out=cca_in[l][:], in_=scA[:])
                qG.wait(g, qG.n)
                cCC.inc(g.collective_compute("AllReduce", Alu.add,
                                             replica_groups=RG,
                                             ins=[cca_in[l].opt()],
                                             outs=[cca_out[l].opt()]))
            gp(_cag)
            cca_done = cCC.n
            barrier()
            lx.close()
            return cca_done

        def emit_mlp(l, is_c2, cca_done, fire_cc2):
            cc2_mark = {0: 0}
            bi_nonlocal = [0]
            mx = ExitStack()
            blkb = sb(mx, f"blkb{l}", (128, DC, 1025))
            hnorm = sb(mx, f"hnorm{l}", (128, DC, 1024), BF16)
            h2S = sb(mx, f"h2S{l}", (128, NBK, 1024), BF16)
            gel = sb(mx, f"gel{l}", (128, 1024))
            gb2 = [gel, sb(mx, f"gelb{l}", (128, 1024))]
            xsqm = sb(mx, f"xsqm{l}", (128, 1024))
            w1b = [sb(mx, f"w1b{l}_{i}", (128, DC, 256), BF16) for i in range(2)]
            w2b = [sb(mx, f"w2b{l}_{i}", (128, NBK, 128), BF16) for i in range(2)]
            srow = sb(mx, f"srow{l}", (1, 1024))
            sxrow = sb(mx, f"sxrow{l}", (1, 1024))
            murow = sb(mx, f"murow{l}", (1, 1024))
            rstrow = sb(mx, f"rstrow{l}", (1, 1024))
            ccb = sb(mx, f"ccb{l}", (128, DC, HALO)) if fire_cc2 else None
            gbuf = gel if is_c2 else None
            sbuf_s = sb(mx, f"sbuf_s{l}", (128, DC, 1024)) if is_c2 else None
            xfin = sb(mx, f"xfin{l}", (128, 4, D)) if is_c2 else None

            w1r = h_[f"w1p{l}"].rearrange("(c p) f -> c p f", p=128)
            w2r = h_[f"w2p{l}"].rearrange("(c p) f -> c p f", p=128)

            for bi, blk in enumerate([1, 0]):
                bi_nonlocal[0] = bi
                base = blk * 1024
                ve_now, te_now, sy_now = cVE.n, cTE.n, qS.n
                def _bl(a, blk=blk, base=base, ve_now=ve_now, te_now=te_now,
                        sy_now=sy_now):
                    if bi_nonlocal[0] == 1:
                        a.wait_ge(cVE.sem, ve_now)
                        a.wait_ge(cTE.sem, te_now)
                        qS.wait(a, sy_now)
                    if blk == 1:
                        qC.dma(a, out=blkb[:, :, 0:1025],
                                           in_=sp_xat[:, :, base - 1:base + 1024])
                    else:
                        qC.dma(a, out=blkb[:, :, 1:1025],
                                           in_=sp_xat[:, :, 0:1024])
                        a.wait_ge(cCC.sem, cca_done)
                        qC.dma(a, out=scA[:], in_=cca_out[l][:])
                ac(_bl)
                bld = qC.n
                def _bv(v, blk=blk, bld=bld):
                    qC.wait(v, bld)
                    if blk == 0:
                        cVE.inc(v.tensor_scalar(blkb[:, :, 0:1],
                                                scA.ap().unsqueeze(2),
                                                sel_second[:], None, Alu.mult))
                    else:
                        cVE.inc(v.tensor_copy(bsc[:, 0:1], bsc[:, 3:4]))
                ve(_bv)
                blv = cVE.n
                if dbg_attn and l == 0 and blk == 0:
                    def _tapb(s):
                        s.wait_ge(cVE.sem, blv)
                        qS.dma(s, out=dbg_attn["blkb0"][:], in_=blkb[:])
                    sy(_tapb)
                    _tm = qS.n
                    ve(lambda v, m=_tm: qS.wait(v, m))

                def shview(dc, h0, n):
                    off = (1 if dc < 4 else 0) + h0
                    return blkb[:, dc, off:off + n]

                te(lambda t, blv=blv: t.wait_ge(cVE.sem, blv))
                ac(lambda a, blv=blv: a.wait_ge(cVE.sem, blv))
                for half in range(2):
                    def _sx(t, half=half):
                        ii = None
                        for dc in range(DC):
                            ii = t.matmul(psR[0:1, :], ones_col[:],
                                          shview(dc, 512 * half, 512),
                                          start=(dc == 0), stop=(dc == DC - 1))
                        cTE.inc(ii)
                    te(_sx)
                    sxn = cTE.n
                    def _sxe(v, half=half, sxn=sxn):
                        v.wait_ge(cTE.sem, sxn)
                        cVE.inc(v.tensor_copy(sxrow[0:1, 512 * half:512 * half + 512],
                                              psR[0:1, :]))
                    ve(_sxe)
                    sxe = cVE.n
                    te(lambda t, sxe=sxe: t.wait_ge(cVE.sem, sxe))
                qm_mark = {}
                q_mark = {}
                for half in range(2):
                    for dc0 in range(0, DC, 2):
                        def _q(a, half=half, dc0=dc0):
                            for dc in (dc0, dc0 + 1):
                                k = half * DC + dc
                                xq = xsqm[:, 512 * (k % 2):512 * (k % 2) + 512]
                                if k >= 2:
                                    a.wait_ge(cTE.sem, qm_mark[k - 2])
                                cAC.inc(a.activation(xq,
                                                     shview(dc, 512 * half, 512),
                                                     Act.Square))
                                q_mark[k] = cAC.n
                        ac(_q)
                        for dc in (dc0, dc0 + 1):
                            k = half * DC + dc
                            xq = xsqm[:, 512 * (k % 2):512 * (k % 2) + 512]
                            def _qm(t, dc=dc, k=k, xq=xq):
                                t.wait_ge(cAC.sem, q_mark[k])
                                cTE.inc(t.matmul(psR[0:1, :], ones_col[:],
                                                 xq, start=(dc == 0),
                                                 stop=(dc == DC - 1)))
                            te(_qm)
                            qm_mark[k] = cTE.n
                    qmn = cTE.n
                    def _qe(v, half=half, qmn=qmn):
                        v.wait_ge(cTE.sem, qmn)
                        cVE.inc(v.tensor_copy(srow[0:1, 512 * half:512 * half + 512],
                                              psR[0:1, :]))
                    ve(_qe)
                    qe = cVE.n
                    te(lambda t, qe=qe: t.wait_ge(cVE.sem, qe))
                stn = cVE.n
                def _st(v):
                    v.wait_ge(cVE.sem, cVE.n)
                    cVE.inc(v.tensor_scalar(murow[:], sxrow[:], R1024, None,
                                            Alu.mult))
                    v.wait_ge(cVE.sem, cVE.n)
                    cVE.inc(v.tensor_tensor(rstrow[:], murow[:], murow[:],
                                            Alu.mult))
                    v.wait_ge(cVE.sem, cVE.n)
                    cVE.inc(v.scalar_tensor_tensor(rstrow[:], srow[:], R1024,
                                                   rstrow[:], Alu.mult,
                                                   Alu.subtract))
                ve(_st)
                stv = cVE.n
                def _sa(a):
                    a.wait_ge(cVE.sem, stv)
                    cAC.inc(a.activation(rstrow[:], rstrow[:], Act.Sqrt,
                                         bias=eps_col[0:1, :], scale=1.0))
                ac(_sa)
                san = cAC.n
                def _sr(v):
                    v.wait_ge(cAC.sem, san)
                    cVE.inc(v.reciprocal(rstrow[:], rstrow[:]))
                ve(_sr)
                srn = cVE.n
                if dbg_attn and l == 0 and blk == 0:
                    def _taps2(s):
                        s.wait_ge(cVE.sem, srn)
                        qS.dma(s, out=dbg_attn["rows0"][0:1, :], in_=sxrow[:])
                        qS.dma(s, out=dbg_attn["rows0"][1:2, :], in_=srow[:])
                        qS.dma(s, out=dbg_attn["rows0"][2:3, :], in_=murow[:])
                        qS.dma(s, out=dbg_attn["rows0"][3:4, :], in_=rstrow[:])
                    sy(_taps2)
                    _tm = qS.n
                    ve(lambda v, m=_tm: qS.wait(v, m))
                for half in range(2):
                    def _bc(t, half=half, srn=srn):
                        t.wait_ge(cVE.sem, srn)
                        t.matmul(psR[:, :], ones_row[:],
                                 murow[0:1, 512 * half:512 * half + 512],
                                 start=True, stop=True)
                        cTE.inc(t.matmul(psA[:, 0:512], ones_row[:],
                                         rstrow[0:1, 512 * half:512 * half + 512],
                                         start=True, stop=True))
                    te(_bc)
                    bcn = cTE.n
                    for dc in range(DC):
                        def _ap(v, dc=dc, half=half, bcn=bcn):
                            v.wait_ge(cTE.sem, bcn)
                            cVE.inc(v.tensor_tensor(gel[:, 0:512],
                                                    shview(dc, 512 * half, 512),
                                                    psR[:], Alu.subtract))
                            v.wait_ge(cVE.sem, cVE.n)
                            cVE.inc(v.scalar_tensor_tensor(
                                gel[:, 512:1024], gel[:, 0:512],
                                lvec[f"lng{l}"][:, dc:dc + 1],
                                psA[:, 0:512], Alu.mult, Alu.mult))
                            v.wait_ge(cVE.sem, cVE.n)
                            cVE.inc(v.tensor_scalar(
                                hnorm[:, dc, 512 * half:512 * half + 512],
                                gel[:, 512:1024], lvec[f"lnb{l}"][:, dc:dc + 1],
                                None, Alu.add))
                        ve(_ap)
                    apn = cVE.n
                    te(lambda t, apn=apn: t.wait_ge(cVE.sem, apn))
                hn_done = cVE.n

                def _w1l(g, i=0):
                    qG.dma(g, out=w1b[0][:],
                                       in_=w1r[:, :, 0:256].transpose([1, 0, 2]))
                gp(_w1l)
                w1_mark = {0: qG.n}
                h2_mark = {}
                gl_mark = {}
                ama_mark = {}
                w1mm_mark = {}
                for i in range(NBK):
                    if i + 1 < NBK:
                        def _w1n(g, i=i):
                            if i >= 1:
                                g.wait_ge(cTE.sem, w1mm_mark[i - 1])
                            qG.dma(g,
                                out=w1b[(i + 1) % 2][:],
                                in_=w1r[:, :, 256 * (i + 1):256 * (i + 1) + 256]
                                    .transpose([1, 0, 2]))
                        gp(_w1n)
                        w1_mark[i + 1] = qG.n
                    sa = S1024[(2 * i) % 3]
                    sg_ = S1024[(2 * i + 1) % 3]
                    def _ama(t, i=i, sa=sa, hn_done=hn_done):
                        qG.wait(t, w1_mark[i])
                        if i == 0:
                            t.wait_ge(cVE.sem, hn_done)
                        if i >= 2:
                            t.wait_ge(cAC.sem, gl_mark[i - 2])
                        ii = None
                        for half in range(2):
                            for dc in range(DC):
                                ii = t.matmul(sa[:, 512 * half:512 * half + 512],
                                              w1b[i % 2][:, dc, 0:128],
                                              hnorm[:, dc, 512 * half:512 * half + 512],
                                              start=(dc == 0), stop=(dc == DC - 1))
                        cTE.inc(ii)
                    te(_ama)
                    ama_mark[i] = cTE.n
                    def _amg(t, i=i, sg_=sg_):
                        if i >= 1:
                            t.wait_ge(cVE.sem, h2_mark[i - 1])
                        ii = None
                        for half in range(2):
                            for dc in range(DC):
                                ii = t.matmul(sg_[:, 512 * half:512 * half + 512],
                                              w1b[i % 2][:, dc, 128:256],
                                              hnorm[:, dc, 512 * half:512 * half + 512],
                                              start=(dc == 0), stop=(dc == DC - 1))
                        cTE.inc(ii)
                    te(_amg)
                    w1mm_mark[i] = cTE.n
                    amn = cTE.n
                    def _gl(a, i=i, amn=amn, sg_=sg_):
                        a.wait_ge(cTE.sem, amn)
                        if i >= 2:
                            a.wait_ge(cVE.sem, h2_mark[i - 2])
                        cAC.inc(a.activation(gb2[i % 2][:], sg_, Act.Gelu,
                                             bias=lvec[f"b1p{l}"][:, 2 * i + 1:2 * i + 2],
                                             scale=1.0))
                    ac(_gl)
                    gl_mark[i] = cAC.n
                    gln = cAC.n
                    def _h2(v, i=i, gln=gln, sa=sa):
                        v.wait_ge(cAC.sem, gln)
                        cVE.inc(v.scalar_tensor_tensor(
                            h2S[:, i, :], sa,
                            lvec[f"b1p{l}"][:, 2 * i:2 * i + 1], gb2[i % 2][:],
                            Alu.add, Alu.mult))
                    ve(_h2)
                    h2_mark[i] = cVE.n
                h2_done = cVE.n
                if dbg_attn and l == 0 and blk == 0:
                    def _tapm(s):
                        s.wait_ge(cVE.sem, h2_done)
                        qS.dma(s, out=dbg_attn["hn0"][:], in_=hnorm[:])
                        qS.dma(s, out=dbg_attn["h20"][:], in_=h2S[:])
                    sy(_tapm)
                    _tm = qS.n
                    ve(lambda v, m=_tm: qS.wait(v, m))

                def _w2l(g):
                    qG.dma(g, out=w2b[0][:],
                                       in_=w2r[:, :, 0:128].transpose([1, 0, 2]))
                gp(_w2l)
                w2_mark = {0: qG.n}
                w2e_mark = {}
                for dt in range(DC):
                    if dt + 1 < DC:
                        def _w2n(g, dt=dt):
                            if dt >= 1:
                                g.wait_ge(cTE.sem, w2mm_mark[dt - 1])
                            qG.dma(g, 
                                out=w2b[(dt + 1) % 2][:],
                                in_=w2r[:, :, 128 * (dt + 1):128 * (dt + 1) + 128]
                                    .transpose([1, 0, 2]))
                        if dt == 0:
                            w2mm_mark = {}
                        gp(_w2n)
                        w2_mark[dt + 1] = qG.n
                    slot = S1024[dt % 3]
                    def _wm(t, dt=dt, slot=slot, h2_done=h2_done):
                        qG.wait(t, w2_mark[dt])
                        if dt == 0:
                            t.wait_ge(cVE.sem, h2_done)
                        if dt >= 3:
                            t.wait_ge(cVE.sem, w2e_mark[dt - 3])
                        ii = None
                        for half in range(2):
                            for hb in range(NBK):
                                ii = t.matmul(slot[:, 512 * half:512 * half + 512],
                                              w2b[dt % 2][:, hb, :],
                                              h2S[:, hb, 512 * half:512 * half + 512],
                                              start=(hb == 0), stop=(hb == NBK - 1))
                        cTE.inc(ii)
                    te(_wm)
                    w2mm_mark[dt] = cTE.n
                    wmn = cTE.n
                    def _we(v, dt=dt, slot=slot, wmn=wmn):
                        v.wait_ge(cTE.sem, wmn)
                        cVE.inc(v.scalar_tensor_tensor(
                            blkb[:, dt, 1:1025], slot,
                            lvec[f"b2{l}"][:, dt:dt + 1],
                            blkb[:, dt, 1:1025],
                            Alu.add, Alu.add))
                    ve(_we)
                    w2e_mark[dt] = cVE.n
                blk_done = cVE.n

                if not is_c2:
                    def _spb(s, base=base, blk_done=blk_done):
                        s.wait_ge(cVE.sem, blk_done)
                        qS.dma(s, out=sp_xl1[:, :, base:base + 1024],
                                           in_=blkb[:, :, 1:1025])
                    sy(_spb)
                    if fire_cc2 and blk == 1:
                        def _cc2v(v):
                            v.wait_ge(cVE.sem, cVE.n)
                            ii = None
                            for dc in range(DC):
                                ii = v.tensor_scalar(
                                    ccb[:, dc, :],
                                    blkb[:, dc, 1025 - HALO:1025],
                                    sel_first[:], None, Alu.mult)
                            cVE.inc(ii)
                        ve(_cc2v)
                        c2v = cVE.n
                        def _cc2g(g):
                            g.wait_ge(cVE.sem, c2v)
                            qG.dma(g, out=cc2_in[:], in_=ccb[:])
                            qG.wait(g, qG.n)
                            cCC.inc(g.collective_compute(
                                "AllReduce", Alu.add, replica_groups=RG,
                                ins=[cc2_in.opt()], outs=[cc2_out.opt()]))
                        gp(_cc2g)
                        cc2_mark[0] = cCC.n
                else:
                    def _sl(a, base=base, blk_done=blk_done):
                        a.wait_ge(cVE.sem, blk_done)
                        qC.dma(a, out=sbuf_s[:],
                                           in_=sp_s[:, :, base:base + 1024])
                    ac(_sl)
                    sld = qC.n
                    fin_mark = {}
                    for dc in range(DC):
                        def _gs(a, dc=dc, blk_done=blk_done):
                            a.wait_ge(cVE.sem, blk_done)
                            if dc >= 1:
                                a.wait_ge(cVE.sem, fin_mark[dc - 1])
                            cAC.inc(a.activation(gbuf[:],
                                                 blkb[:, dc, 1:1025],
                                                 Act.Silu))
                        ac(_gs)
                        gsn = cAC.n
                        def _gm(v, dc=dc, gsn=gsn, sld=sld):
                            v.wait_ge(cAC.sem, gsn)
                            qC.wait(v, sld)
                            cVE.inc(v.tensor_tensor(sbuf_s[:, dc, :], sbuf_s[:, dc, :],
                                                    gbuf[:], Alu.mult))
                        ve(_gm)
                        fin_mark[dc] = cVE.n
                    sg_done = cVE.n
                    out_mark = {}
                    qs_prev = qS.n
                    for ts in range(8):
                        def _xl(a, ts=ts, base=base, qs_prev=qs_prev):
                            if ts >= 4:
                                qS.wait(a, out_mark[ts - 4])
                            else:
                                qS.wait(a, qs_prev)
                            qC.dma(a, 
                                out=xfin[:, ts % 4, :],
                                in_=xr_own[(base + ts * 128) // 128])
                        ac(_xl)
                        xln = qC.n
                        def _ft(t, ts=ts, sg_done=sg_done):
                            t.wait_ge(cVE.sem, sg_done)
                            ii = None
                            for dc in range(DC):
                                ii = t.transpose(psA[:, dc * 128:dc * 128 + 128],
                                                 sbuf_s[:, dc, ts * 128:ts * 128 + 128],
                                                 id_f32[:])
                            cTE.inc(ii)
                        te(_ft)
                        ftn = cTE.n
                        def _fe(v, ts=ts, ftn=ftn, xln=xln):
                            v.wait_ge(cTE.sem, ftn)
                            qC.wait(v, xln)
                            cVE.inc(v.tensor_tensor(
                                xfin.ap()[:, ts % 4, :].rearrange("p (c f) -> p c f", f=128),
                                psA3(), xfin.ap()[:, ts % 4, :]
                                .rearrange("p (c f) -> p c f", f=128), Alu.add))
                        ve(_fe)
                        fen = cVE.n
                        te(lambda t, fen=fen: t.wait_ge(cVE.sem, fen))
                        def _fo(s, ts=ts, base=base, fen=fen):
                            s.wait_ge(cVE.sem, fen)
                            qS.dma(s, 
                                out=out_h[base + ts * 128:base + ts * 128 + 128, :],
                                in_=xfin[:, ts % 4, :])
                        sy(_fo)
                        out_mark[ts] = qS.n
            barrier()
            mx.close()
            return cc2_mark[0]

        def emit_c1():
            cx = ExitStack()
            woutS = sb(cx, "woutS", (128, IC, D), BF16)
            rbuf = [sb(cx, f"rbuf{i}", (128, 5, CH)) for i in range(2)]
            fused2 = [sb(cx, f"fusedC{i}", (128, IC, CH), BF16) for i in range(2)]
            xnr2 = [sb(cx, f"xnrC{i}", (128, DC, CH)) for i in range(2)]
            sres = sb(cx, "sresC", (128, DC, CH))

            def _wg(g):
                wv = h_["w_out"].rearrange("(c p) f -> c p f", p=128)
                qG.dma(g, out=woutS[:], in_=wv.transpose([1, 0, 2]))
            gp(_wg)
            wg = qG.n
            def _r12(v):
                v.wait_ge(cCC.sem, cc1_done)
            ve(_r12)
            def _r12l(a):
                a.wait_ge(cCC.sem, cc1_done)
                qC.dma(a, out=R1e[:], in_=cc1_out[:, 0:IC])
                qC.dma(a, out=R2e[:], in_=cc1_out[:, IC:2 * IC])
            ac(_r12l)
            rld = qC.n
            def _r12v(v):
                qC.wait(v, rld)
                v.tensor_scalar(R1e[:], R1e[:], sel_second[:], None, Alu.mult)
                cVE.inc(v.tensor_scalar(R2e[:], R2e[:], sel_first[:], None, Alu.mult))
            ve(_r12v)
            r12 = cVE.n

            fuse_mark = {}
            spill_s_mark = {}
            we_mark = {}
            wm_done = {}
            we_done = {}
            xr_mark = {}

            def emit_ld_fu(c, ct):
                def _ld(a, c=c, ct=ct):
                    gk = (c, ct - 2) if ct >= 2 else (c - 1, ct + 14)
                    if gk in fuse_mark:
                        a.wait_ge(cVE.sem, fuse_mark[gk])
                    sl = slice(c * CH, (c + 1) * CH)
                    bf_ = rbuf[ct % 2]
                    qC.dma(a, out=bf_[:, 0, :], in_=sp_fw[:, ct, sl])
                    qC.dma(a, out=bf_[:, 1, :], in_=sp_pf[:, ct, sl])
                    qC.dma(a, out=bf_[:, 2, :], in_=sp_bw[:, ct, sl])
                    qC.dma(a, out=bf_[:, 3, :], in_=sp_pb[:, ct, sl])
                    qC.dma(a, out=bf_[:, 4, :], in_=sp_sg[:, ct, sl])
                ac(_ld)
                ldn = qC.n
                def _fu(v, c=c, ct=ct, ldn=ldn):
                    qC.wait(v, ldn)
                    if ct == 0 and c - 2 in wm_done:
                        v.wait_ge(cTE.sem, wm_done[c - 2])
                    if (c, ct) == (0, 0):
                        v.wait_ge(cVE.sem, r12)
                    bf_ = rbuf[ct % 2]
                    v.scalar_tensor_tensor(bf_[:, 0, :], bf_[:, 1, :],
                                           R1e[:, ct:ct + 1], bf_[:, 0, :],
                                           Alu.mult, Alu.add)
                    cVE.inc(v.scalar_tensor_tensor(bf_[:, 2, :], bf_[:, 3, :],
                                                   R2e[:, ct:ct + 1],
                                                   bf_[:, 2, :],
                                                   Alu.mult, Alu.add))
                    v.wait_ge(cVE.sem, cVE.n)
                    cVE.inc(v.tensor_tensor(bf_[:, 0, :], bf_[:, 0, :],
                                            bf_[:, 2, :], Alu.add))
                    v.wait_ge(cVE.sem, cVE.n)
                    cVE.inc(v.tensor_tensor(fused2[c % 2][:, ct, :], bf_[:, 0, :],
                                            bf_[:, 4, :], Alu.mult))
                ve(_fu)
                fuse_mark[(c, ct)] = cVE.n

            def emit_xr(c):
                def _xr(a, c=c):
                    if c - 2 in we_done:
                        a.wait_ge(cVE.sem, we_done[c - 2])
                    qC.dma(a, out=xnr2[c % 2][:],
                           in_=sp_xn[:, :, c * CH:(c + 1) * CH])
                ac(_xr)
                xr_mark[c] = qC.n

            def emit_wout_dt(c, dt):
                slot = S512[dt % 6]
                def _wm(t, c=c, dt=dt, slot=slot, wg=wg):
                    if (c, dt) == (0, 0):
                        qG.wait(t, wg)
                    t.wait_ge(cVE.sem, fuse_mark[(c, IC - 1)])
                    if dt >= 6:
                        t.wait_ge(cVE.sem, we_mark[(c, dt - 6)])
                    elif c >= 1:
                        t.wait_ge(cVE.sem, we_done[c - 1])
                    ii = None
                    for ct in range(IC):
                        ii = t.matmul(slot, woutS[:, ct, dt * 128:dt * 128 + 128],
                                      fused2[c % 2][:, ct, :], start=(ct == 0),
                                      stop=(ct == IC - 1))
                    cTE.inc(ii)
                te(_wm)
                wmn = cTE.n
                def _we(v, c=c, dt=dt, slot=slot, wmn=wmn):
                    v.wait_ge(cTE.sem, wmn)
                    qC.wait(v, xr_mark[c])
                    if c >= 1:
                        qS.wait(v, spill_s_mark[c - 1])
                    cVE.inc(v.scalar_tensor_tensor(sres[:, dt, :], slot,
                                                   b_out[:, dt:dt + 1],
                                                   xnr2[c % 2][:, dt, :],
                                                   Alu.add, Alu.add))
                ve(_we)
                we_mark[(c, dt)] = cVE.n
                if dt == DC - 1:
                    we_done[c] = cVE.n
                    wm_done[c] = cTE.n
                    sdone = cVE.n
                    def _ssp(s, c=c, sdone=sdone):
                        s.wait_ge(cVE.sem, sdone)
                        qS.dma(s, out=sp_s[:, :, c * CH:(c + 1) * CH],
                               in_=sres[:])
                    sy(_ssp)
                    spill_s_mark[c] = qS.n

            for c in range(NCH):
                for ct in range(IC):
                    emit_ld_fu(c, ct)
                    if ct == 0:
                        emit_xr(c)
                    if c >= 1 and ct % 2 == 1:
                        emit_wout_dt(c - 1, ct // 2)
            for dt in range(DC):
                emit_wout_dt(NCH - 1, dt)
            barrier()
            cx.close()

        xr_own = h_["x_own"].rearrange("(a p) d -> a p d", p=128)

        _sub = {"l1s1": 1, "l1s2": 2, "l1s3": 3, "l1s4": 4,
                "l1s4a": 4.7, "l1s5": 5.5}.get(upto, 6)
        cca0 = emit_attn(0, sp_xn, False, sub=_sub)
        if upto in ("l1s1", "l1s2", "l1s3", "l1s4", "l1s4a", "l1s5"):
            _endprog()
            return nc
        if dbg_attn:
            def _tapx(g):
                qG.dma(g, out=dbg_attn["xat1"][:], in_=sp_xat[:])
                qG.wait(g, qG.n)
            gp(_tapx)
        if upto == "l1a":
            _endprog()
            return nc
        cc2m = emit_mlp(0, False, cca0, True)
        if upto == "l1m":
            _endprog()
            return nc
        emit_c1()
        if upto == "c1":
            _endprog()
            return nc
        def _sd2(a):
            a.wait_ge(cCC.sem, cc2m)
            qC.dma(a, out=sideL2[:], in_=cc2_out[:, :, 515:516],
                               allow_slow_non_contiguous=True)
        ac(_sd2)
        sd2 = qC.n
        def _sd2v(v):
            qC.wait(v, sd2)
            cVE.inc(v.tensor_scalar(sideL2[:], sideL2[:], sel_second[:],
                                    None, Alu.mult))
        ve(_sd2v)
        cca1 = emit_attn(1, sp_xl1, True, cc2_wait=cc2m)
        if upto == "l2a":
            _endprog()
            return nc
        emit_mlp(1, True, cca1, False)

        if dbg_h:
            def _tap(g):
                for nm, (dst, src) in dbg_h.items():
                    qG.dma(g, out=dst[:], in_=src[:])
                qG.wait(g, qG.n)
            gp(_tap)

        barrier()
        def _fin(s):
            qS.wait(s, qS.n)
        sy(_fin)

    return nc


_NC_CACHE = {}


def kernel(**inputs):
    maps = prepare_inputs(inputs)
    key = "main"
    if key not in _NC_CACHE:
        _NC_CACHE[key] = build_nc()
    nc = _NC_CACHE[key]
    res = run_bass_kernel_spmd(nc, maps, core_ids=list(range(8)))
    out = np.empty((4, 4096, 1024), np.float32)
    for c in range(8):
        b, h = c // 2, c % 2
        out[b, h * T:(h + 1) * T] = res.results[c]["out"]
    return out
